# version 04
# baseline (speedup 1.0000x reference)
import contextlib
import math

import numpy as np
import ml_dtypes

import concourse.bass as bass
import concourse.mybir as mybir
from concourse.alu_op_type import AluOpType as OP

AF = mybir.ActivationFunctionType
F32 = mybir.dt.float32
BF16 = mybir.dt.bfloat16
I32 = mybir.dt.int32

NEG = -1.0e30
B_EXP = 8.0
LN_EPS = 1e-5
Q_EPS = 1e-5
QTW = 512
SQ = 256


class Prm:
    def __init__(self, B=4, T=2048, D=2048, H=16, I=8192):
        self.B, self.T, self.D, self.H, self.I = B, T, D, H, I
        self.NC = 2 * B
        self.TOKC = T // 2
        self.NQT = self.TOKC // QTW
        self.NFC = D // 128
        self.NKT = T // 128
        assert D == H * 128 and self.TOKC % QTW == 0 and (120 * H) % 128 == 0
        self.FD = 3 * D
        self.NFCF = self.FD // 128
        self.R = 8
        self.DSTD = 120
        self.qs_end = H * 120
        self.ks_end = 2 * H * 120
        self.v_end = self.ks_end + D
        self.ql_end = self.v_end + 8 * H
        self.ktmax = [(self.TOKC + (qt + 1) * QTW) // 128 for qt in range(self.NQT)]
        self.NDT = sum(self.ktmax)
        self.E8 = D // self.NC
        self.I8 = I // self.NC
        self.NIC = I // 128
        self.slabs_qkv_q = mk_slabs(0, D)
        self.slabs_qkv_k = mk_slabs(D, 2 * D)
        self.slabs_qkv_v = mk_slabs(2 * D, 3 * D)
        self.slabs_wa_q = mk_slabs(0, self.qs_end)
        self.slabs_wa_k = mk_slabs(self.qs_end, self.ks_end)
        self.slabs_wa_v = mk_slabs(self.ks_end, self.v_end)
        self.slabs_wa_t = [(self.v_end, self.FD)]
        self.slabs_d = mk_slabs(0, D)


def mk_slabs(a, b, extra=()):
    bnds = sorted({x for x in range(0, b + 1, 512) if a < x < b} | {a, b} | {x for x in extra if a < x < b})
    return [(bnds[i], bnds[i + 1]) for i in range(len(bnds) - 1)]


def _slopes(H):
    return [2.0 ** (-8.0 / (i + 1)) for i in range(H)]



def perm_w(w, slabs):
    D, F = w.shape
    ndc = D // 128
    w3 = w.reshape(ndc, 128, F)
    parts = []
    for (a, b) in slabs:
        parts.append(np.ascontiguousarray(w3[:, :, a:b].transpose(1, 0, 2)).reshape(-1))
    return np.concatenate(parts)


def mk_dtiles(prm, qoff, alibi):
    tiles = np.empty((prm.NDT, 128, QTW), dtype=np.float32)
    dk = np.arange(128)[:, None]
    dq = np.arange(QTW)[None, :]
    i = 0
    for qt in range(prm.NQT):
        for kt in range(prm.ktmax[qt]):
            k = kt * 128 + dk
            q = qoff + qt * QTW + dq
            base = (k - q).astype(np.float32) if alibi else np.zeros((128, QTW), np.float32)
            tiles[i] = np.where(k <= q, base, NEG)
            i += 1
    return tiles.astype(ml_dtypes.bfloat16)


def host_prepare(prm, inputs):
    p = prm
    x = np.asarray(inputs["L0_attn_x"], np.float32)
    wqkv = perm_w(np.asarray(inputs["L0_attn_W_qkv"], np.float32),
                  p.slabs_qkv_q + p.slabs_qkv_k + p.slabs_qkv_v)
    wo = perm_w(np.asarray(inputs["L0_attn_W_o"], np.float32), p.slabs_d)
    wa = perm_w(np.asarray(inputs["L0_ffn_Wa"], np.float32),
                p.slabs_wa_q + p.slabs_wa_k + p.slabs_wa_v + p.slabs_wa_t)
    wp = perm_w(np.asarray(inputs["L0_ffn_Wp"], np.float32), p.slabs_d)
    ident = np.eye(128, dtype=np.float32)
    mats = [np.asarray(inputs[k], np.float32) for k in
            ("L1_attn_Wq", "L1_attn_Wk", "L1_attn_Wv", "L1_attn_Wo")]
    wg = np.asarray(inputs["L1_ffn_w_gate"], np.float32)
    wu = np.asarray(inputs["L1_ffn_w_up"], np.float32)
    wd = np.asarray(inputs["L1_ffn_w_down"], np.float32)
    wstd = np.asarray(inputs["L0_ffn_w_std"], np.float32).reshape(1, p.H)
    wrec = np.asarray(inputs["L0_ffn_w_rec"], np.float32).reshape(1, p.H)
    qk = 128.0 ** -0.25
    nsv = 1 + 2 * p.H
    sel = np.zeros((nsv, p.NFCF * 128), np.float32)
    for j in range(p.FD):
        if j < p.qs_end:
            sel[1 + j // 120, j] = qk
        elif j < p.ks_end:
            sel[1 + (j - p.qs_end) // 120, j] = qk
        elif j < p.v_end:
            sel[0, j] = 1.0
        elif j < p.ql_end:
            sel[1 + p.H + (j - p.v_end) // p.R, j] = qk
        else:
            sel[1 + p.H + (j - p.ql_end) // p.R, j] = qk
    rep8 = np.zeros((p.H, 128), np.float32)
    for pp_ in range(8 * p.H):
        rep8[pp_ // 8, pp_] = 1.0
    in_maps = []
    for c in range(p.NC):
        b, hf = c // 2, c % 2
        qoff = hf * p.TOKC
        in_maps.append({
            "x_my": np.ascontiguousarray(x[b, qoff:qoff + p.TOKC, :]),
            "wqkv_p": wqkv, "wo_p": wo, "wa_p": wa, "wp_p": wp,
            "ident": ident, "w_std": wstd, "w_rec": wrec,
            "is_b": np.full((1, 1), float(hf), np.float32),
            "sel": sel, "rep8": rep8,
            "dA": mk_dtiles(p, qoff, True),
            "dM": mk_dtiles(p, qoff, False),
            "wq1_s": np.ascontiguousarray(mats[0][c * p.E8:(c + 1) * p.E8, :]),
            "wk1_s": np.ascontiguousarray(mats[1][c * p.E8:(c + 1) * p.E8, :]),
            "wv1_s": np.ascontiguousarray(mats[2][c * p.E8:(c + 1) * p.E8, :]),
            "wo1_s": np.ascontiguousarray(mats[3][c * p.E8:(c + 1) * p.E8, :]),
            "wg_s": np.ascontiguousarray(wg[c * p.I8:(c + 1) * p.I8, :]),
            "wu_s": np.ascontiguousarray(wu[c * p.I8:(c + 1) * p.I8, :]),
            "wd_s": np.ascontiguousarray(wd[c * p.E8:(c + 1) * p.E8, :]),
        })
    return in_maps



ENGS = ("tensor", "vector", "scalar", "sync", "gpsimd")
SEMS = ("pe", "dve", "act", "spq", "plq", "cc", "ini",
        "s_in", "s_ins", "s_pq", "s_pq2", "s_wf", "s_msc", "s_cm", "s_out",
        "s_dsb", "s_tl", "s_xt0", "s_xt1", "s_ws0", "s_ws1", "s_kv0", "s_kv1",
        "s_evb0", "s_evb1", "s_w10", "s_w11", "s_w12", "s_wg0", "s_wg1",
        "s_fst0", "s_fst1", "s_fst2", "s_fst3")


class Bld:
    def __init__(self, nc):
        self.nc = nc
        self.q = {e: [] for e in ENGS}
        self.cnt = {s: 0 for s in SEMS}
        self.consts = set()
        self._cm = {}

    def sb(self, name, shape, dt):
        cm = self.nc.sbuf_tensor(name, list(shape), dt)
        t = cm.__enter__()
        self._cm[name] = cm
        return t

    def free(self, *names):
        order = list(self._cm)
        for n in sorted(names, key=order.index, reverse=True):
            self._cm.pop(n).__exit__(None, None, None)

    def ps(self, name, shape, dt):
        cm = self.nc.psum_tensor(name, list(shape), dt)
        t = cm.__enter__()
        self._cm["psum_" + name] = cm
        return t

    def K(self, v):
        v = float(v)
        self.consts.add(v)
        return v

    def emit(self, eng, fn, waits=(), inc=None, amt=1):
        self.q[eng].append((tuple(w for w in waits if w is not None and w[1] > 0), fn, inc, amt))
        if inc is not None:
            self.cnt[inc] += amt
            return self.cnt[inc]
        return None

    def w(self, sem):
        return (sem, self.cnt[sem])

    def dma(self, qname, out, in_, waits=(), sem=None, **kw):
        if sem is None:
            sem = "spq" if qname == "sync" else "plq"
        self.emit(qname, lambda e, o=out, i=in_, kw=kw: e.dma_start(out=o, in_=i, **kw),
                  waits=waits, inc=sem, amt=16)
        return self.cnt[sem]

    def dmaT(self, out, in_, waits=(), sem="spq"):
        self.emit("sync", lambda e, o=out, i=in_: e.dma_start_transpose(o, i),
                  waits=waits, inc=sem, amt=16)
        return self.cnt[sem]

    def mm(self, outap, lh, rh, start, stop, waits=()):
        return self.emit("tensor", lambda e, o=outap, a=lh, b=rh, s1=start, s2=stop:
                         e.matmul(o, a, b, start=s1, stop=s2), waits=waits, inc="pe")

    def vo(self, fn, waits=(), drain=False):
        if drain:
            self.emit("vector", lambda e: e.drain())
        return self.emit("vector", fn, waits=waits, inc="dve")

    def ao(self, fn, waits=()):
        return self.emit("scalar", fn, waits=waits, inc="act")

    def fence(self, engines, waits):
        for eng in engines:
            self.emit(eng, lambda e: e.nop(), waits=waits)

    def coll(self, kind, op, groups, ins, outs, waits=()):
        return self.emit("gpsimd", lambda e: e.collective_compute(
            kind, op, replica_groups=groups, ins=ins, outs=outs), waits=waits, inc="cc")



def build(prm, debug=False):
    p = prm
    nc = bass.Bass("TRN2", target_bir_lowering=False)
    H, D, I = p.H, p.D, p.I
    TOKC, NQT, NFC, NKT = p.TOKC, p.NQT, p.NFC, p.NKT
    NTT = TOKC // 128
    E8, I8, NIC = p.E8, p.I8, p.NIC
    NC = p.NC
    NSQ = TOKC // SQ
    I8h = I8 // 2
    I2h = I8h // 128
    E2 = E8 // 128
    NICh = NIC // 2
    WSW = max(NFC * 512, NFC * I8h, NICh * E8)
    inv_sqrt_dh = 1.0 / math.sqrt(128.0)
    qk_quarter = 128.0 ** -0.25
    denom_l1 = math.sqrt(float(D)) / float(p.T)
    slopes = _slopes(H)

    din = {}
    for nm, shape, dt in [
        ("x_my", [TOKC, D], F32), ("wqkv_p", [D * 3 * D], F32), ("wo_p", [D * D], F32),
        ("wa_p", [D * 3 * D], F32), ("wp_p", [D * D], F32), ("ident", [128, 128], F32),
        ("w_std", [1, H], F32), ("w_rec", [1, H], F32), ("is_b", [1, 1], F32),
        ("dA", [p.NDT, 128, QTW], BF16), ("dM", [p.NDT, 128, QTW], BF16),
        ("wq1_s", [E8, D], F32), ("wk1_s", [E8, D], F32), ("wv1_s", [E8, D], F32),
        ("wo1_s", [E8, D], F32), ("wg_s", [I8, D], F32), ("wu_s", [I8, D], F32),
        ("wd_s", [E8, I], F32), ("sel", [1 + 2 * H, p.NFCF * 128], F32),
        ("rep8", [H, 128], F32),
    ]:
        din[nm] = nc.dram_tensor(nm, shape, dt, kind="ExternalInput")
    out_my = nc.dram_tensor("out_my", [TOKC, D], F32, kind="ExternalOutput")

    KVW = max(H * TOKC, NTT * D)
    kv_snd = [nc.dram_tensor(f"kv_snd{i}", [2, 128, KVW], BF16) for i in range(2)]
    kv_out = [nc.dram_tensor(f"kv_out{i}", [2, 2, 128, KVW], BF16) for i in range(2)]
    off_snd = nc.dram_tensor("off_snd", [128, NFC], F32)
    off_out = nc.dram_tensor("off_out", [2, 128, NFC], F32)
    ar_snd = nc.dram_tensor("ar_snd", [1, 4], F32)
    ar_out = nc.dram_tensor("ar_out", [1, 4], F32)
    tmp_qkvo = nc.dram_tensor("tmp_qkvo", [4, E8, D], BF16)
    tmp_gu = nc.dram_tensor("tmp_gu", [2, I8, D], BF16)
    tmp_d = nc.dram_tensor("tmp_d", [E8, I], BF16)
    PR_QKVO, PR_GU, PR_D = D * E8, D * I8, I * E8
    PR_ALL = 4 * PR_QKVO + 2 * PR_GU + PR_D
    OFF_M = [0, PR_QKVO, 2 * PR_QKVO, 3 * PR_QKVO, 4 * PR_QKVO,
             4 * PR_QKVO + PR_GU, 4 * PR_QKVO + 2 * PR_GU]
    wT_snd = nc.dram_tensor("wT_snd", [PR_ALL], BF16)
    wT_out = nc.dram_tensor("wT_out", [NC, PR_ALL], BF16)
    sm_dram = nc.dram_tensor("sm_dram", [128, 4], F32)
    dbg_kind = dict(kind="ExternalOutput") if debug else {}
    dbg = [nc.dram_tensor(f"dbg_x{i}", [128, NFC * TOKC], BF16, **dbg_kind) for i in (1, 2, 3)]
    dbg_xq = nc.dram_tensor("dbg_xq", [128, (D // 128) * SQ], BF16, **dbg_kind)
    dbg_gu = nc.dram_tensor("dbg_gu", [128, (I // 128) * SQ], BF16, **dbg_kind)
    dbg_gs = nc.dram_tensor("dbg_gs", [1, 8 * QTW], F32, **dbg_kind)
    dbg_rb = nc.dram_tensor("dbg_rb", [128, SQ], F32, **dbg_kind)
    dbg_cap = nc.dram_tensor("dbg_cap", [128, 4 * SQ], F32, **dbg_kind)

    groups_pair = [[2 * i, 2 * i + 1] for i in range(NC // 2)]
    groups_all = [list(range(NC))]

    bd = Bld(nc)
    K = bd.K
    st = {"ydve": 0, "lnprev": 0, "rb_pe": 0, "ab_pe": 0}

    cpool = bd.sb("cpool", (128, 64), F32)
    cpool_b = bd.sb("cpool_b", (128, 64), BF16)
    ones_b = bd.sb("ones_b", (128, 1), BF16)
    onesr_f = bd.sb("onesr_f", (1, 128), F32)
    ones_f = bd.sb("ones_f", (128, 1), F32)
    ident_b = bd.sb("ident_b", (128, 128), BF16)
    xs = bd.sb("xs", (128, NFC * TOKC), BF16)
    stg = bd.sb("stg", (128, 2 * QTW), F32)
    stg2 = bd.sb("stg2", (128, 2 * QTW), F32)
    pbuf = bd.sb("pbuf", (128, 2 * QTW), BF16)
    hb = bd.sb("hb", (128, NFC * QTW), BF16)
    hsq = bd.sb("hsq", (128, 2 * QTW), BF16)
    rows = bd.sb("rows", (1, 6 * QTW), F32)
    rowm = bd.sb("rowm", (1, 2 * QTW), F32)
    srow = bd.sb("srow", (1, 128), F32)
    small = bd.sb("small", (128, 32), F32)
    evb = bd.sb("evb", (128, 2 * 512), BF16)
    i32s = bd.sb("i32s", (128, 1024), I32)
    csvec = bd.sb("csvec", (128, p.NFCF), F32)
    u128 = bd.sb("u128", (128, 2), F32)
    rbv = bd.sb("rbv", (128, QTW), F32)
    kvsb = bd.sb("kvsb", (128, QTW), F32)
    csc = bd.sb("csc", (128, 3 * SQ), F32)
    dbgsb = bd.sb("dbgsb", (128, 2 * SQ), F32)

    ps0 = bd.ps("ps0", (128, 512), F32)
    ps1 = bd.ps("ps1", (128, 512), F32)
    ps2 = bd.ps("ps2", (128, 512), F32)
    ps3 = bd.ps("ps3", (128, 512), F32)
    ps4 = bd.ps("ps4", (128, 512), F32)
    ps5 = bd.ps("ps5", (128, 512), F32)
    psS = bd.ps("psS", (128, 512), F32)
    psS2 = bd.ps("psS2", (128, 512), F32)
    pp = [ps0, ps1]
    ptr = [ps4, ps5]

    INI = ("ini", 1)

    xtm = bd.sb("xtm", (128, 2 * D), BF16)
    spi = bd.dma("gpsimd", ident_b[:, :], din["ident"][:, :], waits=(INI,), sem="s_in")
    for tt in range(NTT):
        sp = bd.dma("gpsimd", xtm[:, (tt % 2) * D:(tt % 2) * D + D],
                    din["x_my"][tt * 128:(tt + 1) * 128, :], sem=f"s_xt{tt % 2}",
                    waits=(INI,) + ((("act", st.get(("xg", tt - 2), 0)),) if tt >= 2 else ()))
        for fc in range(NFC):
            t = ptr[fc % 2]
            pe_m = bd.mm(t[:, 0:128],
                         xtm[:, (tt % 2) * D + fc * 128:(tt % 2) * D + fc * 128 + 128],
                         ident_b[:, :], True, True,
                         waits=(((f"s_xt{tt % 2}", sp), ("s_in", spi)) if fc == 0 else ()) +
                               (("act", st.get(("xb", fc % 2), 0)),))
            st[("xb", fc % 2)] = bd.ao(
                lambda e, t=t, dst=xs[:, fc * TOKC + tt * 128:fc * TOKC + tt * 128 + 128]:
                e.copy(dst, t[:, 0:128]), waits=(("pe", pe_m),))
        st[("xg", tt)] = bd.cnt["act"]
    bd.free("xtm")
    act_x = bd.cnt["act"]
    bd.fence(("gpsimd", "vector"), (bd.w("pe"),))

    wtmp = bd.sb("wtmp", (128, 2048), BF16)
    wf32 = bd.sb("wf32", (128, 2048), F32)
    racc = bd.sb("racc", (128, 16), F32)
    for mi, nm in enumerate(("wq1_s", "wk1_s", "wv1_s", "wo1_s")):
        for rt in range(E8 // 128):
            c1 = bd.dma("gpsimd", wtmp[:, 0:D], din[nm][rt * 128:(rt + 1) * 128, 0:D],
                        waits=(INI, bd.w("s_pq")))
            bd.dma("gpsimd", tmp_qkvo[mi, rt * 128:(rt + 1) * 128, :], wtmp[:, 0:D],
                   waits=(("plq", c1),), sem="s_pq")
    for mi, (nm, rows_n, cols) in enumerate((("wg_s", I8, D), ("wu_s", I8, D), ("wd_s", E8, I))):
        CW = min(2048, cols)
        k = 0
        for rt in range(rows_n // 128):
            for ccc in range(cols // CW):
                c1 = bd.dma("gpsimd", wf32[:, 0:CW],
                            din[nm][rt * 128:(rt + 1) * 128, ccc * CW:(ccc + 1) * CW],
                            waits=(bd.w("dve"),), sem="s_wf")
                if k < 15:
                    bd.vo(lambda e, k=k, cw=CW: e.reduce_sum(racc[:, k:k + 1], wf32[:, 0:cw],
                                                             axis=mybir.AxisListType.X,
                                                             apply_absolute_value=True),
                          waits=(("s_wf", c1),))
                else:
                    bd.vo(lambda e, cw=CW: e.reduce_sum(small[:, 0:1], wf32[:, 0:cw],
                                                        axis=mybir.AxisListType.X,
                                                        apply_absolute_value=True),
                          waits=(("s_wf", c1),))
                    bd.vo(lambda e, k=k: e.tensor_tensor(racc[:, k % 15:k % 15 + 1],
                                                         racc[:, k % 15:k % 15 + 1],
                                                         small[:, 0:1], OP.add), drain=True)
                k += 1
        nacc = min(k, 15)
        dvm = bd.vo(lambda e, nacc=nacc: e.reduce_sum(racc[:, 15:16], racc[:, 0:nacc],
                                                      axis=mybir.AxisListType.X),
                    waits=(("pe", st.get("pqs_pe", 0)),), drain=True)
        pe_m = bd.mm(psS[0:1, mi:mi + 1], ones_f[:, :], racc[:, 15:16], True, True,
                     waits=(("dve", dvm),))
        st["pqs_pe"] = pe_m
        bd.ao(lambda e, mi=mi: e.copy(srow[0:1, mi:mi + 1], psS[0:1, mi:mi + 1]),
              waits=(("pe", pe_m),))
    act_part = bd.cnt["act"]
    c1 = bd.dma("gpsimd", ar_snd[0:1, 0:4], srow[0:1, 0:4], waits=(("act", act_part),))
    cc_ar = bd.coll("AllReduce", OP.add, groups_all, [ar_snd.ap().opt()], [ar_out.ap().opt()],
                    waits=(("plq", c1),))
    c2 = bd.dma("gpsimd", srow[0:1, 4:8], ar_out[0:1, 0:4], waits=(("cc", cc_ar),), sem="s_msc")
    K(Q_EPS)
    dmean = K(1.0 / (I * D))
    bd.vo(lambda e: e.tensor_scalar(srow[0:1, 8:11], srow[0:1, 4:7], dmean, Q_EPS,
                                    OP.mult, OP.add), waits=(("s_msc", c2),))
    d2 = bd.vo(lambda e: e.reciprocal(srow[0:1, 12:15], srow[0:1, 8:11]), drain=True)
    pe_rs = None
    for mi in range(3):
        pe_rs = bd.mm(psS2[:, mi:mi + 1], onesr_f[0:1, :], srow[0:1, 12 + mi:13 + mi],
                      mi == 0, mi == 2, waits=(("dve", d2),) if mi == 0 else ())
    a_rs = bd.ao(lambda e: e.copy(small[:, 8:11], psS2[:, 0:3]), waits=(("pe", pe_rs),))
    K(1.0); K(-1.0)
    for mi, (nm, rows_n, cols) in enumerate((("wg_s", I8, D), ("wu_s", I8, D), ("wd_s", E8, I))):
        CW = min(2048, cols)
        for rt in range(rows_n // 128):
            for ccc in range(cols // CW):
                c1 = bd.dma("gpsimd", wf32[:, 0:CW],
                            din[nm][rt * 128:(rt + 1) * 128, ccc * CW:(ccc + 1) * CW],
                            waits=(bd.w("dve"), bd.w("plq")), sem="s_wf")
                d3 = None
                spq_prev = bd.w("s_pq")
                for pc in range(CW // 1024):
                    bd.vo(lambda e, mi=mi, pc=pc: e.tensor_scalar(i32s[:, 0:1024],
                                                                  wf32[:, pc * 1024:(pc + 1) * 1024],
                                                                  small[:, 8 + mi:9 + mi], None, OP.mult),
                          waits=(("s_wf", c1), ("act", a_rs), spq_prev), drain=pc > 0)
                    d3 = bd.vo(lambda e, pc=pc: e.tensor_scalar(wtmp[:, pc * 1024:(pc + 1) * 1024],
                                                                i32s[:, 0:1024], 1.0, -1.0,
                                                                OP.min, OP.max), drain=True)
                if mi < 2:
                    bd.dma("gpsimd", tmp_gu[mi, rt * 128:(rt + 1) * 128, ccc * CW:(ccc + 1) * CW],
                           wtmp[:, 0:CW], waits=(("dve", d3),), sem="s_pq")
                else:
                    bd.dma("gpsimd", tmp_d[rt * 128:(rt + 1) * 128, ccc * CW:(ccc + 1) * CW],
                           wtmp[:, 0:CW], waits=(("dve", d3),), sem="s_pq")
    plq_tmp = bd.cnt["s_pq"]
    wstg = bd.sb("wstg", (128, max(NFC * E8, NFC * I8, NIC * E8)), BF16)
    for mi in range(4):
        for dc in range(NFC):
            bd.dmaT(wstg[:, dc * E8:(dc + 1) * E8], tmp_qkvo[mi, 0:E8, dc * 128:(dc + 1) * 128],
                    waits=((("s_pq", plq_tmp),) if (mi == 0 and dc == 0) else ()) +
                          ((bd.w("spq"), bd.w("s_pq2")) if dc == 0 else ()))
        bd.dma("sync", wT_snd[OFF_M[mi]:OFF_M[mi] + PR_QKVO].rearrange("(pp c e) -> pp c e", pp=128, c=NFC),
               wstg[:, 0:NFC * E8].rearrange("pp (c e) -> pp c e", c=NFC), waits=(bd.w("spq"),), sem="s_pq2")
    for gi in range(2):
        for dc in range(NFC):
            bd.dmaT(wstg[:, dc * I8:(dc + 1) * I8], tmp_gu[gi, 0:I8, dc * 128:(dc + 1) * 128],
                    waits=(bd.w("spq"), bd.w("s_pq2")) if dc == 0 else ())
        bd.dma("sync", wT_snd[OFF_M[4 + gi]:OFF_M[4 + gi] + PR_GU].rearrange("(pp c e) -> pp c e", pp=128, c=NFC),
               wstg[:, 0:NFC * I8].rearrange("pp (c e) -> pp c e", c=NFC), waits=(bd.w("spq"),), sem="s_pq2")
    for ic in range(NIC):
        bd.dmaT(wstg[:, ic * E8:(ic + 1) * E8], tmp_d[0:E8, ic * 128:(ic + 1) * 128],
                waits=(bd.w("spq"), bd.w("s_pq2")) if ic == 0 else ())
    sp_wt = bd.dma("sync", wT_snd[OFF_M[6]:OFF_M[6] + PR_D].rearrange("(pp c e) -> pp c e", pp=128, c=NIC),
                   wstg[:, 0:NIC * E8].rearrange("pp (c e) -> pp c e", c=NIC), waits=(bd.w("spq"),), sem="s_pq2")
    cc_wT = bd.coll("AllGather", OP.bypass, groups_all, [wT_snd.ap().opt()], [wT_out.ap().opt()],
                    waits=(("s_pq2", sp_wt),))
    bd.dma("sync", srow[0:1, 16:16 + H], din["w_std"][:, :], waits=(INI,), sem="s_ins")
    bd.dma("sync", srow[0:1, 16 + H:16 + 2 * H], din["w_rec"][:, :], sem="s_ins")
    nsv = 1 + 2 * H
    sel_sb = bd.sb("sel_sb", (nsv, p.NFCF * 128), F32)
    rep_sb = bd.sb("rep_sb", (H, 128), F32)
    bd.dma("sync", sel_sb[:, :], din["sel"][:, :], sem="s_ins")
    bd.dma("sync", rep_sb[:, :], din["rep8"][:, :], sem="s_ins")
    sp_isb = bd.dma("sync", srow[0:1, 64:65], din["is_b"][:, :], sem="s_ins")
    K(1e-8); K(0.1); K(qk_quarter)
    du = bd.vo(lambda e: e.tensor_scalar(srow[0:1, 80:80 + H], srow[0:1, 16 + H:16 + 2 * H],
                                         0.1, None, OP.is_gt), waits=(("s_ins", sp_isb),))
    dstd = bd.vo(lambda e: e.tensor_scalar(srow[0:1, 16:16 + 2 * H], srow[0:1, 16:16 + 2 * H],
                                           1e-8, None, OP.max), drain=True)
    a_sq = bd.ao(lambda e: e.activation(srow[0:1, 16:16 + 2 * H], srow[0:1, 16:16 + 2 * H], AF.Sqrt),
                 waits=(("dve", dstd),))
    pe_sv = bd.mm(ps3[0:nsv, 0:1], srow[0:1, 15:16 + 2 * H], onesr_f[0:1, 0:1], True, True,
                  waits=(("act", a_sq), ("s_ins", sp_isb)))
    a_sv = bd.ao(lambda e: e.copy(small[0:nsv, 14:15], ps3[0:nsv, 0:1]), waits=(("pe", pe_sv),))
    pe_cs = None
    for c in range(p.NFCF):
        pe_cs = bd.mm(psS[:, c:c + 1], sel_sb[0:nsv, c * 128:(c + 1) * 128], small[0:nsv, 14:15],
                      True, True, waits=(("act", a_sv),) if c == 0 else ())
    a_cs = bd.ao(lambda e: e.copy(csvec[:, 0:p.NFCF], psS[:, 0:p.NFCF]), waits=(("pe", pe_cs),))
    pe_uc = bd.mm(ps3[0:H, 0:1], srow[0:1, 80:80 + H], onesr_f[0:1, 0:1], True, True,
                  waits=(("dve", du), ("act", a_sv)))
    a_uc = bd.ao(lambda e: e.copy(small[0:H, 15:16], ps3[0:H, 0:1]), waits=(("pe", pe_uc),))
    pe_u1 = bd.mm(psS2[:, 4:5], rep_sb[0:H, :], small[0:H, 15:16], True, True,
                  waits=(("act", a_uc),))
    a_u1 = bd.ao(lambda e: e.copy(u128[:, 0:1], psS2[:, 4:5]), waits=(("pe", pe_u1),))
    d_1mu = bd.vo(lambda e: e.tensor_scalar(u128[0:8 * H, 1:2], u128[0:8 * H, 0:1], -1.0, 1.0,
                                            OP.mult, OP.add), waits=(("act", a_u1),))
    pe_ib = bd.mm(psS2[:, 3:4], onesr_f[0:1, :], srow[0:1, 64:65], True, True,
                  waits=(("s_ins", sp_isb), ("act", a_rs)))
    a_ib = bd.ao(lambda e: e.copy(small[:, 12:13], psS2[:, 3:4]), waits=(("pe", pe_ib),))
    csvec_rdy = ("act", a_cs)
    plq_cs = None
    bd.free("wstg", "sel_sb", "rep_sb", "racc", "wf32", "wtmp")
    bd.fence(("gpsimd", "sync", "vector", "scalar"),
             (bd.w("pe"), bd.w("dve"), bd.w("s_pq"), bd.w("s_pq2"), bd.w("s_in")))

    def slab_offsets(all_slabs):
        offs, o = {}, 0
        for (a, b) in all_slabs:
            offs[(a, b)] = o
            o += D * (b - a)
        return offs

    offs_qkv = slab_offsets(p.slabs_qkv_q + p.slabs_qkv_k + p.slabs_qkv_v)
    offs_wa = slab_offsets(p.slabs_wa_q + p.slabs_wa_k + p.slabs_wa_v + p.slabs_wa_t)
    offs_d = slab_offsets(p.slabs_d)

    wsA = bd.sb("wsA", (128, WSW), BF16)
    wsB = bd.sb("wsB", (128, WSW), BF16)
    dsb = bd.sb("dsb", (128, max(p.ktmax) * QTW), BF16)
    kst = bd.sb("kst", (128, 2 * NKT * 128), BF16)
    vst = bd.sb("vst", (128, 2 * NKT * 128), BF16)
    yall = bd.sb("yall", (128, H * QTW), BF16)
    qall = bd.sb("qall", (128, H * QTW), BF16)

    wg_guard = [0, 0]
    slab_i = [0]

    def load_slab(perm, offmap, slab, extra=()):
        i = slab_i[0] % 2
        slab_i[0] += 1
        buf = (wsA, wsB)[i]
        wdt = slab[1] - slab[0]
        c = bd.dma("gpsimd", buf[:, 0:NFC * wdt].rearrange("pp (c e) -> pp c e", c=NFC),
                   perm[offmap[slab]:offmap[slab] + D * wdt]
                   .rearrange("(pp c e) -> pp c e", pp=128, c=NFC),
                   waits=(("pe", wg_guard[i]),) + tuple(extra), sem=f"s_ws{i}")
        return buf, wdt, (f"s_ws{i}", c), i

    def done_slab(i):
        wg_guard[i] = bd.cnt["pe"]

    K(inv_sqrt_dh); K(-B_EXP); K(1.0); K(0.0)
    for s in slopes:
        K(s)
    dt_off = []
    o = 0
    for qt in range(NQT):
        dt_off.append(o)
        o += p.ktmax[qt]

    def scores_qt(layer, qt, cc_kv, kv_gout, use_alibi, sp_dt, qrdy=()):
        nkt = p.ktmax[qt]
        for h in range(H):
            kbase = (h % 2) * NKT * 128
            guard = st.get(("kvg", h % 2), 0)
            spkv = None
            for r in range(2):
                lo, hi = max(0, r * NTT), min(nkt, (r + 1) * NTT)
                if hi <= lo:
                    continue
                bd.dma("sync", kst[:, kbase + lo * 128:kbase + hi * 128],
                       kv_gout[r, 0, :, h * TOKC + (lo - r * NTT) * 128:h * TOKC + (hi - r * NTT) * 128],
                       waits=(("cc", cc_kv), ("pe", guard)), sem=f"s_kv{h % 2}")
                spkv = bd.dma("sync", vst[:, kbase + lo * 128:kbase + hi * 128]
                              .rearrange("pp (t c) -> pp t c", c=128),
                              kv_gout[r, 1, :, :].rearrange("pp (t f) -> pp t f", f=D)
                              [:, (lo - r * NTT):(hi - r * NTT), h * 128:(h + 1) * 128],
                              sem=f"s_kv{h % 2}")
            for kt in range(nkt):
                sb = pp[kt % 2]
                pe_s = bd.mm(sb[:, 0:QTW], kst[:, kbase + kt * 128:kbase + (kt + 1) * 128],
                             qall[:, h * QTW:(h + 1) * QTW], True, True,
                             waits=(((f"s_kv{h % 2}", spkv),) + tuple(qrdy) if kt == 0 else ()) +
                                   (st.get(("bk", kt % 2), ("dve", 0)),))
                slot = stg[:, (kt % 2) * QTW:(kt % 2) * QTW + QTW]
                sc = slopes[h] if use_alibi else 1.0
                dv_s = bd.vo(lambda e, sl=slot, db=dsb[:, kt * QTW:(kt + 1) * QTW], sb_=sb, sc=sc:
                             e.scalar_tensor_tensor(sl, db, sc, sb_[:, 0:QTW], OP.mult, OP.add),
                             waits=(("pe", pe_s),) + ((sp_dt,) if kt == 0 else ()))
                st[("bk", kt % 2)] = ("dve", dv_s)
                pslot = pbuf[:, (kt % 2) * QTW:(kt % 2) * QTW + QTW]
                ac_p = bd.ao(lambda e, pl=pslot, sl=slot: e.activation(pl, sl, AF.Exp,
                                                                       bias=-B_EXP, scale=1.0),
                             waits=(("dve", dv_s),))
                bd.mm(psS[0:1, 0:QTW], ones_b[:, :], pslot, kt == 0, kt == nkt - 1,
                      waits=(("act", ac_p),) + ((("dve", st["ydve"]),) if kt == 0 else ()))
                bd.mm(ps2[:, 0:QTW], vst[:, kbase + kt * 128:kbase + (kt + 1) * 128], pslot,
                      kt == 0, kt == nkt - 1)
            st[("kvg", h % 2)] = bd.cnt["pe"]
            pe_fin = bd.cnt["pe"]
            dv_r = bd.vo(lambda e: e.reciprocal(rowm[0:1, 0:QTW], psS[0:1, 0:QTW]),
                         waits=(("pe", pe_fin), ("pe", st["rb_pe"])))
            st["rb_pe"] = bd.mm(ps3[:, 0:QTW], onesr_f[0:1, :], rowm[0:1, 0:QTW], True, True,
                                waits=(("dve", dv_r), ("dve", st["ydve"])))
            a_rb = bd.ao(lambda e: e.copy(rbv[:, 0:QTW], ps3[:, 0:QTW]),
                         waits=(("pe", st["rb_pe"]),))
            st["ydve"] = bd.vo(lambda e, dst=yall[:, h * QTW:(h + 1) * QTW]:
                               e.tensor_tensor(dst, ps2[:, 0:QTW], rbv[:, 0:QTW], OP.mult),
                               waits=(("act", a_rb),))

    def layer_norm(toff, w):
        K(1.0 / D); K(LN_EPS)
        for fc in range(NFC):
            a_sq2 = bd.ao(lambda e, fc=fc, sl=hsq[:, (fc % 2) * QTW:(fc % 2) * QTW + w]:
                          e.activation(sl, hb[:, fc * QTW:fc * QTW + w], AF.Square),
                          waits=(("dve", st.get(("og", fc % 2), 0)), ("pe", st.get(("lq", fc % 2), 0))))
            bd.mm(psS[0:1, 0:w], ones_b[:, :], hb[:, fc * QTW:fc * QTW + w], fc == 0, fc == NFC - 1,
                  waits=(("dve", st["lnprev"]), ("dve", st.get(("og", 0), 0)),
                         ("dve", st.get(("og", 1), 0))) if fc == 0 else ())
            st[("lq", fc % 2)] = bd.mm(psS2[0:1, 0:w], ones_b[:, :],
                                       hsq[:, (fc % 2) * QTW:(fc % 2) * QTW + w],
                                       fc == 0, fc == NFC - 1, waits=(("act", a_sq2),))
        pe_sums = bd.cnt["pe"]
        bd.vo(lambda e: e.tensor_scalar(rows[0:1, 0:w], psS[0:1, 0:w], 1.0 / D, None, OP.mult),
              waits=(("pe", pe_sums),))
        bd.vo(lambda e: e.tensor_scalar(rows[0:1, QTW:QTW + w], psS2[0:1, 0:w], 1.0 / D, None, OP.mult))
        bd.vo(lambda e: e.tensor_tensor(rows[0:1, 2 * QTW:2 * QTW + w], rows[0:1, 0:w],
                                        rows[0:1, 0:w], OP.mult), drain=True)
        d4 = bd.vo(lambda e: e.scalar_tensor_tensor(rows[0:1, QTW:QTW + w], rows[0:1, QTW:QTW + w],
                                                    LN_EPS, rows[0:1, 2 * QTW:2 * QTW + w],
                                                    OP.add, OP.subtract), drain=True)
        a1 = bd.ao(lambda e: e.activation(rows[0:1, QTW:QTW + w], rows[0:1, QTW:QTW + w], AF.Sqrt),
                   waits=(("dve", d4),))
        bd.vo(lambda e: e.reciprocal(rowm[0:1, 0:w], rows[0:1, QTW:QTW + w]),
              waits=(("act", a1), ("pe", st["ab_pe"]), ("pe", st["rb_pe"])))
        d6 = bd.vo(lambda e: e.scalar_tensor_tensor(rowm[0:1, QTW:QTW + w], rows[0:1, 0:w],
                                                    -1.0, rowm[0:1, 0:w], OP.mult, OP.mult),
                   drain=True)
        pe_a = bd.mm(ps3[:, 0:w], onesr_f[0:1, :], rowm[0:1, 0:w], True, True,
                     waits=(("dve", d6), ("dve", st["ydve"])))
        pe_b = bd.mm(psS[:, 0:w], onesr_f[0:1, :], rowm[0:1, QTW:QTW + w], True, True)
        st["ab_pe"] = pe_b
        st["rb_pe"] = pe_b
        for fc in range(NFC):
            sl = stg2[:, (fc % 2) * QTW:(fc % 2) * QTW + w]
            bd.vo(lambda e, sl=sl, fc=fc: e.tensor_tensor(sl, hb[:, fc * QTW:fc * QTW + w],
                                                          ps3[:, 0:w], OP.mult),
                  waits=(("pe", pe_b),) if fc == 0 else (), drain=fc >= 2)
            st["lnprev"] = bd.vo(
                lambda e, sl=sl, dst=xs[:, fc * TOKC + toff:fc * TOKC + toff + w]:
                e.tensor_tensor(dst, sl, psS[:, 0:w], OP.add), drain=True)

    def out_proj_ln(qt, perm, offmap, slabs, res_scale):
        dve_y = st["ydve"]
        for slab in slabs:
            buf, wdt, cld, bi = load_slab(perm, offmap, slab)
            for fcl in range(wdt // 128):
                fc = (slab[0] + fcl * 128) // 128
                sb = pp[fc % 2]
                for hc in range(H):
                    bd.mm(sb[:, 0:QTW], buf[:, hc * wdt + fcl * 128:hc * wdt + fcl * 128 + 128],
                          yall[:, hc * QTW:(hc + 1) * QTW], hc == 0, hc == H - 1,
                          waits=((cld, ("dve", dve_y),
                                  st.get(("bk", fc % 2), ("dve", 0))) if hc == 0 else ()))
                pe_m = bd.cnt["pe"]
                dog = bd.vo(
                    lambda e, sb=sb, fc=fc,
                    xsr=xs[:, fc * TOKC + qt * QTW:fc * TOKC + qt * QTW + QTW],
                    dst=hb[:, fc * QTW:(fc + 1) * QTW], rs=res_scale:
                    e.scalar_tensor_tensor(dst, sb[:, 0:QTW], rs, xsr, OP.mult, OP.add),
                    waits=(("pe", pe_m),))
                st[("og", fc % 2)] = dog
                st[("bk", fc % 2)] = ("dve", dog)
            done_slab(bi)
        layer_norm(qt * QTW, QTW)

    for slab in p.slabs_qkv_k:
        buf, wdt, cld, bi = load_slab(din["wqkv_p"], offs_qkv, slab, extra=(("act", act_x),))
        for hc in range(wdt // 128):
            h = (slab[0] - D + hc * 128) // 128
            for tg in range(NQT):
                m_ = (hc * NQT + tg) % 2
                sb = pp[m_]
                for dc in range(NFC):
                    bd.mm(sb[:, 0:QTW], buf[:, dc * wdt + hc * 128:dc * wdt + hc * 128 + 128],
                          xs[:, dc * TOKC + tg * QTW:dc * TOKC + tg * QTW + QTW],
                          dc == 0, dc == NFC - 1,
                          waits=((cld, (f"s_evb{m_}", st.get(("evb", m_), 0))) if dc == 0 else ()))
                pe_m = bd.cnt["pe"]
                a_e = bd.ao(lambda e, sb=sb, m_=m_: e.copy(evb[:, m_ * 512:m_ * 512 + QTW],
                                                           sb[:, 0:QTW]), waits=(("pe", pe_m),))
                st[("evb", m_)] = bd.dma(
                    "gpsimd", kv_snd[0][0, :, h * TOKC + tg * QTW:h * TOKC + (tg + 1) * QTW],
                    evb[:, m_ * 512:m_ * 512 + QTW], waits=(("act", a_e),),
                    sem=f"s_evb{m_}")
        done_slab(bi)
    for slab in p.slabs_qkv_v:
        buf, wdt, cld, bi = load_slab(din["wqkv_p"], offs_qkv, slab)
        for tt in range(NTT):
            for wo_ in range(0, wdt, 512):
                ww = min(512, wdt - wo_)
                sb = pp[tt % 2]
                for dc in range(NFC):
                    bd.mm(sb[:, 0:ww], xs[:, dc * TOKC + tt * 128:dc * TOKC + tt * 128 + 128],
                          buf[:, dc * wdt + wo_:dc * wdt + wo_ + ww], dc == 0, dc == NFC - 1,
                          waits=((cld, (f"s_evb{tt % 2}", st.get(("evb", tt % 2), 0)),
                                  st.get(("bk", tt % 2), ("dve", 0))) if dc == 0 else ()))
                pe_m = bd.cnt["pe"]
                a_e = bd.ao(lambda e, sb=sb, tt=tt, ww=ww:
                            e.copy(evb[:, (tt % 2) * 512:(tt % 2) * 512 + ww], sb[:, 0:ww]),
                            waits=(("pe", pe_m),))
                st[("bk", tt % 2)] = ("act", a_e)
                st[("evb", tt % 2)] = bd.dma(
                    "gpsimd", kv_snd[0][1, :, tt * D + (slab[0] - 2 * D) + wo_:
                                       tt * D + (slab[0] - 2 * D) + wo_ + ww],
                    evb[:, (tt % 2) * 512:(tt % 2) * 512 + ww], waits=(("act", a_e),),
                    sem=f"s_evb{tt % 2}")
        done_slab(bi)
    cc_kv0 = bd.coll("AllGather", OP.bypass, groups_pair,
                     [kv_snd[0].ap().opt()], [kv_out[0].ap().opt()],
                     waits=(bd.w("s_evb0"), bd.w("s_evb1")))
    for qt in range(NQT):
        sp_dt = ("s_dsb", bd.dma("sync", dsb[:, 0:p.ktmax[qt] * QTW].rearrange("pp (n w) -> pp n w", w=QTW),
                                 din["dA"][dt_off[qt]:dt_off[qt] + p.ktmax[qt], :, :]
                                 .rearrange("n pp w -> pp n w"), waits=(bd.w("dve"),), sem="s_dsb"))
        for slab in p.slabs_qkv_q:
            buf, wdt, cld, bi = load_slab(din["wqkv_p"], offs_qkv, slab)
            for hc in range(wdt // 128):
                h = (slab[0] + hc * 128) // 128
                sb = pp[hc % 2]
                for dc in range(NFC):
                    bd.mm(sb[:, 0:QTW], buf[:, dc * wdt + hc * 128:dc * wdt + hc * 128 + 128],
                          xs[:, dc * TOKC + qt * QTW:dc * TOKC + qt * QTW + QTW],
                          dc == 0, dc == NFC - 1,
                          waits=((cld, st.get(("bk", hc % 2), ("dve", 0))) if dc == 0 else ()))
                pe_m = bd.cnt["pe"]
                a_q = bd.ao(
                    lambda e, sb=sb, h=h: e.activation(qall[:, h * QTW:(h + 1) * QTW], sb[:, 0:QTW],
                                                       AF.Copy, bias=0.0, scale=inv_sqrt_dh),
                    waits=(("pe", pe_m),))
                st[("bk", hc % 2)] = ("act", a_q)
            done_slab(bi)
        scores_qt(0, qt, cc_kv0, kv_out[0], True, sp_dt, qrdy=(bd.w("act"),))
        out_proj_ln(qt, din["wo_p"], offs_d, p.slabs_d, 1.0)
    st["xrdy"] = bd.cnt["dve"]
    if debug:
        bd.dma("gpsimd", dbg[0][:, :], xs[:, :], waits=(("dve", st["xrdy"]),), sem="s_msc")

    fst = bd.sb("fst", (128, 4 * QTW), BF16)
    NTC = 16 * H // 128
    tls = bd.sb("tls", (128, max(1, NTC) * TOKC), BF16)
    tb = bd.sb("tb", (128, 4 * TOKC), BF16)

    def head_pieces(base):
        out = {}
        for h in range(H):
            c0, c1 = base + h * p.DSTD, base + (h + 1) * p.DSTD
            ps_ = []
            c = c0
            while c < c1:
                nxt = min(c1, (c // 128 + 1) * 128)
                ps_.append((c, nxt - c))
                c = nxt
            out[h] = ps_
        return out

    qpieces = head_pieces(0)
    kpieces = head_pieces(p.qs_end)
    slotc = [0]

    def fused_chunk(buf, wdt, slab0, cld, gcol0, tg, slot):
        cc = gcol0 // 128
        sb = pp[slot % 2]
        for dc in range(NFC):
            bd.mm(sb[:, 0:QTW], buf[:, dc * wdt + (gcol0 - slab0):dc * wdt + (gcol0 - slab0) + 128],
                  xs[:, dc * TOKC + tg * QTW:dc * TOKC + tg * QTW + QTW],
                  dc == 0, dc == NFC - 1,
                  waits=((cld, (f"s_fst{slot}", st.get(("fe", slot), 0)),
                          csvec_rdy, ("dve", st.get("xrdy", 0)),
                          st.get(("bk", slot % 2), ("dve", 0))) if dc == 0 else ()))
        pe_m = bd.cnt["pe"]
        dv = bd.vo(lambda e, sb=sb, cc=cc, sl=fst[:, slot * QTW:(slot + 1) * QTW]:
                   e.tensor_scalar(sl, sb[:, 0:QTW], csvec[:, cc:cc + 1], None, OP.mult),
                   waits=(("pe", pe_m),))
        st[("bk", slot % 2)] = ("dve", dv)
        return dv

    for slabs, which in ((p.slabs_wa_k, "k"), (p.slabs_wa_t, "t")):
        for slab in slabs:
            buf, wdt, cld, bi = load_slab(din["wa_p"], offs_wa, slab)
            for gcol0 in range(slab[0], slab[1], 128):
                for tg in range(NQT):
                    slot = slotc[0] % 4
                    slotc[0] += 1
                    dv = fused_chunk(buf, wdt, slab[0], cld, gcol0, tg, slot)
                    if which == "k":
                        for h in range(H):
                            for (c0, ln) in kpieces[h]:
                                if c0 // 128 != gcol0 // 128:
                                    continue
                                po = c0 % 128
                                ro = c0 - (p.qs_end + h * p.DSTD)
                                st[("fe", slot)] = bd.dma(
                                    "gpsimd",
                                    kv_snd[1][0, ro:ro + ln, h * TOKC + tg * QTW:h * TOKC + (tg + 1) * QTW],
                                    fst[po:po + ln, slot * QTW:(slot + 1) * QTW],
                                    waits=(("dve", dv),), sem=f"s_fst{slot}")
                    else:
                        ci = (gcol0 - p.v_end) // 128
                        st[("fe", slot)] = bd.dma(
                            "gpsimd", tls[:, ci * TOKC + tg * QTW:ci * TOKC + (tg + 1) * QTW],
                            fst[:, slot * QTW:(slot + 1) * QTW], waits=(("dve", dv),), sem=f"s_fst{slot}")
            done_slab(bi)
    nt8 = 8 * H
    khi_ci, khi_po = (p.ql_end - p.v_end) // 128, (p.ql_end - p.v_end) % 128
    c5 = bd.dma("gpsimd", tb[0:nt8, 0:TOKC], tls[0:nt8, 0:TOKC],
                waits=(bd.w("s_fst0"), bd.w("s_fst1"), bd.w("s_fst2"), bd.w("s_fst3")), sem="s_msc")
    c6 = bd.dma("gpsimd", tb[0:nt8, TOKC:2 * TOKC],
                tls[khi_po:khi_po + nt8, khi_ci * TOKC:(khi_ci + 1) * TOKC], sem="s_msc")
    dvq = None
    for tg in range(NQT):
        a_, b_ = tg * QTW, tg * QTW + QTW
        bd.vo(lambda e, a_=a_, b_=b_: e.tensor_scalar(stg[0:nt8, 0:QTW], tb[0:nt8, a_:b_],
                                                      u128[0:nt8, 1:2], None, OP.mult),
              waits=(("s_msc", c6), ("dve", d_1mu)), drain=True)
        bd.vo(lambda e, a_=a_, b_=b_: e.tensor_scalar(stg2[0:nt8, 0:QTW], tb[0:nt8, TOKC + a_:TOKC + b_],
                                                      u128[0:nt8, 0:1], None, OP.mult))
        dvq = bd.vo(lambda e, a_=a_, b_=b_: e.tensor_tensor(tb[0:nt8, 2 * TOKC + a_:2 * TOKC + b_],
                                                            stg[0:nt8, 0:QTW], stg2[0:nt8, 0:QTW], OP.add),
                    drain=True)
        bd.vo(lambda e, a_=a_, b_=b_: e.tensor_scalar(stg[0:nt8, 0:QTW], tb[0:nt8, a_:b_],
                                                      u128[0:nt8, 0:1], None, OP.mult), drain=True)
        bd.vo(lambda e, a_=a_, b_=b_: e.tensor_scalar(stg2[0:nt8, 0:QTW], tb[0:nt8, TOKC + a_:TOKC + b_],
                                                      u128[0:nt8, 1:2], None, OP.mult))
        dvq = bd.vo(lambda e, a_=a_, b_=b_: e.tensor_tensor(tb[0:nt8, 3 * TOKC + a_:3 * TOKC + b_],
                                                            stg[0:nt8, 0:QTW], stg2[0:nt8, 0:QTW], OP.add),
                    drain=True)
    for h in range(H):
        bd.dma("gpsimd", kv_snd[1][0, 120:128, h * TOKC:(h + 1) * TOKC],
               tb[8 * h:8 * h + 8, 3 * TOKC:4 * TOKC], waits=(("dve", dvq),), sem="s_tl")
    for slab in p.slabs_wa_v:
        buf, wdt, cld, bi = load_slab(din["wa_p"], offs_wa, slab)
        for tt in range(NTT):
            for wo_ in range(0, wdt, 512):
                ww = min(512, wdt - wo_)
                sb = pp[tt % 2]
                for dc in range(NFC):
                    bd.mm(sb[:, 0:ww], xs[:, dc * TOKC + tt * 128:dc * TOKC + tt * 128 + 128],
                          buf[:, dc * wdt + wo_:dc * wdt + wo_ + ww], dc == 0, dc == NFC - 1,
                          waits=((cld, (f"s_evb{tt % 2}", st.get(("evb", tt % 2), 0)),
                                  ("dve", st.get("xrdy", 0)),
                                  st.get(("bk", tt % 2), ("dve", 0))) if dc == 0 else ()))
                pe_m = bd.cnt["pe"]
                a_e = bd.ao(lambda e, sb=sb, tt=tt, ww=ww:
                            e.copy(evb[:, (tt % 2) * 512:(tt % 2) * 512 + ww], sb[:, 0:ww]),
                            waits=(("pe", pe_m),))
                st[("bk", tt % 2)] = ("act", a_e)
                st[("evb", tt % 2)] = bd.dma(
                    "gpsimd", kv_snd[1][1, :, tt * D + (slab[0] - p.ks_end) + wo_:
                                       tt * D + (slab[0] - p.ks_end) + wo_ + ww],
                    evb[:, (tt % 2) * 512:(tt % 2) * 512 + ww], waits=(("act", a_e),),
                    sem=f"s_evb{tt % 2}")
        done_slab(bi)
    cc_kv1 = bd.coll("AllGather", OP.bypass, groups_pair,
                     [kv_snd[1].ap().opt()], [kv_out[1].ap().opt()],
                     waits=(bd.w("s_evb0"), bd.w("s_evb1"), bd.w("s_fst0"), bd.w("s_fst1"),
                            bd.w("s_fst2"), bd.w("s_fst3"), bd.w("s_tl")))
    for qt in range(NQT):
        sp_dt = ("s_dsb", bd.dma("sync", dsb[:, 0:p.ktmax[qt] * QTW].rearrange("pp (n w) -> pp n w", w=QTW),
                                 din["dM"][dt_off[qt]:dt_off[qt] + p.ktmax[qt], :, :]
                                 .rearrange("n pp w -> pp n w"), waits=(bd.w("dve"),), sem="s_dsb"))
        for slab in p.slabs_wa_q:
            buf, wdt, cld, bi = load_slab(din["wa_p"], offs_wa, slab)
            for gcol0 in range(slab[0], slab[1], 128):
                slot = slotc[0] % 4
                slotc[0] += 1
                dv = fused_chunk(buf, wdt, slab[0], cld, gcol0, qt, slot)
                for h in range(H):
                    for (c0, ln) in qpieces[h]:
                        if c0 // 128 != gcol0 // 128:
                            continue
                        po = c0 % 128
                        ro = c0 - h * p.DSTD
                        st[("fe", slot)] = bd.dma(
                            "gpsimd", qall[ro:ro + ln, h * QTW:(h + 1) * QTW],
                            fst[po:po + ln, slot * QTW:(slot + 1) * QTW], waits=(("dve", dv),),
                            sem=f"s_fst{slot}")
            done_slab(bi)
        for h in range(H):
            bd.dma("gpsimd", qall[120:128, h * QTW:(h + 1) * QTW],
                   tb[8 * h:8 * h + 8, 2 * TOKC + qt * QTW:2 * TOKC + (qt + 1) * QTW],
                   waits=(("dve", dvq),), sem="s_tl")
        scores_qt(1, qt, cc_kv1, kv_out[1], False, sp_dt,
                  qrdy=(bd.w("s_fst0"), bd.w("s_fst1"), bd.w("s_fst2"), bd.w("s_fst3"), bd.w("s_tl")))
        out_proj_ln(qt, din["wp_p"], offs_d, p.slabs_d, 1.0)
    st["xrdy"] = bd.cnt["dve"]
    if debug:
        bd.dma("gpsimd", dbg[1][:, :], xs[:, :], waits=(("dve", st["xrdy"]),), sem="s_msc")
    bd.free("fst", "tls", "tb", "wsA", "wsB", "dsb", "kst", "vst", "yall", "qall")
    bd.fence(("gpsimd", "sync", "vector"),
             (bd.w("pe"), bd.w("dve"), bd.w("s_fst0"), bd.w("s_fst1"), bd.w("s_fst2"),
              bd.w("s_fst3"), bd.w("s_tl"), bd.w("s_msc")))

    kvt = bd.sb("kvt", (128, NFC * TOKC), F32)
    vh = bd.sb("vh", (128, NFC * QTW), BF16)
    carr = bd.sb("carr", (128, 2 * NFC), F32)
    offa = bd.sb("offa", (128, NFC), F32)
    w1A = bd.sb("w1A", (128, NFC * E8), BF16)
    w1B = bd.sb("w1B", (128, NFC * E8), BF16)
    w1C = bd.sb("w1C", (128, NFC * E8), BF16)
    w1g = {0: 0, 1: 0, 2: 0}
    bd.emit("gpsimd", lambda e: e.memset(carr[:, 0:NFC], 0.0), inc="ini")

    def load_w1(mi, r, bufi, extra=()):
        buf = (w1A, w1B, w1C)[bufi]
        c = bd.dma("sync", buf[:, 0:NFC * E8].rearrange("pp (c e) -> pp c e", c=NFC),
                   wT_out[r, OFF_M[mi]:OFF_M[mi] + PR_QKVO]
                   .rearrange("(pp c e) -> pp c e", pp=128, c=NFC),
                   waits=(("cc", cc_wT), ("pe", w1g[bufi])) + tuple(extra), sem=f"s_w1{bufi}")
        return buf, (f"s_w1{bufi}", c)

    for r in range(NC):
        bk, ck = load_w1(1, r, 0)
        bv, cv = load_w1(2, r, 1)
        for el in range(E2):
            e_g = r * E2 + el
            for qt in range(NQT):
                for dc in range(NFC):
                    bd.mm(ps0[:, 0:QTW], bk[:, dc * E8 + el * 128:dc * E8 + el * 128 + 128],
                          xs[:, dc * TOKC + qt * QTW:dc * TOKC + qt * QTW + QTW],
                          dc == 0, dc == NFC - 1,
                          waits=((ck, ("dve", st.get("kvd", 0)), ("dve", st.get("xrdy", 0)),
                                  st.get(("bk", 0), ("dve", 0))) if dc == 0 else ()))
                for dc in range(NFC):
                    bd.mm(ps1[:, 0:QTW], bv[:, dc * E8 + el * 128:dc * E8 + el * 128 + 128],
                          xs[:, dc * TOKC + qt * QTW:dc * TOKC + qt * QTW + QTW],
                          dc == 0, dc == NFC - 1,
                          waits=((cv, st.get(("bk", 1), ("dve", 0))) if dc == 0 else ()))
                pe_m = bd.cnt["pe"]
                a_kc = bd.ao(lambda e: e.copy(kvsb[:, 0:QTW], ps0[:, 0:QTW]),
                             waits=(("pe", pe_m), ("dve", st.get("kvd", 0))))
                dkv = bd.vo(lambda e: e.tensor_tensor(stg[:, 0:QTW], kvsb[:, 0:QTW], ps1[:, 0:QTW], OP.mult),
                            waits=(("act", a_kc), ("ini", 2)))
                st[("bk", 0)] = ("dve", dkv)
                st[("bk", 1)] = ("dve", dkv)
                bd.vo(lambda e, e_g=e_g, qt=qt:
                      e.tensor_tensor_scan(kvt[:, e_g * TOKC + qt * QTW:e_g * TOKC + (qt + 1) * QTW],
                                           stg[:, 0:QTW], stg[:, 0:QTW],
                                           carr[:, e_g:e_g + 1], OP.add, OP.bypass), drain=True)
                st["kvd"] = bd.vo(lambda e, e_g=e_g, qt=qt:
                                  e.tensor_copy(carr[:, e_g:e_g + 1],
                                                kvt[:, e_g * TOKC + (qt + 1) * QTW - 1:
                                                    e_g * TOKC + (qt + 1) * QTW]), drain=True)
        w1g[0] = bd.cnt["pe"]
        w1g[1] = bd.cnt["pe"]
    c7 = bd.dma("gpsimd", off_snd[:, :], carr[:, 0:NFC], waits=(("dve", st["kvd"]),), sem="s_msc")
    cc_off = bd.coll("AllGather", OP.bypass, groups_pair,
                     [off_snd.ap().opt()], [off_out.ap().opt()], waits=(("s_msc", c7),))
    c8 = bd.dma("gpsimd", carr[:, NFC:2 * NFC], off_out[0, :, :], waits=(("cc", cc_off),), sem="s_msc")
    dvo = bd.vo(lambda e: e.tensor_scalar(offa[:, 0:NFC], carr[:, NFC:2 * NFC], small[:, 12:13],
                                          None, OP.mult), waits=(("s_msc", c8), ("act", a_ib)))
    K(denom_l1)
    for qt in range(NQT):
        for r in range(NC):
            bq, cq = load_w1(0, r, r % 2)
            for el in range(E2):
                e_g = r * E2 + el
                for dc in range(NFC):
                    bd.mm(ps0[:, 0:QTW], bq[:, dc * E8 + el * 128:dc * E8 + el * 128 + 128],
                          xs[:, dc * TOKC + qt * QTW:dc * TOKC + qt * QTW + QTW],
                          dc == 0, dc == NFC - 1,
                          waits=((cq, ("dve", st.get("vhd", 0)), ("dve", st.get("xrdy", 0)),
                                  st.get(("bk", 0), ("dve", 0))) if dc == 0 else ()))
                pe_m = bd.cnt["pe"]
                st["vhd"] = bd.vo(
                    lambda e, e_g=e_g, qt=qt:
                    e.scalar_tensor_tensor(vh[:, e_g * QTW:(e_g + 1) * QTW],
                                           kvt[:, e_g * TOKC + qt * QTW:e_g * TOKC + (qt + 1) * QTW],
                                           offa[:, e_g:e_g + 1], ps0[:, 0:QTW], OP.add, OP.mult),
                    waits=(("pe", pe_m), ("dve", dvo)))
                st[("bk", 0)] = ("dve", st["vhd"])
            w1g[r % 2] = bd.cnt["pe"]
        for r in range(NC):
            bo, co = load_w1(3, r, 2)
            for fl in range(E2):
                fc = r * E2 + fl
                sb = pp[fc % 2]
                for ec in range(NFC):
                    bd.mm(sb[:, 0:QTW], bo[:, ec * E8 + fl * 128:ec * E8 + fl * 128 + 128],
                          vh[:, ec * QTW:(ec + 1) * QTW], ec == 0, ec == NFC - 1,
                          waits=((co, st.get(("bk", fc % 2), ("dve", 0)),
                                  ("dve", st["vhd"])) if ec == 0 else ()))
                pe_m = bd.cnt["pe"]
                dog = bd.vo(
                    lambda e, sb=sb, fc=fc,
                    xsr=xs[:, fc * TOKC + qt * QTW:fc * TOKC + qt * QTW + QTW],
                    dst=hb[:, fc * QTW:(fc + 1) * QTW]:
                    e.scalar_tensor_tensor(dst, sb[:, 0:QTW], denom_l1, xsr, OP.mult, OP.add),
                    waits=(("pe", pe_m),))
                st[("og", fc % 2)] = dog
                st[("bk", fc % 2)] = ("dve", dog)
            w1g[2] = bd.cnt["pe"]
        layer_norm(qt * QTW, QTW)
    st["xrdy"] = bd.cnt["dve"]
    if debug:
        bd.dma("gpsimd", dbg[2][:, :], xs[:, :], waits=(("dve", st["xrdy"]),), sem="s_msc")
    bd.free("kvt", "vh", "carr", "offa", "w1A", "w1B", "w1C")
    bd.fence(("sync", "vector", "scalar"),
             (bd.w("pe"), bd.w("dve"), bd.w("s_msc"),
              bd.w("s_w10"), bd.w("s_w11"), bd.w("s_w12")))

    if debug:
        dcap0 = bd.vo(lambda e: e.tensor_copy(dbgsb[:, SQ:2 * SQ], xs[:, 0:SQ]), drain=True)
        bd.dma("gpsimd", dbg_cap[:, SQ:2 * SQ], dbgsb[:, SQ:2 * SQ],
               waits=(("dve", dcap0),), sem="s_msc")
    wsA = bd.sb("wsA2", (128, WSW), BF16)
    wsB = bd.sb("wsB2", (128, WSW), BF16)
    xq = bd.sb("xq", (128, NFC * SQ), BF16)
    gsi = bd.sb("gsi", (128, I2h * SQ), BF16)
    gu = bd.sb("gu", (128, NIC * SQ), BF16)
    mfold = bd.sb("mfold", (128, 2 * SQ), BF16)
    wgg = [0, 0]
    K(1.0 / 127.0)

    def colmax_to_row(src_fn, nch, row_off):
        dvm = None
        for i in range(nch):
            if i == 0:
                bd.vo(lambda e, s=src_fn(0): e.tensor_copy(mfold[:, 0:SQ], s), drain=True)
                dvm = bd.vo(lambda e, s=src_fn(0): e.tensor_copy(mfold[:, SQ:2 * SQ], s))
            else:
                bd.vo(lambda e, s=src_fn(i): e.tensor_tensor(mfold[:, 0:SQ], mfold[:, 0:SQ],
                                                             s, OP.max), drain=True)
                dvm = bd.vo(lambda e, s=src_fn(i): e.tensor_tensor(mfold[:, SQ:2 * SQ],
                                                                   mfold[:, SQ:2 * SQ], s, OP.min),
                            drain=True)
        bd.vo(lambda e: e.tensor_scalar(mfold[:, SQ:2 * SQ], mfold[:, SQ:2 * SQ], -1.0, None, OP.mult),
              drain=True)
        dvm = bd.vo(lambda e: e.tensor_tensor(mfold[:, 0:SQ], mfold[:, 0:SQ], mfold[:, SQ:2 * SQ],
                                              OP.max), drain=True)

        for j in range(SQ // 128):
            t = ptr[j % 2]
            pe_t = bd.mm(t[:, 0:128], mfold[:, j * 128:(j + 1) * 128], ident_b[:, :], True, True,
                         waits=(("dve", dvm), ("dve", st.get(("cmr", j % 2), 0)),
                                ("act", st.get(("xb", j % 2), 0))))
            st[("cmr", j % 2)] = bd.vo(
                lambda e, t=t, j=j: e.reduce_max(small[:, 16 + j:17 + j], t[:, 0:128],
                                                 axis=mybir.AxisListType.X, apply_absolute_value=True),
                waits=(("pe", pe_t),))
        dvm2 = bd.cnt["dve"]
        c0_ = bd.dma("gpsimd", sm_dram[:, 0:SQ // 128], small[:, 16:16 + SQ // 128],
                     waits=(("dve", dvm2),))
        c = bd.dma("gpsimd", rows[0:1, row_off:row_off + SQ].rearrange("o (j pp) -> o j pp", pp=128),
                   sm_dram[:, 0:SQ // 128].rearrange("(o pp) j -> o j pp", o=1),
                   waits=(("plq", c0_), bd.w("s_msc")), allow_slow_non_contiguous=True, sem="s_cm")
        dts = bd.vo(lambda e: e.tensor_scalar(rows[0:1, row_off:row_off + SQ],
                                              rows[0:1, row_off:row_off + SQ],
                                              1.0 / 127.0, Q_EPS, OP.mult, OP.add),
                    waits=(("s_cm", c),), drain=True)
        if debug and st.get("cm_calls", 0) == 0:
            dcapB = bd.vo(lambda e: e.tensor_copy(dbgsb[:, SQ:2 * SQ], xs[:, 0:SQ]),
                          waits=(bd.w("s_msc"),), drain=True)
            bd.dma("gpsimd", dbg_cap[:, 3 * SQ:4 * SQ], dbgsb[:, SQ:2 * SQ],
                   waits=(("dve", dcapB),), sem="s_msc")
        st["cm_calls"] = st.get("cm_calls", 0) + 1
        d2_ = bd.vo(lambda e: e.reciprocal(rowm[0:1, 0:SQ], rows[0:1, row_off:row_off + SQ]),
                    waits=(("pe", st["rb_pe"]),), drain=True)
        return d2_

    for qs in range(NSQ):
        toff = qs * SQ
        if qs > 0:
            bd.fence(("vector", "scalar"), (bd.w("pe"), bd.w("s_msc")))
        d2 = colmax_to_row(lambda i: xs[:, i * TOKC + toff:i * TOKC + toff + SQ], NFC, 3 * QTW)
        if debug and qs == 0:
            dcapC = bd.vo(lambda e: e.tensor_copy(dbgsb[:, SQ:2 * SQ], xs[:, 0:SQ]),
                          waits=(bd.w("s_msc"),), drain=True)
            bd.dma("gpsimd", dbg_cap[:, 2 * SQ:3 * SQ], dbgsb[:, SQ:2 * SQ],
                   waits=(("dve", dcapC),), sem="s_msc")
        st["rb_pe"] = pe_rbx = bd.mm(ps3[:, 0:SQ], onesr_f[0:1, :], rowm[0:1, 0:SQ], True, True,
                                     waits=(("dve", d2), ("dve", st["ydve"])))
        if debug and qs == 0:
            dcap = bd.vo(lambda e: e.tensor_copy(dbgsb[:, 0:SQ], ps3[:, 0:SQ]),
                         waits=(("pe", pe_rbx), bd.w("s_msc")), drain=True)
            dcap2 = bd.vo(lambda e, xsr=xs[:, toff:toff + SQ]:
                          e.tensor_copy(dbgsb[:, SQ:2 * SQ], xsr), drain=True)
            bd.dma("gpsimd", dbg_rb[:, :], dbgsb[:, 0:SQ], waits=(("dve", dcap),), sem="s_msc")
            bd.dma("gpsimd", dbg_cap[:, 0:SQ], dbgsb[:, SQ:2 * SQ], waits=(("dve", dcap2),),
                   sem="s_msc")
        for fc in range(NFC):
            dvq1 = bd.vo(lambda e, fc=fc,
                         xsr=xs[:, fc * TOKC + toff:fc * TOKC + toff + SQ]:
                         e.tensor_tensor(i32s[:, (fc % 2) * 512:(fc % 2) * 512 + SQ],
                                         xsr, ps3[:, 0:SQ], OP.mult),
                         waits=(("pe", pe_rbx),) + ((("act", st.get(("xqe", fc % 2), 0)),) if fc >= 2 else ()),
                         drain=fc >= 2)
            st[("xqe", fc % 2)] = bd.ao(lambda e, fc=fc:
                                        e.copy(xq[:, fc * SQ:(fc + 1) * SQ],
                                               i32s[:, (fc % 2) * 512:(fc % 2) * 512 + SQ]),
                                        waits=(("dve", dvq1),))
        act_xq = bd.cnt["act"]
        dve_xq = bd.cnt["dve"]
        if debug and qs == 0:
            bd.dma("gpsimd", dbg_xq[:, :], xq[:, :], waits=(("act", act_xq),), sem="s_msc")
        d3 = bd.vo(lambda e: e.tensor_scalar(rowm[0:1, 0:SQ], rows[0:1, 3 * QTW:3 * QTW + SQ],
                                             srow[0:1, 8:9], None, OP.mult),
                   waits=(("pe", pe_rbx),), drain=True)
        pe_cg = bd.mm(ps4[:, 0:SQ], onesr_f[0:1, :], rowm[0:1, 0:SQ], True, True,
                      waits=(("dve", d3), ("dve", st.get(("cmr", 0), 0)), ("dve", st.get(("cmr", 1), 0))))
        a_cg = bd.ao(lambda e: e.copy(csc[:, 0:SQ], ps4[:, 0:SQ]), waits=(("pe", pe_cg),))
        d4 = bd.vo(lambda e: e.tensor_scalar(rowm[0:1, QTW:QTW + SQ], rows[0:1, 3 * QTW:3 * QTW + SQ],
                                             srow[0:1, 9:10], None, OP.mult))
        pe_cu = bd.mm(ps5[:, 0:SQ], onesr_f[0:1, :], rowm[0:1, QTW:QTW + SQ], True, True,
                      waits=(("dve", d4),))
        a_cu = bd.ao(lambda e: e.copy(csc[:, SQ:2 * SQ], ps5[:, 0:SQ]), waits=(("pe", pe_cu),))
        st[("xb", 0)] = a_cg
        st[("xb", 1)] = a_cu
        st["rb_pe"] = pe_cu
        for r in range(NC):
            for hf in range(2):
                cg_ = bd.dma("sync", wsA[:, 0:NFC * I8h].rearrange("pp (c e) -> pp c e", c=NFC),
                             wT_out[r, OFF_M[4]:OFF_M[4] + PR_GU]
                             .rearrange("(pp c e) -> pp c e", pp=128, c=NFC)
                             [:, :, hf * I8h:(hf + 1) * I8h],
                             waits=(("cc", cc_wT), ("pe", wgg[0])), sem="s_wg0")
                for il in range(I2h):
                    sb = pp[il % 2]
                    for fc in range(NFC):
                        bd.mm(sb[:, 0:SQ], wsA[:, fc * I8h + il * 128:fc * I8h + il * 128 + 128],
                              xq[:, fc * SQ:(fc + 1) * SQ], fc == 0, fc == NFC - 1,
                              waits=((("s_wg0", cg_), ("act", act_xq),
                                      st.get(("bk", il % 2), ("dve", 0))) if fc == 0 else ()))
                    pe_m = bd.cnt["pe"]
                    dvg = bd.vo(lambda e, sb=sb, sl=stg[:, (il % 2) * SQ:(il % 2) * SQ + SQ]:
                                e.tensor_tensor(sl, sb[:, 0:SQ], csc[:, 0:SQ], OP.mult),
                                waits=(("pe", pe_m), ("act", a_cg),
                                       ("act", st.get(("ge", il % 2), 0))))
                    st[("bk", il % 2)] = ("dve", dvg)
                    st[("ge", il % 2)] = bd.ao(lambda e, il=il,
                                               sl=stg[:, (il % 2) * SQ:(il % 2) * SQ + SQ]:
                                               e.activation(gsi[:, il * SQ:(il + 1) * SQ], sl,
                                                            AF.Silu), waits=(("dve", dvg),))
                wgg[0] = bd.cnt["pe"]
                cu_ = bd.dma("sync", wsB[:, 0:NFC * I8h].rearrange("pp (c e) -> pp c e", c=NFC),
                             wT_out[r, OFF_M[5]:OFF_M[5] + PR_GU]
                             .rearrange("(pp c e) -> pp c e", pp=128, c=NFC)
                             [:, :, hf * I8h:(hf + 1) * I8h],
                             waits=(("cc", cc_wT), ("pe", wgg[1])), sem="s_wg1")
                for il in range(I2h):
                    ic = r * 2 * I2h + hf * I2h + il
                    sb = pp[il % 2]
                    for fc in range(NFC):
                        bd.mm(sb[:, 0:SQ], wsB[:, fc * I8h + il * 128:fc * I8h + il * 128 + 128],
                              xq[:, fc * SQ:(fc + 1) * SQ], fc == 0, fc == NFC - 1,
                              waits=((("s_wg1", cu_), ("dve", st.get(("ue", il % 2), 0)),
                                      st.get(("bk", il % 2), ("dve", 0)),
                                      ("act", st.get(("ge", il % 2), 0))) if fc == 0 else ()))
                    pe_m = bd.cnt["pe"]
                    dvu = bd.vo(lambda e, sb=sb: e.tensor_tensor(stg2[:, 0:SQ], sb[:, 0:SQ],
                                                                 csc[:, SQ:2 * SQ], OP.mult),
                                waits=(("pe", pe_m), ("act", a_cu), ("act", st.get(("ge", il % 2), 0))),
                                drain=True)
                    st[("bk", il % 2)] = ("dve", dvu)
                    st[("ue", il % 2)] = bd.vo(
                        lambda e, ic=ic, il=il: e.tensor_tensor(gu[:, ic * SQ:(ic + 1) * SQ],
                                                                gsi[:, il * SQ:(il + 1) * SQ],
                                                                stg2[:, 0:SQ], OP.mult), drain=True)
                wgg[1] = bd.cnt["pe"]
        d2g = colmax_to_row(lambda i: gu[:, i * SQ:(i + 1) * SQ], NIC, 4 * QTW)
        st["rb_pe"] = pe_rbg = bd.mm(ps3[:, 0:SQ], onesr_f[0:1, :], rowm[0:1, 0:SQ], True, True,
                                     waits=(("dve", d2g), ("dve", dve_xq)))
        for ic in range(NIC):
            dvq1 = bd.vo(lambda e, ic=ic: e.tensor_tensor(i32s[:, (ic % 2) * 512:(ic % 2) * 512 + SQ],
                                                          gu[:, ic * SQ:(ic + 1) * SQ],
                                                          ps3[:, 0:SQ], OP.mult),
                         waits=(("pe", pe_rbg),) + ((("act", st.get(("gqe", ic % 2), 0)),) if ic >= 2 else ()),
                         drain=ic >= 2)
            st[("gqe", ic % 2)] = bd.ao(lambda e, ic=ic:
                                        e.copy(gu[:, ic * SQ:(ic + 1) * SQ],
                                               i32s[:, (ic % 2) * 512:(ic % 2) * 512 + SQ]),
                                        waits=(("dve", dvq1),))
        act_guq = bd.cnt["act"]
        if debug and qs == 0:
            bd.dma("gpsimd", dbg_gu[:, :], gu[:, :], waits=(("act", act_guq),), sem="s_msc")
            bd.dma("gpsimd", dbg_gs[0:1, 0:2 * QTW], rows[0:1, 3 * QTW:5 * QTW], sem="s_msc")
        d5 = bd.vo(lambda e: e.tensor_scalar(rowm[0:1, 0:SQ], rows[0:1, 4 * QTW:4 * QTW + SQ],
                                             srow[0:1, 10:11], None, OP.mult),
                   waits=(("pe", pe_rbg),), drain=True)
        dve_pre_cd = bd.cnt["dve"]
        pe_cd = bd.mm(ps4[:, 0:SQ], onesr_f[0:1, :], rowm[0:1, 0:SQ], True, True,
                      waits=(("dve", d5), ("dve", dve_pre_cd), ("act", a_cg)))
        a_cd = bd.ao(lambda e: e.copy(csc[:, 2 * SQ:3 * SQ], ps4[:, 0:SQ]), waits=(("pe", pe_cd),))
        st["rb_pe"] = pe_cd
        for r in range(NC):
            bufs_dn = (wsA, wsB)
            for fl2 in range(0, E2, 2):
                for ihf in range(2):
                    cd_ = bd.dma("sync", bufs_dn[ihf][:, 0:NICh * E8].rearrange("pp (c e) -> pp c e", c=NICh),
                                 wT_out[r, OFF_M[6]:OFF_M[6] + PR_D]
                                 .rearrange("(pp c e) -> pp c e", pp=128, c=NIC)
                                 [:, ihf * NICh:(ihf + 1) * NICh, :],
                                 waits=(("cc", cc_wT), ("pe", wgg[ihf])), sem=f"s_wg{ihf}")
                    for fl in (fl2, fl2 + 1):
                        sb = pp[fl % 2]
                        for icl in range(NICh):
                            ic = ihf * NICh + icl
                            bd.mm(sb[:, 0:SQ],
                                  bufs_dn[ihf][:, icl * E8 + fl * 128:icl * E8 + fl * 128 + 128],
                                  gu[:, ic * SQ:(ic + 1) * SQ],
                                  ihf == 0 and icl == 0, ihf == 1 and icl == NICh - 1,
                                  waits=(((f"s_wg{ihf}", cd_), ("act", act_guq),
                                          st.get(("bk", fl % 2), ("dve", 0))) if icl == 0 else ()))
                    wgg[ihf] = bd.cnt["pe"]
                for fl in (fl2, fl2 + 1):
                    fc = r * E2 + fl
                    sb = pp[fl % 2]
                    pe_m = bd.cnt["pe"]
                    dvd = bd.vo(lambda e, sb=sb: e.tensor_tensor(stg[:, 0:SQ], sb[:, 0:SQ],
                                                                 csc[:, 2 * SQ:3 * SQ], OP.mult),
                                waits=(("pe", pe_m), ("act", a_cd)), drain=True)
                    st[("bk", fl % 2)] = ("dve", dvd)
                    st[("og", fl % 2)] = bd.vo(
                        lambda e, fc=fc,
                        xsr=xs[:, fc * TOKC + toff:fc * TOKC + toff + SQ],
                        dst=hb[:, fc * QTW:fc * QTW + SQ]:
                        e.scalar_tensor_tensor(dst, stg[:, 0:SQ], 1.0, xsr, OP.mult, OP.add), drain=True)
        layer_norm(toff, SQ)
    bd.free("wsA2", "wsB2", "xq", "gsi", "gu", "mfold")
    bd.fence(("scalar",), (bd.w("pe"), bd.w("dve")))

    otm = bd.sb("otm", (128, D), F32)
    for tt in range(NTT):
        for fc in range(NFC):
            t = ptr[fc % 2]
            pe_t = bd.mm(t[:, 0:128], xs[:, fc * TOKC + tt * 128:fc * TOKC + tt * 128 + 128],
                         ident_b[:, :], True, True,
                         waits=(("dve", st["lnprev"]), ("act", st.get(("xb", fc % 2), 0))) +
                               ((("s_out", st.get("odma", 0)),) if fc < 2 else ()))
            st[("xb", fc % 2)] = bd.ao(lambda e, t=t, fc=fc: e.copy(otm[:, fc * 128:(fc + 1) * 128],
                                                                    t[:, 0:128]),
                                       waits=(("pe", pe_t),))
        a_fin = bd.cnt["act"]
        st["odma"] = bd.dma("gpsimd", out_my[tt * 128:(tt + 1) * 128, :], otm[:, :],
                            waits=(("act", a_fin),), sem="s_out")
    final_out = bd.cnt["s_out"]
    bd.emit("gpsimd", lambda e: e.nop(), waits=(("s_out", final_out),))

    vals = sorted(bd.consts)
    assert len(vals) <= 64, len(vals)
    for i, v in enumerate(vals):
        nc.const_aps.register(F32, float(v), cpool[:, i:i + 1])
        nc.const_aps.register(BF16, float(v), cpool_b[:, i:i + 1])

    def memset_fn(e):
        for i, v in enumerate(vals):
            e.memset(cpool[:, i:i + 1], float(v))
            e.memset(cpool_b[:, i:i + 1], float(v))
        e.memset(ones_b[:, :], 1.0)
        e.memset(ones_f[:, :], 1.0)
        e.memset(srow[:, :], 0.0)
        e.drain()
        e.memset(srow[0:1, 15:16], 1.0)
        return e.memset(onesr_f[:, :], 1.0)

    bd.q["gpsimd"].insert(0, ((), memset_fn, "ini", 1))

    es = contextlib.ExitStack()
    sems = {}
    for s in SEMS:
        sems[s] = es.enter_context(nc.semaphore("sem_" + s))
    with es:
        with nc.Block() as block:
            for eng_name in ENGS:
                ops = bd.q[eng_name]

                def mk(ops=ops):
                    def body(e):
                        for waits, fn, inc, amt in ops:
                            for (s, v) in waits:
                                e.wait_ge(sems[s], v)
                            ins = fn(e)
                            if inc is not None and ins is not None:
                                ins.then_inc(sems[inc], amt)
                    return body
                getattr(block, eng_name)(mk())
    return nc


def kernel(**inputs):
    from concourse.bass_utils import run_bass_kernel_spmd
    prm = Prm()
    in_maps = host_prepare(prm, inputs)
    nc = build(prm)
    res = run_bass_kernel_spmd(nc, in_maps, core_ids=list(range(prm.NC)))
    out = np.empty((prm.B, prm.T, prm.D), np.float32)
    for c in range(prm.NC):
        b, hf = c // 2, c % 2
        out[b, hf * prm.TOKC:(hf + 1) * prm.TOKC, :] = np.asarray(res.results[c]["out_my"])
    return out


# version 05
# speedup vs baseline: 1.0060x; 1.0060x over previous
import contextlib
import math

import numpy as np
import ml_dtypes

import concourse.bass as bass
import concourse.mybir as mybir
from concourse.alu_op_type import AluOpType as OP

AF = mybir.ActivationFunctionType
F32 = mybir.dt.float32
BF16 = mybir.dt.bfloat16
I32 = mybir.dt.int32

NEG = -1.0e30
B_EXP = 8.0
LN_EPS = 1e-5
Q_EPS = 1e-5
QTW = 512
SQ = 256


class Prm:
    def __init__(self, B=4, T=2048, D=2048, H=16, I=8192):
        self.B, self.T, self.D, self.H, self.I = B, T, D, H, I
        self.NC = 2 * B
        self.TOKC = T // 2
        self.NQT = self.TOKC // QTW
        self.NFC = D // 128
        self.NKT = T // 128
        assert D == H * 128 and self.TOKC % QTW == 0 and (120 * H) % 128 == 0
        self.FD = 3 * D
        self.NFCF = self.FD // 128
        self.R = 8
        self.DSTD = 120
        self.qs_end = H * 120
        self.ks_end = 2 * H * 120
        self.v_end = self.ks_end + D
        self.ql_end = self.v_end + 8 * H
        self.ktmax = [(self.TOKC + (qt + 1) * QTW) // 128 for qt in range(self.NQT)]
        self.NDT = sum(self.ktmax)
        self.E8 = D // self.NC
        self.I8 = I // self.NC
        self.NIC = I // 128
        self.slabs_qkv_q = mk_slabs(0, D)
        self.slabs_qkv_k = mk_slabs(D, 2 * D)
        self.slabs_qkv_v = mk_slabs(2 * D, 3 * D)
        self.slabs_wa_q = mk_slabs(0, self.qs_end)
        self.slabs_wa_k = mk_slabs(self.qs_end, self.ks_end)
        self.slabs_wa_v = mk_slabs(self.ks_end, self.v_end)
        self.slabs_wa_t = [(self.v_end, self.FD)]
        self.slabs_d = mk_slabs(0, D)


def mk_slabs(a, b, extra=()):
    bnds = sorted({x for x in range(0, b + 1, 512) if a < x < b} | {a, b} | {x for x in extra if a < x < b})
    return [(bnds[i], bnds[i + 1]) for i in range(len(bnds) - 1)]


def _slopes(H):
    return [2.0 ** (-8.0 / (i + 1)) for i in range(H)]



def perm_w(w, slabs):
    D, F = w.shape
    ndc = D // 128
    w3 = w.reshape(ndc, 128, F)
    parts = []
    for (a, b) in slabs:
        parts.append(np.ascontiguousarray(w3[:, :, a:b].transpose(1, 0, 2)).reshape(-1))
    return np.concatenate(parts)


def mk_dtiles(prm, qoff, alibi):
    tiles = np.empty((prm.NDT, 128, QTW), dtype=np.float32)
    dk = np.arange(128)[:, None]
    dq = np.arange(QTW)[None, :]
    i = 0
    for qt in range(prm.NQT):
        for kt in range(prm.ktmax[qt]):
            k = kt * 128 + dk
            q = qoff + qt * QTW + dq
            base = (k - q).astype(np.float32) if alibi else np.zeros((128, QTW), np.float32)
            tiles[i] = np.where(k <= q, base, NEG)
            i += 1
    return tiles.astype(ml_dtypes.bfloat16)


def host_prepare(prm, inputs):
    p = prm
    x = np.asarray(inputs["L0_attn_x"], np.float32)
    wqkv = perm_w(np.asarray(inputs["L0_attn_W_qkv"], np.float32),
                  p.slabs_qkv_q + p.slabs_qkv_k + p.slabs_qkv_v)
    wo = perm_w(np.asarray(inputs["L0_attn_W_o"], np.float32), p.slabs_d)
    wa = perm_w(np.asarray(inputs["L0_ffn_Wa"], np.float32),
                p.slabs_wa_q + p.slabs_wa_k + p.slabs_wa_v + p.slabs_wa_t)
    wp = perm_w(np.asarray(inputs["L0_ffn_Wp"], np.float32), p.slabs_d)
    ident = np.eye(128, dtype=np.float32)
    mats = [np.asarray(inputs[k], np.float32) for k in
            ("L1_attn_Wq", "L1_attn_Wk", "L1_attn_Wv", "L1_attn_Wo")]
    wg = np.asarray(inputs["L1_ffn_w_gate"], np.float32)
    wu = np.asarray(inputs["L1_ffn_w_up"], np.float32)
    wd = np.asarray(inputs["L1_ffn_w_down"], np.float32)
    wstd = np.asarray(inputs["L0_ffn_w_std"], np.float32).reshape(1, p.H)
    wrec = np.asarray(inputs["L0_ffn_w_rec"], np.float32).reshape(1, p.H)
    qk = 128.0 ** -0.25
    nsv = 1 + 2 * p.H
    sel = np.zeros((nsv, p.NFCF * 128), np.float32)
    for j in range(p.FD):
        if j < p.qs_end:
            sel[1 + j // 120, j] = qk
        elif j < p.ks_end:
            sel[1 + (j - p.qs_end) // 120, j] = qk
        elif j < p.v_end:
            sel[0, j] = 1.0
        elif j < p.ql_end:
            sel[1 + p.H + (j - p.v_end) // p.R, j] = qk
        else:
            sel[1 + p.H + (j - p.ql_end) // p.R, j] = qk
    rep8 = np.zeros((p.H, 128), np.float32)
    for pp_ in range(8 * p.H):
        rep8[pp_ // 8, pp_] = 1.0
    in_maps = []
    for c in range(p.NC):
        b, hf = c // 2, c % 2
        qoff = hf * p.TOKC
        in_maps.append({
            "x_my": np.ascontiguousarray(x[b, qoff:qoff + p.TOKC, :]),
            "wqkv_p": wqkv, "wo_p": wo, "wa_p": wa, "wp_p": wp,
            "ident": ident, "w_std": wstd, "w_rec": wrec,
            "is_b": np.full((1, 1), float(hf), np.float32),
            "sel": sel, "rep8": rep8,
            "dA": mk_dtiles(p, qoff, True),
            "dM": mk_dtiles(p, qoff, False),
            "wq1_s": np.ascontiguousarray(mats[0][c * p.E8:(c + 1) * p.E8, :]),
            "wk1_s": np.ascontiguousarray(mats[1][c * p.E8:(c + 1) * p.E8, :]),
            "wv1_s": np.ascontiguousarray(mats[2][c * p.E8:(c + 1) * p.E8, :]),
            "wo1_s": np.ascontiguousarray(mats[3][c * p.E8:(c + 1) * p.E8, :]),
            "wg_s": np.ascontiguousarray(wg[c * p.I8:(c + 1) * p.I8, :]),
            "wu_s": np.ascontiguousarray(wu[c * p.I8:(c + 1) * p.I8, :]),
            "wd_s": np.ascontiguousarray(wd[c * p.E8:(c + 1) * p.E8, :]),
        })
    return in_maps



ENGS = ("tensor", "vector", "scalar", "sync", "gpsimd")
SEMS = ("pe", "dve", "act", "spq", "plq", "cc", "ini",
        "s_in", "s_ins", "s_pq", "s_pq2", "s_wf", "s_msc", "s_cm", "s_out",
        "s_dsb", "s_tl", "s_xt0", "s_xt1", "s_ws0", "s_ws1", "s_kv0", "s_kv1",
        "s_evb0", "s_evb1", "s_w10", "s_w11", "s_w12", "s_wg0", "s_wg1",
        "s_fst0", "s_fst1", "s_fst2", "s_fst3")


class Bld:
    def __init__(self, nc):
        self.nc = nc
        self.q = {e: [] for e in ENGS}
        self.cnt = {s: 0 for s in SEMS}
        self.consts = set()
        self._cm = {}

    def sb(self, name, shape, dt):
        cm = self.nc.sbuf_tensor(name, list(shape), dt)
        t = cm.__enter__()
        self._cm[name] = cm
        return t

    def free(self, *names):
        order = list(self._cm)
        for n in sorted(names, key=order.index, reverse=True):
            self._cm.pop(n).__exit__(None, None, None)

    def ps(self, name, shape, dt):
        cm = self.nc.psum_tensor(name, list(shape), dt)
        t = cm.__enter__()
        self._cm["psum_" + name] = cm
        return t

    def K(self, v):
        v = float(v)
        self.consts.add(v)
        return v

    def emit(self, eng, fn, waits=(), inc=None, amt=1):
        self.q[eng].append((tuple(w for w in waits if w is not None and w[1] > 0), fn, inc, amt))
        if inc is not None:
            self.cnt[inc] += amt
            return self.cnt[inc]
        return None

    def w(self, sem):
        return (sem, self.cnt[sem])

    def dma(self, qname, out, in_, waits=(), sem=None, **kw):
        if sem is None:
            sem = "spq" if qname == "sync" else "plq"
        self.emit(qname, lambda e, o=out, i=in_, kw=kw: e.dma_start(out=o, in_=i, **kw),
                  waits=waits, inc=sem, amt=16)
        return self.cnt[sem]

    def dmaT(self, out, in_, waits=(), sem="spq"):
        self.emit("sync", lambda e, o=out, i=in_: e.dma_start_transpose(o, i),
                  waits=waits, inc=sem, amt=16)
        return self.cnt[sem]

    def mm(self, outap, lh, rh, start, stop, waits=()):
        return self.emit("tensor", lambda e, o=outap, a=lh, b=rh, s1=start, s2=stop:
                         e.matmul(o, a, b, start=s1, stop=s2), waits=waits, inc="pe")

    def vo(self, fn, waits=(), drain=False):
        if drain:
            self.emit("vector", lambda e: e.drain())
        return self.emit("vector", fn, waits=waits, inc="dve")

    def ao(self, fn, waits=()):
        return self.emit("scalar", fn, waits=waits, inc="act")

    def fence(self, engines, waits):
        for eng in engines:
            self.emit(eng, lambda e: e.nop(), waits=waits)

    def coll(self, kind, op, groups, ins, outs, waits=()):
        return self.emit("gpsimd", lambda e: e.collective_compute(
            kind, op, replica_groups=groups, ins=ins, outs=outs), waits=waits, inc="cc")



def build(prm, debug=False):
    p = prm
    nc = bass.Bass("TRN2", target_bir_lowering=False)
    H, D, I = p.H, p.D, p.I
    TOKC, NQT, NFC, NKT = p.TOKC, p.NQT, p.NFC, p.NKT
    NTT = TOKC // 128
    E8, I8, NIC = p.E8, p.I8, p.NIC
    NC = p.NC
    NSQ = TOKC // SQ
    I8h = I8 // 2
    I2h = I8h // 128
    E2 = E8 // 128
    NICh = NIC // 2
    WSW = max(NFC * 512, NFC * I8h, NICh * E8)
    inv_sqrt_dh = 1.0 / math.sqrt(128.0)
    qk_quarter = 128.0 ** -0.25
    denom_l1 = math.sqrt(float(D)) / float(p.T)
    slopes = _slopes(H)

    din = {}
    for nm, shape, dt in [
        ("x_my", [TOKC, D], F32), ("wqkv_p", [D * 3 * D], F32), ("wo_p", [D * D], F32),
        ("wa_p", [D * 3 * D], F32), ("wp_p", [D * D], F32), ("ident", [128, 128], F32),
        ("w_std", [1, H], F32), ("w_rec", [1, H], F32), ("is_b", [1, 1], F32),
        ("dA", [p.NDT, 128, QTW], BF16), ("dM", [p.NDT, 128, QTW], BF16),
        ("wq1_s", [E8, D], F32), ("wk1_s", [E8, D], F32), ("wv1_s", [E8, D], F32),
        ("wo1_s", [E8, D], F32), ("wg_s", [I8, D], F32), ("wu_s", [I8, D], F32),
        ("wd_s", [E8, I], F32), ("sel", [1 + 2 * H, p.NFCF * 128], F32),
        ("rep8", [H, 128], F32),
    ]:
        din[nm] = nc.dram_tensor(nm, shape, dt, kind="ExternalInput")
    out_my = nc.dram_tensor("out_my", [TOKC, D], F32, kind="ExternalOutput")

    KVW = max(H * TOKC, NTT * D)
    kv_snd = [nc.dram_tensor(f"kv_snd{i}", [2, 128, KVW], BF16) for i in range(2)]
    kv_out = [nc.dram_tensor(f"kv_out{i}", [2, 2, 128, KVW], BF16) for i in range(2)]
    off_snd = nc.dram_tensor("off_snd", [128, NFC], F32)
    off_out = nc.dram_tensor("off_out", [2, 128, NFC], F32)
    ar_snd = nc.dram_tensor("ar_snd", [1, 4], F32)
    ar_out = nc.dram_tensor("ar_out", [1, 4], F32)
    tmp_qkvo = nc.dram_tensor("tmp_qkvo", [4, E8, D], BF16)
    tmp_gu = nc.dram_tensor("tmp_gu", [2, I8, D], BF16)
    tmp_d = nc.dram_tensor("tmp_d", [E8, I], BF16)
    PR_QKVO, PR_GU, PR_D = D * E8, D * I8, I * E8
    PR_ALL = 4 * PR_QKVO + 2 * PR_GU + PR_D
    OFF_M = [0, PR_QKVO, 2 * PR_QKVO, 3 * PR_QKVO, 4 * PR_QKVO,
             4 * PR_QKVO + PR_GU, 4 * PR_QKVO + 2 * PR_GU]
    wT_snd = nc.dram_tensor("wT_snd", [PR_ALL], BF16)
    wT_out = nc.dram_tensor("wT_out", [NC, PR_ALL], BF16)
    sm_dram = nc.dram_tensor("sm_dram", [128, 4], F32)
    dbg_kind = dict(kind="ExternalOutput") if debug else {}
    dbg = [nc.dram_tensor(f"dbg_x{i}", [128, NFC * TOKC], BF16, **dbg_kind) for i in (1, 2, 3)]
    dbg_xq = nc.dram_tensor("dbg_xq", [128, (D // 128) * SQ], BF16, **dbg_kind)
    dbg_gu = nc.dram_tensor("dbg_gu", [128, (I // 128) * SQ], BF16, **dbg_kind)
    dbg_gs = nc.dram_tensor("dbg_gs", [1, 8 * QTW], F32, **dbg_kind)
    dbg_rb = nc.dram_tensor("dbg_rb", [128, SQ], F32, **dbg_kind)
    dbg_cap = nc.dram_tensor("dbg_cap", [128, 4 * SQ], F32, **dbg_kind)

    groups_pair = [[2 * i, 2 * i + 1] for i in range(NC // 2)]
    groups_all = [list(range(NC))]

    bd = Bld(nc)
    K = bd.K
    st = {"ydve": 0, "lnprev": 0, "rb_pe": 0, "ab_pe": 0}

    cpool = bd.sb("cpool", (128, 64), F32)
    cpool_b = bd.sb("cpool_b", (128, 64), BF16)
    ones_b = bd.sb("ones_b", (128, 1), BF16)
    onesr_f = bd.sb("onesr_f", (1, 128), F32)
    ones_f = bd.sb("ones_f", (128, 1), F32)
    ident_b = bd.sb("ident_b", (128, 128), BF16)
    xs = bd.sb("xs", (128, NFC * TOKC), BF16)
    stg = bd.sb("stg", (128, 2 * QTW), F32)
    stg2 = bd.sb("stg2", (128, 2 * QTW), F32)
    pbuf = bd.sb("pbuf", (128, 2 * QTW), BF16)
    hb = bd.sb("hb", (128, NFC * QTW), BF16)
    hsq = bd.sb("hsq", (128, 2 * QTW), BF16)
    rows = bd.sb("rows", (1, 6 * QTW), F32)
    rowm = bd.sb("rowm", (1, 2 * QTW), F32)
    srow = bd.sb("srow", (1, 128), F32)
    small = bd.sb("small", (128, 32), F32)
    evb = bd.sb("evb", (128, 2 * 512), BF16)
    i32s = bd.sb("i32s", (128, 1024), I32)
    csvec = bd.sb("csvec", (128, p.NFCF), F32)
    u128 = bd.sb("u128", (128, 2), F32)
    rbv = bd.sb("rbv", (128, QTW), F32)
    kvsb = bd.sb("kvsb", (128, QTW), F32)
    csc = bd.sb("csc", (128, 3 * SQ), F32)
    dbgsb = bd.sb("dbgsb", (128, 2 * SQ), F32)

    ps0 = bd.ps("ps0", (128, 512), F32)
    ps1 = bd.ps("ps1", (128, 512), F32)
    ps2 = bd.ps("ps2", (128, 512), F32)
    ps3 = bd.ps("ps3", (128, 512), F32)
    ps4 = bd.ps("ps4", (128, 512), F32)
    ps5 = bd.ps("ps5", (128, 512), F32)
    psS = bd.ps("psS", (128, 512), F32)
    psS2 = bd.ps("psS2", (128, 512), F32)
    pp = [ps0, ps1]
    ptr = [ps4, ps5]

    INI = ("ini", 1)

    xtm = bd.sb("xtm", (128, 2 * D), BF16)
    spi = bd.dma("gpsimd", ident_b[:, :], din["ident"][:, :], waits=(INI,), sem="s_in")
    for tt in range(NTT):
        sp = bd.dma("gpsimd", xtm[:, (tt % 2) * D:(tt % 2) * D + D],
                    din["x_my"][tt * 128:(tt + 1) * 128, :], sem=f"s_xt{tt % 2}",
                    waits=(INI,) + ((("act", st.get(("xg", tt - 2), 0)),) if tt >= 2 else ()))
        for fc in range(NFC):
            t = ptr[fc % 2]
            pe_m = bd.mm(t[:, 0:128],
                         xtm[:, (tt % 2) * D + fc * 128:(tt % 2) * D + fc * 128 + 128],
                         ident_b[:, :], True, True,
                         waits=(((f"s_xt{tt % 2}", sp), ("s_in", spi)) if fc == 0 else ()) +
                               (("act", st.get(("xb", fc % 2), 0)),))
            st[("xb", fc % 2)] = bd.ao(
                lambda e, t=t, dst=xs[:, fc * TOKC + tt * 128:fc * TOKC + tt * 128 + 128]:
                e.copy(dst, t[:, 0:128]), waits=(("pe", pe_m),))
        st[("xg", tt)] = bd.cnt["act"]
    bd.free("xtm")
    act_x = bd.cnt["act"]
    bd.fence(("gpsimd", "vector"), (bd.w("pe"),))

    wtmp = bd.sb("wtmp", (128, 2048), BF16)
    wf32 = bd.sb("wf32", (128, 2048), F32)
    racc = bd.sb("racc", (128, 16), F32)
    for mi, nm in enumerate(("wq1_s", "wk1_s", "wv1_s", "wo1_s")):
        for rt in range(E8 // 128):
            c1 = bd.dma("gpsimd", wtmp[:, 0:D], din[nm][rt * 128:(rt + 1) * 128, 0:D],
                        waits=(INI, bd.w("s_pq")))
            bd.dma("gpsimd", tmp_qkvo[mi, rt * 128:(rt + 1) * 128, :], wtmp[:, 0:D],
                   waits=(("plq", c1),), sem="s_pq")
    for mi, (nm, rows_n, cols) in enumerate((("wg_s", I8, D), ("wu_s", I8, D), ("wd_s", E8, I))):
        CW = min(2048, cols)
        k = 0
        for rt in range(rows_n // 128):
            for ccc in range(cols // CW):
                c1 = bd.dma("gpsimd", wf32[:, 0:CW],
                            din[nm][rt * 128:(rt + 1) * 128, ccc * CW:(ccc + 1) * CW],
                            waits=(bd.w("dve"),), sem="s_wf")
                if k < 15:
                    bd.vo(lambda e, k=k, cw=CW: e.reduce_sum(racc[:, k:k + 1], wf32[:, 0:cw],
                                                             axis=mybir.AxisListType.X,
                                                             apply_absolute_value=True),
                          waits=(("s_wf", c1),))
                else:
                    bd.vo(lambda e, cw=CW: e.reduce_sum(small[:, 0:1], wf32[:, 0:cw],
                                                        axis=mybir.AxisListType.X,
                                                        apply_absolute_value=True),
                          waits=(("s_wf", c1),))
                    bd.vo(lambda e, k=k: e.tensor_tensor(racc[:, k % 15:k % 15 + 1],
                                                         racc[:, k % 15:k % 15 + 1],
                                                         small[:, 0:1], OP.add), drain=True)
                k += 1
        nacc = min(k, 15)
        dvm = bd.vo(lambda e, nacc=nacc: e.reduce_sum(racc[:, 15:16], racc[:, 0:nacc],
                                                      axis=mybir.AxisListType.X),
                    waits=(("pe", st.get("pqs_pe", 0)),), drain=True)
        pe_m = bd.mm(psS[0:1, mi:mi + 1], ones_f[:, :], racc[:, 15:16], True, True,
                     waits=(("dve", dvm),))
        st["pqs_pe"] = pe_m
        bd.ao(lambda e, mi=mi: e.copy(srow[0:1, mi:mi + 1], psS[0:1, mi:mi + 1]),
              waits=(("pe", pe_m),))
    act_part = bd.cnt["act"]
    c1 = bd.dma("gpsimd", ar_snd[0:1, 0:4], srow[0:1, 0:4], waits=(("act", act_part),))
    cc_ar = bd.coll("AllReduce", OP.add, groups_all, [ar_snd.ap().opt()], [ar_out.ap().opt()],
                    waits=(("plq", c1),))
    c2 = bd.dma("gpsimd", srow[0:1, 4:8], ar_out[0:1, 0:4], waits=(("cc", cc_ar),), sem="s_msc")
    K(Q_EPS)
    dmean = K(1.0 / (I * D))
    bd.vo(lambda e: e.tensor_scalar(srow[0:1, 8:11], srow[0:1, 4:7], dmean, Q_EPS,
                                    OP.mult, OP.add), waits=(("s_msc", c2),))
    d2 = bd.vo(lambda e: e.reciprocal(srow[0:1, 12:15], srow[0:1, 8:11]), drain=True)
    pe_rs = None
    for mi in range(3):
        pe_rs = bd.mm(psS2[:, mi:mi + 1], onesr_f[0:1, :], srow[0:1, 12 + mi:13 + mi],
                      mi == 0, mi == 2, waits=(("dve", d2),) if mi == 0 else ())
    a_rs = bd.ao(lambda e: e.copy(small[:, 8:11], psS2[:, 0:3]), waits=(("pe", pe_rs),))
    K(1.0); K(-1.0)
    for mi, (nm, rows_n, cols) in enumerate((("wg_s", I8, D), ("wu_s", I8, D), ("wd_s", E8, I))):
        CW = min(2048, cols)
        for rt in range(rows_n // 128):
            for ccc in range(cols // CW):
                c1 = bd.dma("gpsimd", wf32[:, 0:CW],
                            din[nm][rt * 128:(rt + 1) * 128, ccc * CW:(ccc + 1) * CW],
                            waits=(bd.w("dve"), bd.w("plq")), sem="s_wf")
                d3 = None
                spq_prev = bd.w("s_pq")
                for pc in range(CW // 1024):
                    bd.vo(lambda e, mi=mi, pc=pc: e.tensor_scalar(i32s[:, 0:1024],
                                                                  wf32[:, pc * 1024:(pc + 1) * 1024],
                                                                  small[:, 8 + mi:9 + mi], None, OP.mult),
                          waits=(("s_wf", c1), ("act", a_rs), spq_prev), drain=pc > 0)
                    d3 = bd.vo(lambda e, pc=pc: e.tensor_scalar(wtmp[:, pc * 1024:(pc + 1) * 1024],
                                                                i32s[:, 0:1024], 1.0, -1.0,
                                                                OP.min, OP.max), drain=True)
                if mi < 2:
                    bd.dma("gpsimd", tmp_gu[mi, rt * 128:(rt + 1) * 128, ccc * CW:(ccc + 1) * CW],
                           wtmp[:, 0:CW], waits=(("dve", d3),), sem="s_pq")
                else:
                    bd.dma("gpsimd", tmp_d[rt * 128:(rt + 1) * 128, ccc * CW:(ccc + 1) * CW],
                           wtmp[:, 0:CW], waits=(("dve", d3),), sem="s_pq")
    plq_tmp = bd.cnt["s_pq"]
    wstg = bd.sb("wstg", (128, max(NFC * E8, NFC * I8, NIC * E8)), BF16)
    for mi in range(4):
        for dc in range(NFC):
            bd.dmaT(wstg[:, dc * E8:(dc + 1) * E8], tmp_qkvo[mi, 0:E8, dc * 128:(dc + 1) * 128],
                    waits=((("s_pq", plq_tmp),) if (mi == 0 and dc == 0) else ()) +
                          ((bd.w("spq"), bd.w("s_pq2")) if dc == 0 else ()))
        bd.dma("sync", wT_snd[OFF_M[mi]:OFF_M[mi] + PR_QKVO].rearrange("(pp c e) -> pp c e", pp=128, c=NFC),
               wstg[:, 0:NFC * E8].rearrange("pp (c e) -> pp c e", c=NFC), waits=(bd.w("spq"),), sem="s_pq2")
    for gi in range(2):
        for dc in range(NFC):
            bd.dmaT(wstg[:, dc * I8:(dc + 1) * I8], tmp_gu[gi, 0:I8, dc * 128:(dc + 1) * 128],
                    waits=(bd.w("spq"), bd.w("s_pq2")) if dc == 0 else ())
        for ghf in range(2):
            bd.dma("sync",
                   wT_snd[OFF_M[4 + gi] + ghf * (PR_GU // 2):OFF_M[4 + gi] + (ghf + 1) * (PR_GU // 2)]
                   .rearrange("(pp c e) -> pp c e", pp=128, c=NFC),
                   wstg[:, 0:NFC * I8].rearrange("pp (c e) -> pp c e", c=NFC)
                   [:, :, ghf * I8h:(ghf + 1) * I8h],
                   waits=(bd.w("spq"),), sem="s_pq2")
    for ic in range(NIC):
        bd.dmaT(wstg[:, ic * E8:(ic + 1) * E8], tmp_d[0:E8, ic * 128:(ic + 1) * 128],
                waits=(bd.w("spq"), bd.w("s_pq2")) if ic == 0 else ())
    sp_wt = bd.dma("sync", wT_snd[OFF_M[6]:OFF_M[6] + PR_D].rearrange("(pp c e) -> pp c e", pp=128, c=NIC),
                   wstg[:, 0:NIC * E8].rearrange("pp (c e) -> pp c e", c=NIC), waits=(bd.w("spq"),), sem="s_pq2")
    bd.dma("sync", srow[0:1, 16:16 + H], din["w_std"][:, :], waits=(INI,), sem="s_ins")
    bd.dma("sync", srow[0:1, 16 + H:16 + 2 * H], din["w_rec"][:, :], sem="s_ins")
    nsv = 1 + 2 * H
    sel_sb = bd.sb("sel_sb", (nsv, p.NFCF * 128), F32)
    rep_sb = bd.sb("rep_sb", (H, 128), F32)
    bd.dma("sync", sel_sb[:, :], din["sel"][:, :], sem="s_ins")
    bd.dma("sync", rep_sb[:, :], din["rep8"][:, :], sem="s_ins")
    sp_isb = bd.dma("sync", srow[0:1, 64:65], din["is_b"][:, :], sem="s_ins")
    K(1e-8); K(0.1); K(qk_quarter)
    du = bd.vo(lambda e: e.tensor_scalar(srow[0:1, 80:80 + H], srow[0:1, 16 + H:16 + 2 * H],
                                         0.1, None, OP.is_gt), waits=(("s_ins", sp_isb),))
    dstd = bd.vo(lambda e: e.tensor_scalar(srow[0:1, 16:16 + 2 * H], srow[0:1, 16:16 + 2 * H],
                                           1e-8, None, OP.max), drain=True)
    a_sq = bd.ao(lambda e: e.activation(srow[0:1, 16:16 + 2 * H], srow[0:1, 16:16 + 2 * H], AF.Sqrt),
                 waits=(("dve", dstd),))
    pe_sv = bd.mm(ps3[0:nsv, 0:1], srow[0:1, 15:16 + 2 * H], onesr_f[0:1, 0:1], True, True,
                  waits=(("act", a_sq), ("s_ins", sp_isb)))
    a_sv = bd.ao(lambda e: e.copy(small[0:nsv, 14:15], ps3[0:nsv, 0:1]), waits=(("pe", pe_sv),))
    pe_cs = None
    for c in range(p.NFCF):
        pe_cs = bd.mm(psS[:, c:c + 1], sel_sb[0:nsv, c * 128:(c + 1) * 128], small[0:nsv, 14:15],
                      True, True, waits=(("act", a_sv),) if c == 0 else ())
    a_cs = bd.ao(lambda e: e.copy(csvec[:, 0:p.NFCF], psS[:, 0:p.NFCF]), waits=(("pe", pe_cs),))
    pe_uc = bd.mm(ps3[0:H, 0:1], srow[0:1, 80:80 + H], onesr_f[0:1, 0:1], True, True,
                  waits=(("dve", du), ("act", a_sv)))
    a_uc = bd.ao(lambda e: e.copy(small[0:H, 15:16], ps3[0:H, 0:1]), waits=(("pe", pe_uc),))
    pe_u1 = bd.mm(psS2[:, 4:5], rep_sb[0:H, :], small[0:H, 15:16], True, True,
                  waits=(("act", a_uc),))
    a_u1 = bd.ao(lambda e: e.copy(u128[:, 0:1], psS2[:, 4:5]), waits=(("pe", pe_u1),))
    d_1mu = bd.vo(lambda e: e.tensor_scalar(u128[0:8 * H, 1:2], u128[0:8 * H, 0:1], -1.0, 1.0,
                                            OP.mult, OP.add), waits=(("act", a_u1),))
    pe_ib = bd.mm(psS2[:, 3:4], onesr_f[0:1, :], srow[0:1, 64:65], True, True,
                  waits=(("s_ins", sp_isb), ("act", a_rs)))
    a_ib = bd.ao(lambda e: e.copy(small[:, 12:13], psS2[:, 3:4]), waits=(("pe", pe_ib),))
    csvec_rdy = ("act", a_cs)
    plq_cs = None
    bd.free("wstg", "sel_sb", "rep_sb", "racc", "wf32", "wtmp")
    bd.fence(("gpsimd", "sync", "vector", "scalar"),
             (bd.w("pe"), bd.w("dve"), bd.w("s_pq"), bd.w("s_pq2"), bd.w("s_in")))

    def slab_offsets(all_slabs):
        offs, o = {}, 0
        for (a, b) in all_slabs:
            offs[(a, b)] = o
            o += D * (b - a)
        return offs

    offs_qkv = slab_offsets(p.slabs_qkv_q + p.slabs_qkv_k + p.slabs_qkv_v)
    offs_wa = slab_offsets(p.slabs_wa_q + p.slabs_wa_k + p.slabs_wa_v + p.slabs_wa_t)
    offs_d = slab_offsets(p.slabs_d)

    wsA = bd.sb("wsA", (128, WSW), BF16)
    wsB = bd.sb("wsB", (128, WSW), BF16)
    dsb = bd.sb("dsb", (128, max(p.ktmax) * QTW), BF16)
    kst = bd.sb("kst", (128, 2 * NKT * 128), BF16)
    vst = bd.sb("vst", (128, 2 * NKT * 128), BF16)
    yall = bd.sb("yall", (128, H * QTW), BF16)
    qall = bd.sb("qall", (128, H * QTW), BF16)

    wg_guard = [0, 0]
    slab_i = [0]

    def load_slab(perm, offmap, slab, extra=()):
        i = slab_i[0] % 2
        slab_i[0] += 1
        buf = (wsA, wsB)[i]
        wdt = slab[1] - slab[0]
        c = bd.dma("gpsimd", buf[:, 0:NFC * wdt].rearrange("pp (c e) -> pp c e", c=NFC),
                   perm[offmap[slab]:offmap[slab] + D * wdt]
                   .rearrange("(pp c e) -> pp c e", pp=128, c=NFC),
                   waits=(("pe", wg_guard[i]),) + tuple(extra), sem=f"s_ws{i}")
        return buf, wdt, (f"s_ws{i}", c), i

    def done_slab(i):
        wg_guard[i] = bd.cnt["pe"]

    K(inv_sqrt_dh); K(-B_EXP); K(1.0); K(0.0)
    for s in slopes:
        K(s)
    dt_off = []
    o = 0
    for qt in range(NQT):
        dt_off.append(o)
        o += p.ktmax[qt]

    def scores_qt(layer, qt, cc_kv, kv_gout, use_alibi, sp_dt, qrdy=()):
        nkt = p.ktmax[qt]
        for h in range(H):
            kbase = (h % 2) * NKT * 128
            guard = st.get(("kvg", h % 2), 0)
            spkv = None
            for r in range(2):
                lo, hi = max(0, r * NTT), min(nkt, (r + 1) * NTT)
                if hi <= lo:
                    continue
                bd.dma("sync", kst[:, kbase + lo * 128:kbase + hi * 128],
                       kv_gout[r, 0, :, h * TOKC + (lo - r * NTT) * 128:h * TOKC + (hi - r * NTT) * 128],
                       waits=(("cc", cc_kv), ("pe", guard)), sem=f"s_kv{h % 2}")
                spkv = bd.dma("sync", vst[:, kbase + lo * 128:kbase + hi * 128]
                              .rearrange("pp (t c) -> pp t c", c=128),
                              kv_gout[r, 1, :, :].rearrange("pp (t f) -> pp t f", f=D)
                              [:, (lo - r * NTT):(hi - r * NTT), h * 128:(h + 1) * 128],
                              sem=f"s_kv{h % 2}")
            for kt in range(nkt):
                sb = pp[kt % 2]
                pe_s = bd.mm(sb[:, 0:QTW], kst[:, kbase + kt * 128:kbase + (kt + 1) * 128],
                             qall[:, h * QTW:(h + 1) * QTW], True, True,
                             waits=(((f"s_kv{h % 2}", spkv),) + tuple(qrdy) if kt == 0 else ()) +
                                   (st.get(("bk", kt % 2), ("dve", 0)),))
                slot = stg[:, (kt % 2) * QTW:(kt % 2) * QTW + QTW]
                sc = slopes[h] if use_alibi else 1.0
                dv_s = bd.vo(lambda e, sl=slot, db=dsb[:, kt * QTW:(kt + 1) * QTW], sb_=sb, sc=sc:
                             e.scalar_tensor_tensor(sl, db, sc, sb_[:, 0:QTW], OP.mult, OP.add),
                             waits=(("pe", pe_s),) + ((sp_dt,) if kt == 0 else ()))
                st[("bk", kt % 2)] = ("dve", dv_s)
                pslot = pbuf[:, (kt % 2) * QTW:(kt % 2) * QTW + QTW]
                ac_p = bd.ao(lambda e, pl=pslot, sl=slot: e.activation(pl, sl, AF.Exp,
                                                                       bias=-B_EXP, scale=1.0),
                             waits=(("dve", dv_s),))
                bd.mm(psS[0:1, 0:QTW], ones_b[:, :], pslot, kt == 0, kt == nkt - 1,
                      waits=(("act", ac_p),) + ((("dve", st["ydve"]),) if kt == 0 else ()))
                bd.mm(ps2[:, 0:QTW], vst[:, kbase + kt * 128:kbase + (kt + 1) * 128], pslot,
                      kt == 0, kt == nkt - 1)
            st[("kvg", h % 2)] = bd.cnt["pe"]
            pe_fin = bd.cnt["pe"]
            dv_r = bd.vo(lambda e: e.reciprocal(rowm[0:1, 0:QTW], psS[0:1, 0:QTW]),
                         waits=(("pe", pe_fin), ("pe", st["rb_pe"])))
            st["rb_pe"] = bd.mm(ps3[:, 0:QTW], onesr_f[0:1, :], rowm[0:1, 0:QTW], True, True,
                                waits=(("dve", dv_r), ("dve", st["ydve"])))
            a_rb = bd.ao(lambda e: e.copy(rbv[:, 0:QTW], ps3[:, 0:QTW]),
                         waits=(("pe", st["rb_pe"]),))
            st["ydve"] = bd.vo(lambda e, dst=yall[:, h * QTW:(h + 1) * QTW]:
                               e.tensor_tensor(dst, ps2[:, 0:QTW], rbv[:, 0:QTW], OP.mult),
                               waits=(("act", a_rb),))

    def layer_norm(toff, w):
        K(1.0 / D); K(LN_EPS)
        for fc in range(NFC):
            a_sq2 = bd.ao(lambda e, fc=fc, sl=hsq[:, (fc % 2) * QTW:(fc % 2) * QTW + w]:
                          e.activation(sl, hb[:, fc * QTW:fc * QTW + w], AF.Square),
                          waits=(("dve", st.get(("og", fc % 2), 0)), ("pe", st.get(("lq", fc % 2), 0))))
            bd.mm(psS[0:1, 0:w], ones_b[:, :], hb[:, fc * QTW:fc * QTW + w], fc == 0, fc == NFC - 1,
                  waits=(("dve", st["lnprev"]), ("dve", st.get(("og", 0), 0)),
                         ("dve", st.get(("og", 1), 0))) if fc == 0 else ())
            st[("lq", fc % 2)] = bd.mm(psS2[0:1, 0:w], ones_b[:, :],
                                       hsq[:, (fc % 2) * QTW:(fc % 2) * QTW + w],
                                       fc == 0, fc == NFC - 1, waits=(("act", a_sq2),))
        pe_sums = bd.cnt["pe"]
        bd.vo(lambda e: e.tensor_scalar(rows[0:1, 0:w], psS[0:1, 0:w], 1.0 / D, None, OP.mult),
              waits=(("pe", pe_sums),))
        bd.vo(lambda e: e.tensor_scalar(rows[0:1, QTW:QTW + w], psS2[0:1, 0:w], 1.0 / D, None, OP.mult))
        bd.vo(lambda e: e.tensor_tensor(rows[0:1, 2 * QTW:2 * QTW + w], rows[0:1, 0:w],
                                        rows[0:1, 0:w], OP.mult), drain=True)
        d4 = bd.vo(lambda e: e.scalar_tensor_tensor(rows[0:1, QTW:QTW + w], rows[0:1, QTW:QTW + w],
                                                    LN_EPS, rows[0:1, 2 * QTW:2 * QTW + w],
                                                    OP.add, OP.subtract), drain=True)
        a1 = bd.ao(lambda e: e.activation(rows[0:1, QTW:QTW + w], rows[0:1, QTW:QTW + w], AF.Sqrt),
                   waits=(("dve", d4),))
        bd.vo(lambda e: e.reciprocal(rowm[0:1, 0:w], rows[0:1, QTW:QTW + w]),
              waits=(("act", a1), ("pe", st["ab_pe"]), ("pe", st["rb_pe"])))
        d6 = bd.vo(lambda e: e.scalar_tensor_tensor(rowm[0:1, QTW:QTW + w], rows[0:1, 0:w],
                                                    -1.0, rowm[0:1, 0:w], OP.mult, OP.mult),
                   drain=True)
        pe_a = bd.mm(ps3[:, 0:w], onesr_f[0:1, :], rowm[0:1, 0:w], True, True,
                     waits=(("dve", d6), ("dve", st["ydve"])))
        pe_b = bd.mm(psS[:, 0:w], onesr_f[0:1, :], rowm[0:1, QTW:QTW + w], True, True)
        st["ab_pe"] = pe_b
        st["rb_pe"] = pe_b
        for fc in range(NFC):
            sl = stg2[:, (fc % 2) * QTW:(fc % 2) * QTW + w]
            bd.vo(lambda e, sl=sl, fc=fc: e.tensor_tensor(sl, hb[:, fc * QTW:fc * QTW + w],
                                                          ps3[:, 0:w], OP.mult),
                  waits=(("pe", pe_b),) if fc == 0 else (), drain=fc >= 2)
            st["lnprev"] = bd.vo(
                lambda e, sl=sl, dst=xs[:, fc * TOKC + toff:fc * TOKC + toff + w]:
                e.tensor_tensor(dst, sl, psS[:, 0:w], OP.add), drain=True)

    def out_proj_ln(qt, perm, offmap, slabs, res_scale):
        dve_y = st["ydve"]
        for slab in slabs:
            buf, wdt, cld, bi = load_slab(perm, offmap, slab)
            for fcl in range(wdt // 128):
                fc = (slab[0] + fcl * 128) // 128
                sb = pp[fc % 2]
                for hc in range(H):
                    bd.mm(sb[:, 0:QTW], buf[:, hc * wdt + fcl * 128:hc * wdt + fcl * 128 + 128],
                          yall[:, hc * QTW:(hc + 1) * QTW], hc == 0, hc == H - 1,
                          waits=((cld, ("dve", dve_y),
                                  st.get(("bk", fc % 2), ("dve", 0))) if hc == 0 else ()))
                pe_m = bd.cnt["pe"]
                dog = bd.vo(
                    lambda e, sb=sb, fc=fc,
                    xsr=xs[:, fc * TOKC + qt * QTW:fc * TOKC + qt * QTW + QTW],
                    dst=hb[:, fc * QTW:(fc + 1) * QTW], rs=res_scale:
                    e.scalar_tensor_tensor(dst, sb[:, 0:QTW], rs, xsr, OP.mult, OP.add),
                    waits=(("pe", pe_m),))
                st[("og", fc % 2)] = dog
                st[("bk", fc % 2)] = ("dve", dog)
            done_slab(bi)
        layer_norm(qt * QTW, QTW)

    for slab in p.slabs_qkv_k:
        buf, wdt, cld, bi = load_slab(din["wqkv_p"], offs_qkv, slab, extra=(("act", act_x),))
        for hc in range(wdt // 128):
            h = (slab[0] - D + hc * 128) // 128
            for tg in range(NQT):
                m_ = (hc * NQT + tg) % 2
                sb = pp[m_]
                for dc in range(NFC):
                    bd.mm(sb[:, 0:QTW], buf[:, dc * wdt + hc * 128:dc * wdt + hc * 128 + 128],
                          xs[:, dc * TOKC + tg * QTW:dc * TOKC + tg * QTW + QTW],
                          dc == 0, dc == NFC - 1,
                          waits=((cld, (f"s_evb{m_}", st.get(("evb", m_), 0))) if dc == 0 else ()))
                pe_m = bd.cnt["pe"]
                a_e = bd.ao(lambda e, sb=sb, m_=m_: e.copy(evb[:, m_ * 512:m_ * 512 + QTW],
                                                           sb[:, 0:QTW]), waits=(("pe", pe_m),))
                st[("evb", m_)] = bd.dma(
                    "gpsimd", kv_snd[0][0, :, h * TOKC + tg * QTW:h * TOKC + (tg + 1) * QTW],
                    evb[:, m_ * 512:m_ * 512 + QTW], waits=(("act", a_e),),
                    sem=f"s_evb{m_}")
        done_slab(bi)
    for slab in p.slabs_qkv_v:
        buf, wdt, cld, bi = load_slab(din["wqkv_p"], offs_qkv, slab)
        for tt in range(NTT):
            for wo_ in range(0, wdt, 512):
                ww = min(512, wdt - wo_)
                sb = pp[tt % 2]
                for dc in range(NFC):
                    bd.mm(sb[:, 0:ww], xs[:, dc * TOKC + tt * 128:dc * TOKC + tt * 128 + 128],
                          buf[:, dc * wdt + wo_:dc * wdt + wo_ + ww], dc == 0, dc == NFC - 1,
                          waits=((cld, (f"s_evb{tt % 2}", st.get(("evb", tt % 2), 0)),
                                  st.get(("bk", tt % 2), ("dve", 0))) if dc == 0 else ()))
                pe_m = bd.cnt["pe"]
                a_e = bd.ao(lambda e, sb=sb, tt=tt, ww=ww:
                            e.copy(evb[:, (tt % 2) * 512:(tt % 2) * 512 + ww], sb[:, 0:ww]),
                            waits=(("pe", pe_m),))
                st[("bk", tt % 2)] = ("act", a_e)
                st[("evb", tt % 2)] = bd.dma(
                    "gpsimd", kv_snd[0][1, :, tt * D + (slab[0] - 2 * D) + wo_:
                                       tt * D + (slab[0] - 2 * D) + wo_ + ww],
                    evb[:, (tt % 2) * 512:(tt % 2) * 512 + ww], waits=(("act", a_e),),
                    sem=f"s_evb{tt % 2}")
        done_slab(bi)
    cc_kv0 = bd.coll("AllGather", OP.bypass, groups_pair,
                     [kv_snd[0].ap().opt()], [kv_out[0].ap().opt()],
                     waits=(bd.w("s_evb0"), bd.w("s_evb1")))
    cc_wT = bd.coll("AllGather", OP.bypass, groups_all, [wT_snd.ap().opt()], [wT_out.ap().opt()],
                    waits=(("s_pq2", sp_wt),))
    for qt in range(NQT):
        sp_dt = ("s_dsb", bd.dma("sync", dsb[:, 0:p.ktmax[qt] * QTW].rearrange("pp (n w) -> pp n w", w=QTW),
                                 din["dA"][dt_off[qt]:dt_off[qt] + p.ktmax[qt], :, :]
                                 .rearrange("n pp w -> pp n w"), waits=(bd.w("dve"),), sem="s_dsb"))
        for slab in p.slabs_qkv_q:
            buf, wdt, cld, bi = load_slab(din["wqkv_p"], offs_qkv, slab)
            for hc in range(wdt // 128):
                h = (slab[0] + hc * 128) // 128
                sb = pp[hc % 2]
                for dc in range(NFC):
                    bd.mm(sb[:, 0:QTW], buf[:, dc * wdt + hc * 128:dc * wdt + hc * 128 + 128],
                          xs[:, dc * TOKC + qt * QTW:dc * TOKC + qt * QTW + QTW],
                          dc == 0, dc == NFC - 1,
                          waits=((cld, st.get(("bk", hc % 2), ("dve", 0))) if dc == 0 else ()))
                pe_m = bd.cnt["pe"]
                a_q = bd.ao(
                    lambda e, sb=sb, h=h: e.activation(qall[:, h * QTW:(h + 1) * QTW], sb[:, 0:QTW],
                                                       AF.Copy, bias=0.0, scale=inv_sqrt_dh),
                    waits=(("pe", pe_m),))
                st[("bk", hc % 2)] = ("act", a_q)
            done_slab(bi)
        scores_qt(0, qt, cc_kv0, kv_out[0], True, sp_dt, qrdy=(bd.w("act"),))
        out_proj_ln(qt, din["wo_p"], offs_d, p.slabs_d, 1.0)
    st["xrdy"] = bd.cnt["dve"]
    if debug:
        bd.dma("gpsimd", dbg[0][:, :], xs[:, :], waits=(("dve", st["xrdy"]),), sem="s_msc")

    fst = bd.sb("fst", (128, 4 * QTW), BF16)
    NTC = 16 * H // 128
    tls = bd.sb("tls", (128, max(1, NTC) * TOKC), BF16)
    tb = bd.sb("tb", (128, 4 * TOKC), BF16)

    def head_pieces(base):
        out = {}
        for h in range(H):
            c0, c1 = base + h * p.DSTD, base + (h + 1) * p.DSTD
            ps_ = []
            c = c0
            while c < c1:
                nxt = min(c1, (c // 128 + 1) * 128)
                ps_.append((c, nxt - c))
                c = nxt
            out[h] = ps_
        return out

    qpieces = head_pieces(0)
    kpieces = head_pieces(p.qs_end)
    slotc = [0]

    def fused_chunk(buf, wdt, slab0, cld, gcol0, tg, slot):
        cc = gcol0 // 128
        sb = pp[slot % 2]
        for dc in range(NFC):
            bd.mm(sb[:, 0:QTW], buf[:, dc * wdt + (gcol0 - slab0):dc * wdt + (gcol0 - slab0) + 128],
                  xs[:, dc * TOKC + tg * QTW:dc * TOKC + tg * QTW + QTW],
                  dc == 0, dc == NFC - 1,
                  waits=((cld, (f"s_fst{slot}", st.get(("fe", slot), 0)),
                          csvec_rdy, ("dve", st.get("xrdy", 0)),
                          st.get(("bk", slot % 2), ("dve", 0))) if dc == 0 else ()))
        pe_m = bd.cnt["pe"]
        dv = bd.vo(lambda e, sb=sb, cc=cc, sl=fst[:, slot * QTW:(slot + 1) * QTW]:
                   e.tensor_scalar(sl, sb[:, 0:QTW], csvec[:, cc:cc + 1], None, OP.mult),
                   waits=(("pe", pe_m),))
        st[("bk", slot % 2)] = ("dve", dv)
        return dv

    for slabs, which in ((p.slabs_wa_k, "k"), (p.slabs_wa_t, "t")):
        for slab in slabs:
            buf, wdt, cld, bi = load_slab(din["wa_p"], offs_wa, slab)
            for gcol0 in range(slab[0], slab[1], 128):
                for tg in range(NQT):
                    slot = slotc[0] % 4
                    slotc[0] += 1
                    dv = fused_chunk(buf, wdt, slab[0], cld, gcol0, tg, slot)
                    if which == "k":
                        for h in range(H):
                            for (c0, ln) in kpieces[h]:
                                if c0 // 128 != gcol0 // 128:
                                    continue
                                po = c0 % 128
                                ro = c0 - (p.qs_end + h * p.DSTD)
                                st[("fe", slot)] = bd.dma(
                                    "gpsimd",
                                    kv_snd[1][0, ro:ro + ln, h * TOKC + tg * QTW:h * TOKC + (tg + 1) * QTW],
                                    fst[po:po + ln, slot * QTW:(slot + 1) * QTW],
                                    waits=(("dve", dv),), sem=f"s_fst{slot}")
                    else:
                        ci = (gcol0 - p.v_end) // 128
                        st[("fe", slot)] = bd.dma(
                            "gpsimd", tls[:, ci * TOKC + tg * QTW:ci * TOKC + (tg + 1) * QTW],
                            fst[:, slot * QTW:(slot + 1) * QTW], waits=(("dve", dv),), sem=f"s_fst{slot}")
            done_slab(bi)
    nt8 = 8 * H
    khi_ci, khi_po = (p.ql_end - p.v_end) // 128, (p.ql_end - p.v_end) % 128
    c5 = bd.dma("gpsimd", tb[0:nt8, 0:TOKC], tls[0:nt8, 0:TOKC],
                waits=(bd.w("s_fst0"), bd.w("s_fst1"), bd.w("s_fst2"), bd.w("s_fst3")), sem="s_msc")
    c6 = bd.dma("gpsimd", tb[0:nt8, TOKC:2 * TOKC],
                tls[khi_po:khi_po + nt8, khi_ci * TOKC:(khi_ci + 1) * TOKC], sem="s_msc")
    dvq = None
    for tg in range(NQT):
        a_, b_ = tg * QTW, tg * QTW + QTW
        bd.vo(lambda e, a_=a_, b_=b_: e.tensor_scalar(stg[0:nt8, 0:QTW], tb[0:nt8, a_:b_],
                                                      u128[0:nt8, 1:2], None, OP.mult),
              waits=(("s_msc", c6), ("dve", d_1mu)), drain=True)
        bd.vo(lambda e, a_=a_, b_=b_: e.tensor_scalar(stg2[0:nt8, 0:QTW], tb[0:nt8, TOKC + a_:TOKC + b_],
                                                      u128[0:nt8, 0:1], None, OP.mult))
        dvq = bd.vo(lambda e, a_=a_, b_=b_: e.tensor_tensor(tb[0:nt8, 2 * TOKC + a_:2 * TOKC + b_],
                                                            stg[0:nt8, 0:QTW], stg2[0:nt8, 0:QTW], OP.add),
                    drain=True)
        bd.vo(lambda e, a_=a_, b_=b_: e.tensor_scalar(stg[0:nt8, 0:QTW], tb[0:nt8, a_:b_],
                                                      u128[0:nt8, 0:1], None, OP.mult), drain=True)
        bd.vo(lambda e, a_=a_, b_=b_: e.tensor_scalar(stg2[0:nt8, 0:QTW], tb[0:nt8, TOKC + a_:TOKC + b_],
                                                      u128[0:nt8, 1:2], None, OP.mult))
        dvq = bd.vo(lambda e, a_=a_, b_=b_: e.tensor_tensor(tb[0:nt8, 3 * TOKC + a_:3 * TOKC + b_],
                                                            stg[0:nt8, 0:QTW], stg2[0:nt8, 0:QTW], OP.add),
                    drain=True)
    for h in range(H):
        bd.dma("gpsimd", kv_snd[1][0, 120:128, h * TOKC:(h + 1) * TOKC],
               tb[8 * h:8 * h + 8, 3 * TOKC:4 * TOKC], waits=(("dve", dvq),), sem="s_tl")
    for slab in p.slabs_wa_v:
        buf, wdt, cld, bi = load_slab(din["wa_p"], offs_wa, slab)
        for tt in range(NTT):
            for wo_ in range(0, wdt, 512):
                ww = min(512, wdt - wo_)
                sb = pp[tt % 2]
                for dc in range(NFC):
                    bd.mm(sb[:, 0:ww], xs[:, dc * TOKC + tt * 128:dc * TOKC + tt * 128 + 128],
                          buf[:, dc * wdt + wo_:dc * wdt + wo_ + ww], dc == 0, dc == NFC - 1,
                          waits=((cld, (f"s_evb{tt % 2}", st.get(("evb", tt % 2), 0)),
                                  ("dve", st.get("xrdy", 0)),
                                  st.get(("bk", tt % 2), ("dve", 0))) if dc == 0 else ()))
                pe_m = bd.cnt["pe"]
                a_e = bd.ao(lambda e, sb=sb, tt=tt, ww=ww:
                            e.copy(evb[:, (tt % 2) * 512:(tt % 2) * 512 + ww], sb[:, 0:ww]),
                            waits=(("pe", pe_m),))
                st[("bk", tt % 2)] = ("act", a_e)
                st[("evb", tt % 2)] = bd.dma(
                    "gpsimd", kv_snd[1][1, :, tt * D + (slab[0] - p.ks_end) + wo_:
                                       tt * D + (slab[0] - p.ks_end) + wo_ + ww],
                    evb[:, (tt % 2) * 512:(tt % 2) * 512 + ww], waits=(("act", a_e),),
                    sem=f"s_evb{tt % 2}")
        done_slab(bi)
    cc_kv1 = bd.coll("AllGather", OP.bypass, groups_pair,
                     [kv_snd[1].ap().opt()], [kv_out[1].ap().opt()],
                     waits=(bd.w("s_evb0"), bd.w("s_evb1"), bd.w("s_fst0"), bd.w("s_fst1"),
                            bd.w("s_fst2"), bd.w("s_fst3"), bd.w("s_tl")))
    for qt in range(NQT):
        sp_dt = ("s_dsb", bd.dma("sync", dsb[:, 0:p.ktmax[qt] * QTW].rearrange("pp (n w) -> pp n w", w=QTW),
                                 din["dM"][dt_off[qt]:dt_off[qt] + p.ktmax[qt], :, :]
                                 .rearrange("n pp w -> pp n w"), waits=(bd.w("dve"),), sem="s_dsb"))
        for slab in p.slabs_wa_q:
            buf, wdt, cld, bi = load_slab(din["wa_p"], offs_wa, slab)
            for gcol0 in range(slab[0], slab[1], 128):
                slot = slotc[0] % 4
                slotc[0] += 1
                dv = fused_chunk(buf, wdt, slab[0], cld, gcol0, qt, slot)
                for h in range(H):
                    for (c0, ln) in qpieces[h]:
                        if c0 // 128 != gcol0 // 128:
                            continue
                        po = c0 % 128
                        ro = c0 - h * p.DSTD
                        st[("fe", slot)] = bd.dma(
                            "gpsimd", qall[ro:ro + ln, h * QTW:(h + 1) * QTW],
                            fst[po:po + ln, slot * QTW:(slot + 1) * QTW], waits=(("dve", dv),),
                            sem=f"s_fst{slot}")
            done_slab(bi)
        for h in range(H):
            bd.dma("gpsimd", qall[120:128, h * QTW:(h + 1) * QTW],
                   tb[8 * h:8 * h + 8, 2 * TOKC + qt * QTW:2 * TOKC + (qt + 1) * QTW],
                   waits=(("dve", dvq),), sem="s_tl")
        scores_qt(1, qt, cc_kv1, kv_out[1], False, sp_dt,
                  qrdy=(bd.w("s_fst0"), bd.w("s_fst1"), bd.w("s_fst2"), bd.w("s_fst3"), bd.w("s_tl")))
        out_proj_ln(qt, din["wp_p"], offs_d, p.slabs_d, 1.0)
    st["xrdy"] = bd.cnt["dve"]
    if debug:
        bd.dma("gpsimd", dbg[1][:, :], xs[:, :], waits=(("dve", st["xrdy"]),), sem="s_msc")
    bd.free("fst", "tls", "tb", "wsA", "wsB", "dsb", "kst", "vst", "yall", "qall")
    bd.fence(("gpsimd", "sync", "vector"),
             (bd.w("pe"), bd.w("dve"), bd.w("s_fst0"), bd.w("s_fst1"), bd.w("s_fst2"),
              bd.w("s_fst3"), bd.w("s_tl"), bd.w("s_msc")))

    kvt = bd.sb("kvt", (128, NFC * TOKC), F32)
    vh = bd.sb("vh", (128, NFC * QTW), BF16)
    carr = bd.sb("carr", (128, 2 * NFC), F32)
    offa = bd.sb("offa", (128, NFC), F32)
    w1A = bd.sb("w1A", (128, NFC * E8), BF16)
    w1B = bd.sb("w1B", (128, NFC * E8), BF16)
    w1C = bd.sb("w1C", (128, NFC * E8), BF16)
    w1g = {0: 0, 1: 0, 2: 0}
    bd.emit("gpsimd", lambda e: e.memset(carr[:, 0:NFC], 0.0), inc="ini")

    def load_w1(mi, r, bufi, extra=()):
        buf = (w1A, w1B, w1C)[bufi]
        c = bd.dma("sync", buf[:, 0:NFC * E8].rearrange("pp (c e) -> pp c e", c=NFC),
                   wT_out[r, OFF_M[mi]:OFF_M[mi] + PR_QKVO]
                   .rearrange("(pp c e) -> pp c e", pp=128, c=NFC),
                   waits=(("cc", cc_wT), ("pe", w1g[bufi])) + tuple(extra), sem=f"s_w1{bufi}")
        return buf, (f"s_w1{bufi}", c)

    for r in range(NC):
        bk, ck = load_w1(1, r, 0)
        bv, cv = load_w1(2, r, 1)
        for el in range(E2):
            e_g = r * E2 + el
            for qt in range(NQT):
                for dc in range(NFC):
                    bd.mm(ps0[:, 0:QTW], bk[:, dc * E8 + el * 128:dc * E8 + el * 128 + 128],
                          xs[:, dc * TOKC + qt * QTW:dc * TOKC + qt * QTW + QTW],
                          dc == 0, dc == NFC - 1,
                          waits=((ck, ("dve", st.get("kvd", 0)), ("dve", st.get("xrdy", 0)),
                                  st.get(("bk", 0), ("dve", 0))) if dc == 0 else ()))
                for dc in range(NFC):
                    bd.mm(ps1[:, 0:QTW], bv[:, dc * E8 + el * 128:dc * E8 + el * 128 + 128],
                          xs[:, dc * TOKC + qt * QTW:dc * TOKC + qt * QTW + QTW],
                          dc == 0, dc == NFC - 1,
                          waits=((cv, st.get(("bk", 1), ("dve", 0))) if dc == 0 else ()))
                pe_m = bd.cnt["pe"]
                a_kc = bd.ao(lambda e: e.copy(kvsb[:, 0:QTW], ps0[:, 0:QTW]),
                             waits=(("pe", pe_m), ("dve", st.get("kvd", 0))))
                dkv = bd.vo(lambda e: e.tensor_tensor(stg[:, 0:QTW], kvsb[:, 0:QTW], ps1[:, 0:QTW], OP.mult),
                            waits=(("act", a_kc), ("ini", 2)))
                st[("bk", 0)] = ("dve", dkv)
                st[("bk", 1)] = ("dve", dkv)
                bd.vo(lambda e, e_g=e_g, qt=qt:
                      e.tensor_tensor_scan(kvt[:, e_g * TOKC + qt * QTW:e_g * TOKC + (qt + 1) * QTW],
                                           stg[:, 0:QTW], stg[:, 0:QTW],
                                           carr[:, e_g:e_g + 1], OP.add, OP.bypass), drain=True)
                st["kvd"] = bd.vo(lambda e, e_g=e_g, qt=qt:
                                  e.tensor_copy(carr[:, e_g:e_g + 1],
                                                kvt[:, e_g * TOKC + (qt + 1) * QTW - 1:
                                                    e_g * TOKC + (qt + 1) * QTW]), drain=True)
        w1g[0] = bd.cnt["pe"]
        w1g[1] = bd.cnt["pe"]
    c7 = bd.dma("gpsimd", off_snd[:, :], carr[:, 0:NFC], waits=(("dve", st["kvd"]),), sem="s_msc")
    cc_off = bd.coll("AllGather", OP.bypass, groups_pair,
                     [off_snd.ap().opt()], [off_out.ap().opt()], waits=(("s_msc", c7),))
    c8 = bd.dma("gpsimd", carr[:, NFC:2 * NFC], off_out[0, :, :], waits=(("cc", cc_off),), sem="s_msc")
    dvo = bd.vo(lambda e: e.tensor_scalar(offa[:, 0:NFC], carr[:, NFC:2 * NFC], small[:, 12:13],
                                          None, OP.mult), waits=(("s_msc", c8), ("act", a_ib)))
    K(denom_l1)
    for qt in range(NQT):
        for r in range(NC):
            bq, cq = load_w1(0, r, r % 2)
            for el in range(E2):
                e_g = r * E2 + el
                for dc in range(NFC):
                    bd.mm(ps0[:, 0:QTW], bq[:, dc * E8 + el * 128:dc * E8 + el * 128 + 128],
                          xs[:, dc * TOKC + qt * QTW:dc * TOKC + qt * QTW + QTW],
                          dc == 0, dc == NFC - 1,
                          waits=((cq, ("dve", st.get("vhd", 0)), ("dve", st.get("xrdy", 0)),
                                  st.get(("bk", 0), ("dve", 0))) if dc == 0 else ()))
                pe_m = bd.cnt["pe"]
                st["vhd"] = bd.vo(
                    lambda e, e_g=e_g, qt=qt:
                    e.scalar_tensor_tensor(vh[:, e_g * QTW:(e_g + 1) * QTW],
                                           kvt[:, e_g * TOKC + qt * QTW:e_g * TOKC + (qt + 1) * QTW],
                                           offa[:, e_g:e_g + 1], ps0[:, 0:QTW], OP.add, OP.mult),
                    waits=(("pe", pe_m), ("dve", dvo)))
                st[("bk", 0)] = ("dve", st["vhd"])
            w1g[r % 2] = bd.cnt["pe"]
        for r in range(NC):
            bo, co = load_w1(3, r, 2)
            for fl in range(E2):
                fc = r * E2 + fl
                sb = pp[fc % 2]
                for ec in range(NFC):
                    bd.mm(sb[:, 0:QTW], bo[:, ec * E8 + fl * 128:ec * E8 + fl * 128 + 128],
                          vh[:, ec * QTW:(ec + 1) * QTW], ec == 0, ec == NFC - 1,
                          waits=((co, st.get(("bk", fc % 2), ("dve", 0)),
                                  ("dve", st["vhd"])) if ec == 0 else ()))
                pe_m = bd.cnt["pe"]
                dog = bd.vo(
                    lambda e, sb=sb, fc=fc,
                    xsr=xs[:, fc * TOKC + qt * QTW:fc * TOKC + qt * QTW + QTW],
                    dst=hb[:, fc * QTW:(fc + 1) * QTW]:
                    e.scalar_tensor_tensor(dst, sb[:, 0:QTW], denom_l1, xsr, OP.mult, OP.add),
                    waits=(("pe", pe_m),))
                st[("og", fc % 2)] = dog
                st[("bk", fc % 2)] = ("dve", dog)
            w1g[2] = bd.cnt["pe"]
        layer_norm(qt * QTW, QTW)
    st["xrdy"] = bd.cnt["dve"]
    if debug:
        bd.dma("gpsimd", dbg[2][:, :], xs[:, :], waits=(("dve", st["xrdy"]),), sem="s_msc")
    bd.free("kvt", "vh", "carr", "offa", "w1A", "w1B", "w1C")
    bd.fence(("sync", "vector", "scalar"),
             (bd.w("pe"), bd.w("dve"), bd.w("s_msc"),
              bd.w("s_w10"), bd.w("s_w11"), bd.w("s_w12")))

    if debug:
        dcap0 = bd.vo(lambda e: e.tensor_copy(dbgsb[:, SQ:2 * SQ], xs[:, 0:SQ]), drain=True)
        bd.dma("gpsimd", dbg_cap[:, SQ:2 * SQ], dbgsb[:, SQ:2 * SQ],
               waits=(("dve", dcap0),), sem="s_msc")
    wsA = bd.sb("wsA2", (128, WSW), BF16)
    wsB = bd.sb("wsB2", (128, WSW), BF16)
    xq = bd.sb("xq", (128, NFC * SQ), BF16)
    gsi = bd.sb("gsi", (128, I2h * SQ), BF16)
    gu = bd.sb("gu", (128, NIC * SQ), BF16)
    mfold = bd.sb("mfold", (128, 2 * SQ), BF16)
    wgg = [0, 0]
    K(1.0 / 127.0)

    def colmax_to_row(src_fn, nch, row_off):
        dvm = None
        for i in range(nch):
            if i == 0:
                bd.vo(lambda e, s=src_fn(0): e.tensor_copy(mfold[:, 0:SQ], s), drain=True)
                dvm = bd.vo(lambda e, s=src_fn(0): e.tensor_copy(mfold[:, SQ:2 * SQ], s))
            else:
                bd.vo(lambda e, s=src_fn(i): e.tensor_tensor(mfold[:, 0:SQ], mfold[:, 0:SQ],
                                                             s, OP.max), drain=True)
                dvm = bd.vo(lambda e, s=src_fn(i): e.tensor_tensor(mfold[:, SQ:2 * SQ],
                                                                   mfold[:, SQ:2 * SQ], s, OP.min),
                            drain=True)
        bd.vo(lambda e: e.tensor_scalar(mfold[:, SQ:2 * SQ], mfold[:, SQ:2 * SQ], -1.0, None, OP.mult),
              drain=True)
        dvm = bd.vo(lambda e: e.tensor_tensor(mfold[:, 0:SQ], mfold[:, 0:SQ], mfold[:, SQ:2 * SQ],
                                              OP.max), drain=True)

        for j in range(SQ // 128):
            t = ptr[j % 2]
            pe_t = bd.mm(t[:, 0:128], mfold[:, j * 128:(j + 1) * 128], ident_b[:, :], True, True,
                         waits=(("dve", dvm), ("dve", st.get(("cmr", j % 2), 0)),
                                ("act", st.get(("xb", j % 2), 0))))
            st[("cmr", j % 2)] = bd.vo(
                lambda e, t=t, j=j: e.reduce_max(small[:, 16 + j:17 + j], t[:, 0:128],
                                                 axis=mybir.AxisListType.X, apply_absolute_value=True),
                waits=(("pe", pe_t),))
        dvm2 = bd.cnt["dve"]
        c0_ = bd.dma("gpsimd", sm_dram[:, 0:SQ // 128], small[:, 16:16 + SQ // 128],
                     waits=(("dve", dvm2),))
        c = bd.dma("gpsimd", rows[0:1, row_off:row_off + SQ].rearrange("o (j pp) -> o j pp", pp=128),
                   sm_dram[:, 0:SQ // 128].rearrange("(o pp) j -> o j pp", o=1),
                   waits=(("plq", c0_), bd.w("s_msc")), allow_slow_non_contiguous=True, sem="s_cm")
        dts = bd.vo(lambda e: e.tensor_scalar(rows[0:1, row_off:row_off + SQ],
                                              rows[0:1, row_off:row_off + SQ],
                                              1.0 / 127.0, Q_EPS, OP.mult, OP.add),
                    waits=(("s_cm", c),), drain=True)
        if debug and st.get("cm_calls", 0) == 0:
            dcapB = bd.vo(lambda e: e.tensor_copy(dbgsb[:, SQ:2 * SQ], xs[:, 0:SQ]),
                          waits=(bd.w("s_msc"),), drain=True)
            bd.dma("gpsimd", dbg_cap[:, 3 * SQ:4 * SQ], dbgsb[:, SQ:2 * SQ],
                   waits=(("dve", dcapB),), sem="s_msc")
        st["cm_calls"] = st.get("cm_calls", 0) + 1
        d2_ = bd.vo(lambda e: e.reciprocal(rowm[0:1, 0:SQ], rows[0:1, row_off:row_off + SQ]),
                    waits=(("pe", st["rb_pe"]),), drain=True)
        return d2_

    for qs in range(NSQ):
        toff = qs * SQ
        if qs > 0:
            bd.fence(("vector", "scalar"), (bd.w("pe"), bd.w("s_msc")))
        d2 = colmax_to_row(lambda i: xs[:, i * TOKC + toff:i * TOKC + toff + SQ], NFC, 3 * QTW)
        if debug and qs == 0:
            dcapC = bd.vo(lambda e: e.tensor_copy(dbgsb[:, SQ:2 * SQ], xs[:, 0:SQ]),
                          waits=(bd.w("s_msc"),), drain=True)
            bd.dma("gpsimd", dbg_cap[:, 2 * SQ:3 * SQ], dbgsb[:, SQ:2 * SQ],
                   waits=(("dve", dcapC),), sem="s_msc")
        st["rb_pe"] = pe_rbx = bd.mm(ps3[:, 0:SQ], onesr_f[0:1, :], rowm[0:1, 0:SQ], True, True,
                                     waits=(("dve", d2), ("dve", st["ydve"])))
        if debug and qs == 0:
            dcap = bd.vo(lambda e: e.tensor_copy(dbgsb[:, 0:SQ], ps3[:, 0:SQ]),
                         waits=(("pe", pe_rbx), bd.w("s_msc")), drain=True)
            dcap2 = bd.vo(lambda e, xsr=xs[:, toff:toff + SQ]:
                          e.tensor_copy(dbgsb[:, SQ:2 * SQ], xsr), drain=True)
            bd.dma("gpsimd", dbg_rb[:, :], dbgsb[:, 0:SQ], waits=(("dve", dcap),), sem="s_msc")
            bd.dma("gpsimd", dbg_cap[:, 0:SQ], dbgsb[:, SQ:2 * SQ], waits=(("dve", dcap2),),
                   sem="s_msc")
        for fc in range(NFC):
            dvq1 = bd.vo(lambda e, fc=fc,
                         xsr=xs[:, fc * TOKC + toff:fc * TOKC + toff + SQ]:
                         e.tensor_tensor(i32s[:, (fc % 2) * 512:(fc % 2) * 512 + SQ],
                                         xsr, ps3[:, 0:SQ], OP.mult),
                         waits=(("pe", pe_rbx),) + ((("act", st.get(("xqe", fc % 2), 0)),) if fc >= 2 else ()),
                         drain=fc >= 2)
            st[("xqe", fc % 2)] = bd.ao(lambda e, fc=fc:
                                        e.copy(xq[:, fc * SQ:(fc + 1) * SQ],
                                               i32s[:, (fc % 2) * 512:(fc % 2) * 512 + SQ]),
                                        waits=(("dve", dvq1),))
        act_xq = bd.cnt["act"]
        dve_xq = bd.cnt["dve"]
        if debug and qs == 0:
            bd.dma("gpsimd", dbg_xq[:, :], xq[:, :], waits=(("act", act_xq),), sem="s_msc")
        d3 = bd.vo(lambda e: e.tensor_scalar(rowm[0:1, 0:SQ], rows[0:1, 3 * QTW:3 * QTW + SQ],
                                             srow[0:1, 8:9], None, OP.mult),
                   waits=(("pe", pe_rbx),), drain=True)
        pe_cg = bd.mm(ps4[:, 0:SQ], onesr_f[0:1, :], rowm[0:1, 0:SQ], True, True,
                      waits=(("dve", d3), ("dve", st.get(("cmr", 0), 0)), ("dve", st.get(("cmr", 1), 0))))
        a_cg = bd.ao(lambda e: e.copy(csc[:, 0:SQ], ps4[:, 0:SQ]), waits=(("pe", pe_cg),))
        d4 = bd.vo(lambda e: e.tensor_scalar(rowm[0:1, QTW:QTW + SQ], rows[0:1, 3 * QTW:3 * QTW + SQ],
                                             srow[0:1, 9:10], None, OP.mult))
        pe_cu = bd.mm(ps5[:, 0:SQ], onesr_f[0:1, :], rowm[0:1, QTW:QTW + SQ], True, True,
                      waits=(("dve", d4),))
        a_cu = bd.ao(lambda e: e.copy(csc[:, SQ:2 * SQ], ps5[:, 0:SQ]), waits=(("pe", pe_cu),))
        st[("xb", 0)] = a_cg
        st[("xb", 1)] = a_cu
        st["rb_pe"] = pe_cu
        for r in range(NC):
            for hf in range(2):
                cg_ = bd.dma("sync", wsA[:, 0:NFC * I8h].rearrange("pp (c e) -> pp c e", c=NFC),
                             wT_out[r, OFF_M[4] + hf * (PR_GU // 2):OFF_M[4] + (hf + 1) * (PR_GU // 2)]
                             .rearrange("(pp c e) -> pp c e", pp=128, c=NFC),
                             waits=(("cc", cc_wT), ("pe", wgg[0])), sem="s_wg0")
                for il in range(I2h):
                    sb = pp[il % 2]
                    for fc in range(NFC):
                        bd.mm(sb[:, 0:SQ], wsA[:, fc * I8h + il * 128:fc * I8h + il * 128 + 128],
                              xq[:, fc * SQ:(fc + 1) * SQ], fc == 0, fc == NFC - 1,
                              waits=((("s_wg0", cg_), ("act", act_xq),
                                      st.get(("bk", il % 2), ("dve", 0))) if fc == 0 else ()))
                    pe_m = bd.cnt["pe"]
                    dvg = bd.vo(lambda e, sb=sb, sl=stg[:, (il % 2) * SQ:(il % 2) * SQ + SQ]:
                                e.tensor_tensor(sl, sb[:, 0:SQ], csc[:, 0:SQ], OP.mult),
                                waits=(("pe", pe_m), ("act", a_cg),
                                       ("act", st.get(("ge", il % 2), 0))))
                    st[("bk", il % 2)] = ("dve", dvg)
                    st[("ge", il % 2)] = bd.ao(lambda e, il=il,
                                               sl=stg[:, (il % 2) * SQ:(il % 2) * SQ + SQ]:
                                               e.activation(gsi[:, il * SQ:(il + 1) * SQ], sl,
                                                            AF.Silu), waits=(("dve", dvg),))
                wgg[0] = bd.cnt["pe"]
                cu_ = bd.dma("sync", wsB[:, 0:NFC * I8h].rearrange("pp (c e) -> pp c e", c=NFC),
                             wT_out[r, OFF_M[5] + hf * (PR_GU // 2):OFF_M[5] + (hf + 1) * (PR_GU // 2)]
                             .rearrange("(pp c e) -> pp c e", pp=128, c=NFC),
                             waits=(("cc", cc_wT), ("pe", wgg[1])), sem="s_wg1")
                for il in range(I2h):
                    ic = r * 2 * I2h + hf * I2h + il
                    sb = pp[il % 2]
                    for fc in range(NFC):
                        bd.mm(sb[:, 0:SQ], wsB[:, fc * I8h + il * 128:fc * I8h + il * 128 + 128],
                              xq[:, fc * SQ:(fc + 1) * SQ], fc == 0, fc == NFC - 1,
                              waits=((("s_wg1", cu_), ("dve", st.get(("ue", il % 2), 0)),
                                      st.get(("bk", il % 2), ("dve", 0)),
                                      ("act", st.get(("ge", il % 2), 0))) if fc == 0 else ()))
                    pe_m = bd.cnt["pe"]
                    dvu = bd.vo(lambda e, sb=sb: e.tensor_tensor(stg2[:, 0:SQ], sb[:, 0:SQ],
                                                                 csc[:, SQ:2 * SQ], OP.mult),
                                waits=(("pe", pe_m), ("act", a_cu), ("act", st.get(("ge", il % 2), 0))),
                                drain=True)
                    st[("bk", il % 2)] = ("dve", dvu)
                    st[("ue", il % 2)] = bd.vo(
                        lambda e, ic=ic, il=il: e.tensor_tensor(gu[:, ic * SQ:(ic + 1) * SQ],
                                                                gsi[:, il * SQ:(il + 1) * SQ],
                                                                stg2[:, 0:SQ], OP.mult), drain=True)
                wgg[1] = bd.cnt["pe"]
        d2g = colmax_to_row(lambda i: gu[:, i * SQ:(i + 1) * SQ], NIC, 4 * QTW)
        st["rb_pe"] = pe_rbg = bd.mm(ps3[:, 0:SQ], onesr_f[0:1, :], rowm[0:1, 0:SQ], True, True,
                                     waits=(("dve", d2g), ("dve", dve_xq)))
        for ic in range(NIC):
            dvq1 = bd.vo(lambda e, ic=ic: e.tensor_tensor(i32s[:, (ic % 2) * 512:(ic % 2) * 512 + SQ],
                                                          gu[:, ic * SQ:(ic + 1) * SQ],
                                                          ps3[:, 0:SQ], OP.mult),
                         waits=(("pe", pe_rbg),) + ((("act", st.get(("gqe", ic % 2), 0)),) if ic >= 2 else ()),
                         drain=ic >= 2)
            st[("gqe", ic % 2)] = bd.ao(lambda e, ic=ic:
                                        e.copy(gu[:, ic * SQ:(ic + 1) * SQ],
                                               i32s[:, (ic % 2) * 512:(ic % 2) * 512 + SQ]),
                                        waits=(("dve", dvq1),))
        act_guq = bd.cnt["act"]
        if debug and qs == 0:
            bd.dma("gpsimd", dbg_gu[:, :], gu[:, :], waits=(("act", act_guq),), sem="s_msc")
            bd.dma("gpsimd", dbg_gs[0:1, 0:2 * QTW], rows[0:1, 3 * QTW:5 * QTW], sem="s_msc")
        d5 = bd.vo(lambda e: e.tensor_scalar(rowm[0:1, 0:SQ], rows[0:1, 4 * QTW:4 * QTW + SQ],
                                             srow[0:1, 10:11], None, OP.mult),
                   waits=(("pe", pe_rbg),), drain=True)
        dve_pre_cd = bd.cnt["dve"]
        pe_cd = bd.mm(ps4[:, 0:SQ], onesr_f[0:1, :], rowm[0:1, 0:SQ], True, True,
                      waits=(("dve", d5), ("dve", dve_pre_cd), ("act", a_cg)))
        a_cd = bd.ao(lambda e: e.copy(csc[:, 2 * SQ:3 * SQ], ps4[:, 0:SQ]), waits=(("pe", pe_cd),))
        st["rb_pe"] = pe_cd
        for r in range(NC):
            bufs_dn = (wsA, wsB)
            for fl2 in range(0, E2, 2):
                for ihf in range(2):
                    cd_ = bd.dma("sync", bufs_dn[ihf][:, 0:NICh * E8].rearrange("pp (c e) -> pp c e", c=NICh),
                                 wT_out[r, OFF_M[6]:OFF_M[6] + PR_D]
                                 .rearrange("(pp c e) -> pp c e", pp=128, c=NIC)
                                 [:, ihf * NICh:(ihf + 1) * NICh, :],
                                 waits=(("cc", cc_wT), ("pe", wgg[ihf])), sem=f"s_wg{ihf}")
                    for fl in (fl2, fl2 + 1):
                        sb = pp[fl % 2]
                        for icl in range(NICh):
                            ic = ihf * NICh + icl
                            bd.mm(sb[:, 0:SQ],
                                  bufs_dn[ihf][:, icl * E8 + fl * 128:icl * E8 + fl * 128 + 128],
                                  gu[:, ic * SQ:(ic + 1) * SQ],
                                  ihf == 0 and icl == 0, ihf == 1 and icl == NICh - 1,
                                  waits=(((f"s_wg{ihf}", cd_), ("act", act_guq),
                                          st.get(("bk", fl % 2), ("dve", 0))) if icl == 0 else ()))
                    wgg[ihf] = bd.cnt["pe"]
                for fl in (fl2, fl2 + 1):
                    fc = r * E2 + fl
                    sb = pp[fl % 2]
                    pe_m = bd.cnt["pe"]
                    dvd = bd.vo(lambda e, sb=sb: e.tensor_tensor(stg[:, 0:SQ], sb[:, 0:SQ],
                                                                 csc[:, 2 * SQ:3 * SQ], OP.mult),
                                waits=(("pe", pe_m), ("act", a_cd)), drain=True)
                    st[("bk", fl % 2)] = ("dve", dvd)
                    st[("og", fl % 2)] = bd.vo(
                        lambda e, fc=fc,
                        xsr=xs[:, fc * TOKC + toff:fc * TOKC + toff + SQ],
                        dst=hb[:, fc * QTW:fc * QTW + SQ]:
                        e.scalar_tensor_tensor(dst, stg[:, 0:SQ], 1.0, xsr, OP.mult, OP.add), drain=True)
        layer_norm(toff, SQ)
    bd.free("wsA2", "wsB2", "xq", "gsi", "gu", "mfold")
    bd.fence(("scalar",), (bd.w("pe"), bd.w("dve")))

    otm = bd.sb("otm", (128, D), F32)
    for tt in range(NTT):
        for fc in range(NFC):
            t = ptr[fc % 2]
            pe_t = bd.mm(t[:, 0:128], xs[:, fc * TOKC + tt * 128:fc * TOKC + tt * 128 + 128],
                         ident_b[:, :], True, True,
                         waits=(("dve", st["lnprev"]), ("act", st.get(("xb", fc % 2), 0))) +
                               ((("s_out", st.get("odma", 0)),) if fc < 2 else ()))
            st[("xb", fc % 2)] = bd.ao(lambda e, t=t, fc=fc: e.copy(otm[:, fc * 128:(fc + 1) * 128],
                                                                    t[:, 0:128]),
                                       waits=(("pe", pe_t),))
        a_fin = bd.cnt["act"]
        st["odma"] = bd.dma("gpsimd", out_my[tt * 128:(tt + 1) * 128, :], otm[:, :],
                            waits=(("act", a_fin),), sem="s_out")
    final_out = bd.cnt["s_out"]
    bd.emit("gpsimd", lambda e: e.nop(), waits=(("s_out", final_out),))

    vals = sorted(bd.consts)
    assert len(vals) <= 64, len(vals)
    for i, v in enumerate(vals):
        nc.const_aps.register(F32, float(v), cpool[:, i:i + 1])
        nc.const_aps.register(BF16, float(v), cpool_b[:, i:i + 1])

    def memset_fn(e):
        for i, v in enumerate(vals):
            e.memset(cpool[:, i:i + 1], float(v))
            e.memset(cpool_b[:, i:i + 1], float(v))
        e.memset(ones_b[:, :], 1.0)
        e.memset(ones_f[:, :], 1.0)
        e.memset(srow[:, :], 0.0)
        e.drain()
        e.memset(srow[0:1, 15:16], 1.0)
        return e.memset(onesr_f[:, :], 1.0)

    bd.q["gpsimd"].insert(0, ((), memset_fn, "ini", 1))

    es = contextlib.ExitStack()
    sems = {}
    for s in SEMS:
        sems[s] = es.enter_context(nc.semaphore("sem_" + s))
    with es:
        with nc.Block() as block:
            for eng_name in ENGS:
                ops = bd.q[eng_name]

                def mk(ops=ops):
                    def body(e):
                        for waits, fn, inc, amt in ops:
                            for (s, v) in waits:
                                e.wait_ge(sems[s], v)
                            ins = fn(e)
                            if inc is not None and ins is not None:
                                ins.then_inc(sems[inc], amt)
                    return body
                getattr(block, eng_name)(mk())
    return nc


def kernel(**inputs):
    from concourse.bass_utils import run_bass_kernel_spmd
    prm = Prm()
    in_maps = host_prepare(prm, inputs)
    nc = build(prm)
    res = run_bass_kernel_spmd(nc, in_maps, core_ids=list(range(prm.NC)))
    out = np.empty((prm.B, prm.T, prm.D), np.float32)
    for c in range(prm.NC):
        b, hf = c // 2, c % 2
        out[b, hf * prm.TOKC:(hf + 1) * prm.TOKC, :] = np.asarray(res.results[c]["out_my"])
    return out


# version 06
# speedup vs baseline: 1.0645x; 1.0581x over previous
import contextlib
import math

import numpy as np
import ml_dtypes

import concourse.bass as bass
import concourse.mybir as mybir
from concourse.alu_op_type import AluOpType as OP

AF = mybir.ActivationFunctionType
F32 = mybir.dt.float32
BF16 = mybir.dt.bfloat16
I32 = mybir.dt.int32

NEG = -1.0e30
B_EXP = 8.0
LN_EPS = 1e-5
Q_EPS = 1e-5
QTW = 512
SQ = 256


class Prm:
    def __init__(self, B=4, T=2048, D=2048, H=16, I=8192):
        self.B, self.T, self.D, self.H, self.I = B, T, D, H, I
        self.NC = 2 * B
        self.TOKC = T // 2
        self.NQT = self.TOKC // QTW
        self.NFC = D // 128
        self.NKT = T // 128
        assert D == H * 128 and self.TOKC % QTW == 0 and (120 * H) % 128 == 0
        self.FD = 3 * D
        self.NFCF = self.FD // 128
        self.R = 8
        self.DSTD = 120
        self.qs_end = H * 120
        self.ks_end = 2 * H * 120
        self.v_end = self.ks_end + D
        self.ql_end = self.v_end + 8 * H
        self.ktmax = [(self.TOKC + (qt + 1) * QTW) // 128 for qt in range(self.NQT)]
        self.NDT = sum(self.ktmax)
        self.E8 = D // self.NC
        self.I8 = I // self.NC
        self.NIC = I // 128
        self.slabs_qkv_q = mk_slabs(0, D)
        self.slabs_qkv_k = mk_slabs(D, 2 * D)
        self.slabs_qkv_v = mk_slabs(2 * D, 3 * D)
        self.slabs_wa_q = mk_slabs(0, self.qs_end)
        self.slabs_wa_k = mk_slabs(self.qs_end, self.ks_end)
        self.slabs_wa_v = mk_slabs(self.ks_end, self.v_end)
        self.slabs_wa_t = [(self.v_end, self.FD)]
        self.slabs_d = mk_slabs(0, D)


def mk_slabs(a, b, extra=()):
    bnds = sorted({x for x in range(0, b + 1, 512) if a < x < b} | {a, b} | {x for x in extra if a < x < b})
    return [(bnds[i], bnds[i + 1]) for i in range(len(bnds) - 1)]


def _slopes(H):
    return [2.0 ** (-8.0 / (i + 1)) for i in range(H)]



def perm_w(w, slabs):
    D, F = w.shape
    ndc = D // 128
    w3 = w.reshape(ndc, 128, F)
    parts = []
    for (a, b) in slabs:
        parts.append(np.ascontiguousarray(w3[:, :, a:b].transpose(1, 0, 2)).reshape(-1))
    return np.concatenate(parts)


def mk_dtiles(prm, qoff, alibi):
    tiles = np.empty((prm.NDT, 128, QTW), dtype=np.float32)
    dk = np.arange(128)[:, None]
    dq = np.arange(QTW)[None, :]
    i = 0
    for qt in range(prm.NQT):
        for kt in range(prm.ktmax[qt]):
            k = kt * 128 + dk
            q = qoff + qt * QTW + dq
            base = (k - q).astype(np.float32) if alibi else np.zeros((128, QTW), np.float32)
            tiles[i] = np.where(k <= q, base, NEG)
            i += 1
    return tiles.astype(ml_dtypes.bfloat16)


def host_prepare(prm, inputs):
    p = prm
    x = np.asarray(inputs["L0_attn_x"], np.float32)
    wqkv = perm_w(np.asarray(inputs["L0_attn_W_qkv"], np.float32),
                  p.slabs_qkv_q + p.slabs_qkv_k + p.slabs_qkv_v)
    wo = perm_w(np.asarray(inputs["L0_attn_W_o"], np.float32), p.slabs_d)
    wa = perm_w(np.asarray(inputs["L0_ffn_Wa"], np.float32),
                p.slabs_wa_q + p.slabs_wa_k + p.slabs_wa_v + p.slabs_wa_t)
    wp = perm_w(np.asarray(inputs["L0_ffn_Wp"], np.float32), p.slabs_d)
    ident = np.eye(128, dtype=np.float32)
    mats = [np.asarray(inputs[k], np.float32) for k in
            ("L1_attn_Wq", "L1_attn_Wk", "L1_attn_Wv", "L1_attn_Wo")]
    wg = np.asarray(inputs["L1_ffn_w_gate"], np.float32)
    wu = np.asarray(inputs["L1_ffn_w_up"], np.float32)
    wd = np.asarray(inputs["L1_ffn_w_down"], np.float32)
    wstd = np.asarray(inputs["L0_ffn_w_std"], np.float32).reshape(1, p.H)
    wrec = np.asarray(inputs["L0_ffn_w_rec"], np.float32).reshape(1, p.H)
    qk = 128.0 ** -0.25
    nsv = 1 + 2 * p.H
    sel = np.zeros((nsv, p.NFCF * 128), np.float32)
    for j in range(p.FD):
        if j < p.qs_end:
            sel[1 + j // 120, j] = qk
        elif j < p.ks_end:
            sel[1 + (j - p.qs_end) // 120, j] = qk
        elif j < p.v_end:
            sel[0, j] = 1.0
        elif j < p.ql_end:
            sel[1 + p.H + (j - p.v_end) // p.R, j] = qk
        else:
            sel[1 + p.H + (j - p.ql_end) // p.R, j] = qk
    rep8 = np.zeros((p.H, 128), np.float32)
    for pp_ in range(8 * p.H):
        rep8[pp_ // 8, pp_] = 1.0
    in_maps = []
    for c in range(p.NC):
        b, hf = c // 2, c % 2
        qoff = hf * p.TOKC
        in_maps.append({
            "x_my": np.ascontiguousarray(x[b, qoff:qoff + p.TOKC, :]),
            "wqkv_p": wqkv, "wo_p": wo, "wa_p": wa, "wp_p": wp,
            "ident": ident, "w_std": wstd, "w_rec": wrec,
            "is_b": np.full((1, 1), float(hf), np.float32),
            "sel": sel, "rep8": rep8,
            "dA": mk_dtiles(p, qoff, True),
            "dM": mk_dtiles(p, qoff, False),
            "wq1_s": np.ascontiguousarray(mats[0][c * p.E8:(c + 1) * p.E8, :]),
            "wk1_s": np.ascontiguousarray(mats[1][c * p.E8:(c + 1) * p.E8, :]),
            "wv1_s": np.ascontiguousarray(mats[2][c * p.E8:(c + 1) * p.E8, :]),
            "wo1_s": np.ascontiguousarray(mats[3][c * p.E8:(c + 1) * p.E8, :]),
            "wg_s": np.ascontiguousarray(wg[c * p.I8:(c + 1) * p.I8, :]),
            "wu_s": np.ascontiguousarray(wu[c * p.I8:(c + 1) * p.I8, :]),
            "wd_s": np.ascontiguousarray(wd[c * p.E8:(c + 1) * p.E8, :]),
        })
    return in_maps



ENGS = ("tensor", "vector", "scalar", "sync", "gpsimd")
SEMS = ("pe", "dve", "act", "spq", "plq", "cc", "ini",
        "s_in", "s_ins", "s_pq", "s_pq2", "s_wf", "s_msc", "s_cm", "s_out",
        "s_dsb", "s_tl", "s_xt0", "s_xt1", "s_ws0", "s_ws1", "s_kv0", "s_kv1",
        "s_evb0", "s_evb1", "s_w10", "s_w11", "s_w12", "s_wg0", "s_wg1",
        "s_fst0", "s_fst1", "s_fst2", "s_fst3")


class Bld:
    def __init__(self, nc):
        self.nc = nc
        self.q = {e: [] for e in ENGS}
        self.cnt = {s: 0 for s in SEMS}
        self.consts = set()
        self._cm = {}

    def sb(self, name, shape, dt):
        cm = self.nc.sbuf_tensor(name, list(shape), dt)
        t = cm.__enter__()
        self._cm[name] = cm
        return t

    def free(self, *names):
        order = list(self._cm)
        for n in sorted(names, key=order.index, reverse=True):
            self._cm.pop(n).__exit__(None, None, None)

    def ps(self, name, shape, dt):
        cm = self.nc.psum_tensor(name, list(shape), dt)
        t = cm.__enter__()
        self._cm["psum_" + name] = cm
        return t

    def K(self, v):
        v = float(v)
        self.consts.add(v)
        return v

    def emit(self, eng, fn, waits=(), inc=None, amt=1):
        self.q[eng].append((tuple(w for w in waits if w is not None and w[1] > 0), fn, inc, amt))
        if inc is not None:
            self.cnt[inc] += amt
            return self.cnt[inc]
        return None

    def w(self, sem):
        return (sem, self.cnt[sem])

    def dma(self, qname, out, in_, waits=(), sem=None, **kw):
        if sem is None:
            sem = "spq" if qname == "sync" else "plq"
        self.emit(qname, lambda e, o=out, i=in_, kw=kw: e.dma_start(out=o, in_=i, **kw),
                  waits=waits, inc=sem, amt=16)
        return self.cnt[sem]

    def dmaT(self, out, in_, waits=(), sem="spq"):
        self.emit("sync", lambda e, o=out, i=in_: e.dma_start_transpose(o, i),
                  waits=waits, inc=sem, amt=16)
        return self.cnt[sem]

    def mm(self, outap, lh, rh, start, stop, waits=()):
        return self.emit("tensor", lambda e, o=outap, a=lh, b=rh, s1=start, s2=stop:
                         e.matmul(o, a, b, start=s1, stop=s2), waits=waits, inc="pe")

    def vo(self, fn, waits=(), drain=False):
        if drain:
            self.emit("vector", lambda e: e.drain())
        return self.emit("vector", fn, waits=waits, inc="dve")

    def ao(self, fn, waits=()):
        return self.emit("scalar", fn, waits=waits, inc="act")

    def fence(self, engines, waits):
        for eng in engines:
            self.emit(eng, lambda e: e.nop(), waits=waits)

    def coll(self, kind, op, groups, ins, outs, waits=()):
        return self.emit("gpsimd", lambda e: e.collective_compute(
            kind, op, replica_groups=groups, ins=ins, outs=outs), waits=waits, inc="cc")



def build(prm, debug=False):
    p = prm
    nc = bass.Bass("TRN2", target_bir_lowering=False)
    H, D, I = p.H, p.D, p.I
    TOKC, NQT, NFC, NKT = p.TOKC, p.NQT, p.NFC, p.NKT
    NTT = TOKC // 128
    E8, I8, NIC = p.E8, p.I8, p.NIC
    NC = p.NC
    NSQ = TOKC // SQ
    I8h = I8 // 2
    I2h = I8h // 128
    E2 = E8 // 128
    NICh = NIC // 2
    WSW = max(NFC * 512, NFC * I8h, NICh * E8)
    inv_sqrt_dh = 1.0 / math.sqrt(128.0)
    qk_quarter = 128.0 ** -0.25
    denom_l1 = math.sqrt(float(D)) / float(p.T)
    slopes = _slopes(H)

    din = {}
    for nm, shape, dt in [
        ("x_my", [TOKC, D], F32), ("wqkv_p", [D * 3 * D], F32), ("wo_p", [D * D], F32),
        ("wa_p", [D * 3 * D], F32), ("wp_p", [D * D], F32), ("ident", [128, 128], F32),
        ("w_std", [1, H], F32), ("w_rec", [1, H], F32), ("is_b", [1, 1], F32),
        ("dA", [p.NDT, 128, QTW], BF16), ("dM", [p.NDT, 128, QTW], BF16),
        ("wq1_s", [E8, D], F32), ("wk1_s", [E8, D], F32), ("wv1_s", [E8, D], F32),
        ("wo1_s", [E8, D], F32), ("wg_s", [I8, D], F32), ("wu_s", [I8, D], F32),
        ("wd_s", [E8, I], F32), ("sel", [1 + 2 * H, p.NFCF * 128], F32),
        ("rep8", [H, 128], F32),
    ]:
        din[nm] = nc.dram_tensor(nm, shape, dt, kind="ExternalInput")
    out_my = nc.dram_tensor("out_my", [TOKC, D], F32, kind="ExternalOutput")

    KVW = max(H * TOKC, NTT * D)
    kv_snd = [nc.dram_tensor(f"kv_snd{i}", [2, 128, KVW], BF16) for i in range(2)]
    kv_out = [nc.dram_tensor(f"kv_out{i}", [2, 2, 128, KVW], BF16) for i in range(2)]
    off_snd = nc.dram_tensor("off_snd", [128, NFC], F32)
    off_out = nc.dram_tensor("off_out", [2, 128, NFC], F32)
    ar_snd = nc.dram_tensor("ar_snd", [1, 4], F32)
    ar_out = nc.dram_tensor("ar_out", [1, 4], F32)
    tmp_qkvo = nc.dram_tensor("tmp_qkvo", [4, E8, D], BF16)
    tmp_gu = nc.dram_tensor("tmp_gu", [2, I8, D], BF16)
    tmp_d = nc.dram_tensor("tmp_d", [E8, I], BF16)
    PR_QKVO, PR_GU, PR_D = D * E8, D * I8, I * E8
    PR_ALL = 4 * PR_QKVO + 2 * PR_GU + PR_D
    OFF_M = [0, PR_QKVO, 2 * PR_QKVO, 3 * PR_QKVO, 4 * PR_QKVO,
             4 * PR_QKVO + PR_GU, 4 * PR_QKVO + 2 * PR_GU]
    wT_snd = nc.dram_tensor("wT_snd", [PR_ALL], BF16)
    wT_out = nc.dram_tensor("wT_out", [NC, PR_ALL], BF16, addr_space="Shared")
    sm_dram = nc.dram_tensor("sm_dram", [128, 4], F32)
    dbg_kind = dict(kind="ExternalOutput") if debug else {}
    dbg = [nc.dram_tensor(f"dbg_x{i}", [128, NFC * TOKC], BF16, **dbg_kind) for i in (1, 2, 3)]
    dbg_xq = nc.dram_tensor("dbg_xq", [128, (D // 128) * SQ], BF16, **dbg_kind)
    dbg_gu = nc.dram_tensor("dbg_gu", [128, (I // 128) * SQ], BF16, **dbg_kind)
    dbg_gs = nc.dram_tensor("dbg_gs", [1, 8 * QTW], F32, **dbg_kind)
    dbg_rb = nc.dram_tensor("dbg_rb", [128, SQ], F32, **dbg_kind)
    dbg_cap = nc.dram_tensor("dbg_cap", [128, 4 * SQ], F32, **dbg_kind)

    groups_pair = [[2 * i, 2 * i + 1] for i in range(NC // 2)]
    groups_all = [list(range(NC))]

    bd = Bld(nc)
    K = bd.K
    st = {"ydve": 0, "lnprev": 0, "rb_pe": 0, "ab_pe": 0}

    cpool = bd.sb("cpool", (128, 64), F32)
    cpool_b = bd.sb("cpool_b", (128, 64), BF16)
    ones_b = bd.sb("ones_b", (128, 1), BF16)
    onesr_f = bd.sb("onesr_f", (1, 128), F32)
    ones_f = bd.sb("ones_f", (128, 1), F32)
    ident_b = bd.sb("ident_b", (128, 128), BF16)
    xs = bd.sb("xs", (128, NFC * TOKC), BF16)
    stg = bd.sb("stg", (128, 2 * QTW), F32)
    stg2 = bd.sb("stg2", (128, 2 * QTW), F32)
    pbuf = bd.sb("pbuf", (128, 2 * QTW), BF16)
    hb = bd.sb("hb", (128, NFC * QTW), BF16)
    hsq = bd.sb("hsq", (128, 2 * QTW), BF16)
    rows = bd.sb("rows", (1, 6 * QTW), F32)
    rowm = bd.sb("rowm", (1, 2 * QTW), F32)
    srow = bd.sb("srow", (1, 128), F32)
    small = bd.sb("small", (128, 32), F32)
    evb = bd.sb("evb", (128, 2 * 512), BF16)
    i32s = bd.sb("i32s", (128, 1024), I32)
    csvec = bd.sb("csvec", (128, p.NFCF), F32)
    u128 = bd.sb("u128", (128, 2), F32)
    rbv = bd.sb("rbv", (128, QTW), F32)
    kvsb = bd.sb("kvsb", (128, QTW), F32)
    csc = bd.sb("csc", (128, 3 * SQ), F32)
    dbgsb = bd.sb("dbgsb", (128, 2 * SQ), F32)

    ps0 = bd.ps("ps0", (128, 512), F32)
    ps1 = bd.ps("ps1", (128, 512), F32)
    ps2 = bd.ps("ps2", (128, 512), F32)
    ps3 = bd.ps("ps3", (128, 512), F32)
    ps4 = bd.ps("ps4", (128, 512), F32)
    ps5 = bd.ps("ps5", (128, 512), F32)
    psS = bd.ps("psS", (128, 512), F32)
    psS2 = bd.ps("psS2", (128, 512), F32)
    pp = [ps0, ps1]
    ptr = [ps4, ps5]

    INI = ("ini", 1)

    xtm = bd.sb("xtm", (128, 2 * D), BF16)
    spi = bd.dma("gpsimd", ident_b[:, :], din["ident"][:, :], waits=(INI,), sem="s_in")
    for tt in range(NTT):
        sp = bd.dma("gpsimd", xtm[:, (tt % 2) * D:(tt % 2) * D + D],
                    din["x_my"][tt * 128:(tt + 1) * 128, :], sem=f"s_xt{tt % 2}",
                    waits=(INI,) + ((("act", st.get(("xg", tt - 2), 0)),) if tt >= 2 else ()))
        for fc in range(NFC):
            t = ptr[fc % 2]
            pe_m = bd.mm(t[:, 0:128],
                         xtm[:, (tt % 2) * D + fc * 128:(tt % 2) * D + fc * 128 + 128],
                         ident_b[:, :], True, True,
                         waits=(((f"s_xt{tt % 2}", sp), ("s_in", spi)) if fc == 0 else ()) +
                               (("act", st.get(("xb", fc % 2), 0)),))
            st[("xb", fc % 2)] = bd.ao(
                lambda e, t=t, dst=xs[:, fc * TOKC + tt * 128:fc * TOKC + tt * 128 + 128]:
                e.copy(dst, t[:, 0:128]), waits=(("pe", pe_m),))
        st[("xg", tt)] = bd.cnt["act"]
    bd.free("xtm")
    act_x = bd.cnt["act"]
    bd.fence(("gpsimd", "vector"), (bd.w("pe"),))

    wtmp = bd.sb("wtmp", (128, 2048), BF16)
    wf32 = bd.sb("wf32", (128, 2048), F32)
    racc = bd.sb("racc", (128, 16), F32)
    for mi, nm in enumerate(("wq1_s", "wk1_s", "wv1_s", "wo1_s")):
        for rt in range(E8 // 128):
            c1 = bd.dma("gpsimd", wtmp[:, 0:D], din[nm][rt * 128:(rt + 1) * 128, 0:D],
                        waits=(INI, bd.w("s_pq")))
            bd.dma("gpsimd", tmp_qkvo[mi, rt * 128:(rt + 1) * 128, :], wtmp[:, 0:D],
                   waits=(("plq", c1),), sem="s_pq")
    for mi, (nm, rows_n, cols) in enumerate((("wg_s", I8, D), ("wu_s", I8, D), ("wd_s", E8, I))):
        CW = min(2048, cols)
        k = 0
        for rt in range(rows_n // 128):
            for ccc in range(cols // CW):
                c1 = bd.dma("gpsimd", wf32[:, 0:CW],
                            din[nm][rt * 128:(rt + 1) * 128, ccc * CW:(ccc + 1) * CW],
                            waits=(bd.w("dve"),), sem="s_wf")
                if k < 15:
                    bd.vo(lambda e, k=k, cw=CW: e.reduce_sum(racc[:, k:k + 1], wf32[:, 0:cw],
                                                             axis=mybir.AxisListType.X,
                                                             apply_absolute_value=True),
                          waits=(("s_wf", c1),))
                else:
                    bd.vo(lambda e, cw=CW: e.reduce_sum(small[:, 0:1], wf32[:, 0:cw],
                                                        axis=mybir.AxisListType.X,
                                                        apply_absolute_value=True),
                          waits=(("s_wf", c1),))
                    bd.vo(lambda e, k=k: e.tensor_tensor(racc[:, k % 15:k % 15 + 1],
                                                         racc[:, k % 15:k % 15 + 1],
                                                         small[:, 0:1], OP.add), drain=True)
                k += 1
        nacc = min(k, 15)
        dvm = bd.vo(lambda e, nacc=nacc: e.reduce_sum(racc[:, 15:16], racc[:, 0:nacc],
                                                      axis=mybir.AxisListType.X),
                    waits=(("pe", st.get("pqs_pe", 0)),), drain=True)
        pe_m = bd.mm(psS[0:1, mi:mi + 1], ones_f[:, :], racc[:, 15:16], True, True,
                     waits=(("dve", dvm),))
        st["pqs_pe"] = pe_m
        bd.ao(lambda e, mi=mi: e.copy(srow[0:1, mi:mi + 1], psS[0:1, mi:mi + 1]),
              waits=(("pe", pe_m),))
    act_part = bd.cnt["act"]
    c1 = bd.dma("gpsimd", ar_snd[0:1, 0:4], srow[0:1, 0:4], waits=(("act", act_part),))
    cc_ar = bd.coll("AllReduce", OP.add, groups_all, [ar_snd.ap().opt()], [ar_out.ap().opt()],
                    waits=(("plq", c1),))
    c2 = bd.dma("gpsimd", srow[0:1, 4:8], ar_out[0:1, 0:4], waits=(("cc", cc_ar),), sem="s_msc")
    K(Q_EPS)
    dmean = K(1.0 / (I * D))
    bd.vo(lambda e: e.tensor_scalar(srow[0:1, 8:11], srow[0:1, 4:7], dmean, Q_EPS,
                                    OP.mult, OP.add), waits=(("s_msc", c2),))
    d2 = bd.vo(lambda e: e.reciprocal(srow[0:1, 12:15], srow[0:1, 8:11]), drain=True)
    pe_rs = None
    for mi in range(3):
        pe_rs = bd.mm(psS2[:, mi:mi + 1], onesr_f[0:1, :], srow[0:1, 12 + mi:13 + mi],
                      mi == 0, mi == 2, waits=(("dve", d2),) if mi == 0 else ())
    a_rs = bd.ao(lambda e: e.copy(small[:, 8:11], psS2[:, 0:3]), waits=(("pe", pe_rs),))
    K(1.0); K(-1.0)
    for mi, (nm, rows_n, cols) in enumerate((("wg_s", I8, D), ("wu_s", I8, D), ("wd_s", E8, I))):
        CW = min(2048, cols)
        for rt in range(rows_n // 128):
            for ccc in range(cols // CW):
                c1 = bd.dma("gpsimd", wf32[:, 0:CW],
                            din[nm][rt * 128:(rt + 1) * 128, ccc * CW:(ccc + 1) * CW],
                            waits=(bd.w("dve"), bd.w("plq")), sem="s_wf")
                d3 = None
                spq_prev = bd.w("s_pq")
                for pc in range(CW // 1024):
                    bd.vo(lambda e, mi=mi, pc=pc: e.tensor_scalar(i32s[:, 0:1024],
                                                                  wf32[:, pc * 1024:(pc + 1) * 1024],
                                                                  small[:, 8 + mi:9 + mi], None, OP.mult),
                          waits=(("s_wf", c1), ("act", a_rs), spq_prev), drain=pc > 0)
                    d3 = bd.vo(lambda e, pc=pc: e.tensor_scalar(wtmp[:, pc * 1024:(pc + 1) * 1024],
                                                                i32s[:, 0:1024], 1.0, -1.0,
                                                                OP.min, OP.max), drain=True)
                if mi < 2:
                    bd.dma("gpsimd", tmp_gu[mi, rt * 128:(rt + 1) * 128, ccc * CW:(ccc + 1) * CW],
                           wtmp[:, 0:CW], waits=(("dve", d3),), sem="s_pq")
                else:
                    bd.dma("gpsimd", tmp_d[rt * 128:(rt + 1) * 128, ccc * CW:(ccc + 1) * CW],
                           wtmp[:, 0:CW], waits=(("dve", d3),), sem="s_pq")
    plq_tmp = bd.cnt["s_pq"]
    wstg = bd.sb("wstg", (128, max(NFC * E8, NFC * I8, NIC * E8)), BF16)
    for mi in range(4):
        for dc in range(NFC):
            bd.dmaT(wstg[:, dc * E8:(dc + 1) * E8], tmp_qkvo[mi, 0:E8, dc * 128:(dc + 1) * 128],
                    waits=((("s_pq", plq_tmp),) if (mi == 0 and dc == 0) else ()) +
                          ((bd.w("spq"), bd.w("s_pq2")) if dc == 0 else ()))
        bd.dma("sync", wT_snd[OFF_M[mi]:OFF_M[mi] + PR_QKVO].rearrange("(pp c e) -> pp c e", pp=128, c=NFC),
               wstg[:, 0:NFC * E8].rearrange("pp (c e) -> pp c e", c=NFC), waits=(bd.w("spq"),), sem="s_pq2")
    for gi in range(2):
        for dc in range(NFC):
            bd.dmaT(wstg[:, dc * I8:(dc + 1) * I8], tmp_gu[gi, 0:I8, dc * 128:(dc + 1) * 128],
                    waits=(bd.w("spq"), bd.w("s_pq2")) if dc == 0 else ())
        for ghf in range(2):
            bd.dma("sync",
                   wT_snd[OFF_M[4 + gi] + ghf * (PR_GU // 2):OFF_M[4 + gi] + (ghf + 1) * (PR_GU // 2)]
                   .rearrange("(pp c e) -> pp c e", pp=128, c=NFC),
                   wstg[:, 0:NFC * I8].rearrange("pp (c e) -> pp c e", c=NFC)
                   [:, :, ghf * I8h:(ghf + 1) * I8h],
                   waits=(bd.w("spq"),), sem="s_pq2")
    for ic in range(NIC):
        bd.dmaT(wstg[:, ic * E8:(ic + 1) * E8], tmp_d[0:E8, ic * 128:(ic + 1) * 128],
                waits=(bd.w("spq"), bd.w("s_pq2")) if ic == 0 else ())
    sp_wt = bd.dma("sync", wT_snd[OFF_M[6]:OFF_M[6] + PR_D].rearrange("(pp c e) -> pp c e", pp=128, c=NIC),
                   wstg[:, 0:NIC * E8].rearrange("pp (c e) -> pp c e", c=NIC), waits=(bd.w("spq"),), sem="s_pq2")
    bd.dma("sync", srow[0:1, 16:16 + H], din["w_std"][:, :], waits=(INI,), sem="s_ins")
    bd.dma("sync", srow[0:1, 16 + H:16 + 2 * H], din["w_rec"][:, :], sem="s_ins")
    nsv = 1 + 2 * H
    sel_sb = bd.sb("sel_sb", (nsv, p.NFCF * 128), F32)
    rep_sb = bd.sb("rep_sb", (H, 128), F32)
    bd.dma("sync", sel_sb[:, :], din["sel"][:, :], sem="s_ins")
    bd.dma("sync", rep_sb[:, :], din["rep8"][:, :], sem="s_ins")
    sp_isb = bd.dma("sync", srow[0:1, 64:65], din["is_b"][:, :], sem="s_ins")
    K(1e-8); K(0.1); K(qk_quarter)
    du = bd.vo(lambda e: e.tensor_scalar(srow[0:1, 80:80 + H], srow[0:1, 16 + H:16 + 2 * H],
                                         0.1, None, OP.is_gt), waits=(("s_ins", sp_isb),))
    dstd = bd.vo(lambda e: e.tensor_scalar(srow[0:1, 16:16 + 2 * H], srow[0:1, 16:16 + 2 * H],
                                           1e-8, None, OP.max), drain=True)
    a_sq = bd.ao(lambda e: e.activation(srow[0:1, 16:16 + 2 * H], srow[0:1, 16:16 + 2 * H], AF.Sqrt),
                 waits=(("dve", dstd),))
    pe_sv = bd.mm(ps3[0:nsv, 0:1], srow[0:1, 15:16 + 2 * H], onesr_f[0:1, 0:1], True, True,
                  waits=(("act", a_sq), ("s_ins", sp_isb)))
    a_sv = bd.ao(lambda e: e.copy(small[0:nsv, 14:15], ps3[0:nsv, 0:1]), waits=(("pe", pe_sv),))
    pe_cs = None
    for c in range(p.NFCF):
        pe_cs = bd.mm(psS[:, c:c + 1], sel_sb[0:nsv, c * 128:(c + 1) * 128], small[0:nsv, 14:15],
                      True, True, waits=(("act", a_sv),) if c == 0 else ())
    a_cs = bd.ao(lambda e: e.copy(csvec[:, 0:p.NFCF], psS[:, 0:p.NFCF]), waits=(("pe", pe_cs),))
    pe_uc = bd.mm(ps3[0:H, 0:1], srow[0:1, 80:80 + H], onesr_f[0:1, 0:1], True, True,
                  waits=(("dve", du), ("act", a_sv)))
    a_uc = bd.ao(lambda e: e.copy(small[0:H, 15:16], ps3[0:H, 0:1]), waits=(("pe", pe_uc),))
    pe_u1 = bd.mm(psS2[:, 4:5], rep_sb[0:H, :], small[0:H, 15:16], True, True,
                  waits=(("act", a_uc),))
    a_u1 = bd.ao(lambda e: e.copy(u128[:, 0:1], psS2[:, 4:5]), waits=(("pe", pe_u1),))
    d_1mu = bd.vo(lambda e: e.tensor_scalar(u128[0:8 * H, 1:2], u128[0:8 * H, 0:1], -1.0, 1.0,
                                            OP.mult, OP.add), waits=(("act", a_u1),))
    pe_ib = bd.mm(psS2[:, 3:4], onesr_f[0:1, :], srow[0:1, 64:65], True, True,
                  waits=(("s_ins", sp_isb), ("act", a_rs)))
    a_ib = bd.ao(lambda e: e.copy(small[:, 12:13], psS2[:, 3:4]), waits=(("pe", pe_ib),))
    csvec_rdy = ("act", a_cs)
    plq_cs = None
    bd.free("wstg", "sel_sb", "rep_sb", "racc", "wf32", "wtmp")
    bd.fence(("gpsimd", "sync", "vector", "scalar"),
             (bd.w("pe"), bd.w("dve"), bd.w("s_pq"), bd.w("s_pq2"), bd.w("s_in")))

    def slab_offsets(all_slabs):
        offs, o = {}, 0
        for (a, b) in all_slabs:
            offs[(a, b)] = o
            o += D * (b - a)
        return offs

    offs_qkv = slab_offsets(p.slabs_qkv_q + p.slabs_qkv_k + p.slabs_qkv_v)
    offs_wa = slab_offsets(p.slabs_wa_q + p.slabs_wa_k + p.slabs_wa_v + p.slabs_wa_t)
    offs_d = slab_offsets(p.slabs_d)

    wsA = bd.sb("wsA", (128, WSW), BF16)
    wsB = bd.sb("wsB", (128, WSW), BF16)
    dsb = bd.sb("dsb", (128, max(p.ktmax) * QTW), BF16)
    kst = bd.sb("kst", (128, 2 * NKT * 128), BF16)
    vst = bd.sb("vst", (128, 2 * NKT * 128), BF16)
    yall = bd.sb("yall", (128, H * QTW), BF16)
    qall = bd.sb("qall", (128, H * QTW), BF16)

    wg_guard = [0, 0]
    slab_i = [0]

    def load_slab(perm, offmap, slab, extra=()):
        i = slab_i[0] % 2
        slab_i[0] += 1
        buf = (wsA, wsB)[i]
        wdt = slab[1] - slab[0]
        c = bd.dma("gpsimd", buf[:, 0:NFC * wdt].rearrange("pp (c e) -> pp c e", c=NFC),
                   perm[offmap[slab]:offmap[slab] + D * wdt]
                   .rearrange("(pp c e) -> pp c e", pp=128, c=NFC),
                   waits=(("pe", wg_guard[i]),) + tuple(extra), sem=f"s_ws{i}")
        return buf, wdt, (f"s_ws{i}", c), i

    def done_slab(i):
        wg_guard[i] = bd.cnt["pe"]

    K(inv_sqrt_dh); K(-B_EXP); K(1.0); K(0.0)
    for s in slopes:
        K(s)
    dt_off = []
    o = 0
    for qt in range(NQT):
        dt_off.append(o)
        o += p.ktmax[qt]

    def scores_qt(layer, qt, cc_kv, kv_gout, use_alibi, sp_dt, qrdy=()):
        nkt = p.ktmax[qt]
        for h in range(H):
            kbase = (h % 2) * NKT * 128
            guard = st.get(("kvg", h % 2), 0)
            spkv = None
            for r in range(2):
                lo, hi = max(0, r * NTT), min(nkt, (r + 1) * NTT)
                if hi <= lo:
                    continue
                bd.dma("sync", kst[:, kbase + lo * 128:kbase + hi * 128],
                       kv_gout[r, 0, :, h * TOKC + (lo - r * NTT) * 128:h * TOKC + (hi - r * NTT) * 128],
                       waits=(("cc", cc_kv), ("pe", guard)), sem=f"s_kv{h % 2}")
                spkv = bd.dma("sync", vst[:, kbase + lo * 128:kbase + hi * 128]
                              .rearrange("pp (t c) -> pp t c", c=128),
                              kv_gout[r, 1, :, :].rearrange("pp (t f) -> pp t f", f=D)
                              [:, (lo - r * NTT):(hi - r * NTT), h * 128:(h + 1) * 128],
                              sem=f"s_kv{h % 2}")
            for kt in range(nkt):
                sb = pp[kt % 2]
                pe_s = bd.mm(sb[:, 0:QTW], kst[:, kbase + kt * 128:kbase + (kt + 1) * 128],
                             qall[:, h * QTW:(h + 1) * QTW], True, True,
                             waits=(((f"s_kv{h % 2}", spkv),) + tuple(qrdy) if kt == 0 else ()) +
                                   (st.get(("bk", kt % 2), ("dve", 0)),))
                slot = stg[:, (kt % 2) * QTW:(kt % 2) * QTW + QTW]
                sc = slopes[h] if use_alibi else 1.0
                dv_s = bd.vo(lambda e, sl=slot, db=dsb[:, kt * QTW:(kt + 1) * QTW], sb_=sb, sc=sc:
                             e.scalar_tensor_tensor(sl, db, sc, sb_[:, 0:QTW], OP.mult, OP.add),
                             waits=(("pe", pe_s),) + ((sp_dt,) if kt == 0 else ()))
                st[("bk", kt % 2)] = ("dve", dv_s)
                pslot = pbuf[:, (kt % 2) * QTW:(kt % 2) * QTW + QTW]
                ac_p = bd.ao(lambda e, pl=pslot, sl=slot: e.activation(pl, sl, AF.Exp,
                                                                       bias=-B_EXP, scale=1.0),
                             waits=(("dve", dv_s),))
                bd.mm(psS[0:1, 0:QTW], ones_b[:, :], pslot, kt == 0, kt == nkt - 1,
                      waits=(("act", ac_p),) + ((("dve", st["ydve"]),) if kt == 0 else ()))
                bd.mm(ps2[:, 0:QTW], vst[:, kbase + kt * 128:kbase + (kt + 1) * 128], pslot,
                      kt == 0, kt == nkt - 1)
            st[("kvg", h % 2)] = bd.cnt["pe"]
            pe_fin = bd.cnt["pe"]
            dv_r = bd.vo(lambda e: e.reciprocal(rowm[0:1, 0:QTW], psS[0:1, 0:QTW]),
                         waits=(("pe", pe_fin), ("pe", st["rb_pe"])))
            st["rb_pe"] = bd.mm(ps3[:, 0:QTW], onesr_f[0:1, :], rowm[0:1, 0:QTW], True, True,
                                waits=(("dve", dv_r), ("dve", st["ydve"])))
            a_rb = bd.ao(lambda e: e.copy(rbv[:, 0:QTW], ps3[:, 0:QTW]),
                         waits=(("pe", st["rb_pe"]),))
            st["ydve"] = bd.vo(lambda e, dst=yall[:, h * QTW:(h + 1) * QTW]:
                               e.tensor_tensor(dst, ps2[:, 0:QTW], rbv[:, 0:QTW], OP.mult),
                               waits=(("act", a_rb),))

    def layer_norm(toff, w):
        K(1.0 / D); K(LN_EPS)
        for fc in range(NFC):
            a_sq2 = bd.ao(lambda e, fc=fc, sl=hsq[:, (fc % 2) * QTW:(fc % 2) * QTW + w]:
                          e.activation(sl, hb[:, fc * QTW:fc * QTW + w], AF.Square),
                          waits=(("dve", st.get(("og", fc % 2), 0)), ("pe", st.get(("lq", fc % 2), 0))))
            bd.mm(psS[0:1, 0:w], ones_b[:, :], hb[:, fc * QTW:fc * QTW + w], fc == 0, fc == NFC - 1,
                  waits=(("dve", st["lnprev"]), ("dve", st.get(("og", 0), 0)),
                         ("dve", st.get(("og", 1), 0))) if fc == 0 else ())
            st[("lq", fc % 2)] = bd.mm(psS2[0:1, 0:w], ones_b[:, :],
                                       hsq[:, (fc % 2) * QTW:(fc % 2) * QTW + w],
                                       fc == 0, fc == NFC - 1, waits=(("act", a_sq2),))
        pe_sums = bd.cnt["pe"]
        bd.vo(lambda e: e.tensor_scalar(rows[0:1, 0:w], psS[0:1, 0:w], 1.0 / D, None, OP.mult),
              waits=(("pe", pe_sums),))
        bd.vo(lambda e: e.tensor_scalar(rows[0:1, QTW:QTW + w], psS2[0:1, 0:w], 1.0 / D, None, OP.mult))
        bd.vo(lambda e: e.tensor_tensor(rows[0:1, 2 * QTW:2 * QTW + w], rows[0:1, 0:w],
                                        rows[0:1, 0:w], OP.mult), drain=True)
        d4 = bd.vo(lambda e: e.scalar_tensor_tensor(rows[0:1, QTW:QTW + w], rows[0:1, QTW:QTW + w],
                                                    LN_EPS, rows[0:1, 2 * QTW:2 * QTW + w],
                                                    OP.add, OP.subtract), drain=True)
        a1 = bd.ao(lambda e: e.activation(rows[0:1, QTW:QTW + w], rows[0:1, QTW:QTW + w], AF.Sqrt),
                   waits=(("dve", d4),))
        bd.vo(lambda e: e.reciprocal(rowm[0:1, 0:w], rows[0:1, QTW:QTW + w]),
              waits=(("act", a1), ("pe", st["ab_pe"]), ("pe", st["rb_pe"])))
        d6 = bd.vo(lambda e: e.scalar_tensor_tensor(rowm[0:1, QTW:QTW + w], rows[0:1, 0:w],
                                                    -1.0, rowm[0:1, 0:w], OP.mult, OP.mult),
                   drain=True)
        pe_a = bd.mm(ps3[:, 0:w], onesr_f[0:1, :], rowm[0:1, 0:w], True, True,
                     waits=(("dve", d6), ("dve", st["ydve"])))
        pe_b = bd.mm(psS[:, 0:w], onesr_f[0:1, :], rowm[0:1, QTW:QTW + w], True, True)
        st["ab_pe"] = pe_b
        st["rb_pe"] = pe_b
        for fc in range(NFC):
            sl = stg2[:, (fc % 2) * QTW:(fc % 2) * QTW + w]
            bd.vo(lambda e, sl=sl, fc=fc: e.tensor_tensor(sl, hb[:, fc * QTW:fc * QTW + w],
                                                          ps3[:, 0:w], OP.mult),
                  waits=(("pe", pe_b),) if fc == 0 else (), drain=fc >= 2)
            st["lnprev"] = bd.vo(
                lambda e, sl=sl, dst=xs[:, fc * TOKC + toff:fc * TOKC + toff + w]:
                e.tensor_tensor(dst, sl, psS[:, 0:w], OP.add), drain=True)

    def out_proj_ln(qt, perm, offmap, slabs, res_scale):
        dve_y = st["ydve"]
        for slab in slabs:
            buf, wdt, cld, bi = load_slab(perm, offmap, slab)
            for fcl in range(wdt // 128):
                fc = (slab[0] + fcl * 128) // 128
                sb = pp[fc % 2]
                for hc in range(H):
                    bd.mm(sb[:, 0:QTW], buf[:, hc * wdt + fcl * 128:hc * wdt + fcl * 128 + 128],
                          yall[:, hc * QTW:(hc + 1) * QTW], hc == 0, hc == H - 1,
                          waits=((cld, ("dve", dve_y),
                                  st.get(("bk", fc % 2), ("dve", 0))) if hc == 0 else ()))
                pe_m = bd.cnt["pe"]
                dog = bd.vo(
                    lambda e, sb=sb, fc=fc,
                    xsr=xs[:, fc * TOKC + qt * QTW:fc * TOKC + qt * QTW + QTW],
                    dst=hb[:, fc * QTW:(fc + 1) * QTW], rs=res_scale:
                    e.scalar_tensor_tensor(dst, sb[:, 0:QTW], rs, xsr, OP.mult, OP.add),
                    waits=(("pe", pe_m),))
                st[("og", fc % 2)] = dog
                st[("bk", fc % 2)] = ("dve", dog)
            done_slab(bi)
        layer_norm(qt * QTW, QTW)

    for slab in p.slabs_qkv_k:
        buf, wdt, cld, bi = load_slab(din["wqkv_p"], offs_qkv, slab, extra=(("act", act_x),))
        for hc in range(wdt // 128):
            h = (slab[0] - D + hc * 128) // 128
            for tg in range(NQT):
                m_ = (hc * NQT + tg) % 2
                sb = pp[m_]
                for dc in range(NFC):
                    bd.mm(sb[:, 0:QTW], buf[:, dc * wdt + hc * 128:dc * wdt + hc * 128 + 128],
                          xs[:, dc * TOKC + tg * QTW:dc * TOKC + tg * QTW + QTW],
                          dc == 0, dc == NFC - 1,
                          waits=((cld, (f"s_evb{m_}", st.get(("evb", m_), 0))) if dc == 0 else ()))
                pe_m = bd.cnt["pe"]
                a_e = bd.ao(lambda e, sb=sb, m_=m_: e.copy(evb[:, m_ * 512:m_ * 512 + QTW],
                                                           sb[:, 0:QTW]), waits=(("pe", pe_m),))
                st[("evb", m_)] = bd.dma(
                    "gpsimd", kv_snd[0][0, :, h * TOKC + tg * QTW:h * TOKC + (tg + 1) * QTW],
                    evb[:, m_ * 512:m_ * 512 + QTW], waits=(("act", a_e),),
                    sem=f"s_evb{m_}")
        done_slab(bi)
    for slab in p.slabs_qkv_v:
        buf, wdt, cld, bi = load_slab(din["wqkv_p"], offs_qkv, slab)
        for tt in range(NTT):
            for wo_ in range(0, wdt, 512):
                ww = min(512, wdt - wo_)
                sb = pp[tt % 2]
                for dc in range(NFC):
                    bd.mm(sb[:, 0:ww], xs[:, dc * TOKC + tt * 128:dc * TOKC + tt * 128 + 128],
                          buf[:, dc * wdt + wo_:dc * wdt + wo_ + ww], dc == 0, dc == NFC - 1,
                          waits=((cld, (f"s_evb{tt % 2}", st.get(("evb", tt % 2), 0)),
                                  st.get(("bk", tt % 2), ("dve", 0))) if dc == 0 else ()))
                pe_m = bd.cnt["pe"]
                a_e = bd.ao(lambda e, sb=sb, tt=tt, ww=ww:
                            e.copy(evb[:, (tt % 2) * 512:(tt % 2) * 512 + ww], sb[:, 0:ww]),
                            waits=(("pe", pe_m),))
                st[("bk", tt % 2)] = ("act", a_e)
                st[("evb", tt % 2)] = bd.dma(
                    "gpsimd", kv_snd[0][1, :, tt * D + (slab[0] - 2 * D) + wo_:
                                       tt * D + (slab[0] - 2 * D) + wo_ + ww],
                    evb[:, (tt % 2) * 512:(tt % 2) * 512 + ww], waits=(("act", a_e),),
                    sem=f"s_evb{tt % 2}")
        done_slab(bi)
    cc_kv0 = bd.coll("AllGather", OP.bypass, groups_pair,
                     [kv_snd[0].ap().opt()], [kv_out[0].ap().opt()],
                     waits=(bd.w("s_evb0"), bd.w("s_evb1")))
    cc_wT = bd.coll("AllGather", OP.bypass, groups_all, [wT_snd.ap().opt()], [wT_out.ap().opt()],
                    waits=(("s_pq2", sp_wt),))
    for qt in range(NQT):
        sp_dt = ("s_dsb", bd.dma("sync", dsb[:, 0:p.ktmax[qt] * QTW].rearrange("pp (n w) -> pp n w", w=QTW),
                                 din["dA"][dt_off[qt]:dt_off[qt] + p.ktmax[qt], :, :]
                                 .rearrange("n pp w -> pp n w"), waits=(bd.w("dve"),), sem="s_dsb"))
        for slab in p.slabs_qkv_q:
            buf, wdt, cld, bi = load_slab(din["wqkv_p"], offs_qkv, slab)
            for hc in range(wdt // 128):
                h = (slab[0] + hc * 128) // 128
                sb = pp[hc % 2]
                for dc in range(NFC):
                    bd.mm(sb[:, 0:QTW], buf[:, dc * wdt + hc * 128:dc * wdt + hc * 128 + 128],
                          xs[:, dc * TOKC + qt * QTW:dc * TOKC + qt * QTW + QTW],
                          dc == 0, dc == NFC - 1,
                          waits=((cld, st.get(("bk", hc % 2), ("dve", 0))) if dc == 0 else ()))
                pe_m = bd.cnt["pe"]
                a_q = bd.ao(
                    lambda e, sb=sb, h=h: e.activation(qall[:, h * QTW:(h + 1) * QTW], sb[:, 0:QTW],
                                                       AF.Copy, bias=0.0, scale=inv_sqrt_dh),
                    waits=(("pe", pe_m),))
                st[("bk", hc % 2)] = ("act", a_q)
            done_slab(bi)
        scores_qt(0, qt, cc_kv0, kv_out[0], True, sp_dt, qrdy=(bd.w("act"),))
        out_proj_ln(qt, din["wo_p"], offs_d, p.slabs_d, 1.0)
    st["xrdy"] = bd.cnt["dve"]
    if debug:
        bd.dma("gpsimd", dbg[0][:, :], xs[:, :], waits=(("dve", st["xrdy"]),), sem="s_msc")

    fst = bd.sb("fst", (128, 4 * QTW), BF16)
    NTC = 16 * H // 128
    tls = bd.sb("tls", (128, max(1, NTC) * TOKC), BF16)
    tb = bd.sb("tb", (128, 4 * TOKC), BF16)

    def head_pieces(base):
        out = {}
        for h in range(H):
            c0, c1 = base + h * p.DSTD, base + (h + 1) * p.DSTD
            ps_ = []
            c = c0
            while c < c1:
                nxt = min(c1, (c // 128 + 1) * 128)
                ps_.append((c, nxt - c))
                c = nxt
            out[h] = ps_
        return out

    qpieces = head_pieces(0)
    kpieces = head_pieces(p.qs_end)
    slotc = [0]

    def fused_chunk(buf, wdt, slab0, cld, gcol0, tg, slot):
        cc = gcol0 // 128
        sb = pp[slot % 2]
        for dc in range(NFC):
            bd.mm(sb[:, 0:QTW], buf[:, dc * wdt + (gcol0 - slab0):dc * wdt + (gcol0 - slab0) + 128],
                  xs[:, dc * TOKC + tg * QTW:dc * TOKC + tg * QTW + QTW],
                  dc == 0, dc == NFC - 1,
                  waits=((cld, (f"s_fst{slot}", st.get(("fe", slot), 0)),
                          csvec_rdy, ("dve", st.get("xrdy", 0)),
                          st.get(("bk", slot % 2), ("dve", 0))) if dc == 0 else ()))
        pe_m = bd.cnt["pe"]
        dv = bd.vo(lambda e, sb=sb, cc=cc, sl=fst[:, slot * QTW:(slot + 1) * QTW]:
                   e.tensor_scalar(sl, sb[:, 0:QTW], csvec[:, cc:cc + 1], None, OP.mult),
                   waits=(("pe", pe_m),))
        st[("bk", slot % 2)] = ("dve", dv)
        return dv

    for slabs, which in ((p.slabs_wa_k, "k"), (p.slabs_wa_t, "t")):
        for slab in slabs:
            buf, wdt, cld, bi = load_slab(din["wa_p"], offs_wa, slab)
            for gcol0 in range(slab[0], slab[1], 128):
                for tg in range(NQT):
                    slot = slotc[0] % 4
                    slotc[0] += 1
                    dv = fused_chunk(buf, wdt, slab[0], cld, gcol0, tg, slot)
                    if which == "k":
                        for h in range(H):
                            for (c0, ln) in kpieces[h]:
                                if c0 // 128 != gcol0 // 128:
                                    continue
                                po = c0 % 128
                                ro = c0 - (p.qs_end + h * p.DSTD)
                                st[("fe", slot)] = bd.dma(
                                    "gpsimd",
                                    kv_snd[1][0, ro:ro + ln, h * TOKC + tg * QTW:h * TOKC + (tg + 1) * QTW],
                                    fst[po:po + ln, slot * QTW:(slot + 1) * QTW],
                                    waits=(("dve", dv),), sem=f"s_fst{slot}")
                    else:
                        ci = (gcol0 - p.v_end) // 128
                        st[("fe", slot)] = bd.dma(
                            "gpsimd", tls[:, ci * TOKC + tg * QTW:ci * TOKC + (tg + 1) * QTW],
                            fst[:, slot * QTW:(slot + 1) * QTW], waits=(("dve", dv),), sem=f"s_fst{slot}")
            done_slab(bi)
    nt8 = 8 * H
    khi_ci, khi_po = (p.ql_end - p.v_end) // 128, (p.ql_end - p.v_end) % 128
    c5 = bd.dma("gpsimd", tb[0:nt8, 0:TOKC], tls[0:nt8, 0:TOKC],
                waits=(bd.w("s_fst0"), bd.w("s_fst1"), bd.w("s_fst2"), bd.w("s_fst3")), sem="s_msc")
    c6 = bd.dma("gpsimd", tb[0:nt8, TOKC:2 * TOKC],
                tls[khi_po:khi_po + nt8, khi_ci * TOKC:(khi_ci + 1) * TOKC], sem="s_msc")
    dvq = None
    for tg in range(NQT):
        a_, b_ = tg * QTW, tg * QTW + QTW
        bd.vo(lambda e, a_=a_, b_=b_: e.tensor_scalar(stg[0:nt8, 0:QTW], tb[0:nt8, a_:b_],
                                                      u128[0:nt8, 1:2], None, OP.mult),
              waits=(("s_msc", c6), ("dve", d_1mu)), drain=True)
        bd.vo(lambda e, a_=a_, b_=b_: e.tensor_scalar(stg2[0:nt8, 0:QTW], tb[0:nt8, TOKC + a_:TOKC + b_],
                                                      u128[0:nt8, 0:1], None, OP.mult))
        dvq = bd.vo(lambda e, a_=a_, b_=b_: e.tensor_tensor(tb[0:nt8, 2 * TOKC + a_:2 * TOKC + b_],
                                                            stg[0:nt8, 0:QTW], stg2[0:nt8, 0:QTW], OP.add),
                    drain=True)
        bd.vo(lambda e, a_=a_, b_=b_: e.tensor_scalar(stg[0:nt8, 0:QTW], tb[0:nt8, a_:b_],
                                                      u128[0:nt8, 0:1], None, OP.mult), drain=True)
        bd.vo(lambda e, a_=a_, b_=b_: e.tensor_scalar(stg2[0:nt8, 0:QTW], tb[0:nt8, TOKC + a_:TOKC + b_],
                                                      u128[0:nt8, 1:2], None, OP.mult))
        dvq = bd.vo(lambda e, a_=a_, b_=b_: e.tensor_tensor(tb[0:nt8, 3 * TOKC + a_:3 * TOKC + b_],
                                                            stg[0:nt8, 0:QTW], stg2[0:nt8, 0:QTW], OP.add),
                    drain=True)
    for h in range(H):
        bd.dma("gpsimd", kv_snd[1][0, 120:128, h * TOKC:(h + 1) * TOKC],
               tb[8 * h:8 * h + 8, 3 * TOKC:4 * TOKC], waits=(("dve", dvq),), sem="s_tl")
    for slab in p.slabs_wa_v:
        buf, wdt, cld, bi = load_slab(din["wa_p"], offs_wa, slab)
        for tt in range(NTT):
            for wo_ in range(0, wdt, 512):
                ww = min(512, wdt - wo_)
                sb = pp[tt % 2]
                for dc in range(NFC):
                    bd.mm(sb[:, 0:ww], xs[:, dc * TOKC + tt * 128:dc * TOKC + tt * 128 + 128],
                          buf[:, dc * wdt + wo_:dc * wdt + wo_ + ww], dc == 0, dc == NFC - 1,
                          waits=((cld, (f"s_evb{tt % 2}", st.get(("evb", tt % 2), 0)),
                                  ("dve", st.get("xrdy", 0)),
                                  st.get(("bk", tt % 2), ("dve", 0))) if dc == 0 else ()))
                pe_m = bd.cnt["pe"]
                a_e = bd.ao(lambda e, sb=sb, tt=tt, ww=ww:
                            e.copy(evb[:, (tt % 2) * 512:(tt % 2) * 512 + ww], sb[:, 0:ww]),
                            waits=(("pe", pe_m),))
                st[("bk", tt % 2)] = ("act", a_e)
                st[("evb", tt % 2)] = bd.dma(
                    "gpsimd", kv_snd[1][1, :, tt * D + (slab[0] - p.ks_end) + wo_:
                                       tt * D + (slab[0] - p.ks_end) + wo_ + ww],
                    evb[:, (tt % 2) * 512:(tt % 2) * 512 + ww], waits=(("act", a_e),),
                    sem=f"s_evb{tt % 2}")
        done_slab(bi)
    cc_kv1 = bd.coll("AllGather", OP.bypass, groups_pair,
                     [kv_snd[1].ap().opt()], [kv_out[1].ap().opt()],
                     waits=(bd.w("s_evb0"), bd.w("s_evb1"), bd.w("s_fst0"), bd.w("s_fst1"),
                            bd.w("s_fst2"), bd.w("s_fst3"), bd.w("s_tl")))
    for qt in range(NQT):
        sp_dt = ("s_dsb", bd.dma("sync", dsb[:, 0:p.ktmax[qt] * QTW].rearrange("pp (n w) -> pp n w", w=QTW),
                                 din["dM"][dt_off[qt]:dt_off[qt] + p.ktmax[qt], :, :]
                                 .rearrange("n pp w -> pp n w"), waits=(bd.w("dve"),), sem="s_dsb"))
        for slab in p.slabs_wa_q:
            buf, wdt, cld, bi = load_slab(din["wa_p"], offs_wa, slab)
            for gcol0 in range(slab[0], slab[1], 128):
                slot = slotc[0] % 4
                slotc[0] += 1
                dv = fused_chunk(buf, wdt, slab[0], cld, gcol0, qt, slot)
                for h in range(H):
                    for (c0, ln) in qpieces[h]:
                        if c0 // 128 != gcol0 // 128:
                            continue
                        po = c0 % 128
                        ro = c0 - h * p.DSTD
                        st[("fe", slot)] = bd.dma(
                            "gpsimd", qall[ro:ro + ln, h * QTW:(h + 1) * QTW],
                            fst[po:po + ln, slot * QTW:(slot + 1) * QTW], waits=(("dve", dv),),
                            sem=f"s_fst{slot}")
            done_slab(bi)
        for h in range(H):
            bd.dma("gpsimd", qall[120:128, h * QTW:(h + 1) * QTW],
                   tb[8 * h:8 * h + 8, 2 * TOKC + qt * QTW:2 * TOKC + (qt + 1) * QTW],
                   waits=(("dve", dvq),), sem="s_tl")
        scores_qt(1, qt, cc_kv1, kv_out[1], False, sp_dt,
                  qrdy=(bd.w("s_fst0"), bd.w("s_fst1"), bd.w("s_fst2"), bd.w("s_fst3"), bd.w("s_tl")))
        out_proj_ln(qt, din["wp_p"], offs_d, p.slabs_d, 1.0)
    st["xrdy"] = bd.cnt["dve"]
    if debug:
        bd.dma("gpsimd", dbg[1][:, :], xs[:, :], waits=(("dve", st["xrdy"]),), sem="s_msc")
    bd.free("fst", "tls", "tb", "wsA", "wsB", "dsb", "kst", "vst", "yall", "qall")
    bd.fence(("gpsimd", "sync", "vector"),
             (bd.w("pe"), bd.w("dve"), bd.w("s_fst0"), bd.w("s_fst1"), bd.w("s_fst2"),
              bd.w("s_fst3"), bd.w("s_tl"), bd.w("s_msc")))

    kvt = bd.sb("kvt", (128, NFC * TOKC), F32)
    vh = bd.sb("vh", (128, NFC * QTW), BF16)
    carr = bd.sb("carr", (128, 2 * NFC), F32)
    offa = bd.sb("offa", (128, NFC), F32)
    w1A = bd.sb("w1A", (128, NFC * E8), BF16)
    w1B = bd.sb("w1B", (128, NFC * E8), BF16)
    w1C = bd.sb("w1C", (128, NFC * E8), BF16)
    w1g = {0: 0, 1: 0, 2: 0}
    bd.emit("gpsimd", lambda e: e.memset(carr[:, 0:NFC], 0.0), inc="ini")

    def load_w1(mi, r, bufi, extra=()):
        buf = (w1A, w1B, w1C)[bufi]
        c = bd.dma("sync", buf[:, 0:NFC * E8].rearrange("pp (c e) -> pp c e", c=NFC),
                   wT_out[r, OFF_M[mi]:OFF_M[mi] + PR_QKVO]
                   .rearrange("(pp c e) -> pp c e", pp=128, c=NFC),
                   waits=(("cc", cc_wT), ("pe", w1g[bufi])) + tuple(extra), sem=f"s_w1{bufi}")
        return buf, (f"s_w1{bufi}", c)

    for r in range(NC):
        bk, ck = load_w1(1, r, 0)
        bv, cv = load_w1(2, r, 1)
        for el in range(E2):
            e_g = r * E2 + el
            for qt in range(NQT):
                for dc in range(NFC):
                    bd.mm(ps0[:, 0:QTW], bk[:, dc * E8 + el * 128:dc * E8 + el * 128 + 128],
                          xs[:, dc * TOKC + qt * QTW:dc * TOKC + qt * QTW + QTW],
                          dc == 0, dc == NFC - 1,
                          waits=((ck, ("dve", st.get("kvd", 0)), ("dve", st.get("xrdy", 0)),
                                  st.get(("bk", 0), ("dve", 0))) if dc == 0 else ()))
                for dc in range(NFC):
                    bd.mm(ps1[:, 0:QTW], bv[:, dc * E8 + el * 128:dc * E8 + el * 128 + 128],
                          xs[:, dc * TOKC + qt * QTW:dc * TOKC + qt * QTW + QTW],
                          dc == 0, dc == NFC - 1,
                          waits=((cv, st.get(("bk", 1), ("dve", 0))) if dc == 0 else ()))
                pe_m = bd.cnt["pe"]
                a_kc = bd.ao(lambda e: e.copy(kvsb[:, 0:QTW], ps0[:, 0:QTW]),
                             waits=(("pe", pe_m), ("dve", st.get("kvd", 0))))
                dkv = bd.vo(lambda e: e.tensor_tensor(stg[:, 0:QTW], kvsb[:, 0:QTW], ps1[:, 0:QTW], OP.mult),
                            waits=(("act", a_kc), ("ini", 2)))
                st[("bk", 0)] = ("dve", dkv)
                st[("bk", 1)] = ("dve", dkv)
                bd.vo(lambda e, e_g=e_g, qt=qt:
                      e.tensor_tensor_scan(kvt[:, e_g * TOKC + qt * QTW:e_g * TOKC + (qt + 1) * QTW],
                                           stg[:, 0:QTW], stg[:, 0:QTW],
                                           carr[:, e_g:e_g + 1], OP.add, OP.bypass), drain=True)
                st["kvd"] = bd.vo(lambda e, e_g=e_g, qt=qt:
                                  e.tensor_copy(carr[:, e_g:e_g + 1],
                                                kvt[:, e_g * TOKC + (qt + 1) * QTW - 1:
                                                    e_g * TOKC + (qt + 1) * QTW]), drain=True)
        w1g[0] = bd.cnt["pe"]
        w1g[1] = bd.cnt["pe"]
    c7 = bd.dma("gpsimd", off_snd[:, :], carr[:, 0:NFC], waits=(("dve", st["kvd"]),), sem="s_msc")
    cc_off = bd.coll("AllGather", OP.bypass, groups_pair,
                     [off_snd.ap().opt()], [off_out.ap().opt()], waits=(("s_msc", c7),))
    c8 = bd.dma("gpsimd", carr[:, NFC:2 * NFC], off_out[0, :, :], waits=(("cc", cc_off),), sem="s_msc")
    dvo = bd.vo(lambda e: e.tensor_scalar(offa[:, 0:NFC], carr[:, NFC:2 * NFC], small[:, 12:13],
                                          None, OP.mult), waits=(("s_msc", c8), ("act", a_ib)))
    K(denom_l1)
    for qt in range(NQT):
        for r in range(NC):
            bq, cq = load_w1(0, r, r % 2)
            for el in range(E2):
                e_g = r * E2 + el
                for dc in range(NFC):
                    bd.mm(ps0[:, 0:QTW], bq[:, dc * E8 + el * 128:dc * E8 + el * 128 + 128],
                          xs[:, dc * TOKC + qt * QTW:dc * TOKC + qt * QTW + QTW],
                          dc == 0, dc == NFC - 1,
                          waits=((cq, ("dve", st.get("vhd", 0)), ("dve", st.get("xrdy", 0)),
                                  st.get(("bk", 0), ("dve", 0))) if dc == 0 else ()))
                pe_m = bd.cnt["pe"]
                st["vhd"] = bd.vo(
                    lambda e, e_g=e_g, qt=qt:
                    e.scalar_tensor_tensor(vh[:, e_g * QTW:(e_g + 1) * QTW],
                                           kvt[:, e_g * TOKC + qt * QTW:e_g * TOKC + (qt + 1) * QTW],
                                           offa[:, e_g:e_g + 1], ps0[:, 0:QTW], OP.add, OP.mult),
                    waits=(("pe", pe_m), ("dve", dvo)))
                st[("bk", 0)] = ("dve", st["vhd"])
            w1g[r % 2] = bd.cnt["pe"]
        for r in range(NC):
            bo, co = load_w1(3, r, 2)
            for fl in range(E2):
                fc = r * E2 + fl
                sb = pp[fc % 2]
                for ec in range(NFC):
                    bd.mm(sb[:, 0:QTW], bo[:, ec * E8 + fl * 128:ec * E8 + fl * 128 + 128],
                          vh[:, ec * QTW:(ec + 1) * QTW], ec == 0, ec == NFC - 1,
                          waits=((co, st.get(("bk", fc % 2), ("dve", 0)),
                                  ("dve", st["vhd"])) if ec == 0 else ()))
                pe_m = bd.cnt["pe"]
                dog = bd.vo(
                    lambda e, sb=sb, fc=fc,
                    xsr=xs[:, fc * TOKC + qt * QTW:fc * TOKC + qt * QTW + QTW],
                    dst=hb[:, fc * QTW:(fc + 1) * QTW]:
                    e.scalar_tensor_tensor(dst, sb[:, 0:QTW], denom_l1, xsr, OP.mult, OP.add),
                    waits=(("pe", pe_m),))
                st[("og", fc % 2)] = dog
                st[("bk", fc % 2)] = ("dve", dog)
            w1g[2] = bd.cnt["pe"]
        layer_norm(qt * QTW, QTW)
    st["xrdy"] = bd.cnt["dve"]
    if debug:
        bd.dma("gpsimd", dbg[2][:, :], xs[:, :], waits=(("dve", st["xrdy"]),), sem="s_msc")
    bd.free("kvt", "vh", "carr", "offa", "w1A", "w1B", "w1C")
    bd.fence(("sync", "vector", "scalar"),
             (bd.w("pe"), bd.w("dve"), bd.w("s_msc"),
              bd.w("s_w10"), bd.w("s_w11"), bd.w("s_w12")))

    if debug:
        dcap0 = bd.vo(lambda e: e.tensor_copy(dbgsb[:, SQ:2 * SQ], xs[:, 0:SQ]), drain=True)
        bd.dma("gpsimd", dbg_cap[:, SQ:2 * SQ], dbgsb[:, SQ:2 * SQ],
               waits=(("dve", dcap0),), sem="s_msc")
    wsA = bd.sb("wsA2", (128, WSW), BF16)
    wsB = bd.sb("wsB2", (128, WSW), BF16)
    xq = bd.sb("xq", (128, NFC * SQ), BF16)
    gsi = bd.sb("gsi", (128, I2h * SQ), BF16)
    gu = bd.sb("gu", (128, NIC * SQ), BF16)
    mfold = bd.sb("mfold", (128, 2 * SQ), BF16)
    wgg = [0, 0]
    K(1.0 / 127.0)

    def colmax_to_row(src_fn, nch, row_off):
        dvm = None
        for i in range(nch):
            if i == 0:
                bd.vo(lambda e, s=src_fn(0): e.tensor_copy(mfold[:, 0:SQ], s), drain=True)
                dvm = bd.vo(lambda e, s=src_fn(0): e.tensor_copy(mfold[:, SQ:2 * SQ], s))
            else:
                bd.vo(lambda e, s=src_fn(i): e.tensor_tensor(mfold[:, 0:SQ], mfold[:, 0:SQ],
                                                             s, OP.max), drain=True)
                dvm = bd.vo(lambda e, s=src_fn(i): e.tensor_tensor(mfold[:, SQ:2 * SQ],
                                                                   mfold[:, SQ:2 * SQ], s, OP.min),
                            drain=True)
        bd.vo(lambda e: e.tensor_scalar(mfold[:, SQ:2 * SQ], mfold[:, SQ:2 * SQ], -1.0, None, OP.mult),
              drain=True)
        dvm = bd.vo(lambda e: e.tensor_tensor(mfold[:, 0:SQ], mfold[:, 0:SQ], mfold[:, SQ:2 * SQ],
                                              OP.max), drain=True)

        for j in range(SQ // 128):
            t = ptr[j % 2]
            pe_t = bd.mm(t[:, 0:128], mfold[:, j * 128:(j + 1) * 128], ident_b[:, :], True, True,
                         waits=(("dve", dvm), ("dve", st.get(("cmr", j % 2), 0)),
                                ("act", st.get(("xb", j % 2), 0))))
            st[("cmr", j % 2)] = bd.vo(
                lambda e, t=t, j=j: e.reduce_max(small[:, 16 + j:17 + j], t[:, 0:128],
                                                 axis=mybir.AxisListType.X, apply_absolute_value=True),
                waits=(("pe", pe_t),))
        dvm2 = bd.cnt["dve"]
        c0_ = bd.dma("gpsimd", sm_dram[:, 0:SQ // 128], small[:, 16:16 + SQ // 128],
                     waits=(("dve", dvm2),))
        c = bd.dma("gpsimd", rows[0:1, row_off:row_off + SQ].rearrange("o (j pp) -> o j pp", pp=128),
                   sm_dram[:, 0:SQ // 128].rearrange("(o pp) j -> o j pp", o=1),
                   waits=(("plq", c0_), bd.w("s_msc")), allow_slow_non_contiguous=True, sem="s_cm")
        dts = bd.vo(lambda e: e.tensor_scalar(rows[0:1, row_off:row_off + SQ],
                                              rows[0:1, row_off:row_off + SQ],
                                              1.0 / 127.0, Q_EPS, OP.mult, OP.add),
                    waits=(("s_cm", c),), drain=True)
        if debug and st.get("cm_calls", 0) == 0:
            dcapB = bd.vo(lambda e: e.tensor_copy(dbgsb[:, SQ:2 * SQ], xs[:, 0:SQ]),
                          waits=(bd.w("s_msc"),), drain=True)
            bd.dma("gpsimd", dbg_cap[:, 3 * SQ:4 * SQ], dbgsb[:, SQ:2 * SQ],
                   waits=(("dve", dcapB),), sem="s_msc")
        st["cm_calls"] = st.get("cm_calls", 0) + 1
        d2_ = bd.vo(lambda e: e.reciprocal(rowm[0:1, 0:SQ], rows[0:1, row_off:row_off + SQ]),
                    waits=(("pe", st["rb_pe"]),), drain=True)
        return d2_

    for qs in range(NSQ):
        toff = qs * SQ
        if qs > 0:
            bd.fence(("vector", "scalar"), (bd.w("pe"), bd.w("s_msc")))
        d2 = colmax_to_row(lambda i: xs[:, i * TOKC + toff:i * TOKC + toff + SQ], NFC, 3 * QTW)
        if debug and qs == 0:
            dcapC = bd.vo(lambda e: e.tensor_copy(dbgsb[:, SQ:2 * SQ], xs[:, 0:SQ]),
                          waits=(bd.w("s_msc"),), drain=True)
            bd.dma("gpsimd", dbg_cap[:, 2 * SQ:3 * SQ], dbgsb[:, SQ:2 * SQ],
                   waits=(("dve", dcapC),), sem="s_msc")
        st["rb_pe"] = pe_rbx = bd.mm(ps3[:, 0:SQ], onesr_f[0:1, :], rowm[0:1, 0:SQ], True, True,
                                     waits=(("dve", d2), ("dve", st["ydve"])))
        if debug and qs == 0:
            dcap = bd.vo(lambda e: e.tensor_copy(dbgsb[:, 0:SQ], ps3[:, 0:SQ]),
                         waits=(("pe", pe_rbx), bd.w("s_msc")), drain=True)
            dcap2 = bd.vo(lambda e, xsr=xs[:, toff:toff + SQ]:
                          e.tensor_copy(dbgsb[:, SQ:2 * SQ], xsr), drain=True)
            bd.dma("gpsimd", dbg_rb[:, :], dbgsb[:, 0:SQ], waits=(("dve", dcap),), sem="s_msc")
            bd.dma("gpsimd", dbg_cap[:, 0:SQ], dbgsb[:, SQ:2 * SQ], waits=(("dve", dcap2),),
                   sem="s_msc")
        for fc in range(NFC):
            dvq1 = bd.vo(lambda e, fc=fc,
                         xsr=xs[:, fc * TOKC + toff:fc * TOKC + toff + SQ]:
                         e.tensor_tensor(i32s[:, (fc % 2) * 512:(fc % 2) * 512 + SQ],
                                         xsr, ps3[:, 0:SQ], OP.mult),
                         waits=(("pe", pe_rbx),) + ((("act", st.get(("xqe", fc % 2), 0)),) if fc >= 2 else ()),
                         drain=fc >= 2)
            st[("xqe", fc % 2)] = bd.ao(lambda e, fc=fc:
                                        e.copy(xq[:, fc * SQ:(fc + 1) * SQ],
                                               i32s[:, (fc % 2) * 512:(fc % 2) * 512 + SQ]),
                                        waits=(("dve", dvq1),))
        act_xq = bd.cnt["act"]
        dve_xq = bd.cnt["dve"]
        if debug and qs == 0:
            bd.dma("gpsimd", dbg_xq[:, :], xq[:, :], waits=(("act", act_xq),), sem="s_msc")
        d3 = bd.vo(lambda e: e.tensor_scalar(rowm[0:1, 0:SQ], rows[0:1, 3 * QTW:3 * QTW + SQ],
                                             srow[0:1, 8:9], None, OP.mult),
                   waits=(("pe", pe_rbx),), drain=True)
        pe_cg = bd.mm(ps4[:, 0:SQ], onesr_f[0:1, :], rowm[0:1, 0:SQ], True, True,
                      waits=(("dve", d3), ("dve", st.get(("cmr", 0), 0)), ("dve", st.get(("cmr", 1), 0))))
        a_cg = bd.ao(lambda e: e.copy(csc[:, 0:SQ], ps4[:, 0:SQ]), waits=(("pe", pe_cg),))
        d4 = bd.vo(lambda e: e.tensor_scalar(rowm[0:1, QTW:QTW + SQ], rows[0:1, 3 * QTW:3 * QTW + SQ],
                                             srow[0:1, 9:10], None, OP.mult))
        pe_cu = bd.mm(ps5[:, 0:SQ], onesr_f[0:1, :], rowm[0:1, QTW:QTW + SQ], True, True,
                      waits=(("dve", d4),))
        a_cu = bd.ao(lambda e: e.copy(csc[:, SQ:2 * SQ], ps5[:, 0:SQ]), waits=(("pe", pe_cu),))
        st[("xb", 0)] = a_cg
        st[("xb", 1)] = a_cu
        st["rb_pe"] = pe_cu
        for r in range(NC):
            for hf in range(2):
                cg_ = bd.dma("sync", wsA[:, 0:NFC * I8h].rearrange("pp (c e) -> pp c e", c=NFC),
                             wT_out[r, OFF_M[4] + hf * (PR_GU // 2):OFF_M[4] + (hf + 1) * (PR_GU // 2)]
                             .rearrange("(pp c e) -> pp c e", pp=128, c=NFC),
                             waits=(("cc", cc_wT), ("pe", wgg[0])), sem="s_wg0")
                for il in range(I2h):
                    sb = pp[il % 2]
                    for fc in range(NFC):
                        bd.mm(sb[:, 0:SQ], wsA[:, fc * I8h + il * 128:fc * I8h + il * 128 + 128],
                              xq[:, fc * SQ:(fc + 1) * SQ], fc == 0, fc == NFC - 1,
                              waits=((("s_wg0", cg_), ("act", act_xq),
                                      st.get(("bk", il % 2), ("dve", 0))) if fc == 0 else ()))
                    pe_m = bd.cnt["pe"]
                    dvg = bd.vo(lambda e, sb=sb, sl=stg[:, (il % 2) * SQ:(il % 2) * SQ + SQ]:
                                e.tensor_tensor(sl, sb[:, 0:SQ], csc[:, 0:SQ], OP.mult),
                                waits=(("pe", pe_m), ("act", a_cg),
                                       ("act", st.get(("ge", il % 2), 0))))
                    st[("bk", il % 2)] = ("dve", dvg)
                    st[("ge", il % 2)] = bd.ao(lambda e, il=il,
                                               sl=stg[:, (il % 2) * SQ:(il % 2) * SQ + SQ]:
                                               e.activation(gsi[:, il * SQ:(il + 1) * SQ], sl,
                                                            AF.Silu), waits=(("dve", dvg),))
                wgg[0] = bd.cnt["pe"]
                cu_ = bd.dma("sync", wsB[:, 0:NFC * I8h].rearrange("pp (c e) -> pp c e", c=NFC),
                             wT_out[r, OFF_M[5] + hf * (PR_GU // 2):OFF_M[5] + (hf + 1) * (PR_GU // 2)]
                             .rearrange("(pp c e) -> pp c e", pp=128, c=NFC),
                             waits=(("cc", cc_wT), ("pe", wgg[1])), sem="s_wg1")
                for il in range(I2h):
                    ic = r * 2 * I2h + hf * I2h + il
                    sb = pp[il % 2]
                    for fc in range(NFC):
                        bd.mm(sb[:, 0:SQ], wsB[:, fc * I8h + il * 128:fc * I8h + il * 128 + 128],
                              xq[:, fc * SQ:(fc + 1) * SQ], fc == 0, fc == NFC - 1,
                              waits=((("s_wg1", cu_), ("dve", st.get(("ue", il % 2), 0)),
                                      st.get(("bk", il % 2), ("dve", 0)),
                                      ("act", st.get(("ge", il % 2), 0))) if fc == 0 else ()))
                    pe_m = bd.cnt["pe"]
                    dvu = bd.vo(lambda e, sb=sb: e.tensor_tensor(stg2[:, 0:SQ], sb[:, 0:SQ],
                                                                 csc[:, SQ:2 * SQ], OP.mult),
                                waits=(("pe", pe_m), ("act", a_cu), ("act", st.get(("ge", il % 2), 0))),
                                drain=True)
                    st[("bk", il % 2)] = ("dve", dvu)
                    st[("ue", il % 2)] = bd.vo(
                        lambda e, ic=ic, il=il: e.tensor_tensor(gu[:, ic * SQ:(ic + 1) * SQ],
                                                                gsi[:, il * SQ:(il + 1) * SQ],
                                                                stg2[:, 0:SQ], OP.mult), drain=True)
                wgg[1] = bd.cnt["pe"]
        d2g = colmax_to_row(lambda i: gu[:, i * SQ:(i + 1) * SQ], NIC, 4 * QTW)
        st["rb_pe"] = pe_rbg = bd.mm(ps3[:, 0:SQ], onesr_f[0:1, :], rowm[0:1, 0:SQ], True, True,
                                     waits=(("dve", d2g), ("dve", dve_xq)))
        for ic in range(NIC):
            dvq1 = bd.vo(lambda e, ic=ic: e.tensor_tensor(i32s[:, (ic % 2) * 512:(ic % 2) * 512 + SQ],
                                                          gu[:, ic * SQ:(ic + 1) * SQ],
                                                          ps3[:, 0:SQ], OP.mult),
                         waits=(("pe", pe_rbg),) + ((("act", st.get(("gqe", ic % 2), 0)),) if ic >= 2 else ()),
                         drain=ic >= 2)
            st[("gqe", ic % 2)] = bd.ao(lambda e, ic=ic:
                                        e.copy(gu[:, ic * SQ:(ic + 1) * SQ],
                                               i32s[:, (ic % 2) * 512:(ic % 2) * 512 + SQ]),
                                        waits=(("dve", dvq1),))
        act_guq = bd.cnt["act"]
        if debug and qs == 0:
            bd.dma("gpsimd", dbg_gu[:, :], gu[:, :], waits=(("act", act_guq),), sem="s_msc")
            bd.dma("gpsimd", dbg_gs[0:1, 0:2 * QTW], rows[0:1, 3 * QTW:5 * QTW], sem="s_msc")
        d5 = bd.vo(lambda e: e.tensor_scalar(rowm[0:1, 0:SQ], rows[0:1, 4 * QTW:4 * QTW + SQ],
                                             srow[0:1, 10:11], None, OP.mult),
                   waits=(("pe", pe_rbg),), drain=True)
        dve_pre_cd = bd.cnt["dve"]
        pe_cd = bd.mm(ps4[:, 0:SQ], onesr_f[0:1, :], rowm[0:1, 0:SQ], True, True,
                      waits=(("dve", d5), ("dve", dve_pre_cd), ("act", a_cg)))
        a_cd = bd.ao(lambda e: e.copy(csc[:, 2 * SQ:3 * SQ], ps4[:, 0:SQ]), waits=(("pe", pe_cd),))
        st["rb_pe"] = pe_cd
        for r in range(NC):
            bufs_dn = (wsA, wsB)
            for fl2 in range(0, E2, 2):
                for ihf in range(2):
                    cd_ = bd.dma("sync", bufs_dn[ihf][:, 0:NICh * E8].rearrange("pp (c e) -> pp c e", c=NICh),
                                 wT_out[r, OFF_M[6]:OFF_M[6] + PR_D]
                                 .rearrange("(pp c e) -> pp c e", pp=128, c=NIC)
                                 [:, ihf * NICh:(ihf + 1) * NICh, :],
                                 waits=(("cc", cc_wT), ("pe", wgg[ihf])), sem=f"s_wg{ihf}")
                    for fl in (fl2, fl2 + 1):
                        sb = pp[fl % 2]
                        for icl in range(NICh):
                            ic = ihf * NICh + icl
                            bd.mm(sb[:, 0:SQ],
                                  bufs_dn[ihf][:, icl * E8 + fl * 128:icl * E8 + fl * 128 + 128],
                                  gu[:, ic * SQ:(ic + 1) * SQ],
                                  ihf == 0 and icl == 0, ihf == 1 and icl == NICh - 1,
                                  waits=(((f"s_wg{ihf}", cd_), ("act", act_guq),
                                          st.get(("bk", fl % 2), ("dve", 0))) if icl == 0 else ()))
                    wgg[ihf] = bd.cnt["pe"]
                for fl in (fl2, fl2 + 1):
                    fc = r * E2 + fl
                    sb = pp[fl % 2]
                    pe_m = bd.cnt["pe"]
                    dvd = bd.vo(lambda e, sb=sb: e.tensor_tensor(stg[:, 0:SQ], sb[:, 0:SQ],
                                                                 csc[:, 2 * SQ:3 * SQ], OP.mult),
                                waits=(("pe", pe_m), ("act", a_cd)), drain=True)
                    st[("bk", fl % 2)] = ("dve", dvd)
                    st[("og", fl % 2)] = bd.vo(
                        lambda e, fc=fc,
                        xsr=xs[:, fc * TOKC + toff:fc * TOKC + toff + SQ],
                        dst=hb[:, fc * QTW:fc * QTW + SQ]:
                        e.scalar_tensor_tensor(dst, stg[:, 0:SQ], 1.0, xsr, OP.mult, OP.add), drain=True)
        layer_norm(toff, SQ)
    bd.free("wsA2", "wsB2", "xq", "gsi", "gu", "mfold")
    bd.fence(("scalar",), (bd.w("pe"), bd.w("dve")))

    otm = bd.sb("otm", (128, D), F32)
    for tt in range(NTT):
        for fc in range(NFC):
            t = ptr[fc % 2]
            pe_t = bd.mm(t[:, 0:128], xs[:, fc * TOKC + tt * 128:fc * TOKC + tt * 128 + 128],
                         ident_b[:, :], True, True,
                         waits=(("dve", st["lnprev"]), ("act", st.get(("xb", fc % 2), 0))) +
                               ((("s_out", st.get("odma", 0)),) if fc < 2 else ()))
            st[("xb", fc % 2)] = bd.ao(lambda e, t=t, fc=fc: e.copy(otm[:, fc * 128:(fc + 1) * 128],
                                                                    t[:, 0:128]),
                                       waits=(("pe", pe_t),))
        a_fin = bd.cnt["act"]
        st["odma"] = bd.dma("gpsimd", out_my[tt * 128:(tt + 1) * 128, :], otm[:, :],
                            waits=(("act", a_fin),), sem="s_out")
    final_out = bd.cnt["s_out"]
    bd.emit("gpsimd", lambda e: e.nop(), waits=(("s_out", final_out),))

    vals = sorted(bd.consts)
    assert len(vals) <= 64, len(vals)
    for i, v in enumerate(vals):
        nc.const_aps.register(F32, float(v), cpool[:, i:i + 1])
        nc.const_aps.register(BF16, float(v), cpool_b[:, i:i + 1])

    def memset_fn(e):
        for i, v in enumerate(vals):
            e.memset(cpool[:, i:i + 1], float(v))
            e.memset(cpool_b[:, i:i + 1], float(v))
        e.memset(ones_b[:, :], 1.0)
        e.memset(ones_f[:, :], 1.0)
        e.memset(srow[:, :], 0.0)
        e.drain()
        e.memset(srow[0:1, 15:16], 1.0)
        return e.memset(onesr_f[:, :], 1.0)

    bd.q["gpsimd"].insert(0, ((), memset_fn, "ini", 1))

    es = contextlib.ExitStack()
    sems = {}
    for s in SEMS:
        sems[s] = es.enter_context(nc.semaphore("sem_" + s))
    with es:
        with nc.Block() as block:
            for eng_name in ENGS:
                ops = bd.q[eng_name]

                def mk(ops=ops):
                    def body(e):
                        for waits, fn, inc, amt in ops:
                            for (s, v) in waits:
                                e.wait_ge(sems[s], v)
                            ins = fn(e)
                            if inc is not None and ins is not None:
                                ins.then_inc(sems[inc], amt)
                    return body
                getattr(block, eng_name)(mk())
    return nc


def kernel(**inputs):
    from concourse.bass_utils import run_bass_kernel_spmd
    prm = Prm()
    in_maps = host_prepare(prm, inputs)
    nc = build(prm)
    res = run_bass_kernel_spmd(nc, in_maps, core_ids=list(range(prm.NC)))
    out = np.empty((prm.B, prm.T, prm.D), np.float32)
    for c in range(prm.NC):
        b, hf = c // 2, c % 2
        out[b, hf * prm.TOKC:(hf + 1) * prm.TOKC, :] = np.asarray(res.results[c]["out_my"])
    return out


# version 07
# speedup vs baseline: 1.0686x; 1.0039x over previous
import contextlib
import math

import numpy as np
import ml_dtypes

import concourse.bass as bass
import concourse.mybir as mybir
from concourse.alu_op_type import AluOpType as OP

AF = mybir.ActivationFunctionType
F32 = mybir.dt.float32
BF16 = mybir.dt.bfloat16
I32 = mybir.dt.int32

NEG = -1.0e30
B_EXP = 8.0
LN_EPS = 1e-5
Q_EPS = 1e-5
QTW = 512
SQ = 256


class Prm:
    def __init__(self, B=4, T=2048, D=2048, H=16, I=8192):
        self.B, self.T, self.D, self.H, self.I = B, T, D, H, I
        self.NC = 2 * B
        self.TOKC = T // 2
        self.NQT = self.TOKC // QTW
        self.NFC = D // 128
        self.NKT = T // 128
        assert D == H * 128 and self.TOKC % QTW == 0 and (120 * H) % 128 == 0
        self.FD = 3 * D
        self.NFCF = self.FD // 128
        self.R = 8
        self.DSTD = 120
        self.qs_end = H * 120
        self.ks_end = 2 * H * 120
        self.v_end = self.ks_end + D
        self.ql_end = self.v_end + 8 * H
        self.ktmax = [(self.TOKC + (qt + 1) * QTW) // 128 for qt in range(self.NQT)]
        self.NDT = sum(self.ktmax)
        self.E8 = D // self.NC
        self.I8 = I // self.NC
        self.NIC = I // 128
        self.slabs_qkv_q = mk_slabs(0, D)
        self.slabs_qkv_k = mk_slabs(D, 2 * D)
        self.slabs_qkv_v = mk_slabs(2 * D, 3 * D)
        self.slabs_wa_q = mk_slabs(0, self.qs_end)
        self.slabs_wa_k = mk_slabs(self.qs_end, self.ks_end)
        self.slabs_wa_v = mk_slabs(self.ks_end, self.v_end)
        self.slabs_wa_t = [(self.v_end, self.FD)]
        self.slabs_d = mk_slabs(0, D)


def mk_slabs(a, b, extra=()):
    bnds = sorted({x for x in range(0, b + 1, 512) if a < x < b} | {a, b} | {x for x in extra if a < x < b})
    return [(bnds[i], bnds[i + 1]) for i in range(len(bnds) - 1)]


def _slopes(H):
    return [2.0 ** (-8.0 / (i + 1)) for i in range(H)]



def perm_w(w, slabs):
    D, F = w.shape
    ndc = D // 128
    w3 = w.reshape(ndc, 128, F)
    parts = []
    for (a, b) in slabs:
        parts.append(np.ascontiguousarray(w3[:, :, a:b].transpose(1, 0, 2)).reshape(-1))
    return np.concatenate(parts)


def mk_dtiles(prm, qoff, alibi):
    tiles = np.empty((prm.NDT, 128, QTW), dtype=np.float32)
    dk = np.arange(128)[:, None]
    dq = np.arange(QTW)[None, :]
    i = 0
    for qt in range(prm.NQT):
        for kt in range(prm.ktmax[qt]):
            k = kt * 128 + dk
            q = qoff + qt * QTW + dq
            base = (k - q).astype(np.float32) if alibi else np.zeros((128, QTW), np.float32)
            tiles[i] = np.where(k <= q, base, NEG)
            i += 1
    return tiles.astype(ml_dtypes.bfloat16)


def host_prepare(prm, inputs):
    p = prm
    x = np.asarray(inputs["L0_attn_x"], np.float32)
    wqkv = perm_w(np.asarray(inputs["L0_attn_W_qkv"], np.float32),
                  p.slabs_qkv_q + p.slabs_qkv_k + p.slabs_qkv_v)
    wo = perm_w(np.asarray(inputs["L0_attn_W_o"], np.float32), p.slabs_d)
    wa = perm_w(np.asarray(inputs["L0_ffn_Wa"], np.float32),
                p.slabs_wa_q + p.slabs_wa_k + p.slabs_wa_v + p.slabs_wa_t)
    wp = perm_w(np.asarray(inputs["L0_ffn_Wp"], np.float32), p.slabs_d)
    ident = np.eye(128, dtype=np.float32)
    mats = [np.asarray(inputs[k], np.float32) for k in
            ("L1_attn_Wq", "L1_attn_Wk", "L1_attn_Wv", "L1_attn_Wo")]
    wg = np.asarray(inputs["L1_ffn_w_gate"], np.float32)
    wu = np.asarray(inputs["L1_ffn_w_up"], np.float32)
    wd = np.asarray(inputs["L1_ffn_w_down"], np.float32)
    wstd = np.asarray(inputs["L0_ffn_w_std"], np.float32).reshape(1, p.H)
    wrec = np.asarray(inputs["L0_ffn_w_rec"], np.float32).reshape(1, p.H)
    qk = 128.0 ** -0.25
    nsv = 1 + 2 * p.H
    sel = np.zeros((nsv, p.NFCF * 128), np.float32)
    for j in range(p.FD):
        if j < p.qs_end:
            sel[1 + j // 120, j] = qk
        elif j < p.ks_end:
            sel[1 + (j - p.qs_end) // 120, j] = qk
        elif j < p.v_end:
            sel[0, j] = 1.0
        elif j < p.ql_end:
            sel[1 + p.H + (j - p.v_end) // p.R, j] = qk
        else:
            sel[1 + p.H + (j - p.ql_end) // p.R, j] = qk
    rep8 = np.zeros((p.H, 128), np.float32)
    for pp_ in range(8 * p.H):
        rep8[pp_ // 8, pp_] = 1.0
    in_maps = []
    for c in range(p.NC):
        b, hf = c // 2, c % 2
        qoff = hf * p.TOKC
        in_maps.append({
            "x_my": np.ascontiguousarray(x[b, qoff:qoff + p.TOKC, :]),
            "wqkv_p": wqkv, "wo_p": wo, "wa_p": wa, "wp_p": wp,
            "ident": ident, "w_std": wstd, "w_rec": wrec,
            "is_b": np.full((1, 1), float(hf), np.float32),
            "sel": sel, "rep8": rep8,
            "dA": mk_dtiles(p, qoff, True),
            "dM": mk_dtiles(p, qoff, False),
            "wq1_s": np.ascontiguousarray(mats[0][c * p.E8:(c + 1) * p.E8, :]),
            "wk1_s": np.ascontiguousarray(mats[1][c * p.E8:(c + 1) * p.E8, :]),
            "wv1_s": np.ascontiguousarray(mats[2][c * p.E8:(c + 1) * p.E8, :]),
            "wo1_s": np.ascontiguousarray(mats[3][c * p.E8:(c + 1) * p.E8, :]),
            "wg_s": np.ascontiguousarray(wg[c * p.I8:(c + 1) * p.I8, :]),
            "wu_s": np.ascontiguousarray(wu[c * p.I8:(c + 1) * p.I8, :]),
            "wd_s": np.ascontiguousarray(wd[c * p.E8:(c + 1) * p.E8, :]),
        })
    return in_maps



ENGS = ("tensor", "vector", "scalar", "sync", "gpsimd")
SEMS = ("pe", "dve", "act", "spq", "plq", "cc", "ini",
        "s_in", "s_ins", "s_pq", "s_pq2", "s_wf", "s_msc", "s_cm", "s_out",
        "s_dsb", "s_tl", "s_xt0", "s_xt1", "s_ws0", "s_ws1", "s_kv0", "s_kv1",
        "s_evb0", "s_evb1", "s_w10", "s_w11", "s_w12", "s_wg0", "s_wg1",
        "s_fst0", "s_fst1", "s_fst2", "s_fst3")


class Bld:
    def __init__(self, nc):
        self.nc = nc
        self.q = {e: [] for e in ENGS}
        self.cnt = {s: 0 for s in SEMS}
        self.consts = set()
        self._cm = {}

    def sb(self, name, shape, dt):
        cm = self.nc.sbuf_tensor(name, list(shape), dt)
        t = cm.__enter__()
        self._cm[name] = cm
        return t

    def free(self, *names):
        order = list(self._cm)
        for n in sorted(names, key=order.index, reverse=True):
            self._cm.pop(n).__exit__(None, None, None)

    def ps(self, name, shape, dt):
        cm = self.nc.psum_tensor(name, list(shape), dt)
        t = cm.__enter__()
        self._cm["psum_" + name] = cm
        return t

    def K(self, v):
        v = float(v)
        self.consts.add(v)
        return v

    def emit(self, eng, fn, waits=(), inc=None, amt=1):
        self.q[eng].append((tuple(w for w in waits if w is not None and w[1] > 0), fn, inc, amt))
        if inc is not None:
            self.cnt[inc] += amt
            return self.cnt[inc]
        return None

    def w(self, sem):
        return (sem, self.cnt[sem])

    def dma(self, qname, out, in_, waits=(), sem=None, **kw):
        if sem is None:
            sem = "spq" if qname == "sync" else "plq"
        self.emit(qname, lambda e, o=out, i=in_, kw=kw: e.dma_start(out=o, in_=i, **kw),
                  waits=waits, inc=sem, amt=16)
        return self.cnt[sem]

    def dmaT(self, out, in_, waits=(), sem="spq"):
        self.emit("sync", lambda e, o=out, i=in_: e.dma_start_transpose(o, i),
                  waits=waits, inc=sem, amt=16)
        return self.cnt[sem]

    def mm(self, outap, lh, rh, start, stop, waits=()):
        return self.emit("tensor", lambda e, o=outap, a=lh, b=rh, s1=start, s2=stop:
                         e.matmul(o, a, b, start=s1, stop=s2), waits=waits, inc="pe")

    def vo(self, fn, waits=(), drain=False):
        if drain:
            self.emit("vector", lambda e: e.drain())
        return self.emit("vector", fn, waits=waits, inc="dve")

    def ao(self, fn, waits=()):
        return self.emit("scalar", fn, waits=waits, inc="act")

    def fence(self, engines, waits):
        for eng in engines:
            self.emit(eng, lambda e: e.nop(), waits=waits)

    def coll(self, kind, op, groups, ins, outs, waits=()):
        return self.emit("gpsimd", lambda e: e.collective_compute(
            kind, op, replica_groups=groups, ins=ins, outs=outs), waits=waits, inc="cc")



def build(prm, debug=False):
    p = prm
    nc = bass.Bass("TRN2", target_bir_lowering=False)
    H, D, I = p.H, p.D, p.I
    TOKC, NQT, NFC, NKT = p.TOKC, p.NQT, p.NFC, p.NKT
    NTT = TOKC // 128
    E8, I8, NIC = p.E8, p.I8, p.NIC
    NC = p.NC
    NSQ = TOKC // SQ
    I8h = I8 // 2
    I2h = I8h // 128
    E2 = E8 // 128
    NICh = NIC // 2
    WSW = max(NFC * 512, NFC * I8h, NICh * E8)
    inv_sqrt_dh = 1.0 / math.sqrt(128.0)
    qk_quarter = 128.0 ** -0.25
    denom_l1 = math.sqrt(float(D)) / float(p.T)
    slopes = _slopes(H)

    din = {}
    for nm, shape, dt in [
        ("x_my", [TOKC, D], F32), ("wqkv_p", [D * 3 * D], F32), ("wo_p", [D * D], F32),
        ("wa_p", [D * 3 * D], F32), ("wp_p", [D * D], F32), ("ident", [128, 128], F32),
        ("w_std", [1, H], F32), ("w_rec", [1, H], F32), ("is_b", [1, 1], F32),
        ("dA", [p.NDT, 128, QTW], BF16), ("dM", [p.NDT, 128, QTW], BF16),
        ("wq1_s", [E8, D], F32), ("wk1_s", [E8, D], F32), ("wv1_s", [E8, D], F32),
        ("wo1_s", [E8, D], F32), ("wg_s", [I8, D], F32), ("wu_s", [I8, D], F32),
        ("wd_s", [E8, I], F32), ("sel", [1 + 2 * H, p.NFCF * 128], F32),
        ("rep8", [H, 128], F32),
    ]:
        din[nm] = nc.dram_tensor(nm, shape, dt, kind="ExternalInput")
    out_my = nc.dram_tensor("out_my", [TOKC, D], F32, kind="ExternalOutput")

    KVW = max(H * TOKC, NTT * D)
    kv_snd = [nc.dram_tensor(f"kv_snd{i}", [2, 128, KVW], BF16) for i in range(2)]
    kv_out = [nc.dram_tensor(f"kv_out{i}", [2, 2, 128, KVW], BF16) for i in range(2)]
    off_snd = nc.dram_tensor("off_snd", [128, NFC], F32)
    off_out = nc.dram_tensor("off_out", [2, 128, NFC], F32)
    ar_snd = nc.dram_tensor("ar_snd", [1, 4], F32)
    ar_out = nc.dram_tensor("ar_out", [1, 4], F32)
    tmp_qkvo = nc.dram_tensor("tmp_qkvo", [4, E8, D], BF16)
    tmp_gu = nc.dram_tensor("tmp_gu", [2, I8, D], BF16)
    tmp_d = nc.dram_tensor("tmp_d", [E8, I], BF16)
    PR_QKVO, PR_GU, PR_D = D * E8, D * I8, I * E8
    PR_ALL = 4 * PR_QKVO + 2 * PR_GU + PR_D
    OFF_M = [0, PR_QKVO, 2 * PR_QKVO, 3 * PR_QKVO, 4 * PR_QKVO,
             4 * PR_QKVO + PR_GU, 4 * PR_QKVO + 2 * PR_GU]
    wT_snd = nc.dram_tensor("wT_snd", [PR_ALL], BF16)
    wT_out = nc.dram_tensor("wT_out", [NC, PR_ALL], BF16, addr_space="Shared")
    sm_dram = nc.dram_tensor("sm_dram", [128, 4], F32)
    dbg_kind = dict(kind="ExternalOutput") if debug else {}
    dbg = [nc.dram_tensor(f"dbg_x{i}", [128, NFC * TOKC], BF16, **dbg_kind) for i in (1, 2, 3)]
    dbg_xq = nc.dram_tensor("dbg_xq", [128, (D // 128) * SQ], BF16, **dbg_kind)
    dbg_gu = nc.dram_tensor("dbg_gu", [128, (I // 128) * SQ], BF16, **dbg_kind)
    dbg_gs = nc.dram_tensor("dbg_gs", [1, 8 * QTW], F32, **dbg_kind)
    dbg_rb = nc.dram_tensor("dbg_rb", [128, SQ], F32, **dbg_kind)
    dbg_cap = nc.dram_tensor("dbg_cap", [128, 4 * SQ], F32, **dbg_kind)

    groups_pair = [[2 * i, 2 * i + 1] for i in range(NC // 2)]
    groups_all = [list(range(NC))]

    bd = Bld(nc)
    K = bd.K
    st = {"ydve": 0, "lnprev": 0, "rb_pe": 0, "ab_pe": 0}

    cpool = bd.sb("cpool", (128, 64), F32)
    cpool_b = bd.sb("cpool_b", (128, 64), BF16)
    ones_b = bd.sb("ones_b", (128, 1), BF16)
    onesr_f = bd.sb("onesr_f", (1, 128), F32)
    ones_f = bd.sb("ones_f", (128, 1), F32)
    ident_b = bd.sb("ident_b", (128, 128), BF16)
    xs = bd.sb("xs", (128, NFC * TOKC), BF16)
    stg = bd.sb("stg", (128, 2 * QTW), F32)
    stg2 = bd.sb("stg2", (128, 2 * QTW), F32)
    pbuf = bd.sb("pbuf", (128, 2 * QTW), BF16)
    hb = bd.sb("hb", (128, NFC * QTW), BF16)
    hsq = bd.sb("hsq", (128, 2 * QTW), BF16)
    rows = bd.sb("rows", (1, 6 * QTW), F32)
    rowm = bd.sb("rowm", (1, 2 * QTW), F32)
    srow = bd.sb("srow", (1, 128), F32)
    small = bd.sb("small", (128, 32), F32)
    evb = bd.sb("evb", (128, 2 * 512), BF16)
    i32s = bd.sb("i32s", (128, 1024), I32)
    csvec = bd.sb("csvec", (128, p.NFCF), F32)
    u128 = bd.sb("u128", (128, 2), F32)
    rbv = bd.sb("rbv", (128, QTW), F32)
    kvsb = bd.sb("kvsb", (128, QTW), F32)
    csc = bd.sb("csc", (128, 3 * SQ), F32)
    dbgsb = bd.sb("dbgsb", (128, 2 * SQ), F32)

    ps0 = bd.ps("ps0", (128, 512), F32)
    ps1 = bd.ps("ps1", (128, 512), F32)
    ps2 = bd.ps("ps2", (128, 512), F32)
    ps3 = bd.ps("ps3", (128, 512), F32)
    ps4 = bd.ps("ps4", (128, 512), F32)
    ps5 = bd.ps("ps5", (128, 512), F32)
    psS = bd.ps("psS", (128, 512), F32)
    psS2 = bd.ps("psS2", (128, 512), F32)
    pp = [ps0, ps1]
    ptr = [ps4, ps5]

    INI = ("ini", 1)

    xtm = bd.sb("xtm", (128, 2 * D), BF16)
    spi = bd.dma("gpsimd", ident_b[:, :], din["ident"][:, :], waits=(INI,), sem="s_in")
    for tt in range(NTT):
        sp = bd.dma("gpsimd", xtm[:, (tt % 2) * D:(tt % 2) * D + D],
                    din["x_my"][tt * 128:(tt + 1) * 128, :], sem=f"s_xt{tt % 2}",
                    waits=(INI,) + ((("act", st.get(("xg", tt - 2), 0)),) if tt >= 2 else ()))
        for fc in range(NFC):
            t = ptr[fc % 2]
            pe_m = bd.mm(t[:, 0:128],
                         xtm[:, (tt % 2) * D + fc * 128:(tt % 2) * D + fc * 128 + 128],
                         ident_b[:, :], True, True,
                         waits=(((f"s_xt{tt % 2}", sp), ("s_in", spi)) if fc == 0 else ()) +
                               (("act", st.get(("xb", fc % 2), 0)),))
            st[("xb", fc % 2)] = bd.ao(
                lambda e, t=t, dst=xs[:, fc * TOKC + tt * 128:fc * TOKC + tt * 128 + 128]:
                e.copy(dst, t[:, 0:128]), waits=(("pe", pe_m),))
        st[("xg", tt)] = bd.cnt["act"]
    bd.free("xtm")
    act_x = bd.cnt["act"]
    bd.fence(("gpsimd", "vector"), (bd.w("pe"),))

    wtmp = bd.sb("wtmp", (128, 2048), BF16)
    wf32 = bd.sb("wf32", (128, 2048), F32)
    racc = bd.sb("racc", (128, 16), F32)
    for mi, nm in enumerate(("wq1_s", "wk1_s", "wv1_s", "wo1_s")):
        for rt in range(E8 // 128):
            c1 = bd.dma("gpsimd", wtmp[:, 0:D], din[nm][rt * 128:(rt + 1) * 128, 0:D],
                        waits=(INI, bd.w("s_pq")))
            bd.dma("gpsimd", tmp_qkvo[mi, rt * 128:(rt + 1) * 128, :], wtmp[:, 0:D],
                   waits=(("plq", c1),), sem="s_pq")
    spq_qkvo = bd.cnt["s_pq"]
    spq_mat = []
    for mi, (nm, rows_n, cols) in enumerate((("wg_s", I8, D), ("wu_s", I8, D), ("wd_s", E8, I))):
        CW = min(2048, cols)
        k = 0
        for rt in range(rows_n // 128):
            for ccc in range(cols // CW):
                c1 = bd.dma("gpsimd", wf32[:, 0:CW],
                            din[nm][rt * 128:(rt + 1) * 128, ccc * CW:(ccc + 1) * CW],
                            waits=(bd.w("dve"),), sem="s_wf")
                if k < 15:
                    bd.vo(lambda e, k=k, cw=CW: e.reduce_sum(racc[:, k:k + 1], wf32[:, 0:cw],
                                                             axis=mybir.AxisListType.X,
                                                             apply_absolute_value=True),
                          waits=(("s_wf", c1),))
                else:
                    bd.vo(lambda e, cw=CW: e.reduce_sum(small[:, 0:1], wf32[:, 0:cw],
                                                        axis=mybir.AxisListType.X,
                                                        apply_absolute_value=True),
                          waits=(("s_wf", c1),))
                    bd.vo(lambda e, k=k: e.tensor_tensor(racc[:, k % 15:k % 15 + 1],
                                                         racc[:, k % 15:k % 15 + 1],
                                                         small[:, 0:1], OP.add), drain=True)
                k += 1
        nacc = min(k, 15)
        dvm = bd.vo(lambda e, nacc=nacc: e.reduce_sum(racc[:, 15:16], racc[:, 0:nacc],
                                                      axis=mybir.AxisListType.X),
                    waits=(("pe", st.get("pqs_pe", 0)),), drain=True)
        pe_m = bd.mm(psS[0:1, mi:mi + 1], ones_f[:, :], racc[:, 15:16], True, True,
                     waits=(("dve", dvm),))
        st["pqs_pe"] = pe_m
        bd.ao(lambda e, mi=mi: e.copy(srow[0:1, mi:mi + 1], psS[0:1, mi:mi + 1]),
              waits=(("pe", pe_m),))
    act_part = bd.cnt["act"]
    c1 = bd.dma("gpsimd", ar_snd[0:1, 0:4], srow[0:1, 0:4], waits=(("act", act_part),))
    cc_ar = bd.coll("AllReduce", OP.add, groups_all, [ar_snd.ap().opt()], [ar_out.ap().opt()],
                    waits=(("plq", c1),))
    c2 = bd.dma("gpsimd", srow[0:1, 4:8], ar_out[0:1, 0:4], waits=(("cc", cc_ar),), sem="s_msc")
    K(Q_EPS)
    dmean = K(1.0 / (I * D))
    bd.vo(lambda e: e.tensor_scalar(srow[0:1, 8:11], srow[0:1, 4:7], dmean, Q_EPS,
                                    OP.mult, OP.add), waits=(("s_msc", c2),))
    d2 = bd.vo(lambda e: e.reciprocal(srow[0:1, 12:15], srow[0:1, 8:11]), drain=True)
    pe_rs = None
    for mi in range(3):
        pe_rs = bd.mm(psS2[:, mi:mi + 1], onesr_f[0:1, :], srow[0:1, 12 + mi:13 + mi],
                      mi == 0, mi == 2, waits=(("dve", d2),) if mi == 0 else ())
    a_rs = bd.ao(lambda e: e.copy(small[:, 8:11], psS2[:, 0:3]), waits=(("pe", pe_rs),))
    K(1.0); K(-1.0)
    for mi, (nm, rows_n, cols) in enumerate((("wg_s", I8, D), ("wu_s", I8, D), ("wd_s", E8, I))):
        CW = min(2048, cols)
        for rt in range(rows_n // 128):
            for ccc in range(cols // CW):
                c1 = bd.dma("gpsimd", wf32[:, 0:CW],
                            din[nm][rt * 128:(rt + 1) * 128, ccc * CW:(ccc + 1) * CW],
                            waits=(bd.w("dve"), bd.w("plq")), sem="s_wf")
                d3 = None
                spq_prev = bd.w("s_pq")
                for pc in range(CW // 1024):
                    bd.vo(lambda e, mi=mi, pc=pc: e.tensor_scalar(i32s[:, 0:1024],
                                                                  wf32[:, pc * 1024:(pc + 1) * 1024],
                                                                  small[:, 8 + mi:9 + mi], None, OP.mult),
                          waits=(("s_wf", c1), ("act", a_rs), spq_prev), drain=pc > 0)
                    d3 = bd.vo(lambda e, pc=pc: e.tensor_scalar(wtmp[:, pc * 1024:(pc + 1) * 1024],
                                                                i32s[:, 0:1024], 1.0, -1.0,
                                                                OP.min, OP.max), drain=True)
                if mi < 2:
                    bd.dma("gpsimd", tmp_gu[mi, rt * 128:(rt + 1) * 128, ccc * CW:(ccc + 1) * CW],
                           wtmp[:, 0:CW], waits=(("dve", d3),), sem="s_pq")
                else:
                    bd.dma("gpsimd", tmp_d[rt * 128:(rt + 1) * 128, ccc * CW:(ccc + 1) * CW],
                           wtmp[:, 0:CW], waits=(("dve", d3),), sem="s_pq")
        spq_mat.append(bd.cnt["s_pq"])
    plq_tmp = bd.cnt["s_pq"]
    wstg = bd.sb("wstg", (128, max(NFC * E8, NFC * I8, NIC * E8)), BF16)
    for mi in range(4):
        for dc in range(NFC):
            bd.dmaT(wstg[:, dc * E8:(dc + 1) * E8], tmp_qkvo[mi, 0:E8, dc * 128:(dc + 1) * 128],
                    waits=((("s_pq", spq_qkvo),) if (mi == 0 and dc == 0) else ()) +
                          ((bd.w("spq"), bd.w("s_pq2")) if dc == 0 else ()))
        bd.dma("sync", wT_snd[OFF_M[mi]:OFF_M[mi] + PR_QKVO].rearrange("(pp c e) -> pp c e", pp=128, c=NFC),
               wstg[:, 0:NFC * E8].rearrange("pp (c e) -> pp c e", c=NFC), waits=(bd.w("spq"),), sem="s_pq2")
    for gi in range(2):
        for dc in range(NFC):
            bd.dmaT(wstg[:, dc * I8:(dc + 1) * I8], tmp_gu[gi, 0:I8, dc * 128:(dc + 1) * 128],
                    waits=((("s_pq", spq_mat[gi]), bd.w("spq"), bd.w("s_pq2"))
                           if dc == 0 else ()))
        for ghf in range(2):
            bd.dma("sync",
                   wT_snd[OFF_M[4 + gi] + ghf * (PR_GU // 2):OFF_M[4 + gi] + (ghf + 1) * (PR_GU // 2)]
                   .rearrange("(pp c e) -> pp c e", pp=128, c=NFC),
                   wstg[:, 0:NFC * I8].rearrange("pp (c e) -> pp c e", c=NFC)
                   [:, :, ghf * I8h:(ghf + 1) * I8h],
                   waits=(bd.w("spq"),), sem="s_pq2")
    for ic in range(NIC):
        bd.dmaT(wstg[:, ic * E8:(ic + 1) * E8], tmp_d[0:E8, ic * 128:(ic + 1) * 128],
                waits=((("s_pq", spq_mat[2]), bd.w("spq"), bd.w("s_pq2"))
                       if ic == 0 else ()))
    sp_wt = bd.dma("sync", wT_snd[OFF_M[6]:OFF_M[6] + PR_D].rearrange("(pp c e) -> pp c e", pp=128, c=NIC),
                   wstg[:, 0:NIC * E8].rearrange("pp (c e) -> pp c e", c=NIC), waits=(bd.w("spq"),), sem="s_pq2")
    bd.dma("sync", srow[0:1, 16:16 + H], din["w_std"][:, :], waits=(INI,), sem="s_ins")
    bd.dma("sync", srow[0:1, 16 + H:16 + 2 * H], din["w_rec"][:, :], sem="s_ins")
    nsv = 1 + 2 * H
    sel_sb = bd.sb("sel_sb", (nsv, p.NFCF * 128), F32)
    rep_sb = bd.sb("rep_sb", (H, 128), F32)
    bd.dma("sync", sel_sb[:, :], din["sel"][:, :], sem="s_ins")
    bd.dma("sync", rep_sb[:, :], din["rep8"][:, :], sem="s_ins")
    sp_isb = bd.dma("sync", srow[0:1, 64:65], din["is_b"][:, :], sem="s_ins")
    K(1e-8); K(0.1); K(qk_quarter)
    du = bd.vo(lambda e: e.tensor_scalar(srow[0:1, 80:80 + H], srow[0:1, 16 + H:16 + 2 * H],
                                         0.1, None, OP.is_gt), waits=(("s_ins", sp_isb),))
    dstd = bd.vo(lambda e: e.tensor_scalar(srow[0:1, 16:16 + 2 * H], srow[0:1, 16:16 + 2 * H],
                                           1e-8, None, OP.max), drain=True)
    a_sq = bd.ao(lambda e: e.activation(srow[0:1, 16:16 + 2 * H], srow[0:1, 16:16 + 2 * H], AF.Sqrt),
                 waits=(("dve", dstd),))
    pe_sv = bd.mm(ps3[0:nsv, 0:1], srow[0:1, 15:16 + 2 * H], onesr_f[0:1, 0:1], True, True,
                  waits=(("act", a_sq), ("s_ins", sp_isb)))
    a_sv = bd.ao(lambda e: e.copy(small[0:nsv, 14:15], ps3[0:nsv, 0:1]), waits=(("pe", pe_sv),))
    pe_cs = None
    for c in range(p.NFCF):
        pe_cs = bd.mm(psS[:, c:c + 1], sel_sb[0:nsv, c * 128:(c + 1) * 128], small[0:nsv, 14:15],
                      True, True, waits=(("act", a_sv),) if c == 0 else ())
    a_cs = bd.ao(lambda e: e.copy(csvec[:, 0:p.NFCF], psS[:, 0:p.NFCF]), waits=(("pe", pe_cs),))
    pe_uc = bd.mm(ps3[0:H, 0:1], srow[0:1, 80:80 + H], onesr_f[0:1, 0:1], True, True,
                  waits=(("dve", du), ("act", a_sv)))
    a_uc = bd.ao(lambda e: e.copy(small[0:H, 15:16], ps3[0:H, 0:1]), waits=(("pe", pe_uc),))
    pe_u1 = bd.mm(psS2[:, 4:5], rep_sb[0:H, :], small[0:H, 15:16], True, True,
                  waits=(("act", a_uc),))
    a_u1 = bd.ao(lambda e: e.copy(u128[:, 0:1], psS2[:, 4:5]), waits=(("pe", pe_u1),))
    d_1mu = bd.vo(lambda e: e.tensor_scalar(u128[0:8 * H, 1:2], u128[0:8 * H, 0:1], -1.0, 1.0,
                                            OP.mult, OP.add), waits=(("act", a_u1),))
    pe_ib = bd.mm(psS2[:, 3:4], onesr_f[0:1, :], srow[0:1, 64:65], True, True,
                  waits=(("s_ins", sp_isb), ("act", a_rs)))
    a_ib = bd.ao(lambda e: e.copy(small[:, 12:13], psS2[:, 3:4]), waits=(("pe", pe_ib),))
    csvec_rdy = ("act", a_cs)
    plq_cs = None
    bd.free("wstg", "sel_sb", "rep_sb", "racc", "wf32", "wtmp")
    bd.fence(("gpsimd", "sync", "vector", "scalar"),
             (bd.w("pe"), bd.w("dve"), bd.w("s_pq"), bd.w("s_pq2"), bd.w("s_in")))

    def slab_offsets(all_slabs):
        offs, o = {}, 0
        for (a, b) in all_slabs:
            offs[(a, b)] = o
            o += D * (b - a)
        return offs

    offs_qkv = slab_offsets(p.slabs_qkv_q + p.slabs_qkv_k + p.slabs_qkv_v)
    offs_wa = slab_offsets(p.slabs_wa_q + p.slabs_wa_k + p.slabs_wa_v + p.slabs_wa_t)
    offs_d = slab_offsets(p.slabs_d)

    wsA = bd.sb("wsA", (128, WSW), BF16)
    wsB = bd.sb("wsB", (128, WSW), BF16)
    dsb = bd.sb("dsb", (128, max(p.ktmax) * QTW), BF16)
    kst = bd.sb("kst", (128, 2 * NKT * 128), BF16)
    vst = bd.sb("vst", (128, 2 * NKT * 128), BF16)
    yall = bd.sb("yall", (128, H * QTW), BF16)
    qall = bd.sb("qall", (128, H * QTW), BF16)

    wg_guard = [0, 0]
    slab_i = [0]

    def load_slab(perm, offmap, slab, extra=()):
        i = slab_i[0] % 2
        slab_i[0] += 1
        buf = (wsA, wsB)[i]
        wdt = slab[1] - slab[0]
        c = bd.dma("gpsimd", buf[:, 0:NFC * wdt].rearrange("pp (c e) -> pp c e", c=NFC),
                   perm[offmap[slab]:offmap[slab] + D * wdt]
                   .rearrange("(pp c e) -> pp c e", pp=128, c=NFC),
                   waits=(("pe", wg_guard[i]),) + tuple(extra), sem=f"s_ws{i}")
        return buf, wdt, (f"s_ws{i}", c), i

    def done_slab(i):
        wg_guard[i] = bd.cnt["pe"]

    K(inv_sqrt_dh); K(-B_EXP); K(1.0); K(0.0)
    for s in slopes:
        K(s)
    dt_off = []
    o = 0
    for qt in range(NQT):
        dt_off.append(o)
        o += p.ktmax[qt]

    def scores_qt(layer, qt, cc_kv, kv_gout, use_alibi, sp_dt, qrdy=()):
        nkt = p.ktmax[qt]
        for h in range(H):
            kbase = (h % 2) * NKT * 128
            guard = st.get(("kvg", h % 2), 0)
            spkv = None
            for r in range(2):
                lo, hi = max(0, r * NTT), min(nkt, (r + 1) * NTT)
                if hi <= lo:
                    continue
                bd.dma("sync", kst[:, kbase + lo * 128:kbase + hi * 128],
                       kv_gout[r, 0, :, h * TOKC + (lo - r * NTT) * 128:h * TOKC + (hi - r * NTT) * 128],
                       waits=(("cc", cc_kv), ("pe", guard)), sem=f"s_kv{h % 2}")
                spkv = bd.dma("sync", vst[:, kbase + lo * 128:kbase + hi * 128]
                              .rearrange("pp (t c) -> pp t c", c=128),
                              kv_gout[r, 1, :, :].rearrange("pp (t f) -> pp t f", f=D)
                              [:, (lo - r * NTT):(hi - r * NTT), h * 128:(h + 1) * 128],
                              sem=f"s_kv{h % 2}")
            for kt in range(nkt):
                sb = pp[kt % 2]
                pe_s = bd.mm(sb[:, 0:QTW], kst[:, kbase + kt * 128:kbase + (kt + 1) * 128],
                             qall[:, h * QTW:(h + 1) * QTW], True, True,
                             waits=(((f"s_kv{h % 2}", spkv),) + tuple(qrdy) if kt == 0 else ()) +
                                   (st.get(("bk", kt % 2), ("dve", 0)),))
                slot = stg[:, (kt % 2) * QTW:(kt % 2) * QTW + QTW]
                sc = slopes[h] if use_alibi else 1.0
                dv_s = bd.vo(lambda e, sl=slot, db=dsb[:, kt * QTW:(kt + 1) * QTW], sb_=sb, sc=sc:
                             e.scalar_tensor_tensor(sl, db, sc, sb_[:, 0:QTW], OP.mult, OP.add),
                             waits=(("pe", pe_s),) + ((sp_dt,) if kt == 0 else ()))
                st[("bk", kt % 2)] = ("dve", dv_s)
                pslot = pbuf[:, (kt % 2) * QTW:(kt % 2) * QTW + QTW]
                ac_p = bd.ao(lambda e, pl=pslot, sl=slot: e.activation(pl, sl, AF.Exp,
                                                                       bias=-B_EXP, scale=1.0),
                             waits=(("dve", dv_s),))
                bd.mm(psS[0:1, 0:QTW], ones_b[:, :], pslot, kt == 0, kt == nkt - 1,
                      waits=(("act", ac_p),) + ((("dve", st["ydve"]),) if kt == 0 else ()))
                bd.mm(ps2[:, 0:QTW], vst[:, kbase + kt * 128:kbase + (kt + 1) * 128], pslot,
                      kt == 0, kt == nkt - 1)
            st[("kvg", h % 2)] = bd.cnt["pe"]
            pe_fin = bd.cnt["pe"]
            dv_r = bd.vo(lambda e: e.reciprocal(rowm[0:1, 0:QTW], psS[0:1, 0:QTW]),
                         waits=(("pe", pe_fin), ("pe", st["rb_pe"])))
            st["rb_pe"] = bd.mm(ps3[:, 0:QTW], onesr_f[0:1, :], rowm[0:1, 0:QTW], True, True,
                                waits=(("dve", dv_r), ("dve", st["ydve"])))
            a_rb = bd.ao(lambda e: e.copy(rbv[:, 0:QTW], ps3[:, 0:QTW]),
                         waits=(("pe", st["rb_pe"]),))
            st["ydve"] = bd.vo(lambda e, dst=yall[:, h * QTW:(h + 1) * QTW]:
                               e.tensor_tensor(dst, ps2[:, 0:QTW], rbv[:, 0:QTW], OP.mult),
                               waits=(("act", a_rb),))

    def layer_norm(toff, w):
        K(1.0 / D); K(LN_EPS)
        for fc in range(NFC):
            a_sq2 = bd.ao(lambda e, fc=fc, sl=hsq[:, (fc % 2) * QTW:(fc % 2) * QTW + w]:
                          e.activation(sl, hb[:, fc * QTW:fc * QTW + w], AF.Square),
                          waits=(("dve", st.get(("og", fc % 2), 0)), ("pe", st.get(("lq", fc % 2), 0))))
            bd.mm(psS[0:1, 0:w], ones_b[:, :], hb[:, fc * QTW:fc * QTW + w], fc == 0, fc == NFC - 1,
                  waits=(("dve", st["lnprev"]), ("dve", st.get(("og", 0), 0)),
                         ("dve", st.get(("og", 1), 0))) if fc == 0 else ())
            st[("lq", fc % 2)] = bd.mm(psS2[0:1, 0:w], ones_b[:, :],
                                       hsq[:, (fc % 2) * QTW:(fc % 2) * QTW + w],
                                       fc == 0, fc == NFC - 1, waits=(("act", a_sq2),))
        pe_sums = bd.cnt["pe"]
        bd.vo(lambda e: e.tensor_scalar(rows[0:1, 0:w], psS[0:1, 0:w], 1.0 / D, None, OP.mult),
              waits=(("pe", pe_sums),))
        bd.vo(lambda e: e.tensor_scalar(rows[0:1, QTW:QTW + w], psS2[0:1, 0:w], 1.0 / D, None, OP.mult))
        bd.vo(lambda e: e.tensor_tensor(rows[0:1, 2 * QTW:2 * QTW + w], rows[0:1, 0:w],
                                        rows[0:1, 0:w], OP.mult), drain=True)
        d4 = bd.vo(lambda e: e.scalar_tensor_tensor(rows[0:1, QTW:QTW + w], rows[0:1, QTW:QTW + w],
                                                    LN_EPS, rows[0:1, 2 * QTW:2 * QTW + w],
                                                    OP.add, OP.subtract), drain=True)
        a1 = bd.ao(lambda e: e.activation(rows[0:1, QTW:QTW + w], rows[0:1, QTW:QTW + w], AF.Sqrt),
                   waits=(("dve", d4),))
        bd.vo(lambda e: e.reciprocal(rowm[0:1, 0:w], rows[0:1, QTW:QTW + w]),
              waits=(("act", a1), ("pe", st["ab_pe"]), ("pe", st["rb_pe"])))
        d6 = bd.vo(lambda e: e.scalar_tensor_tensor(rowm[0:1, QTW:QTW + w], rows[0:1, 0:w],
                                                    -1.0, rowm[0:1, 0:w], OP.mult, OP.mult),
                   drain=True)
        pe_a = bd.mm(ps3[:, 0:w], onesr_f[0:1, :], rowm[0:1, 0:w], True, True,
                     waits=(("dve", d6), ("dve", st["ydve"])))
        pe_b = bd.mm(psS[:, 0:w], onesr_f[0:1, :], rowm[0:1, QTW:QTW + w], True, True)
        st["ab_pe"] = pe_b
        st["rb_pe"] = pe_b
        for fc in range(NFC):
            sl = stg2[:, (fc % 2) * QTW:(fc % 2) * QTW + w]
            bd.vo(lambda e, sl=sl, fc=fc: e.tensor_tensor(sl, hb[:, fc * QTW:fc * QTW + w],
                                                          ps3[:, 0:w], OP.mult),
                  waits=(("pe", pe_b),) if fc == 0 else (), drain=fc >= 2)
            st["lnprev"] = bd.vo(
                lambda e, sl=sl, dst=xs[:, fc * TOKC + toff:fc * TOKC + toff + w]:
                e.tensor_tensor(dst, sl, psS[:, 0:w], OP.add), drain=True)

    def out_proj_ln(qt, perm, offmap, slabs, res_scale):
        dve_y = st["ydve"]
        for slab in slabs:
            buf, wdt, cld, bi = load_slab(perm, offmap, slab)
            for fcl in range(wdt // 128):
                fc = (slab[0] + fcl * 128) // 128
                sb = pp[fc % 2]
                for hc in range(H):
                    bd.mm(sb[:, 0:QTW], buf[:, hc * wdt + fcl * 128:hc * wdt + fcl * 128 + 128],
                          yall[:, hc * QTW:(hc + 1) * QTW], hc == 0, hc == H - 1,
                          waits=((cld, ("dve", dve_y),
                                  st.get(("bk", fc % 2), ("dve", 0))) if hc == 0 else ()))
                pe_m = bd.cnt["pe"]
                dog = bd.vo(
                    lambda e, sb=sb, fc=fc,
                    xsr=xs[:, fc * TOKC + qt * QTW:fc * TOKC + qt * QTW + QTW],
                    dst=hb[:, fc * QTW:(fc + 1) * QTW], rs=res_scale:
                    e.scalar_tensor_tensor(dst, sb[:, 0:QTW], rs, xsr, OP.mult, OP.add),
                    waits=(("pe", pe_m),))
                st[("og", fc % 2)] = dog
                st[("bk", fc % 2)] = ("dve", dog)
            done_slab(bi)
        layer_norm(qt * QTW, QTW)

    for slab in p.slabs_qkv_k:
        buf, wdt, cld, bi = load_slab(din["wqkv_p"], offs_qkv, slab, extra=(("act", act_x),))
        for hc in range(wdt // 128):
            h = (slab[0] - D + hc * 128) // 128
            for tg in range(NQT):
                m_ = (hc * NQT + tg) % 2
                sb = pp[m_]
                for dc in range(NFC):
                    bd.mm(sb[:, 0:QTW], buf[:, dc * wdt + hc * 128:dc * wdt + hc * 128 + 128],
                          xs[:, dc * TOKC + tg * QTW:dc * TOKC + tg * QTW + QTW],
                          dc == 0, dc == NFC - 1,
                          waits=((cld, (f"s_evb{m_}", st.get(("evb", m_), 0))) if dc == 0 else ()))
                pe_m = bd.cnt["pe"]
                a_e = bd.ao(lambda e, sb=sb, m_=m_: e.copy(evb[:, m_ * 512:m_ * 512 + QTW],
                                                           sb[:, 0:QTW]), waits=(("pe", pe_m),))
                st[("evb", m_)] = bd.dma(
                    "gpsimd", kv_snd[0][0, :, h * TOKC + tg * QTW:h * TOKC + (tg + 1) * QTW],
                    evb[:, m_ * 512:m_ * 512 + QTW], waits=(("act", a_e),),
                    sem=f"s_evb{m_}")
        done_slab(bi)
    for slab in p.slabs_qkv_v:
        buf, wdt, cld, bi = load_slab(din["wqkv_p"], offs_qkv, slab)
        for tt in range(NTT):
            for wo_ in range(0, wdt, 512):
                ww = min(512, wdt - wo_)
                sb = pp[tt % 2]
                for dc in range(NFC):
                    bd.mm(sb[:, 0:ww], xs[:, dc * TOKC + tt * 128:dc * TOKC + tt * 128 + 128],
                          buf[:, dc * wdt + wo_:dc * wdt + wo_ + ww], dc == 0, dc == NFC - 1,
                          waits=((cld, (f"s_evb{tt % 2}", st.get(("evb", tt % 2), 0)),
                                  st.get(("bk", tt % 2), ("dve", 0))) if dc == 0 else ()))
                pe_m = bd.cnt["pe"]
                a_e = bd.ao(lambda e, sb=sb, tt=tt, ww=ww:
                            e.copy(evb[:, (tt % 2) * 512:(tt % 2) * 512 + ww], sb[:, 0:ww]),
                            waits=(("pe", pe_m),))
                st[("bk", tt % 2)] = ("act", a_e)
                st[("evb", tt % 2)] = bd.dma(
                    "gpsimd", kv_snd[0][1, :, tt * D + (slab[0] - 2 * D) + wo_:
                                       tt * D + (slab[0] - 2 * D) + wo_ + ww],
                    evb[:, (tt % 2) * 512:(tt % 2) * 512 + ww], waits=(("act", a_e),),
                    sem=f"s_evb{tt % 2}")
        done_slab(bi)
    cc_kv0 = bd.coll("AllGather", OP.bypass, groups_pair,
                     [kv_snd[0].ap().opt()], [kv_out[0].ap().opt()],
                     waits=(bd.w("s_evb0"), bd.w("s_evb1")))
    cc_wT = bd.coll("AllGather", OP.bypass, groups_all, [wT_snd.ap().opt()], [wT_out.ap().opt()],
                    waits=(("s_pq2", sp_wt),))
    for qt in range(NQT):
        sp_dt = ("s_dsb", bd.dma("sync", dsb[:, 0:p.ktmax[qt] * QTW].rearrange("pp (n w) -> pp n w", w=QTW),
                                 din["dA"][dt_off[qt]:dt_off[qt] + p.ktmax[qt], :, :]
                                 .rearrange("n pp w -> pp n w"), waits=(bd.w("dve"),), sem="s_dsb"))
        for slab in p.slabs_qkv_q:
            buf, wdt, cld, bi = load_slab(din["wqkv_p"], offs_qkv, slab)
            for hc in range(wdt // 128):
                h = (slab[0] + hc * 128) // 128
                sb = pp[hc % 2]
                for dc in range(NFC):
                    bd.mm(sb[:, 0:QTW], buf[:, dc * wdt + hc * 128:dc * wdt + hc * 128 + 128],
                          xs[:, dc * TOKC + qt * QTW:dc * TOKC + qt * QTW + QTW],
                          dc == 0, dc == NFC - 1,
                          waits=((cld, st.get(("bk", hc % 2), ("dve", 0))) if dc == 0 else ()))
                pe_m = bd.cnt["pe"]
                a_q = bd.ao(
                    lambda e, sb=sb, h=h: e.activation(qall[:, h * QTW:(h + 1) * QTW], sb[:, 0:QTW],
                                                       AF.Copy, bias=0.0, scale=inv_sqrt_dh),
                    waits=(("pe", pe_m),))
                st[("bk", hc % 2)] = ("act", a_q)
            done_slab(bi)
        scores_qt(0, qt, cc_kv0, kv_out[0], True, sp_dt, qrdy=(bd.w("act"),))
        out_proj_ln(qt, din["wo_p"], offs_d, p.slabs_d, 1.0)
    st["xrdy"] = bd.cnt["dve"]
    if debug:
        bd.dma("gpsimd", dbg[0][:, :], xs[:, :], waits=(("dve", st["xrdy"]),), sem="s_msc")

    fst = bd.sb("fst", (128, 4 * QTW), BF16)
    NTC = 16 * H // 128
    tls = bd.sb("tls", (128, max(1, NTC) * TOKC), BF16)
    tb = bd.sb("tb", (128, 4 * TOKC), BF16)

    def head_pieces(base):
        out = {}
        for h in range(H):
            c0, c1 = base + h * p.DSTD, base + (h + 1) * p.DSTD
            ps_ = []
            c = c0
            while c < c1:
                nxt = min(c1, (c // 128 + 1) * 128)
                ps_.append((c, nxt - c))
                c = nxt
            out[h] = ps_
        return out

    qpieces = head_pieces(0)
    kpieces = head_pieces(p.qs_end)
    slotc = [0]

    def fused_chunk(buf, wdt, slab0, cld, gcol0, tg, slot):
        cc = gcol0 // 128
        sb = pp[slot % 2]
        for dc in range(NFC):
            bd.mm(sb[:, 0:QTW], buf[:, dc * wdt + (gcol0 - slab0):dc * wdt + (gcol0 - slab0) + 128],
                  xs[:, dc * TOKC + tg * QTW:dc * TOKC + tg * QTW + QTW],
                  dc == 0, dc == NFC - 1,
                  waits=((cld, (f"s_fst{slot}", st.get(("fe", slot), 0)),
                          csvec_rdy, ("dve", st.get("xrdy", 0)),
                          st.get(("bk", slot % 2), ("dve", 0))) if dc == 0 else ()))
        pe_m = bd.cnt["pe"]
        dv = bd.vo(lambda e, sb=sb, cc=cc, sl=fst[:, slot * QTW:(slot + 1) * QTW]:
                   e.tensor_scalar(sl, sb[:, 0:QTW], csvec[:, cc:cc + 1], None, OP.mult),
                   waits=(("pe", pe_m),))
        st[("bk", slot % 2)] = ("dve", dv)
        return dv

    for slabs, which in ((p.slabs_wa_k, "k"), (p.slabs_wa_t, "t")):
        for slab in slabs:
            buf, wdt, cld, bi = load_slab(din["wa_p"], offs_wa, slab)
            for gcol0 in range(slab[0], slab[1], 128):
                for tg in range(NQT):
                    slot = slotc[0] % 4
                    slotc[0] += 1
                    dv = fused_chunk(buf, wdt, slab[0], cld, gcol0, tg, slot)
                    if which == "k":
                        for h in range(H):
                            for (c0, ln) in kpieces[h]:
                                if c0 // 128 != gcol0 // 128:
                                    continue
                                po = c0 % 128
                                ro = c0 - (p.qs_end + h * p.DSTD)
                                st[("fe", slot)] = bd.dma(
                                    "gpsimd",
                                    kv_snd[1][0, ro:ro + ln, h * TOKC + tg * QTW:h * TOKC + (tg + 1) * QTW],
                                    fst[po:po + ln, slot * QTW:(slot + 1) * QTW],
                                    waits=(("dve", dv),), sem=f"s_fst{slot}")
                    else:
                        ci = (gcol0 - p.v_end) // 128
                        st[("fe", slot)] = bd.dma(
                            "gpsimd", tls[:, ci * TOKC + tg * QTW:ci * TOKC + (tg + 1) * QTW],
                            fst[:, slot * QTW:(slot + 1) * QTW], waits=(("dve", dv),), sem=f"s_fst{slot}")
            done_slab(bi)
    nt8 = 8 * H
    khi_ci, khi_po = (p.ql_end - p.v_end) // 128, (p.ql_end - p.v_end) % 128
    c5 = bd.dma("gpsimd", tb[0:nt8, 0:TOKC], tls[0:nt8, 0:TOKC],
                waits=(bd.w("s_fst0"), bd.w("s_fst1"), bd.w("s_fst2"), bd.w("s_fst3")), sem="s_msc")
    c6 = bd.dma("gpsimd", tb[0:nt8, TOKC:2 * TOKC],
                tls[khi_po:khi_po + nt8, khi_ci * TOKC:(khi_ci + 1) * TOKC], sem="s_msc")
    dvq = None
    for tg in range(NQT):
        a_, b_ = tg * QTW, tg * QTW + QTW
        bd.vo(lambda e, a_=a_, b_=b_: e.tensor_scalar(stg[0:nt8, 0:QTW], tb[0:nt8, a_:b_],
                                                      u128[0:nt8, 1:2], None, OP.mult),
              waits=(("s_msc", c6), ("dve", d_1mu)), drain=True)
        bd.vo(lambda e, a_=a_, b_=b_: e.tensor_scalar(stg2[0:nt8, 0:QTW], tb[0:nt8, TOKC + a_:TOKC + b_],
                                                      u128[0:nt8, 0:1], None, OP.mult))
        dvq = bd.vo(lambda e, a_=a_, b_=b_: e.tensor_tensor(tb[0:nt8, 2 * TOKC + a_:2 * TOKC + b_],
                                                            stg[0:nt8, 0:QTW], stg2[0:nt8, 0:QTW], OP.add),
                    drain=True)
        bd.vo(lambda e, a_=a_, b_=b_: e.tensor_scalar(stg[0:nt8, 0:QTW], tb[0:nt8, a_:b_],
                                                      u128[0:nt8, 0:1], None, OP.mult), drain=True)
        bd.vo(lambda e, a_=a_, b_=b_: e.tensor_scalar(stg2[0:nt8, 0:QTW], tb[0:nt8, TOKC + a_:TOKC + b_],
                                                      u128[0:nt8, 1:2], None, OP.mult))
        dvq = bd.vo(lambda e, a_=a_, b_=b_: e.tensor_tensor(tb[0:nt8, 3 * TOKC + a_:3 * TOKC + b_],
                                                            stg[0:nt8, 0:QTW], stg2[0:nt8, 0:QTW], OP.add),
                    drain=True)
    for h in range(H):
        bd.dma("gpsimd", kv_snd[1][0, 120:128, h * TOKC:(h + 1) * TOKC],
               tb[8 * h:8 * h + 8, 3 * TOKC:4 * TOKC], waits=(("dve", dvq),), sem="s_tl")
    for slab in p.slabs_wa_v:
        buf, wdt, cld, bi = load_slab(din["wa_p"], offs_wa, slab)
        for tt in range(NTT):
            for wo_ in range(0, wdt, 512):
                ww = min(512, wdt - wo_)
                sb = pp[tt % 2]
                for dc in range(NFC):
                    bd.mm(sb[:, 0:ww], xs[:, dc * TOKC + tt * 128:dc * TOKC + tt * 128 + 128],
                          buf[:, dc * wdt + wo_:dc * wdt + wo_ + ww], dc == 0, dc == NFC - 1,
                          waits=((cld, (f"s_evb{tt % 2}", st.get(("evb", tt % 2), 0)),
                                  ("dve", st.get("xrdy", 0)),
                                  st.get(("bk", tt % 2), ("dve", 0))) if dc == 0 else ()))
                pe_m = bd.cnt["pe"]
                a_e = bd.ao(lambda e, sb=sb, tt=tt, ww=ww:
                            e.copy(evb[:, (tt % 2) * 512:(tt % 2) * 512 + ww], sb[:, 0:ww]),
                            waits=(("pe", pe_m),))
                st[("bk", tt % 2)] = ("act", a_e)
                st[("evb", tt % 2)] = bd.dma(
                    "gpsimd", kv_snd[1][1, :, tt * D + (slab[0] - p.ks_end) + wo_:
                                       tt * D + (slab[0] - p.ks_end) + wo_ + ww],
                    evb[:, (tt % 2) * 512:(tt % 2) * 512 + ww], waits=(("act", a_e),),
                    sem=f"s_evb{tt % 2}")
        done_slab(bi)
    cc_kv1 = bd.coll("AllGather", OP.bypass, groups_pair,
                     [kv_snd[1].ap().opt()], [kv_out[1].ap().opt()],
                     waits=(bd.w("s_evb0"), bd.w("s_evb1"), bd.w("s_fst0"), bd.w("s_fst1"),
                            bd.w("s_fst2"), bd.w("s_fst3"), bd.w("s_tl")))
    for qt in range(NQT):
        sp_dt = ("s_dsb", bd.dma("sync", dsb[:, 0:p.ktmax[qt] * QTW].rearrange("pp (n w) -> pp n w", w=QTW),
                                 din["dM"][dt_off[qt]:dt_off[qt] + p.ktmax[qt], :, :]
                                 .rearrange("n pp w -> pp n w"), waits=(bd.w("dve"),), sem="s_dsb"))
        for slab in p.slabs_wa_q:
            buf, wdt, cld, bi = load_slab(din["wa_p"], offs_wa, slab)
            for gcol0 in range(slab[0], slab[1], 128):
                slot = slotc[0] % 4
                slotc[0] += 1
                dv = fused_chunk(buf, wdt, slab[0], cld, gcol0, qt, slot)
                for h in range(H):
                    for (c0, ln) in qpieces[h]:
                        if c0 // 128 != gcol0 // 128:
                            continue
                        po = c0 % 128
                        ro = c0 - h * p.DSTD
                        st[("fe", slot)] = bd.dma(
                            "gpsimd", qall[ro:ro + ln, h * QTW:(h + 1) * QTW],
                            fst[po:po + ln, slot * QTW:(slot + 1) * QTW], waits=(("dve", dv),),
                            sem=f"s_fst{slot}")
            done_slab(bi)
        for h in range(H):
            bd.dma("gpsimd", qall[120:128, h * QTW:(h + 1) * QTW],
                   tb[8 * h:8 * h + 8, 2 * TOKC + qt * QTW:2 * TOKC + (qt + 1) * QTW],
                   waits=(("dve", dvq),), sem="s_tl")
        scores_qt(1, qt, cc_kv1, kv_out[1], False, sp_dt,
                  qrdy=(bd.w("s_fst0"), bd.w("s_fst1"), bd.w("s_fst2"), bd.w("s_fst3"), bd.w("s_tl")))
        out_proj_ln(qt, din["wp_p"], offs_d, p.slabs_d, 1.0)
    st["xrdy"] = bd.cnt["dve"]
    if debug:
        bd.dma("gpsimd", dbg[1][:, :], xs[:, :], waits=(("dve", st["xrdy"]),), sem="s_msc")
    bd.free("fst", "tls", "tb", "wsA", "wsB", "dsb", "kst", "vst", "yall", "qall")
    bd.fence(("gpsimd", "sync", "vector"),
             (bd.w("pe"), bd.w("dve"), bd.w("s_fst0"), bd.w("s_fst1"), bd.w("s_fst2"),
              bd.w("s_fst3"), bd.w("s_tl"), bd.w("s_msc")))

    kvt = bd.sb("kvt", (128, NFC * TOKC), F32)
    vh = bd.sb("vh", (128, NFC * QTW), BF16)
    carr = bd.sb("carr", (128, 2 * NFC), F32)
    offa = bd.sb("offa", (128, NFC), F32)
    w1A = bd.sb("w1A", (128, NFC * E8), BF16)
    w1B = bd.sb("w1B", (128, NFC * E8), BF16)
    w1C = bd.sb("w1C", (128, NFC * E8), BF16)
    w1g = {0: 0, 1: 0, 2: 0}
    bd.emit("gpsimd", lambda e: e.memset(carr[:, 0:NFC], 0.0), inc="ini")

    def load_w1(mi, r, bufi, extra=()):
        buf = (w1A, w1B, w1C)[bufi]
        c = bd.dma("sync", buf[:, 0:NFC * E8].rearrange("pp (c e) -> pp c e", c=NFC),
                   wT_out[r, OFF_M[mi]:OFF_M[mi] + PR_QKVO]
                   .rearrange("(pp c e) -> pp c e", pp=128, c=NFC),
                   waits=(("cc", cc_wT), ("pe", w1g[bufi])) + tuple(extra), sem=f"s_w1{bufi}")
        return buf, (f"s_w1{bufi}", c)

    for r in range(NC):
        bk, ck = load_w1(1, r, 0)
        bv, cv = load_w1(2, r, 1)
        for el in range(E2):
            e_g = r * E2 + el
            for qt in range(NQT):
                for dc in range(NFC):
                    bd.mm(ps0[:, 0:QTW], bk[:, dc * E8 + el * 128:dc * E8 + el * 128 + 128],
                          xs[:, dc * TOKC + qt * QTW:dc * TOKC + qt * QTW + QTW],
                          dc == 0, dc == NFC - 1,
                          waits=((ck, ("dve", st.get("kvd", 0)), ("dve", st.get("xrdy", 0)),
                                  st.get(("bk", 0), ("dve", 0))) if dc == 0 else ()))
                for dc in range(NFC):
                    bd.mm(ps1[:, 0:QTW], bv[:, dc * E8 + el * 128:dc * E8 + el * 128 + 128],
                          xs[:, dc * TOKC + qt * QTW:dc * TOKC + qt * QTW + QTW],
                          dc == 0, dc == NFC - 1,
                          waits=((cv, st.get(("bk", 1), ("dve", 0))) if dc == 0 else ()))
                pe_m = bd.cnt["pe"]
                a_kc = bd.ao(lambda e: e.copy(kvsb[:, 0:QTW], ps0[:, 0:QTW]),
                             waits=(("pe", pe_m), ("dve", st.get("kvd", 0))))
                dkv = bd.vo(lambda e: e.tensor_tensor(stg[:, 0:QTW], kvsb[:, 0:QTW], ps1[:, 0:QTW], OP.mult),
                            waits=(("act", a_kc), ("ini", 2)))
                st[("bk", 0)] = ("dve", dkv)
                st[("bk", 1)] = ("dve", dkv)
                bd.vo(lambda e, e_g=e_g, qt=qt:
                      e.tensor_tensor_scan(kvt[:, e_g * TOKC + qt * QTW:e_g * TOKC + (qt + 1) * QTW],
                                           stg[:, 0:QTW], stg[:, 0:QTW],
                                           carr[:, e_g:e_g + 1], OP.add, OP.bypass), drain=True)
                st["kvd"] = bd.vo(lambda e, e_g=e_g, qt=qt:
                                  e.tensor_copy(carr[:, e_g:e_g + 1],
                                                kvt[:, e_g * TOKC + (qt + 1) * QTW - 1:
                                                    e_g * TOKC + (qt + 1) * QTW]), drain=True)
        w1g[0] = bd.cnt["pe"]
        w1g[1] = bd.cnt["pe"]
    c7 = bd.dma("gpsimd", off_snd[:, :], carr[:, 0:NFC], waits=(("dve", st["kvd"]),), sem="s_msc")
    cc_off = bd.coll("AllGather", OP.bypass, groups_pair,
                     [off_snd.ap().opt()], [off_out.ap().opt()], waits=(("s_msc", c7),))
    c8 = bd.dma("gpsimd", carr[:, NFC:2 * NFC], off_out[0, :, :], waits=(("cc", cc_off),), sem="s_msc")
    dvo = bd.vo(lambda e: e.tensor_scalar(offa[:, 0:NFC], carr[:, NFC:2 * NFC], small[:, 12:13],
                                          None, OP.mult), waits=(("s_msc", c8), ("act", a_ib)))
    K(denom_l1)
    for qt in range(NQT):
        for r in range(NC):
            bq, cq = load_w1(0, r, r % 2)
            for el in range(E2):
                e_g = r * E2 + el
                for dc in range(NFC):
                    bd.mm(ps0[:, 0:QTW], bq[:, dc * E8 + el * 128:dc * E8 + el * 128 + 128],
                          xs[:, dc * TOKC + qt * QTW:dc * TOKC + qt * QTW + QTW],
                          dc == 0, dc == NFC - 1,
                          waits=((cq, ("dve", st.get("vhd", 0)), ("dve", st.get("xrdy", 0)),
                                  st.get(("bk", 0), ("dve", 0))) if dc == 0 else ()))
                pe_m = bd.cnt["pe"]
                st["vhd"] = bd.vo(
                    lambda e, e_g=e_g, qt=qt:
                    e.scalar_tensor_tensor(vh[:, e_g * QTW:(e_g + 1) * QTW],
                                           kvt[:, e_g * TOKC + qt * QTW:e_g * TOKC + (qt + 1) * QTW],
                                           offa[:, e_g:e_g + 1], ps0[:, 0:QTW], OP.add, OP.mult),
                    waits=(("pe", pe_m), ("dve", dvo)))
                st[("bk", 0)] = ("dve", st["vhd"])
            w1g[r % 2] = bd.cnt["pe"]
        for r in range(NC):
            bo, co = load_w1(3, r, 2)
            for fl in range(E2):
                fc = r * E2 + fl
                sb = pp[fc % 2]
                for ec in range(NFC):
                    bd.mm(sb[:, 0:QTW], bo[:, ec * E8 + fl * 128:ec * E8 + fl * 128 + 128],
                          vh[:, ec * QTW:(ec + 1) * QTW], ec == 0, ec == NFC - 1,
                          waits=((co, st.get(("bk", fc % 2), ("dve", 0)),
                                  ("dve", st["vhd"])) if ec == 0 else ()))
                pe_m = bd.cnt["pe"]
                dog = bd.vo(
                    lambda e, sb=sb, fc=fc,
                    xsr=xs[:, fc * TOKC + qt * QTW:fc * TOKC + qt * QTW + QTW],
                    dst=hb[:, fc * QTW:(fc + 1) * QTW]:
                    e.scalar_tensor_tensor(dst, sb[:, 0:QTW], denom_l1, xsr, OP.mult, OP.add),
                    waits=(("pe", pe_m),))
                st[("og", fc % 2)] = dog
                st[("bk", fc % 2)] = ("dve", dog)
            w1g[2] = bd.cnt["pe"]
        layer_norm(qt * QTW, QTW)
    st["xrdy"] = bd.cnt["dve"]
    if debug:
        bd.dma("gpsimd", dbg[2][:, :], xs[:, :], waits=(("dve", st["xrdy"]),), sem="s_msc")
    bd.free("kvt", "vh", "carr", "offa", "w1A", "w1B", "w1C")
    bd.fence(("sync", "vector", "scalar"),
             (bd.w("pe"), bd.w("dve"), bd.w("s_msc"),
              bd.w("s_w10"), bd.w("s_w11"), bd.w("s_w12")))

    if debug:
        dcap0 = bd.vo(lambda e: e.tensor_copy(dbgsb[:, SQ:2 * SQ], xs[:, 0:SQ]), drain=True)
        bd.dma("gpsimd", dbg_cap[:, SQ:2 * SQ], dbgsb[:, SQ:2 * SQ],
               waits=(("dve", dcap0),), sem="s_msc")
    wsA = bd.sb("wsA2", (128, WSW), BF16)
    wsB = bd.sb("wsB2", (128, WSW), BF16)
    xq = bd.sb("xq", (128, NFC * SQ), BF16)
    gsi = bd.sb("gsi", (128, I2h * SQ), BF16)
    gu = bd.sb("gu", (128, NIC * SQ), BF16)
    mfold = bd.sb("mfold", (128, 2 * SQ), BF16)
    wgg = [0, 0]
    K(1.0 / 127.0)

    def colmax_to_row(src_fn, nch, row_off):
        dvm = None
        for i in range(nch):
            if i == 0:
                bd.vo(lambda e, s=src_fn(0): e.tensor_copy(mfold[:, 0:SQ], s), drain=True)
                dvm = bd.vo(lambda e, s=src_fn(0): e.tensor_copy(mfold[:, SQ:2 * SQ], s))
            else:
                bd.vo(lambda e, s=src_fn(i): e.tensor_tensor(mfold[:, 0:SQ], mfold[:, 0:SQ],
                                                             s, OP.max), drain=True)
                dvm = bd.vo(lambda e, s=src_fn(i): e.tensor_tensor(mfold[:, SQ:2 * SQ],
                                                                   mfold[:, SQ:2 * SQ], s, OP.min),
                            drain=True)
        bd.vo(lambda e: e.tensor_scalar(mfold[:, SQ:2 * SQ], mfold[:, SQ:2 * SQ], -1.0, None, OP.mult),
              drain=True)
        dvm = bd.vo(lambda e: e.tensor_tensor(mfold[:, 0:SQ], mfold[:, 0:SQ], mfold[:, SQ:2 * SQ],
                                              OP.max), drain=True)

        for j in range(SQ // 128):
            t = ptr[j % 2]
            pe_t = bd.mm(t[:, 0:128], mfold[:, j * 128:(j + 1) * 128], ident_b[:, :], True, True,
                         waits=(("dve", dvm), ("dve", st.get(("cmr", j % 2), 0)),
                                ("act", st.get(("xb", j % 2), 0))))
            st[("cmr", j % 2)] = bd.vo(
                lambda e, t=t, j=j: e.reduce_max(small[:, 16 + j:17 + j], t[:, 0:128],
                                                 axis=mybir.AxisListType.X, apply_absolute_value=True),
                waits=(("pe", pe_t),))
        dvm2 = bd.cnt["dve"]
        c0_ = bd.dma("gpsimd", sm_dram[:, 0:SQ // 128], small[:, 16:16 + SQ // 128],
                     waits=(("dve", dvm2),))
        c = bd.dma("gpsimd", rows[0:1, row_off:row_off + SQ].rearrange("o (j pp) -> o j pp", pp=128),
                   sm_dram[:, 0:SQ // 128].rearrange("(o pp) j -> o j pp", o=1),
                   waits=(("plq", c0_), bd.w("s_msc")), allow_slow_non_contiguous=True, sem="s_cm")
        dts = bd.vo(lambda e: e.tensor_scalar(rows[0:1, row_off:row_off + SQ],
                                              rows[0:1, row_off:row_off + SQ],
                                              1.0 / 127.0, Q_EPS, OP.mult, OP.add),
                    waits=(("s_cm", c),), drain=True)
        if debug and st.get("cm_calls", 0) == 0:
            dcapB = bd.vo(lambda e: e.tensor_copy(dbgsb[:, SQ:2 * SQ], xs[:, 0:SQ]),
                          waits=(bd.w("s_msc"),), drain=True)
            bd.dma("gpsimd", dbg_cap[:, 3 * SQ:4 * SQ], dbgsb[:, SQ:2 * SQ],
                   waits=(("dve", dcapB),), sem="s_msc")
        st["cm_calls"] = st.get("cm_calls", 0) + 1
        d2_ = bd.vo(lambda e: e.reciprocal(rowm[0:1, 0:SQ], rows[0:1, row_off:row_off + SQ]),
                    waits=(("pe", st["rb_pe"]),), drain=True)
        return d2_

    for qs in range(NSQ):
        toff = qs * SQ
        if qs > 0:
            bd.fence(("vector", "scalar"), (bd.w("pe"), bd.w("s_msc")))
        d2 = colmax_to_row(lambda i: xs[:, i * TOKC + toff:i * TOKC + toff + SQ], NFC, 3 * QTW)
        if debug and qs == 0:
            dcapC = bd.vo(lambda e: e.tensor_copy(dbgsb[:, SQ:2 * SQ], xs[:, 0:SQ]),
                          waits=(bd.w("s_msc"),), drain=True)
            bd.dma("gpsimd", dbg_cap[:, 2 * SQ:3 * SQ], dbgsb[:, SQ:2 * SQ],
                   waits=(("dve", dcapC),), sem="s_msc")
        st["rb_pe"] = pe_rbx = bd.mm(ps3[:, 0:SQ], onesr_f[0:1, :], rowm[0:1, 0:SQ], True, True,
                                     waits=(("dve", d2), ("dve", st["ydve"])))
        if debug and qs == 0:
            dcap = bd.vo(lambda e: e.tensor_copy(dbgsb[:, 0:SQ], ps3[:, 0:SQ]),
                         waits=(("pe", pe_rbx), bd.w("s_msc")), drain=True)
            dcap2 = bd.vo(lambda e, xsr=xs[:, toff:toff + SQ]:
                          e.tensor_copy(dbgsb[:, SQ:2 * SQ], xsr), drain=True)
            bd.dma("gpsimd", dbg_rb[:, :], dbgsb[:, 0:SQ], waits=(("dve", dcap),), sem="s_msc")
            bd.dma("gpsimd", dbg_cap[:, 0:SQ], dbgsb[:, SQ:2 * SQ], waits=(("dve", dcap2),),
                   sem="s_msc")
        for fc in range(NFC):
            dvq1 = bd.vo(lambda e, fc=fc,
                         xsr=xs[:, fc * TOKC + toff:fc * TOKC + toff + SQ]:
                         e.tensor_tensor(i32s[:, (fc % 2) * 512:(fc % 2) * 512 + SQ],
                                         xsr, ps3[:, 0:SQ], OP.mult),
                         waits=(("pe", pe_rbx),) + ((("act", st.get(("xqe", fc % 2), 0)),) if fc >= 2 else ()),
                         drain=fc >= 2)
            st[("xqe", fc % 2)] = bd.ao(lambda e, fc=fc:
                                        e.copy(xq[:, fc * SQ:(fc + 1) * SQ],
                                               i32s[:, (fc % 2) * 512:(fc % 2) * 512 + SQ]),
                                        waits=(("dve", dvq1),))
        act_xq = bd.cnt["act"]
        dve_xq = bd.cnt["dve"]
        if debug and qs == 0:
            bd.dma("gpsimd", dbg_xq[:, :], xq[:, :], waits=(("act", act_xq),), sem="s_msc")
        d3 = bd.vo(lambda e: e.tensor_scalar(rowm[0:1, 0:SQ], rows[0:1, 3 * QTW:3 * QTW + SQ],
                                             srow[0:1, 8:9], None, OP.mult),
                   waits=(("pe", pe_rbx),), drain=True)
        pe_cg = bd.mm(ps4[:, 0:SQ], onesr_f[0:1, :], rowm[0:1, 0:SQ], True, True,
                      waits=(("dve", d3), ("dve", st.get(("cmr", 0), 0)), ("dve", st.get(("cmr", 1), 0))))
        a_cg = bd.ao(lambda e: e.copy(csc[:, 0:SQ], ps4[:, 0:SQ]), waits=(("pe", pe_cg),))
        d4 = bd.vo(lambda e: e.tensor_scalar(rowm[0:1, QTW:QTW + SQ], rows[0:1, 3 * QTW:3 * QTW + SQ],
                                             srow[0:1, 9:10], None, OP.mult))
        pe_cu = bd.mm(ps5[:, 0:SQ], onesr_f[0:1, :], rowm[0:1, QTW:QTW + SQ], True, True,
                      waits=(("dve", d4),))
        a_cu = bd.ao(lambda e: e.copy(csc[:, SQ:2 * SQ], ps5[:, 0:SQ]), waits=(("pe", pe_cu),))
        st[("xb", 0)] = a_cg
        st[("xb", 1)] = a_cu
        st["rb_pe"] = pe_cu
        for r in range(NC):
            for hf in range(2):
                cg_ = bd.dma("sync", wsA[:, 0:NFC * I8h].rearrange("pp (c e) -> pp c e", c=NFC),
                             wT_out[r, OFF_M[4] + hf * (PR_GU // 2):OFF_M[4] + (hf + 1) * (PR_GU // 2)]
                             .rearrange("(pp c e) -> pp c e", pp=128, c=NFC),
                             waits=(("cc", cc_wT), ("pe", wgg[0])), sem="s_wg0")
                for il in range(I2h):
                    sb = pp[il % 2]
                    for fc in range(NFC):
                        bd.mm(sb[:, 0:SQ], wsA[:, fc * I8h + il * 128:fc * I8h + il * 128 + 128],
                              xq[:, fc * SQ:(fc + 1) * SQ], fc == 0, fc == NFC - 1,
                              waits=((("s_wg0", cg_), ("act", act_xq),
                                      st.get(("bk", il % 2), ("dve", 0))) if fc == 0 else ()))
                    pe_m = bd.cnt["pe"]
                    dvg = bd.vo(lambda e, sb=sb, sl=stg[:, (il % 2) * SQ:(il % 2) * SQ + SQ]:
                                e.tensor_tensor(sl, sb[:, 0:SQ], csc[:, 0:SQ], OP.mult),
                                waits=(("pe", pe_m), ("act", a_cg),
                                       ("act", st.get(("ge", il % 2), 0))))
                    st[("bk", il % 2)] = ("dve", dvg)
                    st[("ge", il % 2)] = bd.ao(lambda e, il=il,
                                               sl=stg[:, (il % 2) * SQ:(il % 2) * SQ + SQ]:
                                               e.activation(gsi[:, il * SQ:(il + 1) * SQ], sl,
                                                            AF.Silu), waits=(("dve", dvg),))
                wgg[0] = bd.cnt["pe"]
                cu_ = bd.dma("sync", wsB[:, 0:NFC * I8h].rearrange("pp (c e) -> pp c e", c=NFC),
                             wT_out[r, OFF_M[5] + hf * (PR_GU // 2):OFF_M[5] + (hf + 1) * (PR_GU // 2)]
                             .rearrange("(pp c e) -> pp c e", pp=128, c=NFC),
                             waits=(("cc", cc_wT), ("pe", wgg[1])), sem="s_wg1")
                for il in range(I2h):
                    ic = r * 2 * I2h + hf * I2h + il
                    sb = pp[il % 2]
                    for fc in range(NFC):
                        bd.mm(sb[:, 0:SQ], wsB[:, fc * I8h + il * 128:fc * I8h + il * 128 + 128],
                              xq[:, fc * SQ:(fc + 1) * SQ], fc == 0, fc == NFC - 1,
                              waits=((("s_wg1", cu_), ("dve", st.get(("ue", il % 2), 0)),
                                      st.get(("bk", il % 2), ("dve", 0)),
                                      ("act", st.get(("ge", il % 2), 0))) if fc == 0 else ()))
                    pe_m = bd.cnt["pe"]
                    dvu = bd.vo(lambda e, sb=sb: e.tensor_tensor(stg2[:, 0:SQ], sb[:, 0:SQ],
                                                                 csc[:, SQ:2 * SQ], OP.mult),
                                waits=(("pe", pe_m), ("act", a_cu), ("act", st.get(("ge", il % 2), 0))),
                                drain=True)
                    st[("bk", il % 2)] = ("dve", dvu)
                    st[("ue", il % 2)] = bd.vo(
                        lambda e, ic=ic, il=il: e.tensor_tensor(gu[:, ic * SQ:(ic + 1) * SQ],
                                                                gsi[:, il * SQ:(il + 1) * SQ],
                                                                stg2[:, 0:SQ], OP.mult), drain=True)
                wgg[1] = bd.cnt["pe"]
        d2g = colmax_to_row(lambda i: gu[:, i * SQ:(i + 1) * SQ], NIC, 4 * QTW)
        st["rb_pe"] = pe_rbg = bd.mm(ps3[:, 0:SQ], onesr_f[0:1, :], rowm[0:1, 0:SQ], True, True,
                                     waits=(("dve", d2g), ("dve", dve_xq)))
        for ic in range(NIC):
            dvq1 = bd.vo(lambda e, ic=ic: e.tensor_tensor(i32s[:, (ic % 2) * 512:(ic % 2) * 512 + SQ],
                                                          gu[:, ic * SQ:(ic + 1) * SQ],
                                                          ps3[:, 0:SQ], OP.mult),
                         waits=(("pe", pe_rbg),) + ((("act", st.get(("gqe", ic % 2), 0)),) if ic >= 2 else ()),
                         drain=ic >= 2)
            st[("gqe", ic % 2)] = bd.ao(lambda e, ic=ic:
                                        e.copy(gu[:, ic * SQ:(ic + 1) * SQ],
                                               i32s[:, (ic % 2) * 512:(ic % 2) * 512 + SQ]),
                                        waits=(("dve", dvq1),))
        act_guq = bd.cnt["act"]
        if debug and qs == 0:
            bd.dma("gpsimd", dbg_gu[:, :], gu[:, :], waits=(("act", act_guq),), sem="s_msc")
            bd.dma("gpsimd", dbg_gs[0:1, 0:2 * QTW], rows[0:1, 3 * QTW:5 * QTW], sem="s_msc")
        d5 = bd.vo(lambda e: e.tensor_scalar(rowm[0:1, 0:SQ], rows[0:1, 4 * QTW:4 * QTW + SQ],
                                             srow[0:1, 10:11], None, OP.mult),
                   waits=(("pe", pe_rbg),), drain=True)
        dve_pre_cd = bd.cnt["dve"]
        pe_cd = bd.mm(ps4[:, 0:SQ], onesr_f[0:1, :], rowm[0:1, 0:SQ], True, True,
                      waits=(("dve", d5), ("dve", dve_pre_cd), ("act", a_cg)))
        a_cd = bd.ao(lambda e: e.copy(csc[:, 2 * SQ:3 * SQ], ps4[:, 0:SQ]), waits=(("pe", pe_cd),))
        st["rb_pe"] = pe_cd
        for r in range(NC):
            bufs_dn = (wsA, wsB)
            for fl2 in range(0, E2, 2):
                for ihf in range(2):
                    cd_ = bd.dma("sync", bufs_dn[ihf][:, 0:NICh * E8].rearrange("pp (c e) -> pp c e", c=NICh),
                                 wT_out[r, OFF_M[6]:OFF_M[6] + PR_D]
                                 .rearrange("(pp c e) -> pp c e", pp=128, c=NIC)
                                 [:, ihf * NICh:(ihf + 1) * NICh, :],
                                 waits=(("cc", cc_wT), ("pe", wgg[ihf])), sem=f"s_wg{ihf}")
                    for fl in (fl2, fl2 + 1):
                        sb = pp[fl % 2]
                        for icl in range(NICh):
                            ic = ihf * NICh + icl
                            bd.mm(sb[:, 0:SQ],
                                  bufs_dn[ihf][:, icl * E8 + fl * 128:icl * E8 + fl * 128 + 128],
                                  gu[:, ic * SQ:(ic + 1) * SQ],
                                  ihf == 0 and icl == 0, ihf == 1 and icl == NICh - 1,
                                  waits=(((f"s_wg{ihf}", cd_), ("act", act_guq),
                                          st.get(("bk", fl % 2), ("dve", 0))) if icl == 0 else ()))
                    wgg[ihf] = bd.cnt["pe"]
                for fl in (fl2, fl2 + 1):
                    fc = r * E2 + fl
                    sb = pp[fl % 2]
                    pe_m = bd.cnt["pe"]
                    dvd = bd.vo(lambda e, sb=sb: e.tensor_tensor(stg[:, 0:SQ], sb[:, 0:SQ],
                                                                 csc[:, 2 * SQ:3 * SQ], OP.mult),
                                waits=(("pe", pe_m), ("act", a_cd)), drain=True)
                    st[("bk", fl % 2)] = ("dve", dvd)
                    st[("og", fl % 2)] = bd.vo(
                        lambda e, fc=fc,
                        xsr=xs[:, fc * TOKC + toff:fc * TOKC + toff + SQ],
                        dst=hb[:, fc * QTW:fc * QTW + SQ]:
                        e.scalar_tensor_tensor(dst, stg[:, 0:SQ], 1.0, xsr, OP.mult, OP.add), drain=True)
        layer_norm(toff, SQ)
    bd.free("wsA2", "wsB2", "xq", "gsi", "gu", "mfold")
    bd.fence(("scalar",), (bd.w("pe"), bd.w("dve")))

    otm = bd.sb("otm", (128, D), F32)
    for tt in range(NTT):
        for fc in range(NFC):
            t = ptr[fc % 2]
            pe_t = bd.mm(t[:, 0:128], xs[:, fc * TOKC + tt * 128:fc * TOKC + tt * 128 + 128],
                         ident_b[:, :], True, True,
                         waits=(("dve", st["lnprev"]), ("act", st.get(("xb", fc % 2), 0))) +
                               ((("s_out", st.get("odma", 0)),) if fc < 2 else ()))
            st[("xb", fc % 2)] = bd.ao(lambda e, t=t, fc=fc: e.copy(otm[:, fc * 128:(fc + 1) * 128],
                                                                    t[:, 0:128]),
                                       waits=(("pe", pe_t),))
        a_fin = bd.cnt["act"]
        st["odma"] = bd.dma("gpsimd", out_my[tt * 128:(tt + 1) * 128, :], otm[:, :],
                            waits=(("act", a_fin),), sem="s_out")
    final_out = bd.cnt["s_out"]
    bd.emit("gpsimd", lambda e: e.nop(), waits=(("s_out", final_out),))

    vals = sorted(bd.consts)
    assert len(vals) <= 64, len(vals)
    for i, v in enumerate(vals):
        nc.const_aps.register(F32, float(v), cpool[:, i:i + 1])
        nc.const_aps.register(BF16, float(v), cpool_b[:, i:i + 1])

    def memset_fn(e):
        for i, v in enumerate(vals):
            e.memset(cpool[:, i:i + 1], float(v))
            e.memset(cpool_b[:, i:i + 1], float(v))
        e.memset(ones_b[:, :], 1.0)
        e.memset(ones_f[:, :], 1.0)
        e.memset(srow[:, :], 0.0)
        e.drain()
        e.memset(srow[0:1, 15:16], 1.0)
        return e.memset(onesr_f[:, :], 1.0)

    bd.q["gpsimd"].insert(0, ((), memset_fn, "ini", 1))

    es = contextlib.ExitStack()
    sems = {}
    for s in SEMS:
        sems[s] = es.enter_context(nc.semaphore("sem_" + s))
    with es:
        with nc.Block() as block:
            for eng_name in ENGS:
                ops = bd.q[eng_name]

                def mk(ops=ops):
                    def body(e):
                        for waits, fn, inc, amt in ops:
                            for (s, v) in waits:
                                e.wait_ge(sems[s], v)
                            ins = fn(e)
                            if inc is not None and ins is not None:
                                ins.then_inc(sems[inc], amt)
                    return body
                getattr(block, eng_name)(mk())
    return nc


def kernel(**inputs):
    from concourse.bass_utils import run_bass_kernel_spmd
    prm = Prm()
    in_maps = host_prepare(prm, inputs)
    nc = build(prm)
    res = run_bass_kernel_spmd(nc, in_maps, core_ids=list(range(prm.NC)))
    out = np.empty((prm.B, prm.T, prm.D), np.float32)
    for c in range(prm.NC):
        b, hf = c // 2, c % 2
        out[b, hf * prm.TOKC:(hf + 1) * prm.TOKC, :] = np.asarray(res.results[c]["out_my"])
    return out


# version 08 (final)
# speedup vs baseline: 1.0755x; 1.0064x over previous
import contextlib
import math

import numpy as np
import ml_dtypes

import concourse.bass as bass
import concourse.mybir as mybir
from concourse.alu_op_type import AluOpType as OP

AF = mybir.ActivationFunctionType
F32 = mybir.dt.float32
BF16 = mybir.dt.bfloat16
I32 = mybir.dt.int32

NEG = -1.0e30
B_EXP = 8.0
LN_EPS = 1e-5
Q_EPS = 1e-5
QTW = 512
SQ = 256


class Prm:
    def __init__(self, B=4, T=2048, D=2048, H=16, I=8192):
        self.B, self.T, self.D, self.H, self.I = B, T, D, H, I
        self.NC = 2 * B
        self.TOKC = T // 2
        self.NQT = self.TOKC // QTW
        self.NFC = D // 128
        self.NKT = T // 128
        assert D == H * 128 and self.TOKC % QTW == 0 and (120 * H) % 128 == 0
        self.FD = 3 * D
        self.NFCF = self.FD // 128
        self.R = 8
        self.DSTD = 120
        self.qs_end = H * 120
        self.ks_end = 2 * H * 120
        self.v_end = self.ks_end + D
        self.ql_end = self.v_end + 8 * H
        self.ktmax = [(self.TOKC + (qt + 1) * QTW) // 128 for qt in range(self.NQT)]
        self.NDT = sum(self.ktmax)
        self.E8 = D // self.NC
        self.I8 = I // self.NC
        self.NIC = I // 128
        self.slabs_qkv_q = mk_slabs(0, D)
        self.slabs_qkv_k = mk_slabs(D, 2 * D)
        self.slabs_qkv_v = mk_slabs(2 * D, 3 * D)
        self.slabs_wa_q = mk_slabs(0, self.qs_end)
        self.slabs_wa_k = mk_slabs(self.qs_end, self.ks_end)
        self.slabs_wa_v = mk_slabs(self.ks_end, self.v_end)
        self.slabs_wa_t = [(self.v_end, self.FD)]
        self.slabs_d = mk_slabs(0, D)


def mk_slabs(a, b, extra=()):
    bnds = sorted({x for x in range(0, b + 1, 512) if a < x < b} | {a, b} | {x for x in extra if a < x < b})
    return [(bnds[i], bnds[i + 1]) for i in range(len(bnds) - 1)]


def _slopes(H):
    return [2.0 ** (-8.0 / (i + 1)) for i in range(H)]



def perm_w(w, slabs):
    D, F = w.shape
    ndc = D // 128
    w3 = w.reshape(ndc, 128, F)
    parts = []
    for (a, b) in slabs:
        parts.append(np.ascontiguousarray(w3[:, :, a:b].transpose(1, 0, 2)).reshape(-1))
    return np.concatenate(parts)


def mk_dtiles(prm, qoff, alibi):
    tiles = np.empty((prm.NDT, 128, QTW), dtype=np.float32)
    dk = np.arange(128)[:, None]
    dq = np.arange(QTW)[None, :]
    i = 0
    for qt in range(prm.NQT):
        for kt in range(prm.ktmax[qt]):
            k = kt * 128 + dk
            q = qoff + qt * QTW + dq
            base = (k - q).astype(np.float32) if alibi else np.zeros((128, QTW), np.float32)
            tiles[i] = np.where(k <= q, base, NEG)
            i += 1
    return tiles.astype(ml_dtypes.bfloat16)


def host_prepare(prm, inputs):
    p = prm
    x = np.asarray(inputs["L0_attn_x"], np.float32)
    wqkv = perm_w(np.asarray(inputs["L0_attn_W_qkv"], np.float32),
                  p.slabs_qkv_q + p.slabs_qkv_k + p.slabs_qkv_v)
    wo = perm_w(np.asarray(inputs["L0_attn_W_o"], np.float32), p.slabs_d)
    wa = perm_w(np.asarray(inputs["L0_ffn_Wa"], np.float32),
                p.slabs_wa_q + p.slabs_wa_k + p.slabs_wa_v + p.slabs_wa_t)
    wp = perm_w(np.asarray(inputs["L0_ffn_Wp"], np.float32), p.slabs_d)
    ident = np.eye(128, dtype=np.float32)
    mats = [np.asarray(inputs[k], np.float32) for k in
            ("L1_attn_Wq", "L1_attn_Wk", "L1_attn_Wv", "L1_attn_Wo")]
    wg = np.asarray(inputs["L1_ffn_w_gate"], np.float32)
    wu = np.asarray(inputs["L1_ffn_w_up"], np.float32)
    wd = np.asarray(inputs["L1_ffn_w_down"], np.float32)
    wstd = np.asarray(inputs["L0_ffn_w_std"], np.float32).reshape(1, p.H)
    wrec = np.asarray(inputs["L0_ffn_w_rec"], np.float32).reshape(1, p.H)
    qk = 128.0 ** -0.25
    nsv = 1 + 2 * p.H
    sel = np.zeros((nsv, p.NFCF * 128), np.float32)
    for j in range(p.FD):
        if j < p.qs_end:
            sel[1 + j // 120, j] = qk
        elif j < p.ks_end:
            sel[1 + (j - p.qs_end) // 120, j] = qk
        elif j < p.v_end:
            sel[0, j] = 1.0
        elif j < p.ql_end:
            sel[1 + p.H + (j - p.v_end) // p.R, j] = qk
        else:
            sel[1 + p.H + (j - p.ql_end) // p.R, j] = qk
    rep8 = np.zeros((p.H, 128), np.float32)
    for pp_ in range(8 * p.H):
        rep8[pp_ // 8, pp_] = 1.0
    in_maps = []
    for c in range(p.NC):
        b, hf = c // 2, c % 2
        qoff = hf * p.TOKC
        in_maps.append({
            "x_my": np.ascontiguousarray(x[b, qoff:qoff + p.TOKC, :]),
            "wqkv_p": wqkv, "wo_p": wo, "wa_p": wa, "wp_p": wp,
            "ident": ident, "w_std": wstd, "w_rec": wrec,
            "is_b": np.full((1, 1), float(hf), np.float32),
            "sel": sel, "rep8": rep8,
            "dA": mk_dtiles(p, qoff, True),
            "dM": mk_dtiles(p, qoff, False),
            "wq1_s": np.ascontiguousarray(mats[0][c * p.E8:(c + 1) * p.E8, :]),
            "wk1_s": np.ascontiguousarray(mats[1][c * p.E8:(c + 1) * p.E8, :]),
            "wv1_s": np.ascontiguousarray(mats[2][c * p.E8:(c + 1) * p.E8, :]),
            "wo1_s": np.ascontiguousarray(mats[3][c * p.E8:(c + 1) * p.E8, :]),
            "wg_s": np.ascontiguousarray(wg[c * p.I8:(c + 1) * p.I8, :]),
            "wu_s": np.ascontiguousarray(wu[c * p.I8:(c + 1) * p.I8, :]),
            "wd_s": np.ascontiguousarray(wd[c * p.E8:(c + 1) * p.E8, :]),
        })
    return in_maps



ENGS = ("tensor", "vector", "scalar", "sync", "gpsimd")
SEMS = ("pe", "dve", "act", "spq", "plq", "cc", "ini",
        "s_in", "s_ins", "s_pq", "s_pq2", "s_wf", "s_msc", "s_cm", "s_out",
        "s_dsb", "s_tl", "s_xt0", "s_xt1", "s_ws0", "s_ws1", "s_kv0", "s_kv1",
        "s_evb0", "s_evb1", "s_w10", "s_w11", "s_w12", "s_wg0", "s_wg1",
        "s_fst0", "s_fst1", "s_fst2", "s_fst3", "s_tp")


class Bld:
    def __init__(self, nc):
        self.nc = nc
        self.q = {e: [] for e in ENGS}
        self.cnt = {s: 0 for s in SEMS}
        self.consts = set()
        self._cm = {}

    def sb(self, name, shape, dt):
        cm = self.nc.sbuf_tensor(name, list(shape), dt)
        t = cm.__enter__()
        self._cm[name] = cm
        return t

    def free(self, *names):
        order = list(self._cm)
        for n in sorted(names, key=order.index, reverse=True):
            self._cm.pop(n).__exit__(None, None, None)

    def ps(self, name, shape, dt):
        cm = self.nc.psum_tensor(name, list(shape), dt)
        t = cm.__enter__()
        self._cm["psum_" + name] = cm
        return t

    def K(self, v):
        v = float(v)
        self.consts.add(v)
        return v

    def emit(self, eng, fn, waits=(), inc=None, amt=1):
        self.q[eng].append((tuple(w for w in waits if w is not None and w[1] > 0), fn, inc, amt))
        if inc is not None:
            self.cnt[inc] += amt
            return self.cnt[inc]
        return None

    def w(self, sem):
        return (sem, self.cnt[sem])

    def dma(self, qname, out, in_, waits=(), sem=None, **kw):
        if sem is None:
            sem = "spq" if qname == "sync" else "plq"
        self.emit(qname, lambda e, o=out, i=in_, kw=kw: e.dma_start(out=o, in_=i, **kw),
                  waits=waits, inc=sem, amt=16)
        return self.cnt[sem]

    def dmaT(self, out, in_, waits=(), sem="spq", q="sync"):
        self.emit(q, lambda e, o=out, i=in_: e.dma_start_transpose(o, i),
                  waits=waits, inc=sem, amt=16)
        return self.cnt[sem]

    def mm(self, outap, lh, rh, start, stop, waits=()):
        return self.emit("tensor", lambda e, o=outap, a=lh, b=rh, s1=start, s2=stop:
                         e.matmul(o, a, b, start=s1, stop=s2), waits=waits, inc="pe")

    def vo(self, fn, waits=(), drain=False):
        if drain:
            self.emit("vector", lambda e: e.drain())
        return self.emit("vector", fn, waits=waits, inc="dve")

    def ao(self, fn, waits=()):
        return self.emit("scalar", fn, waits=waits, inc="act")

    def fence(self, engines, waits):
        for eng in engines:
            self.emit(eng, lambda e: e.nop(), waits=waits)

    def coll(self, kind, op, groups, ins, outs, waits=()):
        return self.emit("gpsimd", lambda e: e.collective_compute(
            kind, op, replica_groups=groups, ins=ins, outs=outs), waits=waits, inc="cc")



def build(prm, debug=False):
    p = prm
    nc = bass.Bass("TRN2", target_bir_lowering=False)
    H, D, I = p.H, p.D, p.I
    TOKC, NQT, NFC, NKT = p.TOKC, p.NQT, p.NFC, p.NKT
    NTT = TOKC // 128
    E8, I8, NIC = p.E8, p.I8, p.NIC
    NC = p.NC
    NSQ = TOKC // SQ
    I8h = I8 // 2
    I2h = I8h // 128
    E2 = E8 // 128
    NICh = NIC // 2
    WSW = max(NFC * 512, NFC * I8h, NICh * E8)
    inv_sqrt_dh = 1.0 / math.sqrt(128.0)
    qk_quarter = 128.0 ** -0.25
    denom_l1 = math.sqrt(float(D)) / float(p.T)
    slopes = _slopes(H)

    din = {}
    for nm, shape, dt in [
        ("x_my", [TOKC, D], F32), ("wqkv_p", [D * 3 * D], F32), ("wo_p", [D * D], F32),
        ("wa_p", [D * 3 * D], F32), ("wp_p", [D * D], F32), ("ident", [128, 128], F32),
        ("w_std", [1, H], F32), ("w_rec", [1, H], F32), ("is_b", [1, 1], F32),
        ("dA", [p.NDT, 128, QTW], BF16), ("dM", [p.NDT, 128, QTW], BF16),
        ("wq1_s", [E8, D], F32), ("wk1_s", [E8, D], F32), ("wv1_s", [E8, D], F32),
        ("wo1_s", [E8, D], F32), ("wg_s", [I8, D], F32), ("wu_s", [I8, D], F32),
        ("wd_s", [E8, I], F32), ("sel", [1 + 2 * H, p.NFCF * 128], F32),
        ("rep8", [H, 128], F32),
    ]:
        din[nm] = nc.dram_tensor(nm, shape, dt, kind="ExternalInput")
    out_my = nc.dram_tensor("out_my", [TOKC, D], F32, kind="ExternalOutput")

    KVW = max(H * TOKC, NTT * D)
    kv_snd = [nc.dram_tensor(f"kv_snd{i}", [2, 128, KVW], BF16) for i in range(2)]
    kv_out = [nc.dram_tensor(f"kv_out{i}", [2, 2, 128, KVW], BF16) for i in range(2)]
    off_snd = nc.dram_tensor("off_snd", [128, NFC], F32)
    off_out = nc.dram_tensor("off_out", [2, 128, NFC], F32)
    ar_snd = nc.dram_tensor("ar_snd", [1, 4], F32)
    ar_out = nc.dram_tensor("ar_out", [1, 4], F32)
    tmp_qkvo = nc.dram_tensor("tmp_qkvo", [4, E8, D], BF16)
    tmp_gu = nc.dram_tensor("tmp_gu", [2, I8, D], BF16)
    tmp_d = nc.dram_tensor("tmp_d", [E8, I], BF16)
    PR_QKVO, PR_GU, PR_D = D * E8, D * I8, I * E8
    PR_ALL = 4 * PR_QKVO + 2 * PR_GU + PR_D
    OFF_M = [0, PR_QKVO, 2 * PR_QKVO, 3 * PR_QKVO, 4 * PR_QKVO,
             4 * PR_QKVO + PR_GU, 4 * PR_QKVO + 2 * PR_GU]
    wT_snd = nc.dram_tensor("wT_snd", [PR_ALL], BF16)
    wT_out = nc.dram_tensor("wT_out", [NC, PR_ALL], BF16, addr_space="Shared")
    sm_dram = nc.dram_tensor("sm_dram", [128, 4], F32)
    dbg_kind = dict(kind="ExternalOutput") if debug else {}
    dbg = [nc.dram_tensor(f"dbg_x{i}", [128, NFC * TOKC], BF16, **dbg_kind) for i in (1, 2, 3)]
    dbg_xq = nc.dram_tensor("dbg_xq", [128, (D // 128) * SQ], BF16, **dbg_kind)
    dbg_gu = nc.dram_tensor("dbg_gu", [128, (I // 128) * SQ], BF16, **dbg_kind)
    dbg_gs = nc.dram_tensor("dbg_gs", [1, 8 * QTW], F32, **dbg_kind)
    dbg_rb = nc.dram_tensor("dbg_rb", [128, SQ], F32, **dbg_kind)
    dbg_cap = nc.dram_tensor("dbg_cap", [128, 4 * SQ], F32, **dbg_kind)

    groups_pair = [[2 * i, 2 * i + 1] for i in range(NC // 2)]
    groups_all = [list(range(NC))]

    bd = Bld(nc)
    K = bd.K
    st = {"ydve": 0, "lnprev": 0, "rb_pe": 0, "ab_pe": 0}

    cpool = bd.sb("cpool", (128, 64), F32)
    cpool_b = bd.sb("cpool_b", (128, 64), BF16)
    ones_b = bd.sb("ones_b", (128, 1), BF16)
    onesr_f = bd.sb("onesr_f", (1, 128), F32)
    ones_f = bd.sb("ones_f", (128, 1), F32)
    ident_b = bd.sb("ident_b", (128, 128), BF16)
    xs = bd.sb("xs", (128, NFC * TOKC), BF16)
    stg = bd.sb("stg", (128, 2 * QTW), F32)
    stg2 = bd.sb("stg2", (128, 2 * QTW), F32)
    pbuf = bd.sb("pbuf", (128, 2 * QTW), BF16)
    hb = bd.sb("hb", (128, NFC * QTW), BF16)
    hsq = bd.sb("hsq", (128, 2 * QTW), BF16)
    rows = bd.sb("rows", (1, 6 * QTW), F32)
    rowm = bd.sb("rowm", (1, 2 * QTW), F32)
    srow = bd.sb("srow", (1, 128), F32)
    small = bd.sb("small", (128, 32), F32)
    evb = bd.sb("evb", (128, 2 * 512), BF16)
    i32s = bd.sb("i32s", (128, 1024), I32)
    csvec = bd.sb("csvec", (128, p.NFCF), F32)
    u128 = bd.sb("u128", (128, 2), F32)
    rbv = bd.sb("rbv", (128, QTW), F32)
    kvsb = bd.sb("kvsb", (128, QTW), F32)
    csc = bd.sb("csc", (128, 3 * SQ), F32)
    dbgsb = bd.sb("dbgsb", (128, 2 * SQ), F32)

    ps0 = bd.ps("ps0", (128, 512), F32)
    ps1 = bd.ps("ps1", (128, 512), F32)
    ps2 = bd.ps("ps2", (128, 512), F32)
    ps3 = bd.ps("ps3", (128, 512), F32)
    ps4 = bd.ps("ps4", (128, 512), F32)
    ps5 = bd.ps("ps5", (128, 512), F32)
    psS = bd.ps("psS", (128, 512), F32)
    psS2 = bd.ps("psS2", (128, 512), F32)
    pp = [ps0, ps1]
    ptr = [ps4, ps5]

    INI = ("ini", 1)

    xtm = bd.sb("xtm", (128, 2 * D), BF16)
    spi = bd.dma("gpsimd", ident_b[:, :], din["ident"][:, :], waits=(INI,), sem="s_in")
    for tt in range(NTT):
        sp = bd.dma("gpsimd", xtm[:, (tt % 2) * D:(tt % 2) * D + D],
                    din["x_my"][tt * 128:(tt + 1) * 128, :], sem=f"s_xt{tt % 2}",
                    waits=(INI,) + ((("act", st.get(("xg", tt - 2), 0)),) if tt >= 2 else ()))
        for fc in range(NFC):
            t = ptr[fc % 2]
            pe_m = bd.mm(t[:, 0:128],
                         xtm[:, (tt % 2) * D + fc * 128:(tt % 2) * D + fc * 128 + 128],
                         ident_b[:, :], True, True,
                         waits=(((f"s_xt{tt % 2}", sp), ("s_in", spi)) if fc == 0 else ()) +
                               (("act", st.get(("xb", fc % 2), 0)),))
            st[("xb", fc % 2)] = bd.ao(
                lambda e, t=t, dst=xs[:, fc * TOKC + tt * 128:fc * TOKC + tt * 128 + 128]:
                e.copy(dst, t[:, 0:128]), waits=(("pe", pe_m),))
        st[("xg", tt)] = bd.cnt["act"]
    bd.free("xtm")
    act_x = bd.cnt["act"]
    bd.fence(("gpsimd", "vector"), (bd.w("pe"),))

    wtmp = bd.sb("wtmp", (128, 2048), BF16)
    wf32 = bd.sb("wf32", (128, 2048), F32)
    racc = bd.sb("racc", (128, 16), F32)
    for mi, nm in enumerate(("wq1_s", "wk1_s", "wv1_s", "wo1_s")):
        for rt in range(E8 // 128):
            c1 = bd.dma("gpsimd", wtmp[:, 0:D], din[nm][rt * 128:(rt + 1) * 128, 0:D],
                        waits=(INI, bd.w("s_pq")))
            bd.dma("gpsimd", tmp_qkvo[mi, rt * 128:(rt + 1) * 128, :], wtmp[:, 0:D],
                   waits=(("plq", c1),), sem="s_pq")
    spq_qkvo = bd.cnt["s_pq"]
    spq_mat = []
    for mi, (nm, rows_n, cols) in enumerate((("wg_s", I8, D), ("wu_s", I8, D), ("wd_s", E8, I))):
        CW = min(2048, cols)
        k = 0
        for rt in range(rows_n // 128):
            for ccc in range(cols // CW):
                c1 = bd.dma("gpsimd", wf32[:, 0:CW],
                            din[nm][rt * 128:(rt + 1) * 128, ccc * CW:(ccc + 1) * CW],
                            waits=(bd.w("dve"),), sem="s_wf")
                if k < 15:
                    bd.vo(lambda e, k=k, cw=CW: e.reduce_sum(racc[:, k:k + 1], wf32[:, 0:cw],
                                                             axis=mybir.AxisListType.X,
                                                             apply_absolute_value=True),
                          waits=(("s_wf", c1),))
                else:
                    bd.vo(lambda e, cw=CW: e.reduce_sum(small[:, 0:1], wf32[:, 0:cw],
                                                        axis=mybir.AxisListType.X,
                                                        apply_absolute_value=True),
                          waits=(("s_wf", c1),))
                    bd.vo(lambda e, k=k: e.tensor_tensor(racc[:, k % 15:k % 15 + 1],
                                                         racc[:, k % 15:k % 15 + 1],
                                                         small[:, 0:1], OP.add), drain=True)
                k += 1
        nacc = min(k, 15)
        dvm = bd.vo(lambda e, nacc=nacc: e.reduce_sum(racc[:, 15:16], racc[:, 0:nacc],
                                                      axis=mybir.AxisListType.X),
                    waits=(("pe", st.get("pqs_pe", 0)),), drain=True)
        pe_m = bd.mm(psS[0:1, mi:mi + 1], ones_f[:, :], racc[:, 15:16], True, True,
                     waits=(("dve", dvm),))
        st["pqs_pe"] = pe_m
        bd.ao(lambda e, mi=mi: e.copy(srow[0:1, mi:mi + 1], psS[0:1, mi:mi + 1]),
              waits=(("pe", pe_m),))
    act_part = bd.cnt["act"]
    c1 = bd.dma("gpsimd", ar_snd[0:1, 0:4], srow[0:1, 0:4], waits=(("act", act_part),))
    cc_ar = bd.coll("AllReduce", OP.add, groups_all, [ar_snd.ap().opt()], [ar_out.ap().opt()],
                    waits=(("plq", c1),))
    c2 = bd.dma("gpsimd", srow[0:1, 4:8], ar_out[0:1, 0:4], waits=(("cc", cc_ar),), sem="s_msc")
    K(Q_EPS)
    dmean = K(1.0 / (I * D))
    bd.vo(lambda e: e.tensor_scalar(srow[0:1, 8:11], srow[0:1, 4:7], dmean, Q_EPS,
                                    OP.mult, OP.add), waits=(("s_msc", c2),))
    d2 = bd.vo(lambda e: e.reciprocal(srow[0:1, 12:15], srow[0:1, 8:11]), drain=True)
    pe_rs = None
    for mi in range(3):
        pe_rs = bd.mm(psS2[:, mi:mi + 1], onesr_f[0:1, :], srow[0:1, 12 + mi:13 + mi],
                      mi == 0, mi == 2, waits=(("dve", d2),) if mi == 0 else ())
    a_rs = bd.ao(lambda e: e.copy(small[:, 8:11], psS2[:, 0:3]), waits=(("pe", pe_rs),))
    K(1.0); K(-1.0)
    for mi, (nm, rows_n, cols) in enumerate((("wg_s", I8, D), ("wu_s", I8, D), ("wd_s", E8, I))):
        CW = min(2048, cols)
        for rt in range(rows_n // 128):
            for ccc in range(cols // CW):
                c1 = bd.dma("gpsimd", wf32[:, 0:CW],
                            din[nm][rt * 128:(rt + 1) * 128, ccc * CW:(ccc + 1) * CW],
                            waits=(bd.w("dve"), bd.w("plq")), sem="s_wf")
                d3 = None
                spq_prev = bd.w("s_pq")
                for pc in range(CW // 1024):
                    bd.vo(lambda e, mi=mi, pc=pc: e.tensor_scalar(i32s[:, 0:1024],
                                                                  wf32[:, pc * 1024:(pc + 1) * 1024],
                                                                  small[:, 8 + mi:9 + mi], None, OP.mult),
                          waits=(("s_wf", c1), ("act", a_rs), spq_prev), drain=pc > 0)
                    d3 = bd.vo(lambda e, pc=pc: e.tensor_scalar(wtmp[:, pc * 1024:(pc + 1) * 1024],
                                                                i32s[:, 0:1024], 1.0, -1.0,
                                                                OP.min, OP.max), drain=True)
                if mi < 2:
                    bd.dma("gpsimd", tmp_gu[mi, rt * 128:(rt + 1) * 128, ccc * CW:(ccc + 1) * CW],
                           wtmp[:, 0:CW], waits=(("dve", d3),), sem="s_pq")
                else:
                    bd.dma("gpsimd", tmp_d[rt * 128:(rt + 1) * 128, ccc * CW:(ccc + 1) * CW],
                           wtmp[:, 0:CW], waits=(("dve", d3),), sem="s_pq")
        spq_mat.append(bd.cnt["s_pq"])
    plq_tmp = bd.cnt["s_pq"]
    wstg = bd.sb("wstg", (128, max(NFC * E8, NFC * I8)), BF16)
    wstgB = bd.sb("wstgB", (128, NICh * E8), BF16)
    for mi in range(4):
        for dc in range(NFC):
            bd.dmaT(wstg[:, dc * E8:(dc + 1) * E8], tmp_qkvo[mi, 0:E8, dc * 128:(dc + 1) * 128],
                    waits=((("s_pq", spq_qkvo),) if (mi == 0 and dc == 0) else ()) +
                          ((bd.w("spq"), bd.w("s_pq2")) if dc == 0 else ()))
        bd.dma("sync", wT_snd[OFF_M[mi]:OFF_M[mi] + PR_QKVO].rearrange("(pp c e) -> pp c e", pp=128, c=NFC),
               wstg[:, 0:NFC * E8].rearrange("pp (c e) -> pp c e", c=NFC), waits=(bd.w("spq"),), sem="s_pq2")
    for gi in range(2):
        for dc in range(NFC):
            bd.dmaT(wstg[:, dc * I8:(dc + 1) * I8], tmp_gu[gi, 0:I8, dc * 128:(dc + 1) * 128],
                    waits=((("s_pq", spq_mat[gi]), bd.w("spq"), bd.w("s_pq2"))
                           if dc == 0 else ()))
        for ghf in range(2):
            bd.dma("sync",
                   wT_snd[OFF_M[4 + gi] + ghf * (PR_GU // 2):OFF_M[4 + gi] + (ghf + 1) * (PR_GU // 2)]
                   .rearrange("(pp c e) -> pp c e", pp=128, c=NFC),
                   wstg[:, 0:NFC * I8].rearrange("pp (c e) -> pp c e", c=NFC)
                   [:, :, ghf * I8h:(ghf + 1) * I8h],
                   waits=(bd.w("spq"),), sem="s_pq2")
    for dhf in range(2):
        for icl in range(NICh):
            ic = dhf * NICh + icl
            bd.dmaT(wstgB[:, icl * E8:(icl + 1) * E8], tmp_d[0:E8, ic * 128:(ic + 1) * 128],
                    waits=((("s_pq", spq_mat[2]),) if ic == 0 else ()) +
                          ((bd.w("s_tp"),) if icl == 0 else ()), sem="s_tp", q="scalar")
        bd.dma("scalar", wT_snd[OFF_M[6] + dhf * (PR_D // 2):OFF_M[6] + (dhf + 1) * (PR_D // 2)]
               .rearrange("(pp c e) -> pp c e", pp=128, c=NICh),
               wstgB[:, 0:NICh * E8].rearrange("pp (c e) -> pp c e", c=NICh),
               waits=(bd.w("s_tp"),), sem="s_tp")
    sp_wt3 = bd.cnt["s_tp"]
    sp_wt = bd.cnt["s_pq2"]
    bd.dma("sync", srow[0:1, 16:16 + H], din["w_std"][:, :], waits=(INI,), sem="s_ins")
    bd.dma("sync", srow[0:1, 16 + H:16 + 2 * H], din["w_rec"][:, :], sem="s_ins")
    nsv = 1 + 2 * H
    sel_sb = bd.sb("sel_sb", (nsv, p.NFCF * 128), F32)
    rep_sb = bd.sb("rep_sb", (H, 128), F32)
    bd.dma("sync", sel_sb[:, :], din["sel"][:, :], sem="s_ins")
    bd.dma("sync", rep_sb[:, :], din["rep8"][:, :], sem="s_ins")
    sp_isb = bd.dma("sync", srow[0:1, 64:65], din["is_b"][:, :], sem="s_ins")
    K(1e-8); K(0.1); K(qk_quarter)
    du = bd.vo(lambda e: e.tensor_scalar(srow[0:1, 80:80 + H], srow[0:1, 16 + H:16 + 2 * H],
                                         0.1, None, OP.is_gt), waits=(("s_ins", sp_isb),))
    dstd = bd.vo(lambda e: e.tensor_scalar(srow[0:1, 16:16 + 2 * H], srow[0:1, 16:16 + 2 * H],
                                           1e-8, None, OP.max), drain=True)
    a_sq = bd.ao(lambda e: e.activation(srow[0:1, 16:16 + 2 * H], srow[0:1, 16:16 + 2 * H], AF.Sqrt),
                 waits=(("dve", dstd),))
    pe_sv = bd.mm(ps3[0:nsv, 0:1], srow[0:1, 15:16 + 2 * H], onesr_f[0:1, 0:1], True, True,
                  waits=(("act", a_sq), ("s_ins", sp_isb)))
    a_sv = bd.ao(lambda e: e.copy(small[0:nsv, 14:15], ps3[0:nsv, 0:1]), waits=(("pe", pe_sv),))
    pe_cs = None
    for c in range(p.NFCF):
        pe_cs = bd.mm(psS[:, c:c + 1], sel_sb[0:nsv, c * 128:(c + 1) * 128], small[0:nsv, 14:15],
                      True, True, waits=(("act", a_sv),) if c == 0 else ())
    a_cs = bd.ao(lambda e: e.copy(csvec[:, 0:p.NFCF], psS[:, 0:p.NFCF]), waits=(("pe", pe_cs),))
    pe_uc = bd.mm(ps3[0:H, 0:1], srow[0:1, 80:80 + H], onesr_f[0:1, 0:1], True, True,
                  waits=(("dve", du), ("act", a_sv)))
    a_uc = bd.ao(lambda e: e.copy(small[0:H, 15:16], ps3[0:H, 0:1]), waits=(("pe", pe_uc),))
    pe_u1 = bd.mm(psS2[:, 4:5], rep_sb[0:H, :], small[0:H, 15:16], True, True,
                  waits=(("act", a_uc),))
    a_u1 = bd.ao(lambda e: e.copy(u128[:, 0:1], psS2[:, 4:5]), waits=(("pe", pe_u1),))
    d_1mu = bd.vo(lambda e: e.tensor_scalar(u128[0:8 * H, 1:2], u128[0:8 * H, 0:1], -1.0, 1.0,
                                            OP.mult, OP.add), waits=(("act", a_u1),))
    pe_ib = bd.mm(psS2[:, 3:4], onesr_f[0:1, :], srow[0:1, 64:65], True, True,
                  waits=(("s_ins", sp_isb), ("act", a_rs)))
    a_ib = bd.ao(lambda e: e.copy(small[:, 12:13], psS2[:, 3:4]), waits=(("pe", pe_ib),))
    csvec_rdy = ("act", a_cs)
    plq_cs = None
    bd.free("wstg", "wstgB", "sel_sb", "rep_sb", "racc", "wf32", "wtmp")
    bd.fence(("gpsimd", "sync", "vector", "scalar"),
             (bd.w("pe"), bd.w("dve"), bd.w("s_pq"), bd.w("s_pq2"), bd.w("s_tp"), bd.w("s_in")))

    def slab_offsets(all_slabs):
        offs, o = {}, 0
        for (a, b) in all_slabs:
            offs[(a, b)] = o
            o += D * (b - a)
        return offs

    offs_qkv = slab_offsets(p.slabs_qkv_q + p.slabs_qkv_k + p.slabs_qkv_v)
    offs_wa = slab_offsets(p.slabs_wa_q + p.slabs_wa_k + p.slabs_wa_v + p.slabs_wa_t)
    offs_d = slab_offsets(p.slabs_d)

    wsA = bd.sb("wsA", (128, WSW), BF16)
    wsB = bd.sb("wsB", (128, WSW), BF16)
    dsb = bd.sb("dsb", (128, max(p.ktmax) * QTW), BF16)
    kst = bd.sb("kst", (128, 2 * NKT * 128), BF16)
    vst = bd.sb("vst", (128, 2 * NKT * 128), BF16)
    yall = bd.sb("yall", (128, H * QTW), BF16)
    qall = bd.sb("qall", (128, H * QTW), BF16)

    wg_guard = [0, 0]
    slab_i = [0]

    def load_slab(perm, offmap, slab, extra=()):
        i = slab_i[0] % 2
        slab_i[0] += 1
        buf = (wsA, wsB)[i]
        wdt = slab[1] - slab[0]
        c = bd.dma("gpsimd", buf[:, 0:NFC * wdt].rearrange("pp (c e) -> pp c e", c=NFC),
                   perm[offmap[slab]:offmap[slab] + D * wdt]
                   .rearrange("(pp c e) -> pp c e", pp=128, c=NFC),
                   waits=(("pe", wg_guard[i]),) + tuple(extra), sem=f"s_ws{i}")
        return buf, wdt, (f"s_ws{i}", c), i

    def done_slab(i):
        wg_guard[i] = bd.cnt["pe"]

    K(inv_sqrt_dh); K(-B_EXP); K(1.0); K(0.0)
    for s in slopes:
        K(s)
    dt_off = []
    o = 0
    for qt in range(NQT):
        dt_off.append(o)
        o += p.ktmax[qt]

    def scores_qt(layer, qt, cc_kv, kv_gout, use_alibi, sp_dt, qrdy=()):
        nkt = p.ktmax[qt]
        for h in range(H):
            kbase = (h % 2) * NKT * 128
            guard = st.get(("kvg", h % 2), 0)
            spkv = None
            for r in range(2):
                lo, hi = max(0, r * NTT), min(nkt, (r + 1) * NTT)
                if hi <= lo:
                    continue
                bd.dma("sync", kst[:, kbase + lo * 128:kbase + hi * 128],
                       kv_gout[r, 0, :, h * TOKC + (lo - r * NTT) * 128:h * TOKC + (hi - r * NTT) * 128],
                       waits=(("cc", cc_kv), ("pe", guard)), sem=f"s_kv{h % 2}")
                spkv = bd.dma("sync", vst[:, kbase + lo * 128:kbase + hi * 128]
                              .rearrange("pp (t c) -> pp t c", c=128),
                              kv_gout[r, 1, :, :].rearrange("pp (t f) -> pp t f", f=D)
                              [:, (lo - r * NTT):(hi - r * NTT), h * 128:(h + 1) * 128],
                              sem=f"s_kv{h % 2}")
            for kt in range(nkt):
                sb = pp[kt % 2]
                pe_s = bd.mm(sb[:, 0:QTW], kst[:, kbase + kt * 128:kbase + (kt + 1) * 128],
                             qall[:, h * QTW:(h + 1) * QTW], True, True,
                             waits=(((f"s_kv{h % 2}", spkv),) + tuple(qrdy) if kt == 0 else ()) +
                                   (st.get(("bk", kt % 2), ("dve", 0)),))
                slot = stg[:, (kt % 2) * QTW:(kt % 2) * QTW + QTW]
                sc = slopes[h] if use_alibi else 1.0
                dv_s = bd.vo(lambda e, sl=slot, db=dsb[:, kt * QTW:(kt + 1) * QTW], sb_=sb, sc=sc:
                             e.scalar_tensor_tensor(sl, db, sc, sb_[:, 0:QTW], OP.mult, OP.add),
                             waits=(("pe", pe_s),) + ((sp_dt,) if kt == 0 else ()))
                st[("bk", kt % 2)] = ("dve", dv_s)
                pslot = pbuf[:, (kt % 2) * QTW:(kt % 2) * QTW + QTW]
                ac_p = bd.ao(lambda e, pl=pslot, sl=slot: e.activation(pl, sl, AF.Exp,
                                                                       bias=-B_EXP, scale=1.0),
                             waits=(("dve", dv_s),))
                bd.mm(psS[0:1, 0:QTW], ones_b[:, :], pslot, kt == 0, kt == nkt - 1,
                      waits=(("act", ac_p),) + ((("dve", st["ydve"]),) if kt == 0 else ()))
                bd.mm(ps2[:, 0:QTW], vst[:, kbase + kt * 128:kbase + (kt + 1) * 128], pslot,
                      kt == 0, kt == nkt - 1)
            st[("kvg", h % 2)] = bd.cnt["pe"]
            pe_fin = bd.cnt["pe"]
            dv_r = bd.vo(lambda e: e.reciprocal(rowm[0:1, 0:QTW], psS[0:1, 0:QTW]),
                         waits=(("pe", pe_fin), ("pe", st["rb_pe"])))
            st["rb_pe"] = bd.mm(ps3[:, 0:QTW], onesr_f[0:1, :], rowm[0:1, 0:QTW], True, True,
                                waits=(("dve", dv_r), ("dve", st["ydve"])))
            a_rb = bd.ao(lambda e: e.copy(rbv[:, 0:QTW], ps3[:, 0:QTW]),
                         waits=(("pe", st["rb_pe"]),))
            st["ydve"] = bd.vo(lambda e, dst=yall[:, h * QTW:(h + 1) * QTW]:
                               e.tensor_tensor(dst, ps2[:, 0:QTW], rbv[:, 0:QTW], OP.mult),
                               waits=(("act", a_rb),))

    def layer_norm(toff, w):
        K(1.0 / D); K(LN_EPS)
        for fc in range(NFC):
            a_sq2 = bd.ao(lambda e, fc=fc, sl=hsq[:, (fc % 2) * QTW:(fc % 2) * QTW + w]:
                          e.activation(sl, hb[:, fc * QTW:fc * QTW + w], AF.Square),
                          waits=(("dve", st.get(("og", fc % 2), 0)), ("pe", st.get(("lq", fc % 2), 0))))
            bd.mm(psS[0:1, 0:w], ones_b[:, :], hb[:, fc * QTW:fc * QTW + w], fc == 0, fc == NFC - 1,
                  waits=(("dve", st["lnprev"]), ("dve", st.get(("og", 0), 0)),
                         ("dve", st.get(("og", 1), 0))) if fc == 0 else ())
            st[("lq", fc % 2)] = bd.mm(psS2[0:1, 0:w], ones_b[:, :],
                                       hsq[:, (fc % 2) * QTW:(fc % 2) * QTW + w],
                                       fc == 0, fc == NFC - 1, waits=(("act", a_sq2),))
        pe_sums = bd.cnt["pe"]
        bd.vo(lambda e: e.tensor_scalar(rows[0:1, 0:w], psS[0:1, 0:w], 1.0 / D, None, OP.mult),
              waits=(("pe", pe_sums),))
        bd.vo(lambda e: e.tensor_scalar(rows[0:1, QTW:QTW + w], psS2[0:1, 0:w], 1.0 / D, None, OP.mult))
        bd.vo(lambda e: e.tensor_tensor(rows[0:1, 2 * QTW:2 * QTW + w], rows[0:1, 0:w],
                                        rows[0:1, 0:w], OP.mult), drain=True)
        d4 = bd.vo(lambda e: e.scalar_tensor_tensor(rows[0:1, QTW:QTW + w], rows[0:1, QTW:QTW + w],
                                                    LN_EPS, rows[0:1, 2 * QTW:2 * QTW + w],
                                                    OP.add, OP.subtract), drain=True)
        a1 = bd.ao(lambda e: e.activation(rows[0:1, QTW:QTW + w], rows[0:1, QTW:QTW + w], AF.Sqrt),
                   waits=(("dve", d4),))
        bd.vo(lambda e: e.reciprocal(rowm[0:1, 0:w], rows[0:1, QTW:QTW + w]),
              waits=(("act", a1), ("pe", st["ab_pe"]), ("pe", st["rb_pe"])))
        d6 = bd.vo(lambda e: e.scalar_tensor_tensor(rowm[0:1, QTW:QTW + w], rows[0:1, 0:w],
                                                    -1.0, rowm[0:1, 0:w], OP.mult, OP.mult),
                   drain=True)
        pe_a = bd.mm(ps3[:, 0:w], onesr_f[0:1, :], rowm[0:1, 0:w], True, True,
                     waits=(("dve", d6), ("dve", st["ydve"])))
        pe_b = bd.mm(psS[:, 0:w], onesr_f[0:1, :], rowm[0:1, QTW:QTW + w], True, True)
        st["ab_pe"] = pe_b
        st["rb_pe"] = pe_b
        for fc in range(NFC):
            sl = stg2[:, (fc % 2) * QTW:(fc % 2) * QTW + w]
            bd.vo(lambda e, sl=sl, fc=fc: e.tensor_tensor(sl, hb[:, fc * QTW:fc * QTW + w],
                                                          ps3[:, 0:w], OP.mult),
                  waits=(("pe", pe_b),) if fc == 0 else (), drain=fc >= 2)
            st["lnprev"] = bd.vo(
                lambda e, sl=sl, dst=xs[:, fc * TOKC + toff:fc * TOKC + toff + w]:
                e.tensor_tensor(dst, sl, psS[:, 0:w], OP.add), drain=True)

    def out_proj_ln(qt, perm, offmap, slabs, res_scale):
        dve_y = st["ydve"]
        for slab in slabs:
            buf, wdt, cld, bi = load_slab(perm, offmap, slab)
            for fcl in range(wdt // 128):
                fc = (slab[0] + fcl * 128) // 128
                sb = pp[fc % 2]
                for hc in range(H):
                    bd.mm(sb[:, 0:QTW], buf[:, hc * wdt + fcl * 128:hc * wdt + fcl * 128 + 128],
                          yall[:, hc * QTW:(hc + 1) * QTW], hc == 0, hc == H - 1,
                          waits=((cld, ("dve", dve_y),
                                  st.get(("bk", fc % 2), ("dve", 0))) if hc == 0 else ()))
                pe_m = bd.cnt["pe"]
                dog = bd.vo(
                    lambda e, sb=sb, fc=fc,
                    xsr=xs[:, fc * TOKC + qt * QTW:fc * TOKC + qt * QTW + QTW],
                    dst=hb[:, fc * QTW:(fc + 1) * QTW], rs=res_scale:
                    e.scalar_tensor_tensor(dst, sb[:, 0:QTW], rs, xsr, OP.mult, OP.add),
                    waits=(("pe", pe_m),))
                st[("og", fc % 2)] = dog
                st[("bk", fc % 2)] = ("dve", dog)
            done_slab(bi)
        layer_norm(qt * QTW, QTW)

    for slab in p.slabs_qkv_k:
        buf, wdt, cld, bi = load_slab(din["wqkv_p"], offs_qkv, slab, extra=(("act", act_x),))
        for hc in range(wdt // 128):
            h = (slab[0] - D + hc * 128) // 128
            for tg in range(NQT):
                m_ = (hc * NQT + tg) % 2
                sb = pp[m_]
                for dc in range(NFC):
                    bd.mm(sb[:, 0:QTW], buf[:, dc * wdt + hc * 128:dc * wdt + hc * 128 + 128],
                          xs[:, dc * TOKC + tg * QTW:dc * TOKC + tg * QTW + QTW],
                          dc == 0, dc == NFC - 1,
                          waits=((cld, (f"s_evb{m_}", st.get(("evb", m_), 0))) if dc == 0 else ()))
                pe_m = bd.cnt["pe"]
                a_e = bd.ao(lambda e, sb=sb, m_=m_: e.copy(evb[:, m_ * 512:m_ * 512 + QTW],
                                                           sb[:, 0:QTW]), waits=(("pe", pe_m),))
                st[("evb", m_)] = bd.dma(
                    "gpsimd", kv_snd[0][0, :, h * TOKC + tg * QTW:h * TOKC + (tg + 1) * QTW],
                    evb[:, m_ * 512:m_ * 512 + QTW], waits=(("act", a_e),),
                    sem=f"s_evb{m_}")
        done_slab(bi)
    for slab in p.slabs_qkv_v:
        buf, wdt, cld, bi = load_slab(din["wqkv_p"], offs_qkv, slab)
        for tt in range(NTT):
            for wo_ in range(0, wdt, 512):
                ww = min(512, wdt - wo_)
                sb = pp[tt % 2]
                for dc in range(NFC):
                    bd.mm(sb[:, 0:ww], xs[:, dc * TOKC + tt * 128:dc * TOKC + tt * 128 + 128],
                          buf[:, dc * wdt + wo_:dc * wdt + wo_ + ww], dc == 0, dc == NFC - 1,
                          waits=((cld, (f"s_evb{tt % 2}", st.get(("evb", tt % 2), 0)),
                                  st.get(("bk", tt % 2), ("dve", 0))) if dc == 0 else ()))
                pe_m = bd.cnt["pe"]
                a_e = bd.ao(lambda e, sb=sb, tt=tt, ww=ww:
                            e.copy(evb[:, (tt % 2) * 512:(tt % 2) * 512 + ww], sb[:, 0:ww]),
                            waits=(("pe", pe_m),))
                st[("bk", tt % 2)] = ("act", a_e)
                st[("evb", tt % 2)] = bd.dma(
                    "gpsimd", kv_snd[0][1, :, tt * D + (slab[0] - 2 * D) + wo_:
                                       tt * D + (slab[0] - 2 * D) + wo_ + ww],
                    evb[:, (tt % 2) * 512:(tt % 2) * 512 + ww], waits=(("act", a_e),),
                    sem=f"s_evb{tt % 2}")
        done_slab(bi)
    cc_kv0 = bd.coll("AllGather", OP.bypass, groups_pair,
                     [kv_snd[0].ap().opt()], [kv_out[0].ap().opt()],
                     waits=(bd.w("s_evb0"), bd.w("s_evb1")))
    cc_wT = bd.coll("AllGather", OP.bypass, groups_all, [wT_snd.ap().opt()], [wT_out.ap().opt()],
                    waits=(("s_pq2", sp_wt), ("s_tp", sp_wt3)))
    for qt in range(NQT):
        sp_dt = ("s_dsb", bd.dma("sync", dsb[:, 0:p.ktmax[qt] * QTW].rearrange("pp (n w) -> pp n w", w=QTW),
                                 din["dA"][dt_off[qt]:dt_off[qt] + p.ktmax[qt], :, :]
                                 .rearrange("n pp w -> pp n w"), waits=(bd.w("dve"),), sem="s_dsb"))
        for slab in p.slabs_qkv_q:
            buf, wdt, cld, bi = load_slab(din["wqkv_p"], offs_qkv, slab)
            for hc in range(wdt // 128):
                h = (slab[0] + hc * 128) // 128
                sb = pp[hc % 2]
                for dc in range(NFC):
                    bd.mm(sb[:, 0:QTW], buf[:, dc * wdt + hc * 128:dc * wdt + hc * 128 + 128],
                          xs[:, dc * TOKC + qt * QTW:dc * TOKC + qt * QTW + QTW],
                          dc == 0, dc == NFC - 1,
                          waits=((cld, st.get(("bk", hc % 2), ("dve", 0))) if dc == 0 else ()))
                pe_m = bd.cnt["pe"]
                a_q = bd.ao(
                    lambda e, sb=sb, h=h: e.activation(qall[:, h * QTW:(h + 1) * QTW], sb[:, 0:QTW],
                                                       AF.Copy, bias=0.0, scale=inv_sqrt_dh),
                    waits=(("pe", pe_m),))
                st[("bk", hc % 2)] = ("act", a_q)
            done_slab(bi)
        scores_qt(0, qt, cc_kv0, kv_out[0], True, sp_dt, qrdy=(bd.w("act"),))
        out_proj_ln(qt, din["wo_p"], offs_d, p.slabs_d, 1.0)
    st["xrdy"] = bd.cnt["dve"]
    if debug:
        bd.dma("gpsimd", dbg[0][:, :], xs[:, :], waits=(("dve", st["xrdy"]),), sem="s_msc")

    fst = bd.sb("fst", (128, 4 * QTW), BF16)
    NTC = 16 * H // 128
    tls = bd.sb("tls", (128, max(1, NTC) * TOKC), BF16)
    tb = bd.sb("tb", (128, 4 * TOKC), BF16)

    def head_pieces(base):
        out = {}
        for h in range(H):
            c0, c1 = base + h * p.DSTD, base + (h + 1) * p.DSTD
            ps_ = []
            c = c0
            while c < c1:
                nxt = min(c1, (c // 128 + 1) * 128)
                ps_.append((c, nxt - c))
                c = nxt
            out[h] = ps_
        return out

    qpieces = head_pieces(0)
    kpieces = head_pieces(p.qs_end)
    slotc = [0]

    def fused_chunk(buf, wdt, slab0, cld, gcol0, tg, slot):
        cc = gcol0 // 128
        sb = pp[slot % 2]
        for dc in range(NFC):
            bd.mm(sb[:, 0:QTW], buf[:, dc * wdt + (gcol0 - slab0):dc * wdt + (gcol0 - slab0) + 128],
                  xs[:, dc * TOKC + tg * QTW:dc * TOKC + tg * QTW + QTW],
                  dc == 0, dc == NFC - 1,
                  waits=((cld, (f"s_fst{slot}", st.get(("fe", slot), 0)),
                          csvec_rdy, ("dve", st.get("xrdy", 0)),
                          st.get(("bk", slot % 2), ("dve", 0))) if dc == 0 else ()))
        pe_m = bd.cnt["pe"]
        dv = bd.vo(lambda e, sb=sb, cc=cc, sl=fst[:, slot * QTW:(slot + 1) * QTW]:
                   e.tensor_scalar(sl, sb[:, 0:QTW], csvec[:, cc:cc + 1], None, OP.mult),
                   waits=(("pe", pe_m),))
        st[("bk", slot % 2)] = ("dve", dv)
        return dv

    for slabs, which in ((p.slabs_wa_k, "k"), (p.slabs_wa_t, "t")):
        for slab in slabs:
            buf, wdt, cld, bi = load_slab(din["wa_p"], offs_wa, slab)
            for gcol0 in range(slab[0], slab[1], 128):
                for tg in range(NQT):
                    slot = slotc[0] % 4
                    slotc[0] += 1
                    dv = fused_chunk(buf, wdt, slab[0], cld, gcol0, tg, slot)
                    if which == "k":
                        for h in range(H):
                            for (c0, ln) in kpieces[h]:
                                if c0 // 128 != gcol0 // 128:
                                    continue
                                po = c0 % 128
                                ro = c0 - (p.qs_end + h * p.DSTD)
                                st[("fe", slot)] = bd.dma(
                                    "gpsimd",
                                    kv_snd[1][0, ro:ro + ln, h * TOKC + tg * QTW:h * TOKC + (tg + 1) * QTW],
                                    fst[po:po + ln, slot * QTW:(slot + 1) * QTW],
                                    waits=(("dve", dv),), sem=f"s_fst{slot}")
                    else:
                        ci = (gcol0 - p.v_end) // 128
                        st[("fe", slot)] = bd.dma(
                            "gpsimd", tls[:, ci * TOKC + tg * QTW:ci * TOKC + (tg + 1) * QTW],
                            fst[:, slot * QTW:(slot + 1) * QTW], waits=(("dve", dv),), sem=f"s_fst{slot}")
            done_slab(bi)
    nt8 = 8 * H
    khi_ci, khi_po = (p.ql_end - p.v_end) // 128, (p.ql_end - p.v_end) % 128
    c5 = bd.dma("gpsimd", tb[0:nt8, 0:TOKC], tls[0:nt8, 0:TOKC],
                waits=(bd.w("s_fst0"), bd.w("s_fst1"), bd.w("s_fst2"), bd.w("s_fst3")), sem="s_msc")
    c6 = bd.dma("gpsimd", tb[0:nt8, TOKC:2 * TOKC],
                tls[khi_po:khi_po + nt8, khi_ci * TOKC:(khi_ci + 1) * TOKC], sem="s_msc")
    dvq = None
    for tg in range(NQT):
        a_, b_ = tg * QTW, tg * QTW + QTW
        bd.vo(lambda e, a_=a_, b_=b_: e.tensor_scalar(stg[0:nt8, 0:QTW], tb[0:nt8, a_:b_],
                                                      u128[0:nt8, 1:2], None, OP.mult),
              waits=(("s_msc", c6), ("dve", d_1mu)), drain=True)
        bd.vo(lambda e, a_=a_, b_=b_: e.tensor_scalar(stg2[0:nt8, 0:QTW], tb[0:nt8, TOKC + a_:TOKC + b_],
                                                      u128[0:nt8, 0:1], None, OP.mult))
        dvq = bd.vo(lambda e, a_=a_, b_=b_: e.tensor_tensor(tb[0:nt8, 2 * TOKC + a_:2 * TOKC + b_],
                                                            stg[0:nt8, 0:QTW], stg2[0:nt8, 0:QTW], OP.add),
                    drain=True)
        bd.vo(lambda e, a_=a_, b_=b_: e.tensor_scalar(stg[0:nt8, 0:QTW], tb[0:nt8, a_:b_],
                                                      u128[0:nt8, 0:1], None, OP.mult), drain=True)
        bd.vo(lambda e, a_=a_, b_=b_: e.tensor_scalar(stg2[0:nt8, 0:QTW], tb[0:nt8, TOKC + a_:TOKC + b_],
                                                      u128[0:nt8, 1:2], None, OP.mult))
        dvq = bd.vo(lambda e, a_=a_, b_=b_: e.tensor_tensor(tb[0:nt8, 3 * TOKC + a_:3 * TOKC + b_],
                                                            stg[0:nt8, 0:QTW], stg2[0:nt8, 0:QTW], OP.add),
                    drain=True)
    for h in range(H):
        bd.dma("gpsimd", kv_snd[1][0, 120:128, h * TOKC:(h + 1) * TOKC],
               tb[8 * h:8 * h + 8, 3 * TOKC:4 * TOKC], waits=(("dve", dvq),), sem="s_tl")
    for slab in p.slabs_wa_v:
        buf, wdt, cld, bi = load_slab(din["wa_p"], offs_wa, slab)
        for tt in range(NTT):
            for wo_ in range(0, wdt, 512):
                ww = min(512, wdt - wo_)
                sb = pp[tt % 2]
                for dc in range(NFC):
                    bd.mm(sb[:, 0:ww], xs[:, dc * TOKC + tt * 128:dc * TOKC + tt * 128 + 128],
                          buf[:, dc * wdt + wo_:dc * wdt + wo_ + ww], dc == 0, dc == NFC - 1,
                          waits=((cld, (f"s_evb{tt % 2}", st.get(("evb", tt % 2), 0)),
                                  ("dve", st.get("xrdy", 0)),
                                  st.get(("bk", tt % 2), ("dve", 0))) if dc == 0 else ()))
                pe_m = bd.cnt["pe"]
                a_e = bd.ao(lambda e, sb=sb, tt=tt, ww=ww:
                            e.copy(evb[:, (tt % 2) * 512:(tt % 2) * 512 + ww], sb[:, 0:ww]),
                            waits=(("pe", pe_m),))
                st[("bk", tt % 2)] = ("act", a_e)
                st[("evb", tt % 2)] = bd.dma(
                    "gpsimd", kv_snd[1][1, :, tt * D + (slab[0] - p.ks_end) + wo_:
                                       tt * D + (slab[0] - p.ks_end) + wo_ + ww],
                    evb[:, (tt % 2) * 512:(tt % 2) * 512 + ww], waits=(("act", a_e),),
                    sem=f"s_evb{tt % 2}")
        done_slab(bi)
    cc_kv1 = bd.coll("AllGather", OP.bypass, groups_pair,
                     [kv_snd[1].ap().opt()], [kv_out[1].ap().opt()],
                     waits=(bd.w("s_evb0"), bd.w("s_evb1"), bd.w("s_fst0"), bd.w("s_fst1"),
                            bd.w("s_fst2"), bd.w("s_fst3"), bd.w("s_tl")))
    for qt in range(NQT):
        sp_dt = ("s_dsb", bd.dma("sync", dsb[:, 0:p.ktmax[qt] * QTW].rearrange("pp (n w) -> pp n w", w=QTW),
                                 din["dM"][dt_off[qt]:dt_off[qt] + p.ktmax[qt], :, :]
                                 .rearrange("n pp w -> pp n w"), waits=(bd.w("dve"),), sem="s_dsb"))
        for slab in p.slabs_wa_q:
            buf, wdt, cld, bi = load_slab(din["wa_p"], offs_wa, slab)
            for gcol0 in range(slab[0], slab[1], 128):
                slot = slotc[0] % 4
                slotc[0] += 1
                dv = fused_chunk(buf, wdt, slab[0], cld, gcol0, qt, slot)
                for h in range(H):
                    for (c0, ln) in qpieces[h]:
                        if c0 // 128 != gcol0 // 128:
                            continue
                        po = c0 % 128
                        ro = c0 - h * p.DSTD
                        st[("fe", slot)] = bd.dma(
                            "gpsimd", qall[ro:ro + ln, h * QTW:(h + 1) * QTW],
                            fst[po:po + ln, slot * QTW:(slot + 1) * QTW], waits=(("dve", dv),),
                            sem=f"s_fst{slot}")
            done_slab(bi)
        for h in range(H):
            bd.dma("gpsimd", qall[120:128, h * QTW:(h + 1) * QTW],
                   tb[8 * h:8 * h + 8, 2 * TOKC + qt * QTW:2 * TOKC + (qt + 1) * QTW],
                   waits=(("dve", dvq),), sem="s_tl")
        scores_qt(1, qt, cc_kv1, kv_out[1], False, sp_dt,
                  qrdy=(bd.w("s_fst0"), bd.w("s_fst1"), bd.w("s_fst2"), bd.w("s_fst3"), bd.w("s_tl")))
        out_proj_ln(qt, din["wp_p"], offs_d, p.slabs_d, 1.0)
    st["xrdy"] = bd.cnt["dve"]
    if debug:
        bd.dma("gpsimd", dbg[1][:, :], xs[:, :], waits=(("dve", st["xrdy"]),), sem="s_msc")
    bd.free("fst", "tls", "tb", "wsA", "wsB", "dsb", "kst", "vst", "yall", "qall")
    bd.fence(("gpsimd", "sync", "vector"),
             (bd.w("pe"), bd.w("dve"), bd.w("s_fst0"), bd.w("s_fst1"), bd.w("s_fst2"),
              bd.w("s_fst3"), bd.w("s_tl"), bd.w("s_msc")))

    kvt = bd.sb("kvt", (128, NFC * TOKC), F32)
    vh = bd.sb("vh", (128, NFC * QTW), BF16)
    carr = bd.sb("carr", (128, 2 * NFC), F32)
    offa = bd.sb("offa", (128, NFC), F32)
    w1A = bd.sb("w1A", (128, NFC * E8), BF16)
    w1B = bd.sb("w1B", (128, NFC * E8), BF16)
    w1C = bd.sb("w1C", (128, NFC * E8), BF16)
    w1g = {0: 0, 1: 0, 2: 0}
    bd.emit("gpsimd", lambda e: e.memset(carr[:, 0:NFC], 0.0), inc="ini")

    def load_w1(mi, r, bufi, extra=()):
        buf = (w1A, w1B, w1C)[bufi]
        c = bd.dma("sync", buf[:, 0:NFC * E8].rearrange("pp (c e) -> pp c e", c=NFC),
                   wT_out[r, OFF_M[mi]:OFF_M[mi] + PR_QKVO]
                   .rearrange("(pp c e) -> pp c e", pp=128, c=NFC),
                   waits=(("cc", cc_wT), ("pe", w1g[bufi])) + tuple(extra), sem=f"s_w1{bufi}")
        return buf, (f"s_w1{bufi}", c)

    for r in range(NC):
        bk, ck = load_w1(1, r, 0)
        bv, cv = load_w1(2, r, 1)
        for el in range(E2):
            e_g = r * E2 + el
            for qt in range(NQT):
                for dc in range(NFC):
                    bd.mm(ps0[:, 0:QTW], bk[:, dc * E8 + el * 128:dc * E8 + el * 128 + 128],
                          xs[:, dc * TOKC + qt * QTW:dc * TOKC + qt * QTW + QTW],
                          dc == 0, dc == NFC - 1,
                          waits=((ck, ("dve", st.get("kvd", 0)), ("dve", st.get("xrdy", 0)),
                                  st.get(("bk", 0), ("dve", 0))) if dc == 0 else ()))
                for dc in range(NFC):
                    bd.mm(ps1[:, 0:QTW], bv[:, dc * E8 + el * 128:dc * E8 + el * 128 + 128],
                          xs[:, dc * TOKC + qt * QTW:dc * TOKC + qt * QTW + QTW],
                          dc == 0, dc == NFC - 1,
                          waits=((cv, st.get(("bk", 1), ("dve", 0))) if dc == 0 else ()))
                pe_m = bd.cnt["pe"]
                a_kc = bd.ao(lambda e: e.copy(kvsb[:, 0:QTW], ps0[:, 0:QTW]),
                             waits=(("pe", pe_m), ("dve", st.get("kvd", 0))))
                dkv = bd.vo(lambda e: e.tensor_tensor(stg[:, 0:QTW], kvsb[:, 0:QTW], ps1[:, 0:QTW], OP.mult),
                            waits=(("act", a_kc), ("ini", 2)))
                st[("bk", 0)] = ("dve", dkv)
                st[("bk", 1)] = ("dve", dkv)
                bd.vo(lambda e, e_g=e_g, qt=qt:
                      e.tensor_tensor_scan(kvt[:, e_g * TOKC + qt * QTW:e_g * TOKC + (qt + 1) * QTW],
                                           stg[:, 0:QTW], stg[:, 0:QTW],
                                           carr[:, e_g:e_g + 1], OP.add, OP.bypass), drain=True)
                st["kvd"] = bd.vo(lambda e, e_g=e_g, qt=qt:
                                  e.tensor_copy(carr[:, e_g:e_g + 1],
                                                kvt[:, e_g * TOKC + (qt + 1) * QTW - 1:
                                                    e_g * TOKC + (qt + 1) * QTW]), drain=True)
        w1g[0] = bd.cnt["pe"]
        w1g[1] = bd.cnt["pe"]
    c7 = bd.dma("gpsimd", off_snd[:, :], carr[:, 0:NFC], waits=(("dve", st["kvd"]),), sem="s_msc")
    cc_off = bd.coll("AllGather", OP.bypass, groups_pair,
                     [off_snd.ap().opt()], [off_out.ap().opt()], waits=(("s_msc", c7),))
    c8 = bd.dma("gpsimd", carr[:, NFC:2 * NFC], off_out[0, :, :], waits=(("cc", cc_off),), sem="s_msc")
    dvo = bd.vo(lambda e: e.tensor_scalar(offa[:, 0:NFC], carr[:, NFC:2 * NFC], small[:, 12:13],
                                          None, OP.mult), waits=(("s_msc", c8), ("act", a_ib)))
    K(denom_l1)
    for qt in range(NQT):
        for r in range(NC):
            bq, cq = load_w1(0, r, r % 2)
            for el in range(E2):
                e_g = r * E2 + el
                for dc in range(NFC):
                    bd.mm(ps0[:, 0:QTW], bq[:, dc * E8 + el * 128:dc * E8 + el * 128 + 128],
                          xs[:, dc * TOKC + qt * QTW:dc * TOKC + qt * QTW + QTW],
                          dc == 0, dc == NFC - 1,
                          waits=((cq, ("dve", st.get("vhd", 0)), ("dve", st.get("xrdy", 0)),
                                  st.get(("bk", 0), ("dve", 0))) if dc == 0 else ()))
                pe_m = bd.cnt["pe"]
                st["vhd"] = bd.vo(
                    lambda e, e_g=e_g, qt=qt:
                    e.scalar_tensor_tensor(vh[:, e_g * QTW:(e_g + 1) * QTW],
                                           kvt[:, e_g * TOKC + qt * QTW:e_g * TOKC + (qt + 1) * QTW],
                                           offa[:, e_g:e_g + 1], ps0[:, 0:QTW], OP.add, OP.mult),
                    waits=(("pe", pe_m), ("dve", dvo)))
                st[("bk", 0)] = ("dve", st["vhd"])
            w1g[r % 2] = bd.cnt["pe"]
        for r in range(NC):
            bo, co = load_w1(3, r, 2)
            for fl in range(E2):
                fc = r * E2 + fl
                sb = pp[fc % 2]
                for ec in range(NFC):
                    bd.mm(sb[:, 0:QTW], bo[:, ec * E8 + fl * 128:ec * E8 + fl * 128 + 128],
                          vh[:, ec * QTW:(ec + 1) * QTW], ec == 0, ec == NFC - 1,
                          waits=((co, st.get(("bk", fc % 2), ("dve", 0)),
                                  ("dve", st["vhd"])) if ec == 0 else ()))
                pe_m = bd.cnt["pe"]
                dog = bd.vo(
                    lambda e, sb=sb, fc=fc,
                    xsr=xs[:, fc * TOKC + qt * QTW:fc * TOKC + qt * QTW + QTW],
                    dst=hb[:, fc * QTW:(fc + 1) * QTW]:
                    e.scalar_tensor_tensor(dst, sb[:, 0:QTW], denom_l1, xsr, OP.mult, OP.add),
                    waits=(("pe", pe_m),))
                st[("og", fc % 2)] = dog
                st[("bk", fc % 2)] = ("dve", dog)
            w1g[2] = bd.cnt["pe"]
        layer_norm(qt * QTW, QTW)
    st["xrdy"] = bd.cnt["dve"]
    if debug:
        bd.dma("gpsimd", dbg[2][:, :], xs[:, :], waits=(("dve", st["xrdy"]),), sem="s_msc")
    bd.free("kvt", "vh", "carr", "offa", "w1A", "w1B", "w1C")
    bd.fence(("sync", "vector", "scalar"),
             (bd.w("pe"), bd.w("dve"), bd.w("s_msc"),
              bd.w("s_w10"), bd.w("s_w11"), bd.w("s_w12")))

    if debug:
        dcap0 = bd.vo(lambda e: e.tensor_copy(dbgsb[:, SQ:2 * SQ], xs[:, 0:SQ]), drain=True)
        bd.dma("gpsimd", dbg_cap[:, SQ:2 * SQ], dbgsb[:, SQ:2 * SQ],
               waits=(("dve", dcap0),), sem="s_msc")
    wsA = bd.sb("wsA2", (128, WSW), BF16)
    wsB = bd.sb("wsB2", (128, WSW), BF16)
    xq = bd.sb("xq", (128, NFC * SQ), BF16)
    gsi = bd.sb("gsi", (128, I2h * SQ), BF16)
    gu = bd.sb("gu", (128, NIC * SQ), BF16)
    mfold = bd.sb("mfold", (128, 2 * SQ), BF16)
    wgg = [0, 0]
    K(1.0 / 127.0)

    def colmax_to_row(src_fn, nch, row_off):
        dvm = None
        for i in range(nch):
            if i == 0:
                bd.vo(lambda e, s=src_fn(0): e.tensor_copy(mfold[:, 0:SQ], s), drain=True)
                dvm = bd.vo(lambda e, s=src_fn(0): e.tensor_copy(mfold[:, SQ:2 * SQ], s))
            else:
                bd.vo(lambda e, s=src_fn(i): e.tensor_tensor(mfold[:, 0:SQ], mfold[:, 0:SQ],
                                                             s, OP.max), drain=True)
                dvm = bd.vo(lambda e, s=src_fn(i): e.tensor_tensor(mfold[:, SQ:2 * SQ],
                                                                   mfold[:, SQ:2 * SQ], s, OP.min),
                            drain=True)
        bd.vo(lambda e: e.tensor_scalar(mfold[:, SQ:2 * SQ], mfold[:, SQ:2 * SQ], -1.0, None, OP.mult),
              drain=True)
        dvm = bd.vo(lambda e: e.tensor_tensor(mfold[:, 0:SQ], mfold[:, 0:SQ], mfold[:, SQ:2 * SQ],
                                              OP.max), drain=True)

        for j in range(SQ // 128):
            t = ptr[j % 2]
            pe_t = bd.mm(t[:, 0:128], mfold[:, j * 128:(j + 1) * 128], ident_b[:, :], True, True,
                         waits=(("dve", dvm), ("dve", st.get(("cmr", j % 2), 0)),
                                ("act", st.get(("xb", j % 2), 0))))
            st[("cmr", j % 2)] = bd.vo(
                lambda e, t=t, j=j: e.reduce_max(small[:, 16 + j:17 + j], t[:, 0:128],
                                                 axis=mybir.AxisListType.X, apply_absolute_value=True),
                waits=(("pe", pe_t),))
        dvm2 = bd.cnt["dve"]
        c0_ = bd.dma("gpsimd", sm_dram[:, 0:SQ // 128], small[:, 16:16 + SQ // 128],
                     waits=(("dve", dvm2),))
        c = bd.dma("gpsimd", rows[0:1, row_off:row_off + SQ].rearrange("o (j pp) -> o j pp", pp=128),
                   sm_dram[:, 0:SQ // 128].rearrange("(o pp) j -> o j pp", o=1),
                   waits=(("plq", c0_), bd.w("s_msc")), allow_slow_non_contiguous=True, sem="s_cm")
        dts = bd.vo(lambda e: e.tensor_scalar(rows[0:1, row_off:row_off + SQ],
                                              rows[0:1, row_off:row_off + SQ],
                                              1.0 / 127.0, Q_EPS, OP.mult, OP.add),
                    waits=(("s_cm", c),), drain=True)
        if debug and st.get("cm_calls", 0) == 0:
            dcapB = bd.vo(lambda e: e.tensor_copy(dbgsb[:, SQ:2 * SQ], xs[:, 0:SQ]),
                          waits=(bd.w("s_msc"),), drain=True)
            bd.dma("gpsimd", dbg_cap[:, 3 * SQ:4 * SQ], dbgsb[:, SQ:2 * SQ],
                   waits=(("dve", dcapB),), sem="s_msc")
        st["cm_calls"] = st.get("cm_calls", 0) + 1
        d2_ = bd.vo(lambda e: e.reciprocal(rowm[0:1, 0:SQ], rows[0:1, row_off:row_off + SQ]),
                    waits=(("pe", st["rb_pe"]),), drain=True)
        return d2_

    for qs in range(NSQ):
        toff = qs * SQ
        if qs > 0:
            bd.fence(("vector", "scalar"), (bd.w("pe"), bd.w("s_msc")))
        d2 = colmax_to_row(lambda i: xs[:, i * TOKC + toff:i * TOKC + toff + SQ], NFC, 3 * QTW)
        if debug and qs == 0:
            dcapC = bd.vo(lambda e: e.tensor_copy(dbgsb[:, SQ:2 * SQ], xs[:, 0:SQ]),
                          waits=(bd.w("s_msc"),), drain=True)
            bd.dma("gpsimd", dbg_cap[:, 2 * SQ:3 * SQ], dbgsb[:, SQ:2 * SQ],
                   waits=(("dve", dcapC),), sem="s_msc")
        st["rb_pe"] = pe_rbx = bd.mm(ps3[:, 0:SQ], onesr_f[0:1, :], rowm[0:1, 0:SQ], True, True,
                                     waits=(("dve", d2), ("dve", st["ydve"])))
        if debug and qs == 0:
            dcap = bd.vo(lambda e: e.tensor_copy(dbgsb[:, 0:SQ], ps3[:, 0:SQ]),
                         waits=(("pe", pe_rbx), bd.w("s_msc")), drain=True)
            dcap2 = bd.vo(lambda e, xsr=xs[:, toff:toff + SQ]:
                          e.tensor_copy(dbgsb[:, SQ:2 * SQ], xsr), drain=True)
            bd.dma("gpsimd", dbg_rb[:, :], dbgsb[:, 0:SQ], waits=(("dve", dcap),), sem="s_msc")
            bd.dma("gpsimd", dbg_cap[:, 0:SQ], dbgsb[:, SQ:2 * SQ], waits=(("dve", dcap2),),
                   sem="s_msc")
        for fc in range(NFC):
            dvq1 = bd.vo(lambda e, fc=fc,
                         xsr=xs[:, fc * TOKC + toff:fc * TOKC + toff + SQ]:
                         e.tensor_tensor(i32s[:, (fc % 2) * 512:(fc % 2) * 512 + SQ],
                                         xsr, ps3[:, 0:SQ], OP.mult),
                         waits=(("pe", pe_rbx),) + ((("act", st.get(("xqe", fc % 2), 0)),) if fc >= 2 else ()),
                         drain=fc >= 2)
            st[("xqe", fc % 2)] = bd.ao(lambda e, fc=fc:
                                        e.copy(xq[:, fc * SQ:(fc + 1) * SQ],
                                               i32s[:, (fc % 2) * 512:(fc % 2) * 512 + SQ]),
                                        waits=(("dve", dvq1),))
        act_xq = bd.cnt["act"]
        dve_xq = bd.cnt["dve"]
        if debug and qs == 0:
            bd.dma("gpsimd", dbg_xq[:, :], xq[:, :], waits=(("act", act_xq),), sem="s_msc")
        d3 = bd.vo(lambda e: e.tensor_scalar(rowm[0:1, 0:SQ], rows[0:1, 3 * QTW:3 * QTW + SQ],
                                             srow[0:1, 8:9], None, OP.mult),
                   waits=(("pe", pe_rbx),), drain=True)
        pe_cg = bd.mm(ps4[:, 0:SQ], onesr_f[0:1, :], rowm[0:1, 0:SQ], True, True,
                      waits=(("dve", d3), ("dve", st.get(("cmr", 0), 0)), ("dve", st.get(("cmr", 1), 0))))
        a_cg = bd.ao(lambda e: e.copy(csc[:, 0:SQ], ps4[:, 0:SQ]), waits=(("pe", pe_cg),))
        d4 = bd.vo(lambda e: e.tensor_scalar(rowm[0:1, QTW:QTW + SQ], rows[0:1, 3 * QTW:3 * QTW + SQ],
                                             srow[0:1, 9:10], None, OP.mult))
        pe_cu = bd.mm(ps5[:, 0:SQ], onesr_f[0:1, :], rowm[0:1, QTW:QTW + SQ], True, True,
                      waits=(("dve", d4),))
        a_cu = bd.ao(lambda e: e.copy(csc[:, SQ:2 * SQ], ps5[:, 0:SQ]), waits=(("pe", pe_cu),))
        st[("xb", 0)] = a_cg
        st[("xb", 1)] = a_cu
        st["rb_pe"] = pe_cu
        for r in range(NC):
            for hf in range(2):
                cg_ = bd.dma("sync", wsA[:, 0:NFC * I8h].rearrange("pp (c e) -> pp c e", c=NFC),
                             wT_out[r, OFF_M[4] + hf * (PR_GU // 2):OFF_M[4] + (hf + 1) * (PR_GU // 2)]
                             .rearrange("(pp c e) -> pp c e", pp=128, c=NFC),
                             waits=(("cc", cc_wT), ("pe", wgg[0])), sem="s_wg0")
                for il in range(I2h):
                    sb = pp[il % 2]
                    for fc in range(NFC):
                        bd.mm(sb[:, 0:SQ], wsA[:, fc * I8h + il * 128:fc * I8h + il * 128 + 128],
                              xq[:, fc * SQ:(fc + 1) * SQ], fc == 0, fc == NFC - 1,
                              waits=((("s_wg0", cg_), ("act", act_xq),
                                      st.get(("bk", il % 2), ("dve", 0))) if fc == 0 else ()))
                    pe_m = bd.cnt["pe"]
                    dvg = bd.vo(lambda e, sb=sb, sl=stg[:, (il % 2) * SQ:(il % 2) * SQ + SQ]:
                                e.tensor_tensor(sl, sb[:, 0:SQ], csc[:, 0:SQ], OP.mult),
                                waits=(("pe", pe_m), ("act", a_cg),
                                       ("act", st.get(("ge", il % 2), 0))))
                    st[("bk", il % 2)] = ("dve", dvg)
                    st[("ge", il % 2)] = bd.ao(lambda e, il=il,
                                               sl=stg[:, (il % 2) * SQ:(il % 2) * SQ + SQ]:
                                               e.activation(gsi[:, il * SQ:(il + 1) * SQ], sl,
                                                            AF.Silu), waits=(("dve", dvg),))
                wgg[0] = bd.cnt["pe"]
                cu_ = bd.dma("sync", wsB[:, 0:NFC * I8h].rearrange("pp (c e) -> pp c e", c=NFC),
                             wT_out[r, OFF_M[5] + hf * (PR_GU // 2):OFF_M[5] + (hf + 1) * (PR_GU // 2)]
                             .rearrange("(pp c e) -> pp c e", pp=128, c=NFC),
                             waits=(("cc", cc_wT), ("pe", wgg[1])), sem="s_wg1")
                for il in range(I2h):
                    ic = r * 2 * I2h + hf * I2h + il
                    sb = pp[il % 2]
                    for fc in range(NFC):
                        bd.mm(sb[:, 0:SQ], wsB[:, fc * I8h + il * 128:fc * I8h + il * 128 + 128],
                              xq[:, fc * SQ:(fc + 1) * SQ], fc == 0, fc == NFC - 1,
                              waits=((("s_wg1", cu_), ("dve", st.get(("ue", il % 2), 0)),
                                      st.get(("bk", il % 2), ("dve", 0)),
                                      ("act", st.get(("ge", il % 2), 0))) if fc == 0 else ()))
                    pe_m = bd.cnt["pe"]
                    dvu = bd.vo(lambda e, sb=sb: e.tensor_tensor(stg2[:, 0:SQ], sb[:, 0:SQ],
                                                                 csc[:, SQ:2 * SQ], OP.mult),
                                waits=(("pe", pe_m), ("act", a_cu), ("act", st.get(("ge", il % 2), 0))),
                                drain=True)
                    st[("bk", il % 2)] = ("dve", dvu)
                    st[("ue", il % 2)] = bd.vo(
                        lambda e, ic=ic, il=il: e.tensor_tensor(gu[:, ic * SQ:(ic + 1) * SQ],
                                                                gsi[:, il * SQ:(il + 1) * SQ],
                                                                stg2[:, 0:SQ], OP.mult), drain=True)
                wgg[1] = bd.cnt["pe"]
        d2g = colmax_to_row(lambda i: gu[:, i * SQ:(i + 1) * SQ], NIC, 4 * QTW)
        st["rb_pe"] = pe_rbg = bd.mm(ps3[:, 0:SQ], onesr_f[0:1, :], rowm[0:1, 0:SQ], True, True,
                                     waits=(("dve", d2g), ("dve", dve_xq)))
        for ic in range(NIC):
            dvq1 = bd.vo(lambda e, ic=ic: e.tensor_tensor(i32s[:, (ic % 2) * 512:(ic % 2) * 512 + SQ],
                                                          gu[:, ic * SQ:(ic + 1) * SQ],
                                                          ps3[:, 0:SQ], OP.mult),
                         waits=(("pe", pe_rbg),) + ((("act", st.get(("gqe", ic % 2), 0)),) if ic >= 2 else ()),
                         drain=ic >= 2)
            st[("gqe", ic % 2)] = bd.ao(lambda e, ic=ic:
                                        e.copy(gu[:, ic * SQ:(ic + 1) * SQ],
                                               i32s[:, (ic % 2) * 512:(ic % 2) * 512 + SQ]),
                                        waits=(("dve", dvq1),))
        act_guq = bd.cnt["act"]
        if debug and qs == 0:
            bd.dma("gpsimd", dbg_gu[:, :], gu[:, :], waits=(("act", act_guq),), sem="s_msc")
            bd.dma("gpsimd", dbg_gs[0:1, 0:2 * QTW], rows[0:1, 3 * QTW:5 * QTW], sem="s_msc")
        d5 = bd.vo(lambda e: e.tensor_scalar(rowm[0:1, 0:SQ], rows[0:1, 4 * QTW:4 * QTW + SQ],
                                             srow[0:1, 10:11], None, OP.mult),
                   waits=(("pe", pe_rbg),), drain=True)
        dve_pre_cd = bd.cnt["dve"]
        pe_cd = bd.mm(ps4[:, 0:SQ], onesr_f[0:1, :], rowm[0:1, 0:SQ], True, True,
                      waits=(("dve", d5), ("dve", dve_pre_cd), ("act", a_cg)))
        a_cd = bd.ao(lambda e: e.copy(csc[:, 2 * SQ:3 * SQ], ps4[:, 0:SQ]), waits=(("pe", pe_cd),))
        st["rb_pe"] = pe_cd
        for r in range(NC):
            bufs_dn = (wsA, wsB)
            for fl2 in range(0, E2, 2):
                for ihf in range(2):
                    cd_ = bd.dma("sync", bufs_dn[ihf][:, 0:NICh * E8].rearrange("pp (c e) -> pp c e", c=NICh),
                                 wT_out[r, OFF_M[6] + ihf * (PR_D // 2):OFF_M[6] + (ihf + 1) * (PR_D // 2)]
                                 .rearrange("(pp c e) -> pp c e", pp=128, c=NICh),
                                 waits=(("cc", cc_wT), ("pe", wgg[ihf])), sem=f"s_wg{ihf}")
                    for fl in (fl2, fl2 + 1):
                        sb = pp[fl % 2]
                        for icl in range(NICh):
                            ic = ihf * NICh + icl
                            bd.mm(sb[:, 0:SQ],
                                  bufs_dn[ihf][:, icl * E8 + fl * 128:icl * E8 + fl * 128 + 128],
                                  gu[:, ic * SQ:(ic + 1) * SQ],
                                  ihf == 0 and icl == 0, ihf == 1 and icl == NICh - 1,
                                  waits=(((f"s_wg{ihf}", cd_), ("act", act_guq),
                                          st.get(("bk", fl % 2), ("dve", 0))) if icl == 0 else ()))
                    wgg[ihf] = bd.cnt["pe"]
                for fl in (fl2, fl2 + 1):
                    fc = r * E2 + fl
                    sb = pp[fl % 2]
                    pe_m = bd.cnt["pe"]
                    dvd = bd.vo(lambda e, sb=sb: e.tensor_tensor(stg[:, 0:SQ], sb[:, 0:SQ],
                                                                 csc[:, 2 * SQ:3 * SQ], OP.mult),
                                waits=(("pe", pe_m), ("act", a_cd)), drain=True)
                    st[("bk", fl % 2)] = ("dve", dvd)
                    st[("og", fl % 2)] = bd.vo(
                        lambda e, fc=fc,
                        xsr=xs[:, fc * TOKC + toff:fc * TOKC + toff + SQ],
                        dst=hb[:, fc * QTW:fc * QTW + SQ]:
                        e.scalar_tensor_tensor(dst, stg[:, 0:SQ], 1.0, xsr, OP.mult, OP.add), drain=True)
        layer_norm(toff, SQ)
    bd.free("wsA2", "wsB2", "xq", "gsi", "gu", "mfold")
    bd.fence(("scalar",), (bd.w("pe"), bd.w("dve")))

    otm = bd.sb("otm", (128, D), F32)
    for tt in range(NTT):
        for fc in range(NFC):
            t = ptr[fc % 2]
            pe_t = bd.mm(t[:, 0:128], xs[:, fc * TOKC + tt * 128:fc * TOKC + tt * 128 + 128],
                         ident_b[:, :], True, True,
                         waits=(("dve", st["lnprev"]), ("act", st.get(("xb", fc % 2), 0))) +
                               ((("s_out", st.get("odma", 0)),) if fc < 2 else ()))
            st[("xb", fc % 2)] = bd.ao(lambda e, t=t, fc=fc: e.copy(otm[:, fc * 128:(fc + 1) * 128],
                                                                    t[:, 0:128]),
                                       waits=(("pe", pe_t),))
        a_fin = bd.cnt["act"]
        st["odma"] = bd.dma("gpsimd", out_my[tt * 128:(tt + 1) * 128, :], otm[:, :],
                            waits=(("act", a_fin),), sem="s_out")
    final_out = bd.cnt["s_out"]
    bd.emit("gpsimd", lambda e: e.nop(), waits=(("s_out", final_out),))

    vals = sorted(bd.consts)
    assert len(vals) <= 64, len(vals)
    for i, v in enumerate(vals):
        nc.const_aps.register(F32, float(v), cpool[:, i:i + 1])
        nc.const_aps.register(BF16, float(v), cpool_b[:, i:i + 1])

    def memset_fn(e):
        for i, v in enumerate(vals):
            e.memset(cpool[:, i:i + 1], float(v))
            e.memset(cpool_b[:, i:i + 1], float(v))
        e.memset(ones_b[:, :], 1.0)
        e.memset(ones_f[:, :], 1.0)
        e.memset(srow[:, :], 0.0)
        e.drain()
        e.memset(srow[0:1, 15:16], 1.0)
        return e.memset(onesr_f[:, :], 1.0)

    bd.q["gpsimd"].insert(0, ((), memset_fn, "ini", 1))

    es = contextlib.ExitStack()
    sems = {}
    for s in SEMS:
        sems[s] = es.enter_context(nc.semaphore("sem_" + s))
    with es:
        with nc.Block() as block:
            for eng_name in ENGS:
                ops = bd.q[eng_name]

                def mk(ops=ops):
                    def body(e):
                        for waits, fn, inc, amt in ops:
                            for (s, v) in waits:
                                e.wait_ge(sems[s], v)
                            ins = fn(e)
                            if inc is not None and ins is not None:
                                ins.then_inc(sems[inc], amt)
                    return body
                getattr(block, eng_name)(mk())
    return nc


def kernel(**inputs):
    from concourse.bass_utils import run_bass_kernel_spmd
    prm = Prm()
    in_maps = host_prepare(prm, inputs)
    nc = build(prm)
    res = run_bass_kernel_spmd(nc, in_maps, core_ids=list(range(prm.NC)))
    out = np.empty((prm.B, prm.T, prm.D), np.float32)
    for c in range(prm.NC):
        b, hf = c // 2, c % 2
        out[b, hf * prm.TOKC:(hf + 1) * prm.TOKC, :] = np.asarray(res.results[c]["out_my"])
    return out
